# version 11
# baseline (speedup 1.0000x reference)
import contextlib

import numpy as np

import concourse.bass as bass
import concourse.mybir as mybir
from concourse.bass_utils import run_bass_kernel_spmd

F32 = mybir.dt.float32
BF16 = mybir.dt.bfloat16
I32 = mybir.dt.int32
AF = mybir.ActivationFunctionType
ALU = mybir.AluOpType
AX = mybir.AxisListType

NEG = -1.0e30
D = 1024

ENGINES = ("tensor", "vector", "scalar", "gpsimd", "sync")


class _Op:
    __slots__ = ("eng", "fn", "reads", "writes", "dma", "idx", "need_inc",
                 "deps", "dma_idx", "inc_no", "tag")

    def __init__(self, eng, fn, reads, writes, dma, idx, tag):
        self.eng, self.fn = eng, fn
        self.reads, self.writes = reads, writes
        self.dma, self.idx, self.tag = dma, idx, tag
        self.need_inc = False
        self.deps = set()
        self.dma_idx = None
        self.inc_no = None


class Plan:
    def __init__(self):
        self.ops = {e: [] for e in ENGINES}
        self.regions = {}
        self.fence_snaps = []
        self.buf_fence = {}
        self.cur_fence = -1

    def _reg(self, key):
        rg = self.regions.get(key)
        if rg is None:
            rg = [None, []]
            self.regions[key] = rg
        return rg

    def register_buf(self, name):
        self.buf_fence[name] = self.cur_fence

    def op(self, eng, fn, reads=(), writes=(), dma=False, tag=""):
        o = _Op(eng, fn, tuple(reads), tuple(writes), dma, len(self.ops[eng]), tag)
        for rk in o.reads:
            rg = self._reg(rk)
            if rg[0] is not None:
                o.deps.add((rg[0].eng, rg[0].idx, rg[0].dma))
        for wk in o.writes:
            rg = self._reg(wk)
            if rg[0] is not None:
                o.deps.add((rg[0].eng, rg[0].idx, rg[0].dma))
            for rd in rg[1]:
                o.deps.add((rd.eng, rd.idx, rd.dma))
            f = self.buf_fence.get(wk[0], -1)
            if f >= 0:
                snap = self.fence_snaps[f]
                for e2 in ENGINES:
                    n = snap[e2]
                    if n > 0:
                        prev = self.ops[e2][n - 1]
                        o.deps.add((e2, n - 1, prev.dma))
                        last_dmas = []
                        for i2 in range(n - 1, -1, -1):
                            op2 = self.ops[e2][i2]
                            if op2.dma:
                                last_dmas.append(i2)
                                if len(last_dmas) >= 12:
                                    break
                        for i2 in last_dmas:
                            o.deps.add((e2, i2, True))
        for rk in o.reads:
            self._reg(rk)[1].append(o)
        for wk in o.writes:
            rg = self._reg(wk)
            rg[0] = o
            rg[1] = []
        o.deps = {d for d in o.deps if not (d[0] == eng and d[1] == o.idx)}
        self.ops[eng].append(o)
        return o

    def fence(self):
        self.fence_snaps.append({e: len(self.ops[e]) for e in ENGINES})
        self.cur_fence = len(self.fence_snaps) - 1

    def resolve_and_emit(self, block, sems, dma_pools):
        for e in ENGINES:
            for o in self.ops[e]:
                for (e2, i2, isdma) in o.deps:
                    if not isdma and e2 != e:
                        self.ops[e2][i2].need_inc = True
        for e in ENGINES:
            n = nd = 0
            K = len(dma_pools[e])
            for o in self.ops[e]:
                if o.dma:
                    o.dma_idx = nd
                    nd += 1
                elif o.need_inc:
                    n += 1
                    o.inc_no = n

        def dma_wait(eng_name, j):
            K = len(dma_pools[eng_name])
            return dma_pools[eng_name][j % K], 16 * (j // K + 1)

        def emit_engine(eng_name, eng):
            watermark = {}
            drained_upto = -1
            K = len(dma_pools[eng_name])
            for o in self.ops[eng_name]:
                waits = {}
                need_drain = False
                for (e2, i2, isdma) in sorted(o.deps):
                    if e2 == eng_name and not isdma:
                        if i2 > drained_upto:
                            need_drain = True
                        continue
                    if isdma:
                        j = self.ops[e2][i2].dma_idx
                        sem, val = dma_wait(e2, j)
                        k = (e2, "d", j % len(dma_pools[e2]))
                    else:
                        dep = self.ops[e2][i2]
                        sem, val = sems[e2], dep.inc_no
                        k = (e2, "c")
                    if watermark.get(k, -1) < val:
                        if k in waits:
                            waits[k] = (sem, max(val, waits[k][1]))
                        else:
                            waits[k] = (sem, val)
                if need_drain:
                    eng.drain()
                    drained_upto = o.idx - 1
                if o.dma and o.dma_idx >= K:
                    sem, val = dma_pools[eng_name][o.dma_idx % K],                         16 * (o.dma_idx // K)
                    k = (eng_name, "d", o.dma_idx % K)
                    if watermark.get(k, -1) < val:
                        waits[k] = (sem, max(val, waits.get(k, (sem, 0))[1]))
                for k, (sem, val) in waits.items():
                    eng.wait_ge(sem, val)
                    watermark[k] = val
                inst = o.fn(eng)
                if o.dma:
                    inst.then_inc(dma_pools[eng_name][o.dma_idx % K], 16)
                elif o.need_inc:
                    inst.then_inc(sems[eng_name], 1)

        for name in ENGINES:
            getattr(block, name)(lambda eng, n=name: emit_engine(n, eng))


def r(buf, key=()):
    return (buf.name, key)


def sap(t, p0, pn, foff, dims):
    fs = 1
    for d in t.shape[1:]:
        fs *= d
    return bass.AP(t, p0 * fs + foff, [[fs, pn]] + [list(d) for d in dims])


CONSTS = [1e-5, 1e-20, 127.0, 0.5, 4.844187086458591, 2.384185791015625e-07, 0.7]


class Ctx:
    def __init__(self, S):
        self.S = S
        self.nc = bass.Bass("TRN2", target_bir_lowering=False)
        self.plan = Plan()
        self.stack = contextlib.ExitStack()
        self.input_names = []
        self.cvals = None
        self.ones_r = None
        self.ones1 = None
        self.halfsh = None

    def cv(self, value):
        j = CONSTS.index(value)
        return self.cvals[:, j:j + 1]

    def dram_in(self, name, shape, dtype=F32):
        t = self.nc.dram_tensor(name, list(shape), dtype, kind="ExternalInput")
        self.input_names.append(name)
        return t.ap()

    def dram_out(self, name, shape, dtype=F32):
        return self.nc.dram_tensor(name, list(shape), dtype, kind="ExternalOutput").ap()

    def dram_tmp(self, name, shape, dtype=F32):
        return self.nc.dram_tensor(name, list(shape), dtype, kind="Internal").ap()

    def sb(self, name, shape, dtype, stack=None):
        h = (stack or self.stack).enter_context(
            self.nc.sbuf_tensor(name, list(shape), dtype))
        self.plan.register_buf(name)
        return h

    def ps(self, name, shape, dtype=F32, stack=None):
        h = (stack or self.stack).enter_context(
            self.nc.psum_tensor(name, list(shape), dtype))
        self.plan.register_buf(name)
        return h

    def dma(self, eng, out_ap, in_ap, writes, reads=(), tag="dma"):
        return self.plan.op(eng, lambda e: e.dma_start(out=out_ap, in_=in_ap),
                            reads=reads, writes=writes, dma=True, tag=tag)


def emit_ln_stats(C, src, src_rd, pstat, ones_r, m_sb, r_sb, tmpb, tag, c0=0):
    P = C.plan
    for pt in range(8):
        P.op("scalar",
             lambda e, pt=pt: e.activation(tmpb[:, :], src[:, pt, c0:c0 + 512],
                                           AF.Copy),
             reads=src_rd(pt), writes=[r(tmpb)], tag=tag + "c")
        P.op("tensor",
             lambda e, pt=pt: e.matmul(pstat[:, :], ones_r[:, :], tmpb[:, :],
                                       start=(pt == 0), stop=(pt == 7)),
             reads=[r(tmpb), r(ones_r)], writes=[r(pstat)], tag=tag + "ms")
    P.op("scalar", lambda e: e.activation(m_sb[:, :], pstat[:, :], AF.Copy),
         reads=[r(pstat)], writes=[r(m_sb)], tag=tag + "mc")
    for pt in range(8):
        P.op("scalar",
             lambda e, pt=pt: e.activation(tmpb[:, :], src[:, pt, c0:c0 + 512],
                                           AF.Square),
             reads=src_rd(pt), writes=[r(tmpb)], tag=tag + "q")
        P.op("tensor",
             lambda e, pt=pt: e.matmul(pstat[:, :], ones_r[:, :], tmpb[:, :],
                                       start=(pt == 0), stop=(pt == 7)),
             reads=[r(tmpb), r(ones_r)], writes=[r(pstat)], tag=tag + "qs")
    P.op("vector", lambda e: e.tensor_mul(r_sb[:, :], m_sb[:, :], m_sb[:, :]),
         reads=[r(m_sb)], writes=[r(r_sb)], tag=tag + "mm")
    P.op("vector", lambda e: e.tensor_sub(r_sb[:, :], pstat[:, :], r_sb[:, :]),
         reads=[r(pstat), r(r_sb)], writes=[r(r_sb)], tag=tag + "var")
    P.op("scalar", lambda e: e.activation(r_sb[:, :], r_sb[:, :], AF.Ln,
                                          bias=C.cv(1e-5)),
         reads=[r(r_sb), r(C.cvals)], writes=[r(r_sb)], tag=tag + "ln")
    P.op("scalar", lambda e: e.activation(r_sb[:, :], r_sb[:, :], AF.Exp, scale=-0.5),
         reads=[r(r_sb)], writes=[r(r_sb)], tag=tag + "ex")


def emit_l0a(C, x_src_dram, x_dst_dram, src_nm, dst_nm):
    P, nc, S = C.plan, C.nc, C.S
    NT, NG = S // 128, S // 512
    st = contextlib.ExitStack()

    wqkv = C.dram_in("wqkv", [D, 3 * D])
    wo_d = C.dram_in("wo", [D, D])
    bq_d = C.dram_in("bq_sb", [128, 8])
    bk_d = C.dram_in("bk_sb", [128, 8])
    bo_d = C.dram_in("bo_sb", [128, 8])
    bvn_d = C.dram_in("bvn_bc", [128, 16 * 64])
    mB_d = C.dram_in("maskB", [128, 128])
    mA_d = C.dram_in("maskA", [32, 128])
    mC_d = C.dram_in("maskC", [32, 128])

    ones_r = C.ones_r
    ones1 = C.ones1
    mB = C.sb("mB_s", [128, 128], F32, st)
    mA = C.sb("mA_s", [128, 128], F32, st)
    mC = C.sb("mC_s", [128, 128], F32, st)
    bq = C.sb("bq_s", [128, 8], F32, st)
    bk = C.sb("bk_s", [128, 8], F32, st)
    bo = C.sb("bo_s", [128, 8], F32, st)
    bvn = C.sb("bvn_s", [128, 16 * 64], BF16, st)
    wqs = C.sb("wqs", [128, 8, 3 * D], BF16, st)
    wos = C.sb("wos", [128, 8, D], BF16, st)
    xg = [C.sb(f"xg{i}", [128, 8, 512], F32, st) for i in range(2)]
    ln1 = [C.sb(f"l0ln1_{i}", [128, 8, 512], BF16, st) for i in range(2)]
    lntmp = C.sb("l0lntmp", [128, 512], F32, st)
    qT = C.sb("qT", [128, 8, 8 * 128], BF16, st)
    kT = C.sb("kT", [128, 8, 8 * 128], BF16, st)
    vn = C.sb("vn", [128, 8, 16 * 64], BF16, st)
    vals = C.sb("vals", [128, 8, 2 * 512], BF16, st)
    exB = C.sb("exB", [128, 8 * 128], BF16, st)
    exAC = C.sb("exAC", [128, 8 * 128], BF16, st)
    sums = C.sb("sums", [128, 8 * 128], BF16, st)
    rinv = C.sb("rinv", [128, 8 * 128], BF16, st)
    m_sb = C.sb("l0m", [128, 512], F32, st)
    r_sb = C.sb("l0r", [128, 512], F32, st)
    m2_sb = C.sb("l0m2", [128, 512], F32, st)
    r2_sb = C.sb("l0r2", [128, 512], F32, st)
    tmpb = C.sb("l0tmpb", [128, 512], BF16, st)
    oT = C.sb("l0oT", [128, 8, 512], BF16, st)

    pSC = C.ps("pSC", [128, 1024], F32, st)
    pPV = C.ps("pPV", [128, 1024], F32, st)
    pBC = C.ps("pBC", [128, 1024], F32, st)
    pA1 = C.ps("pA1", [128, 512], F32, st)
    pA2 = C.ps("pA2", [128, 512], F32, st)

    C.dma("sync", mB[:, :], mB_d, [r(mB)])
    C.dma("sync", mA[96:128, :], mA_d, [r(mA)])
    C.dma("sync", mC[0:32, :], mC_d, [r(mC)])
    C.dma("sync", bq[:, :], bq_d, [r(bq)])
    C.dma("sync", bk[:, :], bk_d, [r(bk)])
    C.dma("sync", bo[:, :], bo_d, [r(bo)])
    C.dma("gpsimd", bvn[:, :], bvn_d, [r(bvn)])
    wq_r = wqkv.rearrange("(a p) c -> p a c", p=128)
    for h4 in range(4):
        C.dma("gpsimd", wqs[:, 2 * h4:2 * h4 + 2, :], wq_r[:, 2 * h4:2 * h4 + 2, :],
              [r(wqs, h4)], tag="wqkv")
    wo_r = wo_d.rearrange("(a p) c -> p a c", p=128)
    C.dma("gpsimd", wos[:, :, :], wo_r, [r(wos)], tag="wo")

    x_src_r = x_src_dram.rearrange("(a p) c -> p a c", p=128)
    x_dst_r = x_dst_dram.rearrange("(a p) c -> p a c", p=128)

    def ln1_group(g):
        gb = g % 2
        C.dma("sync", xg[gb][:, :, :], x_src_r[:, :, g * 512:(g + 1) * 512],
              [r(xg[gb])], reads=[("dram:" + src_nm, g)], tag=f"xg{g}")
        emit_ln_stats(C, xg[gb], lambda pt: [r(xg[gb])], pA2, ones_r, m_sb, r_sb,
                      tmpb, tag="ln1")
        for pt in range(8):
            P.op("vector",
                 lambda e, pt=pt, gb=gb: e.tensor_sub(
                     lntmp[:, :], xg[gb][:, pt, :], m_sb[:, :]),
                 reads=[r(xg[gb]), r(m_sb)], writes=[r(lntmp)], tag="l1s")
            P.op("vector",
                 lambda e, pt=pt, gb=gb: e.tensor_mul(
                     ln1[gb][:, pt, :], lntmp[:, :], r_sb[:, :]),
                 reads=[r(lntmp), r(r_sb)], writes=[r(ln1[gb])], tag="l1m")

    def qkv_group(g):
        gb = g % 2
        s0 = (4 * g) % 8
        for part, dst, bias, base in (("q", qT, bq, 0), ("k", kT, bk, D)):
            for i in range(8):
                for k in range(8):
                    P.op("tensor",
                         lambda e, i=i, k=k, gb=gb, base=base: e.matmul(
                             pA1[:, :], wqs[:, k, base + i * 128:base + (i + 1) * 128],
                             ln1[gb][:, k, :],
                             start=(k == 0), stop=(k == 7)),
                         reads=[r(wqs, k // 2), r(ln1[gb])], writes=[r(pA1)],
                         tag=f"{part}mm")
                P.op("scalar",
                     lambda e, i=i, s0=s0, dst=dst, bias=bias: e.activation(
                         dst[:, i, s0 * 128:(s0 + 4) * 128], pA1[:, :], AF.Identity,
                         bias=bias[:, i:i + 1]),
                     reads=[r(pA1), r(bias)],
                     writes=[r(dst, (4 * g + w) % 8) for w in range(4)],
                     tag=f"{part}ev")
        for w in range(4):
            t = 4 * g + w
            sl = t % 8
            for ng in range(2):
                for k in range(8):
                    P.op("tensor",
                         lambda e, ng=ng, k=k, gb=gb, w=w: e.matmul(
                             pA2[:, :], ln1[gb][:, k, w * 128:(w + 1) * 128],
                             wqs[:, k, 2 * D + ng * 512:2 * D + (ng + 1) * 512],
                             start=(k == 0), stop=(k == 7)),
                         reads=[r(wqs, k // 2), r(ln1[gb])], writes=[r(pA2)],
                         tag="vmm")
                P.op("vector",
                     lambda e, sl=sl, ng=ng: e.tensor_add(
                         vn[:, sl, ng * 512:(ng + 1) * 512], pA2[:, :],
                         bvn[:, ng * 512:(ng + 1) * 512]),
                     reads=[r(pA2), r(bvn)], writes=[r(vn, sl)], tag="vev")

    def attn_tile(t):
        sl = t % 8
        has_a, has_c = t > 0, t < NT - 1
        g, w = t // 4, t % 4
        gsl = g % 2
        for hg in range(2):
            hbase = hg * 8
            base = 0 if hg == 0 else 64
            for hl in range(8):
                h = hbase + hl
                po = (h % 2) * 64
                kap = sap(kT, po, 64, (h // 2) * 1024 + sl * 128, [[1, 128]])
                qap = sap(qT, po, 64, (h // 2) * 1024 + sl * 128, [[1, 128]])
                P.op("tensor",
                     lambda e, kap=kap, qap=qap, hl=hl, po=po: e.matmul(
                         pSC[:, hl * 128:(hl + 1) * 128], kap, qap,
                         start=True, stop=True),
                     reads=[r(kT, sl), r(qT, sl)], writes=[r(pSC)], tag="scB")
            P.op("vector",
                 lambda e: e.tensor_add(
                     pSC[:, :], pSC[:, :],
                     bass.AP(mB, 0, [[128, 128], [0, 8], [1, 128]])),
                 reads=[r(pSC), r(mB)], writes=[r(pSC)], tag="mskB")
            P.op("scalar", lambda e: e.activation(exB[:, :], pSC[:, :], AF.Exp,
                                                  scale=0.125),
                 reads=[r(pSC)], writes=[r(exB)], tag="expB")
            for hl in range(8):
                h = hbase + hl
                po = (h % 2) * 64
                if has_a:
                    kap = sap(kT, po, 64, (h // 2) * 1024 + ((t - 1) % 8) * 128 + 96,
                              [[1, 32]])
                    qap = sap(qT, po, 64, (h // 2) * 1024 + sl * 128, [[1, 128]])
                    P.op("tensor",
                         lambda e, kap=kap, qap=qap, hl=hl, po=po: e.matmul(
                             sap(pSC, 96, 32, hl * 128, [[1, 128]]), kap, qap,
                             start=True, stop=True, tile_position=(po, 96)),
                         reads=[r(kT, (t - 1) % 8), r(qT, sl)],
                         writes=[r(pSC)], tag="scA")
                if has_c:
                    kap = sap(kT, po, 64, (h // 2) * 1024 + ((t + 1) % 8) * 128,
                              [[1, 32]])
                    qap = sap(qT, po, 64, (h // 2) * 1024 + sl * 128, [[1, 128]])
                    P.op("tensor",
                         lambda e, kap=kap, qap=qap, hl=hl, po=po: e.matmul(
                             sap(pSC, 0, 32, hl * 128, [[1, 128]]), kap, qap,
                             start=True, stop=True, tile_position=(po, 0)),
                         reads=[r(kT, (t + 1) % 8), r(qT, sl)],
                         writes=[r(pSC)], tag="scC")
            if has_a:
                P.op("vector",
                     lambda e: e.tensor_add(
                         sap(pSC, 96, 32, 0, [[128, 8], [1, 128]]),
                         sap(pSC, 96, 32, 0, [[128, 8], [1, 128]]),
                         bass.AP(mA, 96 * 128, [[128, 32], [0, 8], [1, 128]])),
                     reads=[r(pSC), r(mA)], writes=[r(pSC)], tag="mskA")
                P.op("scalar",
                     lambda e: e.activation(exAC[96:128, :], pSC[96:128, :], AF.Exp,
                                            scale=0.125),
                     reads=[r(pSC)], writes=[r(exAC, "A")], tag="expA")
            if has_c:
                P.op("vector",
                     lambda e: e.tensor_add(
                         sap(pSC, 0, 32, 0, [[128, 8], [1, 128]]),
                         sap(pSC, 0, 32, 0, [[128, 8], [1, 128]]),
                         bass.AP(mC, 0, [[128, 32], [0, 8], [1, 128]])),
                     reads=[r(pSC), r(mC)], writes=[r(pSC)], tag="mskC")
                P.op("scalar",
                     lambda e: e.activation(exAC[0:32, :], pSC[0:32, :], AF.Exp,
                                            scale=0.125),
                     reads=[r(pSC)], writes=[r(exAC, "C")], tag="expC")
            chunks = [("B", exB, 0, 128, sl)]
            if has_a:
                chunks.append(("A", exAC, 96, 32, (t - 1) % 8))
            if has_c:
                chunks.append(("C", exAC, 0, 32, (t + 1) % 8))
            for half in range(2):
                for ci, (nm, exb, rb, rn, _ksl) in enumerate(chunks):
                    P.op("tensor",
                         lambda e, exb=exb, rb=rb, rn=rn, half=half,
                                ci=ci, last=(ci == len(chunks) - 1): e.matmul(
                             sap(pBC, 96, 1, half * 512, [[1, 512]]),
                             sap(ones1, rb, rn, 0, [[1, 1]]),
                             sap(exb, rb, rn, half * 512, [[1, 512]]),
                             start=(ci == 0), stop=last, tile_position=(rb, 96)),
                         reads=[r(exb, nm if exb is exAC else ()), r(ones1)],
                         writes=[r(pBC)], tag="sums")
            P.op("vector", lambda e: e.tensor_copy(sums[96:97, :], pBC[96:97, :]),
                 reads=[r(pBC)], writes=[r(sums)], tag="sumcp")
            for half in range(2):
                P.op("tensor",
                     lambda e, half=half, base=base: e.matmul(
                         sap(pBC, base, 64, half * 512, [[1, 512]]),
                         sap(ones1, 96, 1, 0, [[1, 64]]),
                         sap(sums, 96, 1, half * 512, [[1, 512]]),
                         start=True, stop=True, tile_position=(96, base)),
                     reads=[r(sums), r(ones1)], writes=[r(pBC)], tag="bcast")
            P.op("scalar",
                 lambda e, base=base: e.activation(
                     rinv[base:base + 64, :], pBC[base:base + 64, :], AF.Ln),
                 reads=[r(pBC)], writes=[r(rinv)], tag="rln")
            P.op("scalar",
                 lambda e, base=base: e.activation(
                     rinv[base:base + 64, :], rinv[base:base + 64, :], AF.Exp,
                     scale=-1.0),
                 reads=[r(rinv)], writes=[r(rinv)], tag="rex")
            for hl in range(8):
                h = hbase + hl
                vcol = h * 64
                for ci, (nm, exb, rb, rn, ksl) in enumerate(chunks):
                    P.op("tensor",
                         lambda e, exb=exb, rb=rb, rn=rn, ksl=ksl, vcol=vcol, hl=hl,
                                base=base, ci=ci,
                                last=(ci == len(chunks) - 1): e.matmul(
                             sap(pPV, base, 64, hl * 128, [[1, 128]]),
                             sap(vn, rb, rn, ksl * 1024 + vcol, [[1, 64]]),
                             sap(exb, rb, rn, hl * 128, [[1, 128]]),
                             start=(ci == 0), stop=last, tile_position=(rb, base)),
                         reads=[r(vn, ksl), r(exb, nm if exb is exAC else ())],
                         writes=[r(pPV)], tag="pv")
            vo = gsl * 512 + w * 128
            P.op("vector",
                 lambda e, base=base, vo=vo: e.tensor_mul(
                     sap(vals, base, 64, vo, [[1024, 8], [1, 128]]),
                     sap(pPV, base, 64, 0, [[128, 8], [1, 128]]),
                     sap(rinv, base, 64, 0, [[128, 8], [1, 128]])),
                 reads=[r(pPV), r(rinv)], writes=[r(vals, (gsl, w))],
                 tag="valev")

    def wo_group(g):
        gsl = g % 2
        for j in range(8):
            for k in range(8):
                P.op("tensor",
                     lambda e, j=j, k=k, gsl=gsl: e.matmul(
                         pA1[:, :], wos[:, k, j * 128:(j + 1) * 128],
                         vals[:, k, gsl * 512:(gsl + 1) * 512],
                         start=(k == 0), stop=(k == 7)),
                     reads=[r(wos)] + [r(vals, (gsl, w)) for w in range(4)],
                     writes=[r(pA1)], tag="womm")
            P.op("scalar",
                 lambda e, j=j: e.activation(oT[:, j, :], pA1[:, :], AF.Identity,
                                             bias=bo[:, j:j + 1]),
                 reads=[r(pA1), r(bo)], writes=[r(oT, j)], tag="woev")

    def ln2_add_group(g):
        xbuf = xg[g % 2]
        emit_ln_stats(C, oT, lambda pt: [r(oT, pt)], pA2, ones_r, m2_sb, r2_sb,
                      tmpb, tag="ln2")
        for pt in range(8):
            P.op("vector",
                 lambda e, pt=pt: e.tensor_sub(lntmp[:, :], oT[:, pt, :], m2_sb[:, :]),
                 reads=[r(oT, pt), r(m2_sb)], writes=[r(lntmp)], tag="a1")
            P.op("vector",
                 lambda e, pt=pt: e.tensor_mul(lntmp[:, :], lntmp[:, :], r2_sb[:, :]),
                 reads=[r(lntmp), r(r2_sb)], writes=[r(lntmp)], tag="a2")
            P.op("vector",
                 lambda e, pt=pt, xbuf=xbuf: e.tensor_add(
                     xbuf[:, pt, :], xbuf[:, pt, :], lntmp[:, :]),
                 reads=[r(lntmp), r(xbuf)], writes=[r(xbuf)], tag="a3")
        C.dma("sync", x_dst_r[:, :, g * 512:(g + 1) * 512], xbuf[:, :, :],
              [("dram:" + dst_nm, g)], reads=[r(xbuf)], tag=f"xo{g}")

    def attn_batch(g):
        if g == 0:
            ts = list(range(0, min(3, NT)))
        else:
            ts = list(range(4 * g - 1, min(4 * g + 3, NT)))
        if g == NG - 1:
            ts += list(range(max(4 * g + 3, ts[-1] + 1 if ts else 0), NT))
        for t in ts:
            attn_tile(t)

    ln1_group(0)
    qkv_group(0)
    attn_batch(0)
    for g in range(1, NG):
        ln1_group(g)
        qkv_group(g)
        attn_batch(g)
        wo_group(g - 1)
        ln2_add_group(g - 1)
    wo_group(NG - 1)
    ln2_add_group(NG - 1)

    P.fence()
    st.close()


def emit_l0f(C, x_src_dram, x_dst_dram, src_nm, dst_nm):
    P, nc, S = C.plan, C.nc, C.S
    NT, NG = S // 128, S // 512
    st = contextlib.ExitStack()

    gw_d = C.dram_in("gate_wT", [D, 16])
    gb_d = C.dram_in("gbias_bc", [128, 16])
    esel_d = C.dram_in("esel", [16, 16 * 128], BF16)
    idb_d = C.dram_in("ident_bf", [128, 128], BF16)
    wsg_d = C.dram_in("wsg", [D, D])
    wsu_d = C.dram_in("wsu", [D, D])
    wsd_d = C.dram_in("wsd", [D, D])
    wg_d = C.dram_in("wg_t", [16, D, 512])
    wu_d = C.dram_in("wu_t", [16, D, 512])
    wd_d = C.dram_in("wd_t", [16, 512, D])

    ln1f = C.sb("ln1f", [128, 8, S], BF16, st)
    y_sb = C.sb("y_sb", [128, 8, S], F32, st)
    cwT = C.sb("cwT", [128, S], BF16, st)
    identb = C.sb("identb_s", [128, 128], BF16, st)
    gbias = C.sb("gbias_s", [128, 16], F32, st)
    gws = C.sb("gws", [128, 8, 16], BF16, st)
    xg2 = [C.sb(f"xg2_{i}", [128, 8, 512], F32, st) for i in range(2)]
    lnm = C.sb("f0m", [128, 512], F32, st)
    lnr = C.sb("f0r", [128, 512], F32, st)
    lnm2 = C.sb("f0m2", [128, 512], F32, st)
    lnr2 = C.sb("f0r2", [128, 512], F32, st)
    tmpb = C.sb("f0tmpb", [128, 512], BF16, st)
    lnt = C.sb("f0lnt", [128, 512], F32, st)

    C.dma("gpsimd", gws[:, :, :], gw_d.rearrange("(a p) c -> p a c", p=128),
          [r(gws)], tag="gw")
    C.dma("sync", gbias[:, :], gb_d, [r(gbias)])
    C.dma("sync", identb[:, :], idb_d, [r(identb)])

    x_src_r = x_src_dram.rearrange("(a p) c -> p a c", p=128)
    x_dst_r = x_dst_dram.rearrange("(a p) c -> p a c", p=128)

    stA = contextlib.ExitStack()
    wsg = C.sb("wsg_s", [128, 8, D], BF16, stA)
    wsu = C.sb("wsu_s", [128, 8, D], BF16, stA)
    wsd = C.sb("wsd_s", [128, 8, D], BF16, stA)
    sh_g = C.sb("sh_g", [128, 8, 512], BF16, stA)
    pF1 = C.ps("pF1", [128, 512], F32, stA)
    pF2 = C.ps("pF2", [128, 512], F32, stA)
    pF3 = C.ps("pF3", [128, 512], F32, stA)
    sco = C.sb("sco", [128, 16], F32, stA)
    sfc = C.sb("sfc", [128, 16], F32, stA)
    msk = C.sb("msk", [128, 16], F32, stA)
    sel = C.sb("sel", [128, 16], F32, stA)
    oneh = C.sb("oneh", [128, 16], F32, stA)
    emk = C.sb("emk", [128, 16], F32, stA)
    p8a = C.sb("p8a", [128, 8], F32, stA)
    p8b = C.sb("p8b", [128, 8], F32, stA)
    p4a = C.sb("p4a", [128, 4], F32, stA)
    p4b = C.sb("p4b", [128, 4], F32, stA)
    p4c = C.sb("p4c", [128, 4], F32, stA)
    p4d = C.sb("p4d", [128, 4], F32, stA)
    s1a = C.sb("s1a", [128, 1], F32, stA)
    s1b = C.sb("s1b", [128, 1], F32, stA)
    s1c = C.sb("s1c", [128, 1], F32, stA)
    cwb = C.sb("cwb", [128, 16], BF16, stA)

    C.dma("gpsimd", wsg[:, :, :], wsg_d.rearrange("(a p) c -> p a c", p=128),
          [r(wsg)], tag="wsg")
    C.dma("gpsimd", wsu[:, :, :], wsu_d.rearrange("(a p) c -> p a c", p=128),
          [r(wsu)], tag="wsu")
    C.dma("gpsimd", wsd[:, :, :], wsd_d.rearrange("(a p) c -> p a c", p=128),
          [r(wsd)], tag="wsd")

    def ln1_group(g):
        gb = g % 2
        C.dma("sync", xg2[gb][:, :, :], x_src_r[:, :, g * 512:(g + 1) * 512],
              [r(xg2[gb])], reads=[("dram:" + src_nm, g)], tag=f"fxg{g}")
        emit_ln_stats(C, xg2[gb], lambda pt: [r(xg2[gb])], pF1, C.ones_r, lnm, lnr,
                      tmpb, tag="fln1")
        for pt in range(8):
            P.op("vector",
                 lambda e, pt=pt, gb=gb: e.tensor_sub(
                     lnt[:, :], xg2[gb][:, pt, :], lnm[:, :]),
                 reads=[r(xg2[gb]), r(lnm)], writes=[r(lnt)], tag="f1s")
            P.op("vector",
                 lambda e, pt=pt, gb=gb, g=g: e.tensor_mul(
                     ln1f[:, pt, g * 512:(g + 1) * 512], lnt[:, :], lnr[:, :]),
                 reads=[r(lnt), r(lnr)], writes=[r(ln1f, (g, pt))], tag="f1m")

    def routing_tile(t):
        g = t // 4
        for k in range(8):
            P.op("tensor",
                 lambda e, k=k, t=t: e.matmul(
                     pF1[:, 0:16], ln1f[:, k, t * 128:(t + 1) * 128], gws[:, k, :],
                     start=(k == 0), stop=(k == 7)),
                 reads=[r(ln1f, (g, k)), r(gws)], writes=[r(pF1)], tag="gmm")
        P.op("scalar", lambda e: e.activation(sco[:, :], pF1[:, 0:16], AF.Sigmoid),
             reads=[r(pF1)], writes=[r(sco)], tag="sig")
        P.op("vector", lambda e: e.tensor_add(sfc[:, :], sco[:, :], gbias[:, :]),
             reads=[r(sco), r(gbias)], writes=[r(sfc)], tag="sfc")
        ev = lambda b, o, n, s: sap(b, 0, 128, o, [[s, n]])
        P.op("vector", lambda e: e.tensor_max(p8a[:, :], ev(sfc, 0, 8, 2),
                                              ev(sfc, 1, 8, 2)),
             reads=[r(sfc)], writes=[r(p8a)], tag="hi")
        P.op("vector", lambda e: e.tensor_tensor(p8b[:, :], ev(sfc, 0, 8, 2),
                                                 ev(sfc, 1, 8, 2), ALU.min),
             reads=[r(sfc)], writes=[r(p8b)], tag="lo")
        P.op("vector", lambda e: e.tensor_max(p4a[:, :], ev(p8a, 0, 4, 2),
                                              ev(p8a, 1, 4, 2)),
             reads=[r(p8a)], writes=[r(p4a)], tag="t1")
        P.op("vector", lambda e: e.tensor_tensor(p4b[:, :], ev(p8a, 0, 4, 2),
                                                 ev(p8a, 1, 4, 2), ALU.min),
             reads=[r(p8a)], writes=[r(p4b)], tag="mn")
        P.op("vector", lambda e: e.tensor_max(p4c[:, :], ev(p8b, 0, 4, 2),
                                              ev(p8b, 1, 4, 2)),
             reads=[r(p8b)], writes=[r(p4c)], tag="ml")
        P.op("vector", lambda e: e.tensor_max(p4b[:, :], p4b[:, :], p4c[:, :]),
             reads=[r(p4b), r(p4c)], writes=[r(p4b)], tag="sec")
        P.op("vector", lambda e: e.tensor_add(p4a[:, :], p4a[:, :], p4b[:, :]),
             reads=[r(p4a), r(p4b)], writes=[r(p4a)], tag="g2")
        P.op("vector", lambda e: e.tensor_max(s1a[:, :], p4a[:, 0:1], p4a[:, 1:2]),
             reads=[r(p4a)], writes=[r(s1a)], tag="a1")
        P.op("vector", lambda e: e.tensor_max(s1b[:, :], p4a[:, 2:3], p4a[:, 3:4]),
             reads=[r(p4a)], writes=[r(s1b)], tag="b1")
        P.op("vector", lambda e: e.tensor_tensor(s1a[:, :], s1a[:, :], s1b[:, :],
                                                 ALU.min),
             reads=[r(s1a), r(s1b)], writes=[r(s1a)], tag="u")
        P.op("vector", lambda e: e.tensor_tensor(s1b[:, :], p4a[:, 0:1], p4a[:, 1:2],
                                                 ALU.min),
             reads=[r(p4a), r(s1b)], writes=[r(s1b)], tag="c1")
        P.op("vector", lambda e: e.tensor_tensor(s1c[:, :], p4a[:, 2:3], p4a[:, 3:4],
                                                 ALU.min),
             reads=[r(p4a)], writes=[r(s1c)], tag="d1")
        P.op("vector", lambda e: e.tensor_max(s1b[:, :], s1b[:, :], s1c[:, :]),
             reads=[r(s1b), r(s1c)], writes=[r(s1b)], tag="v")
        P.op("vector", lambda e: e.tensor_max(s1a[:, :], s1a[:, :], s1b[:, :]),
             reads=[r(s1a), r(s1b)], writes=[r(s1a)], tag="gthr")
        P.op("vector",
             lambda e: e.tensor_single_scalar(p4d[:, :], p4a[:, :], s1a[:, 0:1],
                                              ALU.is_ge),
             reads=[r(p4a), r(s1a)], writes=[r(p4d)], tag="gmask")
        P.op("vector",
             lambda e: e.tensor_copy(sap(emk, 0, 128, 0, [[4, 4], [1, 4]]),
                                     sap(p4d, 0, 128, 0, [[1, 4], [0, 4]])),
             reads=[r(p4d)], writes=[r(emk)], tag="emk")
        P.op("vector",
             lambda e: e.tensor_scalar(oneh[:, :], emk[:, :], 1e30, -1e30,
                                       ALU.mult, ALU.add),
             reads=[r(emk)], writes=[r(oneh)], tag="emsk")
        P.op("vector", lambda e: e.tensor_add(msk[:, :], sfc[:, :], oneh[:, :]),
             reads=[r(sfc), r(oneh)], writes=[r(msk)], tag="mskd")
        P.op("vector", lambda e: e.memset(sel[:, :], 0.0),
             writes=[r(sel)], tag="sel0")
        for it in range(4):
            P.op("vector", lambda e: e.reduce_max(s1b[:, :], msk[:, :], AX.X),
                 reads=[r(msk)], writes=[r(s1b)], tag="mx")
            P.op("vector",
                 lambda e: e.tensor_single_scalar(oneh[:, :], msk[:, :], s1b[:, 0:1],
                                                  ALU.is_ge),
                 reads=[r(msk), r(s1b)], writes=[r(oneh)], tag="oh")
            P.op("vector", lambda e: e.tensor_add(sel[:, :], sel[:, :], oneh[:, :]),
                 reads=[r(sel), r(oneh)], writes=[r(sel)], tag="seladd")
            if it < 3:
                P.op("vector",
                     lambda e: e.scalar_tensor_tensor(msk[:, :], oneh[:, :], -4e30,
                                                      msk[:, :], ALU.mult, ALU.add),
                     reads=[r(oneh), r(msk)], writes=[r(msk)], tag="knock")
        P.op("vector", lambda e: e.tensor_mul(oneh[:, :], sco[:, :], sel[:, :]),
             reads=[r(sco), r(sel)], writes=[r(oneh)], tag="cwr")
        P.op("vector", lambda e: e.reduce_sum(s1b[:, :], oneh[:, :], AX.X),
             reads=[r(oneh)], writes=[r(s1b)], tag="den")
        P.op("scalar", lambda e: e.activation(s1b[:, :], s1b[:, :], AF.Ln,
                                              bias=C.cv(1e-20)),
             reads=[r(s1b), r(C.cvals)], writes=[r(s1b)], tag="dln")
        P.op("scalar", lambda e: e.activation(s1b[:, :], s1b[:, :], AF.Exp,
                                              scale=-1.0),
             reads=[r(s1b)], writes=[r(s1b)], tag="dex")
        P.op("vector",
             lambda e: e.tensor_scalar(cwb[:, :], oneh[:, :], s1b[:, 0:1], 2.5,
                                       ALU.mult, ALU.mult),
             reads=[r(oneh), r(s1b)], writes=[r(cwb)], tag="cw")
        P.op("tensor",
             lambda e: e.matmul(pF1[0:16, 0:128], cwb[:, :], identb[:, :],
                                start=True, stop=True),
             reads=[r(cwb), r(identb)], writes=[r(pF1)], tag="cwt")
        P.op("vector",
             lambda e, t=t: e.tensor_copy(cwT[0:16, t * 128:(t + 1) * 128],
                                          pF1[0:16, 0:128]),
             reads=[r(pF1)], writes=[r(cwT, t)], tag="cwev")

    def shared_group(g):
        c0 = g * 512
        for j in range(8):
            for k in range(8):
                P.op("tensor",
                     lambda e, j=j, k=k, c0=c0: e.matmul(
                         pF2[:, :], wsg[:, k, j * 128:(j + 1) * 128],
                         ln1f[:, k, c0:c0 + 512], start=(k == 0), stop=(k == 7)),
                     reads=[r(wsg), r(ln1f, (g, k))], writes=[r(pF2)], tag="sgmm")
            P.op("scalar", lambda e, j=j: e.activation(sh_g[:, j, :], pF2[:, :],
                                                       AF.Silu),
                 reads=[r(pF2)], writes=[r(sh_g, j)], tag="sgsilu")
        for j in range(8):
            for k in range(8):
                P.op("tensor",
                     lambda e, j=j, k=k, c0=c0: e.matmul(
                         pF2[:, :], wsu[:, k, j * 128:(j + 1) * 128],
                         ln1f[:, k, c0:c0 + 512], start=(k == 0), stop=(k == 7)),
                     reads=[r(wsu), r(ln1f, (g, k))], writes=[r(pF2)], tag="summ")
            P.op("vector", lambda e, j=j: e.tensor_mul(sh_g[:, j, :], pF2[:, :],
                                                       sh_g[:, j, :]),
                 reads=[r(pF2), r(sh_g, j)], writes=[r(sh_g, j)], tag="suact")
        for j in range(8):
            for k in range(8):
                P.op("tensor",
                     lambda e, j=j, k=k: e.matmul(
                         pF3[:, :], wsd[:, k, j * 128:(j + 1) * 128],
                         sh_g[:, k, :], start=(k == 0), stop=(k == 7)),
                     reads=[r(wsd), r(sh_g, k)], writes=[r(pF3)], tag="sdmm")
            P.op("scalar",
                 lambda e, j=j, c0=c0: e.activation(y_sb[:, j, c0:c0 + 512],
                                                    pF3[:, :], AF.Copy),
                 reads=[r(pF3)], writes=[r(y_sb, ("c", g))], tag="sdev")

    for g in range(NG):
        ln1_group(g)
        for w in range(4):
            routing_tile(4 * g + w)
        shared_group(g)

    P.fence()
    stA.close()

    stB = contextlib.ExitStack()
    esel = C.sb("esel_s", [128, 16 * 128], BF16, stB)
    C.dma("sync", esel[0:16, :], esel_d, [r(esel)])
    wgS = [C.sb(f"wgS{i}", [128, 8, 512], BF16, stB) for i in range(2)]
    wuS = [C.sb(f"wuS{i}", [128, 8, 512], BF16, stB) for i in range(2)]
    wdS = [C.sb(f"wdS{i}", [128, 4, D], BF16, stB) for i in range(2)]
    silu_e = C.sb("silu_e", [128, 2048], BF16, stB)
    act_e = silu_e
    cw_bc = C.sb("cw_bc", [128, S], BF16, stB)
    pGU = C.ps("pGU", [128, 2048], F32, stB)
    pY = C.ps("pY", [128, 2048], F32, stB)
    NC4 = S // 512

    def load_expert(e):
        sl = e % 2
        C.dma("gpsimd", wgS[sl][:, :, :],
              wg_d[e].rearrange("(a p) c -> p a c", p=128), [r(wgS[sl])],
              tag=f"wg{e}")
        C.dma("gpsimd", wuS[sl][:, :, :],
              wu_d[e].rearrange("(a p) c -> p a c", p=128), [r(wuS[sl])],
              tag=f"wu{e}")
        C.dma("gpsimd", wdS[sl][:, :, :],
              wd_d[e].rearrange("(a p) c -> p a c", p=128), [r(wdS[sl])],
              tag=f"wd{e}")

    load_expert(0)
    for e in range(16):
        sl = e % 2
        if e + 1 < 16:
            load_expert(e + 1)
        for q in range(NC4):
            P.op("tensor",
                 lambda e2, e=e, q=q: e2.matmul(
                     pGU[:, q * 512:(q + 1) * 512], esel[0:16, e * 128:(e + 1) * 128],
                     cwT[0:16, q * 512:(q + 1) * 512], start=True, stop=True),
                 reads=[r(esel)] + [r(cwT, 4 * q + w) for w in range(4)],
                 writes=[r(pGU)], tag="cwbc")
        P.op("scalar", lambda e2: e2.activation(cw_bc[:, 0:512 * NC4],
                                                pGU[:, 0:512 * NC4], AF.Copy),
             reads=[r(pGU)], writes=[r(cw_bc)], tag="cwbev")
        for c in range(NC4):
            c0 = c * 512
            for m in range(4):
                for k in range(8):
                    P.op("tensor",
                         lambda e2, m=m, k=k, c0=c0, sl=sl: e2.matmul(
                             pGU[:, m * 512:(m + 1) * 512],
                             wgS[sl][:, k, m * 128:(m + 1) * 128],
                             ln1f[:, k, c0:c0 + 512], start=(k == 0), stop=(k == 7)),
                         reads=[r(wgS[sl]), r(ln1f, (c, k))], writes=[r(pGU)],
                         tag="egmm")
            P.op("scalar", lambda e2: e2.activation(silu_e[:, :], pGU[:, :], AF.Silu),
                 reads=[r(pGU)], writes=[r(silu_e)], tag="esilu")
            for m in range(4):
                for k in range(8):
                    P.op("tensor",
                         lambda e2, m=m, k=k, c0=c0, sl=sl: e2.matmul(
                             pGU[:, m * 512:(m + 1) * 512],
                             wuS[sl][:, k, m * 128:(m + 1) * 128],
                             ln1f[:, k, c0:c0 + 512], start=(k == 0), stop=(k == 7)),
                         reads=[r(wuS[sl]), r(ln1f, (c, k))], writes=[r(pGU)],
                         tag="eumm")
            P.op("vector", lambda e2: e2.tensor_mul(act_e[:, :], pGU[:, :],
                                                    silu_e[:, :]),
                 reads=[r(pGU), r(silu_e)], writes=[r(silu_e)], tag="eact1")
            P.op("vector",
                 lambda e2, c0=c0: e2.tensor_mul(
                     act_e[:, :], act_e[:, :],
                     sap(cw_bc, 0, 128, c0, [[0, 4], [1, 512]])),
                 reads=[r(silu_e), r(cw_bc)], writes=[r(silu_e)], tag="eact2")
            for jh in range(2):
                for m2 in range(4):
                    j = jh * 4 + m2
                    for k in range(4):
                        P.op("tensor",
                             lambda e2, j=j, m2=m2, k=k, sl=sl: e2.matmul(
                                 pY[:, m2 * 512:(m2 + 1) * 512],
                                 wdS[sl][:, k, j * 128:(j + 1) * 128],
                                 sap(act_e, 0, 128, k * 512, [[1, 512]]),
                                 start=(k == 0), stop=(k == 3)),
                             reads=[r(wdS[sl]), r(silu_e)], writes=[r(pY)],
                             tag="edmm")
                P.op("vector",
                     lambda e2, jh=jh, c0=c0: e2.tensor_add(
                         bass.AP(y_sb, (jh * 4) * S + c0, [[8 * S, 128], [S, 4], [1, 512]]),
                         bass.AP(y_sb, (jh * 4) * S + c0, [[8 * S, 128], [S, 4], [1, 512]]),
                         pGU[:, :] if False else bass.AP(pY, 0, [[2048, 128], [512, 4], [1, 512]])),
                     reads=[r(pY), r(y_sb, ("c", c))], writes=[r(y_sb, ("c", c))],
                     tag="eyadd")

    P.fence()
    stB.close()

    stC = contextlib.ExitStack()
    pT = C.ps("pT", [128, 512], F32, stC)
    for g in range(NG):
        c0 = g * 512
        P.op("vector",
             lambda e, c0=c0, g=g: e.tensor_add(
                 bass.AP(y_sb, c0, [[8 * S, 128], [S, 8], [1, 512]]),
                 bass.AP(y_sb, c0, [[8 * S, 128], [S, 8], [1, 512]]),
                 bass.AP(ln1f, c0, [[8 * S, 128], [S, 8], [1, 512]])),
             reads=[r(y_sb, ("c", g))] + [r(ln1f, (g, pt)) for pt in range(8)],
             writes=[r(y_sb, ("c", g))], tag="bout")
        emit_ln_stats(C, y_sb, lambda pt, g=g: [r(y_sb, ("c", g))], pT, C.ones_r,
                      lnm2, lnr2, tmpb, tag="fln2", c0=c0)
        gb = g % 2
        C.dma("sync", xg2[gb][:, :, :], x_src_r[:, :, c0:c0 + 512],
              [r(xg2[gb])], reads=[("dram:" + src_nm, g)], tag=f"fxr{g}")
        for pt in range(8):
            P.op("vector",
                 lambda e, pt=pt, c0=c0: e.tensor_sub(lnt[:, :],
                                                      y_sb[:, pt, c0:c0 + 512],
                                                      lnm2[:, :]),
                 reads=[r(y_sb, ("c", g)), r(lnm2)], writes=[r(lnt)], tag="fa1")
            P.op("vector",
                 lambda e, pt=pt: e.tensor_mul(lnt[:, :], lnt[:, :], lnr2[:, :]),
                 reads=[r(lnt), r(lnr2)], writes=[r(lnt)], tag="fa2")
            P.op("vector",
                 lambda e, pt=pt, gb=gb: e.tensor_add(
                     xg2[gb][:, pt, :], xg2[gb][:, pt, :], lnt[:, :]),
                 reads=[r(lnt), r(xg2[gb])], writes=[r(xg2[gb])], tag="fa3")
        C.dma("sync", x_dst_r[:, :, c0:c0 + 512], xg2[gb][:, :, :],
              [("dram:" + dst_nm, g)], reads=[r(xg2[gb])], tag=f"fxo{g}")

    P.fence()
    stC.close()
    st.close()


def emit_l1a(C, x_src_dram, x_dst_dram, src_nm, dst_nm):
    P, nc, S = C.plan, C.nc, C.S
    NT, NG = S // 128, S // 512
    st = contextlib.ExitStack()

    wq_d = C.dram_in("wq_t", [D, D])
    wk_d = C.dram_in("wk_t", [D, 64])
    wv_d = C.dram_in("wv_t", [D, 64])
    wf_d = C.dram_in("wf_t", [D, D])
    bf_d = C.dram_in("bf_sb", [128, 8])
    mo_d = C.dram_in("maskO", [128, 128])
    mn_d = C.dram_in("maskN", [128, 128])

    mO = C.sb("mO_s", [128, 128], F32, st)
    mN = C.sb("mN_s", [128, 128], F32, st)
    bf_s = C.sb("bf_s", [128, 8], F32, st)
    wqf = C.sb("wqf", [128, 8, D], BF16, st)
    wfs2 = C.sb("wfs2", [128, 8, D], BF16, st)
    wks = C.sb("wks", [128, 8, 64], BF16, st)
    wvs = C.sb("wvs", [128, 8, 64], BF16, st)
    qT1 = C.sb("qT1", [128, 8, S], BF16, st)
    kTd = C.sb("kTd", [128, S], BF16, st)
    v_n = C.sb("v_n", [128, NT * 64], BF16, st)
    ln1a = [C.sb(f"l1ln_{i}", [128, 8, 512], BF16, st) for i in range(2)]
    xga = [C.sb(f"xga{i}", [128, 8, 512], F32, st) for i in range(2)]
    vals1 = C.sb("vals1", [128, 8, 2 * 512], BF16, st)
    exL = C.sb("exL", [128, 3 * 1024], BF16, st)
    sums1 = C.sb("sums1", [128, 1024], BF16, st)
    rinv1 = C.sb("rinv1", [128, 1024], BF16, st)
    m_sb = C.sb("a1m", [128, 512], F32, st)
    r_sb = C.sb("a1r", [128, 512], F32, st)
    m2sb = C.sb("a1m2", [128, 512], F32, st)
    r2sb = C.sb("a1r2", [128, 512], F32, st)
    tmpb = C.sb("a1tmpb", [128, 512], BF16, st)
    lnt1 = C.sb("a1lnt", [128, 512], F32, st)
    oT1 = C.sb("a1oT", [128, 8, 512], BF16, st)

    pSC1 = C.ps("pSC1", [128, 1024], F32, st)
    pPV1 = C.ps("pPV1", [128, 1024], F32, st)
    pBC1 = C.ps("pBC1", [128, 1024], F32, st)
    pQ1 = C.ps("pQ1", [128, 512], F32, st)
    pS1 = C.ps("pS1", [128, 512], F32, st)

    C.dma("sync", mO[:, :], mo_d, [r(mO)])
    C.dma("sync", mN[:, :], mn_d, [r(mN)])
    C.dma("sync", bf_s[:, :], bf_d, [r(bf_s)])
    C.dma("gpsimd", wqf[:, :, :], wq_d.rearrange("(a p) c -> p a c", p=128),
          [r(wqf, "q")], tag="wq")
    C.dma("gpsimd", wfs2[:, :, :], wf_d.rearrange("(a p) c -> p a c", p=128),
          [r(wfs2)], tag="wf")
    C.dma("gpsimd", wks[:, :, :], wk_d.rearrange("(a p) c -> p a c", p=128),
          [r(wks)], tag="wk")
    C.dma("gpsimd", wvs[:, :, :], wv_d.rearrange("(a p) c -> p a c", p=128),
          [r(wvs)], tag="wv")

    x_src_r = x_src_dram.rearrange("(a p) c -> p a c", p=128)
    x_dst_r = x_dst_dram.rearrange("(a p) c -> p a c", p=128)

    def ln1_group(g):
        gb = g % 2
        C.dma("sync", xga[gb][:, :, :], x_src_r[:, :, g * 512:(g + 1) * 512],
              [r(xga[gb])], reads=[("dram:" + src_nm, g)], tag=f"axg{g}")
        emit_ln_stats(C, xga[gb], lambda pt: [r(xga[gb])], pS1, C.ones_r, m_sb,
                      r_sb, tmpb, tag="aln1")
        for pt in range(8):
            P.op("vector",
                 lambda e, pt=pt, gb=gb: e.tensor_sub(
                     lnt1[:, :], xga[gb][:, pt, :], m_sb[:, :]),
                 reads=[r(xga[gb]), r(m_sb)], writes=[r(lnt1)], tag="as")
            P.op("vector",
                 lambda e, pt=pt, gb=gb: e.tensor_mul(
                     ln1a[gb][:, pt, :], lnt1[:, :], r_sb[:, :]),
                 reads=[r(lnt1), r(r_sb)], writes=[r(ln1a[gb])], tag="am")

    def qkv_group(g):
        gb = g % 2
        c0 = g * 512
        for i in range(8):
            for k in range(8):
                P.op("tensor",
                     lambda e, i=i, k=k, gb=gb: e.matmul(
                         pQ1[:, :], wqf[:, k, i * 128:(i + 1) * 128],
                         ln1a[gb][:, k, :], start=(k == 0), stop=(k == 7)),
                     reads=[r(wqf, "q"), r(ln1a[gb])], writes=[r(pQ1)], tag="qmm")
            P.op("scalar",
                 lambda e, i=i, c0=c0: e.activation(
                     qT1[:, i, c0:c0 + 512], pQ1[:, :], AF.Copy),
                 reads=[r(pQ1)], writes=[r(qT1, g)], tag="qev")
        for half in range(2):
            for k in range(8):
                P.op("tensor",
                     lambda e, half=half, k=k, gb=gb: e.matmul(
                         sap(pQ1, 64 * half, 64, 0, [[1, 512]]),
                         wks[:, k, :], ln1a[gb][:, k, :],
                         start=(k == 0), stop=(k == 7),
                         tile_position=(0, 64 * half)),
                     reads=[r(wks), r(ln1a[gb])], writes=[r(pQ1)], tag="kmm")
        P.op("scalar",
             lambda e, c0=c0: e.activation(kTd[:, c0:c0 + 512], pQ1[:, :], AF.Copy),
             reads=[r(pQ1)], writes=[r(kTd, g)], tag="kev")
        for w in range(4):
            t = 4 * g + w
            for k in range(8):
                P.op("tensor",
                     lambda e, w=w, k=k, gb=gb: e.matmul(
                         sap(pQ1, 0, 128, 0, [[1, 64]]),
                         ln1a[gb][:, k, w * 128:(w + 1) * 128], wvs[:, k, :],
                         start=(k == 0), stop=(k == 7)),
                     reads=[r(wvs), r(ln1a[gb])], writes=[r(pQ1)], tag="vmm1")
            P.op("scalar",
                 lambda e, t=t: e.activation(v_n[:, t * 64:(t + 1) * 64],
                                             sap(pQ1, 0, 128, 0, [[1, 64]]),
                                             AF.Copy),
                 reads=[r(pQ1)], writes=[r(v_n, t)], tag="vev1")

    def attn_tile(t):
        g, w = t // 4, t % 4
        gsl = g % 2
        kts = [kt for kt in (t - 2, t - 1, t) if kt >= 0]
        for hg in range(2):
            hbase = hg * 8
            base = 0 if hg == 0 else 64
            for ci, kt in enumerate(kts):
                for hl in range(8):
                    h = hbase + hl
                    po = (h % 2) * 64
                    P.op("tensor",
                         lambda e, hl=hl, h=h, po=po, kt=kt: e.matmul(
                             pSC1[:, hl * 128:(hl + 1) * 128],
                             sap(kTd, po, 64, kt * 128, [[1, 128]]),
                             sap(qT1, po, 64, (h // 2) * S + t * 128, [[1, 128]]),
                             start=True, stop=True),
                         reads=[r(kTd, kt // 4), r(qT1, g)], writes=[r(pSC1)],
                         tag="sc1")
                if kt == t:
                    P.op("vector",
                         lambda e: e.tensor_add(
                             pSC1[:, :], pSC1[:, :],
                             bass.AP(mN, 0, [[128, 128], [0, 8], [1, 128]])),
                         reads=[r(pSC1), r(mN)], writes=[r(pSC1)], tag="mn1")
                elif kt == t - 2:
                    P.op("vector",
                         lambda e: e.tensor_add(
                             pSC1[:, :], pSC1[:, :],
                             bass.AP(mO, 0, [[128, 128], [0, 8], [1, 128]])),
                         reads=[r(pSC1), r(mO)], writes=[r(pSC1)], tag="mo1")
                P.op("scalar",
                     lambda e, ci=ci: e.activation(
                         exL[:, ci * 1024:(ci + 1) * 1024], pSC1[:, :], AF.Exp,
                         scale=0.125),
                     reads=[r(pSC1)], writes=[r(exL, ci)], tag="ex1")
            for half in range(2):
                for ci in range(len(kts)):
                    P.op("tensor",
                         lambda e, ci=ci, half=half,
                                last=(ci == len(kts) - 1): e.matmul(
                             sap(pBC1, 96, 1, half * 512, [[1, 512]]),
                             sap(C.ones1, 0, 128, 0, [[1, 1]]),
                             sap(exL, 0, 128, ci * 1024 + half * 512, [[1, 512]]),
                             start=(ci == 0), stop=last, tile_position=(0, 96)),
                         reads=[r(exL, ci), r(C.ones1)], writes=[r(pBC1)],
                         tag="sums1")
            P.op("vector", lambda e: e.tensor_copy(sums1[96:97, :], pBC1[96:97, :]),
                 reads=[r(pBC1)], writes=[r(sums1)], tag="sumcp1")
            for half in range(2):
                P.op("tensor",
                     lambda e, half=half, base=base: e.matmul(
                         sap(pBC1, base, 64, half * 512, [[1, 512]]),
                         sap(C.ones1, 96, 1, 0, [[1, 64]]),
                         sap(sums1, 96, 1, half * 512, [[1, 512]]),
                         start=True, stop=True, tile_position=(96, base)),
                     reads=[r(sums1), r(C.ones1)], writes=[r(pBC1)], tag="bc1")
            P.op("scalar",
                 lambda e, base=base: e.activation(
                     rinv1[base:base + 64, :], pBC1[base:base + 64, :], AF.Ln),
                 reads=[r(pBC1)], writes=[r(rinv1)], tag="rln1")
            P.op("scalar",
                 lambda e, base=base: e.activation(
                     rinv1[base:base + 64, :], rinv1[base:base + 64, :], AF.Exp,
                     scale=-1.0),
                 reads=[r(rinv1)], writes=[r(rinv1)], tag="rex1")
            for hl in range(8):
                for ci, kt in enumerate(kts):
                    P.op("tensor",
                         lambda e, hl=hl, ci=ci, kt=kt, base=base,
                                last=(ci == len(kts) - 1): e.matmul(
                             sap(pPV1, base, 64, hl * 128, [[1, 128]]),
                             sap(v_n, 0, 128, kt * 64, [[1, 64]]),
                             sap(exL, 0, 128, ci * 1024 + hl * 128, [[1, 128]]),
                             start=(ci == 0), stop=last, tile_position=(0, base)),
                         reads=[r(v_n, kt), r(exL, ci)], writes=[r(pPV1)],
                         tag="pv1")
            vo = gsl * 512 + w * 128
            P.op("vector",
                 lambda e, base=base, vo=vo: e.tensor_mul(
                     sap(vals1, base, 64, vo, [[1024, 8], [1, 128]]),
                     sap(pPV1, base, 64, 0, [[128, 8], [1, 128]]),
                     sap(rinv1, base, 64, 0, [[128, 8], [1, 128]])),
                 reads=[r(pPV1), r(rinv1)], writes=[r(vals1, (gsl, w))],
                 tag="vev1b")

    def wf_group(g):
        gsl = g % 2

        for j in range(8):
            for k in range(8):
                P.op("tensor",
                     lambda e, j=j, k=k, gsl=gsl: e.matmul(
                         pQ1[:, :], wfs2[:, k, j * 128:(j + 1) * 128],
                         vals1[:, k, gsl * 512:(gsl + 1) * 512],
                         start=(k == 0), stop=(k == 7)),
                     reads=[r(wfs2)] + [r(vals1, (gsl, w)) for w in range(4)],
                     writes=[r(pQ1)], tag="wfmm")
            P.op("scalar",
                 lambda e, j=j: e.activation(oT1[:, j, :], pQ1[:, :], AF.Identity,
                                             bias=bf_s[:, j:j + 1]),
                 reads=[r(pQ1), r(bf_s)], writes=[r(oT1, j)], tag="wfev")

    def ln2_add_group(g):
        xbuf = xga[g % 2]
        emit_ln_stats(C, oT1, lambda pt: [r(oT1, pt)], pS1, C.ones_r, m2sb, r2sb,
                      tmpb, tag="aln2")
        for pt in range(8):
            P.op("vector",
                 lambda e, pt=pt: e.tensor_sub(lnt1[:, :], oT1[:, pt, :], m2sb[:, :]),
                 reads=[r(oT1, pt), r(m2sb)], writes=[r(lnt1)], tag="aa1")
            P.op("vector",
                 lambda e, pt=pt: e.tensor_mul(lnt1[:, :], lnt1[:, :], r2sb[:, :]),
                 reads=[r(lnt1), r(r2sb)], writes=[r(lnt1)], tag="aa2")
            P.op("vector",
                 lambda e, pt=pt, xbuf=xbuf: e.tensor_add(
                     xbuf[:, pt, :], xbuf[:, pt, :], lnt1[:, :]),
                 reads=[r(lnt1), r(xbuf)], writes=[r(xbuf)], tag="aa3")
        C.dma("sync", x_dst_r[:, :, g * 512:(g + 1) * 512], xbuf[:, :, :],
              [("dram:" + dst_nm, g)], reads=[r(xbuf)], tag=f"axo{g}")

    for g in range(NG):
        ln1_group(g)
        qkv_group(g)
        for w in range(4):
            attn_tile(4 * g + w)
        wf_group(g)
        ln2_add_group(g)

    P.fence()
    st.close()


def emit_l1f(C, x_src_dram, x_dst_dram, src_nm, dst_nm):
    P, nc, S = C.plan, C.nc, C.S
    NT, NG = S // 128, S // 512
    LN127 = 4.844187086458591
    st = contextlib.ExitStack()

    w1_d = C.dram_in("w1_t", [D, 4096])
    idf_d = C.dram_in("ident_f", [128, 128])
    sel_d = C.dram_in("sel16f", [16, 16 * 128])
    w2_d = C.dram_in("w2_t", [4096, D])
    cvd_d = C.dram_in("conv_diag", [32 * 3 * 128, 128])
    cb_d = C.dram_in("convb_sb", [128, 32])
    h2_h = C.dram_tmp("h2_hbm", [32 * 128, S])

    cb_s = C.sb("cb_s", [128, 32], F32, st)
    idf = C.sb("idf", [128, 128], F32, st)
    sel16 = C.sb("sel16", [128, 16 * 128], F32, st)
    amcmp = C.sb("amcmp", [128, 16], F32, st)
    amrow = C.sb("amrow", [128, 128], F32, st)
    amax2 = C.sb("amax2", [128, S], F32, st)
    runm2 = C.sb("runm2", [128, S], F32, st)
    xgf = [C.sb(f"xgf{i}", [128, 8, 512], F32, st) for i in range(2)]
    m2sb = C.sb("f1m2", [128, 512], F32, st)
    r2sb = C.sb("f1r2", [128, 512], F32, st)
    tmpb = C.sb("f1tmpb", [128, 512], BF16, st)
    lntf = C.sb("f1lnt", [128, 512], F32, st)
    racc = C.sb("racc", [128, 16], F32, st)
    rac1 = C.sb("rac1", [128, 1], F32, st)
    racb = C.sb("racb", [128, 1], BF16, st)
    msw1 = C.sb("msw1", [128, 1], F32, st)
    sw1 = C.sb("sw1", [128, 1], F32, st)
    msw2 = C.sb("msw2", [128, 1], F32, st)
    sw2 = C.sb("sw2", [128, 1], F32, st)

    stX = contextlib.ExitStack()
    xq1 = C.sb("xq1", [128, 8, S], BF16, stX)
    inv1 = C.sb("inv1", [128, S], BF16, stX)
    m_sb = C.sb("f1m", [128, 512], F32, stX)
    r_sb = C.sb("f1r", [128, 512], F32, stX)
    runm1 = C.sb("runm1", [128, 512], F32, stX)
    qs1 = C.sb("qs1", [128, 512], F32, stX)
    xqi = C.sb("xqi", [128, 512], I32, stX)
    sgn1 = C.sb("sgn1", [128, 512], F32, stX)

    pH = C.ps("pH", [128, 512], F32, st)
    pC = C.ps("pC", [128, 512], F32, st)
    pM = C.ps("pM", [128, 512], F32, st)
    pS2 = C.ps("pS2", [128, 512], F32, st)

    C.dma("sync", cb_s[:, :], cb_d, [r(cb_s)])
    C.dma("sync", idf[:, :], idf_d, [r(idf)])
    C.dma("sync", sel16[0:16, :], sel_d, [r(sel16)])

    x_src_r = x_src_dram.rearrange("(a p) c -> p a c", p=128)
    x_dst_r = x_dst_dram.rearrange("(a p) c -> p a c", p=128)
    h2_r = h2_h.rearrange("(a p) c -> p a c", p=128)


    def q1_pmax(src_r, dst_w):
        for blk in range(4):
            P.op("tensor",
                 lambda e, blk=blk: e.matmul(
                     pS2[:, (blk % 4) * 128:(blk % 4) * 128 + 128],
                     runm1[:, blk * 128:(blk + 1) * 128], idf[:, :],
                     start=True, stop=True),
                 reads=src_r + [r(idf)], writes=[r(pS2)], tag="q1t")
            P.op("vector",
                 lambda e, blk=blk: e.reduce_max(
                     amcmp[:, blk:blk + 1],
                     pS2[:, (blk % 4) * 128:(blk % 4) * 128 + 128], AX.X),
                 reads=[r(pS2)], writes=[r(amcmp, blk)], tag="q1r")
        P.op("tensor",
             lambda e: e.matmul(pM[0:4, 0:128], amcmp[:, 0:4],
                                idf[:, :], start=True, stop=True),
             reads=[r(amcmp, bb) for bb in range(4)] + [r(idf)],
             writes=[r(pM)], tag="q1tb")
        P.op("vector", lambda e: e.tensor_copy(amrow[0:4, :],
                                               pM[0:4, 0:128]),
             reads=[r(pM)], writes=[r(amrow)], tag="q1ev")
        for blk in range(4):
            P.op("tensor",
                 lambda e, blk=blk: e.matmul(
                     pS2[:, (blk % 4) * 128:(blk % 4) * 128 + 128],
                     sel16[0:4, blk * 128:(blk + 1) * 128],
                     amrow[0:4, :], start=True, stop=True),
                 reads=[r(sel16), r(amrow)], writes=[r(pS2)], tag="q1b")
            P.op("scalar",
                 lambda e, blk=blk: e.activation(
                     runm1[:, blk * 128:(blk + 1) * 128],
                     pS2[:, (blk % 4) * 128:(blk % 4) * 128 + 128], AF.Copy),
                 reads=[r(pS2)], writes=dst_w, tag="q1be")

    def quant1_group(g):
        gb = g % 2
        c0 = g * 512
        C.dma("sync", xgf[gb][:, :, :], x_src_r[:, :, c0:c0 + 512],
              [r(xgf[gb])], reads=[("dram:" + src_nm, g)], tag=f"qxg{g}")
        emit_ln_stats(C, xgf[gb], lambda pt: [r(xgf[gb])], pS2, C.ones_r, m_sb,
                      r_sb, tmpb, tag="q1ln")
        P.op("vector", lambda e: e.memset(runm1[:, :], 0.0),
             writes=[r(runm1)], tag="q1a0")
        for pt in range(8):
            P.op("vector",
                 lambda e, pt=pt, gb=gb: e.tensor_sub(
                     lntf[:, :], xgf[gb][:, pt, :], m_sb[:, :]),
                 reads=[r(xgf[gb]), r(m_sb)], writes=[r(lntf)], tag="q1s")
            P.op("scalar",
                 lambda e: e.activation(sgn1[:, :], lntf[:, :], AF.Abs),
                 reads=[r(lntf)], writes=[r(sgn1)], tag="q1ab")
            P.op("vector",
                 lambda e: e.tensor_max(runm1[:, :], runm1[:, :], sgn1[:, :]),
                 reads=[r(runm1), r(sgn1)], writes=[r(runm1)], tag="q1a")
        q1_pmax([r(runm1)], [r(runm1)])
        P.op("vector", lambda e: e.tensor_mul(qs1[:, :], runm1[:, :], r_sb[:, :]),
             reads=[r(runm1), r(r_sb)], writes=[r(qs1)], tag="q1t")
        P.op("vector",
             lambda e: e.tensor_scalar(qs1[:, :], qs1[:, :], 1e-5, None, ALU.max),
             reads=[r(qs1)], writes=[r(qs1)], tag="q1c")
        P.op("vector",
             lambda e, c0=c0: e.tensor_scalar(inv1[:, c0:c0 + 512], qs1[:, :],
                                              1.0 / 127.0, None, ALU.mult),
             reads=[r(qs1)], writes=[r(inv1, g)], tag="q1i")
        P.op("scalar", lambda e: e.activation(qs1[:, :], qs1[:, :], AF.Ln),
             reads=[r(qs1)], writes=[r(qs1)], tag="q1l")
        P.op("scalar", lambda e: e.activation(qs1[:, :], qs1[:, :], AF.Exp,
                                              scale=-1.0, bias=C.cv(LN127)),
             reads=[r(qs1), r(C.cvals)], writes=[r(qs1)], tag="q1e")
        P.op("vector", lambda e: e.tensor_mul(qs1[:, :], qs1[:, :], r_sb[:, :]),
             reads=[r(qs1), r(r_sb)], writes=[r(qs1)], tag="q1rs")
        for pt in range(8):
            P.op("vector",
                 lambda e, pt=pt, gb=gb: e.tensor_sub(
                     lntf[:, :], xgf[gb][:, pt, :], m_sb[:, :]),
                 reads=[r(xgf[gb]), r(m_sb)], writes=[r(lntf)], tag="q1s2")
            P.op("vector",
                 lambda e: e.tensor_mul(lntf[:, :], lntf[:, :], qs1[:, :]),
                 reads=[r(lntf), r(qs1)], writes=[r(lntf)], tag="q1m")
            P.op("scalar",
                 lambda e: e.sign(sgn1[:, :], lntf[:, :]),
                 reads=[r(lntf)], writes=[r(sgn1)], tag="q1sg")
            P.op("vector",
                 lambda e: e.scalar_tensor_tensor(
                     lntf[:, :], sgn1[:, :], C.halfsh[:, 0:1], lntf[:, :],
                     ALU.mult, ALU.add),
                 reads=[r(sgn1), r(C.halfsh), r(lntf)], writes=[r(lntf)],
                 tag="q1sh")
            P.op("vector",
                 lambda e: e.tensor_copy(xqi[:, :], lntf[:, :]),
                 reads=[r(lntf)], writes=[r(xqi)], tag="q1r")
            P.op("vector",
                 lambda e, pt=pt, c0=c0: e.tensor_copy(xq1[:, pt, c0:c0 + 512],
                                                       xqi[:, :]),
                 reads=[r(xqi)], writes=[r(xq1, (g, pt))], tag="q1b")

    for g in range(NG):
        quant1_group(g)

    stW = contextlib.ExitStack()
    cds2 = [C.sb(f"cds{i}", [128, 12, 128], BF16, stW) for i in range(2)]
    cvd_r = cvd_d.rearrange("(a p) c -> p a c", p=128)
    w1S = [C.sb(f"w1S{i}", [128, 8, 512], F32, stW) for i in range(2)]
    w1q = [C.sb(f"w1q{i}", [128, 8, 512], BF16, stW) for i in range(2)]
    w1i = C.sb("w1i", [128, 8, 512], I32, stW)
    h_s = [C.sb(f"h_s{i}", [128, S + 2], BF16, stW) for i in range(2)]
    h2m = [C.sb(f"h2m{i}", [128, S], F32, stW) for i in range(2)]
    habs = lntf

    def w_mean_pass(w_r, wS, name):
        for j in range(8):
            sl = j % 2
            C.dma("sync", wS[sl][:, :, :], w_r[:, :, j * 512:(j + 1) * 512],
                  [r(wS[sl])], tag=f"{name}p1_{j}")
            P.op("vector",
                 lambda e, j=j, sl=sl: e.reduce_sum(
                     racc[:, j:j + 1], wS[sl][:, :, :], AX.XY,
                     apply_absolute_value=True),
                 reads=[r(wS[sl])], writes=[r(racc, j)], tag=f"{name}r")
        P.op("vector",
             lambda e: e.reduce_sum(rac1[:, :], racc[:, 0:8], AX.X),
             reads=[r(racc, jj) for jj in range(8)],
             writes=[r(rac1)], tag=f"{name}r2")
        P.op("scalar", lambda e: e.activation(racb[:, :], rac1[:, :], AF.Copy),
             reads=[r(rac1)], writes=[r(racb)], tag=f"{name}cb")
        P.op("tensor",
             lambda e: e.matmul(pM[:, 0:1], C.ones1[:, :], racb[:, :],
                                start=True, stop=True),
             reads=[r(racb), r(C.ones1)], writes=[r(pM)], tag=f"{name}sm")

    def w_scale(msw, sw, name):
        P.op("vector",
             lambda e: e.tensor_scalar(msw[:, :], pM[:, 0:1],
                                       2.384185791015625e-07, 1e-5,
                                       ALU.mult, ALU.max),
             reads=[r(pM)], writes=[r(msw)], tag=f"{name}ms")
        P.op("scalar", lambda e: e.activation(sw[:, :], msw[:, :], AF.Ln),
             reads=[r(msw)], writes=[r(sw)], tag=f"{name}l")
        P.op("scalar", lambda e: e.activation(sw[:, :], sw[:, :], AF.Exp,
                                              scale=-1.0),
             reads=[r(sw)], writes=[r(sw)], tag=f"{name}e")

    w1_r = w1_d.rearrange("(a p) c -> p a c", p=128)
    w_mean_pass(w1_r, w1S, "w1")
    w_scale(msw1, sw1, "w1")

    P.op("vector", lambda e: e.memset(runm2[:, :], 0.0), writes=[r(runm2)],
         tag="rm20")
    for j8 in range(8):
        sl = j8 % 2
        C.dma("sync", w1S[sl][:, :, :], w1_r[:, :, j8 * 512:(j8 + 1) * 512],
              [r(w1S[sl])], tag=f"w1p2_{j8}")
        C.dma("gpsimd", cds2[sl][:, :, :],
              cvd_r[:, j8 * 12:(j8 + 1) * 12, :], [r(cds2[sl])],
              tag=f"cds{j8}")
        P.op("vector",
             lambda e, sl=sl: e.tensor_scalar(w1S[sl][:, :, :], w1S[sl][:, :, :],
                                              sw1[:, 0:1], None, ALU.mult),
             reads=[r(w1S[sl]), r(sw1)], writes=[r(w1S[sl])], tag="w1qm")
        P.op("scalar", lambda e, sl=sl: e.sign(w1i[:, :, :], w1S[sl][:, :, :]),
             reads=[r(w1S[sl])], writes=[r(w1i)], tag="w1sg")
        P.op("vector",
             lambda e, sl=sl: e.scalar_tensor_tensor(
                 w1S[sl][:, :, :], w1i[:, :, :], C.halfsh[:, 0:1],
                 w1S[sl][:, :, :], ALU.mult, ALU.add),
             reads=[r(w1i), r(C.halfsh), r(w1S[sl])], writes=[r(w1S[sl])],
             tag="w1sh")
        P.op("vector", lambda e, sl=sl: e.tensor_copy(w1i[:, :, :],
                                                      w1S[sl][:, :, :]),
             reads=[r(w1S[sl])], writes=[r(w1i)], tag="w1ri")
        P.op("vector",
             lambda e, sl=sl: e.tensor_scalar(w1q[sl][:, :, :], w1i[:, :, :],
                                              1.0, -1.0, ALU.min, ALU.max),
             reads=[r(w1i)], writes=[r(w1q[sl])], tag="w1qc")
        for m4 in range(4):
            m = j8 * 4 + m4
            msl = m % 2
            for ng in range(NG):
                for k in range(8):
                    P.op("tensor",
                         lambda e, m4=m4, k=k, ng=ng, sl=sl: e.matmul(
                             pH[:, :], w1q[sl][:, k, m4 * 128:(m4 + 1) * 128],
                             xq1[:, k, ng * 512:(ng + 1) * 512],
                             start=(k == 0), stop=(k == 7)),
                         reads=[r(w1q[sl]), r(xq1, (ng, k))], writes=[r(pH)],
                         tag="h1mm")
                P.op("vector",
                     lambda e, ng=ng, msl=msl: e.scalar_tensor_tensor(
                         h_s[msl][:, 1 + ng * 512:1 + (ng + 1) * 512], pH[:, :],
                         msw1[:, 0:1], inv1[:, ng * 512:(ng + 1) * 512],
                         ALU.mult, ALU.mult),
                     reads=[r(pH), r(msw1), r(inv1, ng)], writes=[r(h_s[msl])],
                     tag="h1ev")
            P.op("vector", lambda e, msl=msl: e.memset(h_s[msl][:, 0:1], 0.0),
                 writes=[r(h_s[msl])], tag="hz0")
            P.op("vector",
                 lambda e, msl=msl: e.memset(h_s[msl][:, S + 1:S + 2], 0.0),
                 writes=[r(h_s[msl])], tag="hz1")
            for ng in range(NG):
                for tap in range(3):
                    P.op("tensor",
                         lambda e, m4=m4, tap=tap, ng=ng, msl=msl, sl=sl: e.matmul(
                             pC[:, :],
                             cds2[sl][:, m4 * 3 + tap, :],
                             h_s[msl][:, ng * 512 + tap:ng * 512 + tap + 512],
                             start=(tap == 0), stop=(tap == 2)),
                         reads=[r(cds2[sl]), r(h_s[msl])], writes=[r(pC)],
                         tag="cvmm")
                P.op("scalar",
                     lambda e, m=m, ng=ng, msl=msl: e.activation(
                         h2m[msl][:, ng * 512:(ng + 1) * 512], pC[:, :], AF.Silu,
                         bias=cb_s[:, m:m + 1]),
                     reads=[r(pC), r(cb_s)], writes=[r(h2m[msl], ng)], tag="cvev")
            for ng in range(NG):
                P.op("scalar",
                     lambda e, msl=msl, ng=ng: e.activation(
                         habs[:, :], h2m[msl][:, ng * 512:(ng + 1) * 512], AF.Abs),
                     reads=[r(h2m[msl], ng)], writes=[r(habs)], tag="rm2a")
                P.op("vector",
                     lambda e, ng=ng: e.tensor_max(
                         runm2[:, ng * 512:(ng + 1) * 512],
                         runm2[:, ng * 512:(ng + 1) * 512], habs[:, :]),
                     reads=[r(runm2), r(habs)], writes=[r(runm2)], tag="rm2")
            C.dma("sync", h2_r[:, m, :], h2m[msl][:, :], [("dram:h2", m)],
                  reads=[r(h2m[msl], ng) for ng in range(NG)], tag=f"h2o{m}")


    def q2_pmax(src_r, dst_w):
        NBLK2 = S // 128
        for blk in range(NBLK2):
            P.op("tensor",
                 lambda e, blk=blk: e.matmul(
                     pH[:, (blk % 4) * 128:(blk % 4) * 128 + 128],
                     runm2[:, blk * 128:(blk + 1) * 128], idf[:, :],
                     start=True, stop=True),
                 reads=src_r + [r(idf)], writes=[r(pH)], tag="q2t")
            P.op("vector",
                 lambda e, blk=blk: e.reduce_max(
                     amcmp[:, blk:blk + 1],
                     pH[:, (blk % 4) * 128:(blk % 4) * 128 + 128], AX.X),
                 reads=[r(pH)], writes=[r(amcmp, blk)], tag="q2r")
        P.op("tensor",
             lambda e: e.matmul(pM[0:NBLK2, 0:128], amcmp[:, 0:NBLK2],
                                idf[:, :], start=True, stop=True),
             reads=[r(amcmp, bb) for bb in range(NBLK2)] + [r(idf)],
             writes=[r(pM)], tag="q2tb")
        P.op("vector", lambda e: e.tensor_copy(amrow[0:NBLK2, :],
                                               pM[0:NBLK2, 0:128]),
             reads=[r(pM)], writes=[r(amrow)], tag="q2ev")
        for blk in range(NBLK2):
            P.op("tensor",
                 lambda e, blk=blk: e.matmul(
                     pH[:, (blk % 4) * 128:(blk % 4) * 128 + 128],
                     sel16[0:NBLK2, blk * 128:(blk + 1) * 128],
                     amrow[0:NBLK2, :], start=True, stop=True),
                 reads=[r(sel16), r(amrow)], writes=[r(pH)], tag="q2b")
            P.op("scalar",
                 lambda e, blk=blk: e.activation(
                     amax2[:, blk * 128:(blk + 1) * 128],
                     pH[:, (blk % 4) * 128:(blk % 4) * 128 + 128], AF.Copy),
                 reads=[r(pH)], writes=dst_w, tag="q2be")

    q2_pmax([r(runm2)], [r(amax2)])
    qsc2 = runm2
    P.op("vector",
         lambda e: e.tensor_scalar(amax2[:, :], amax2[:, :], 1e-5, None, ALU.max),
         reads=[r(amax2)], writes=[r(amax2)], tag="q2c")
    P.op("vector",
         lambda e: e.tensor_scalar(qsc2[:, :], amax2[:, :], 1.0 / 127.0, None,
                                   ALU.mult),
         reads=[r(amax2)], writes=[r(qsc2, "inv"), r(runm2)], tag="q2i")
    P.op("scalar", lambda e: e.activation(amax2[:, :], amax2[:, :], AF.Ln),
         reads=[r(amax2)], writes=[r(amax2)], tag="q2l")
    P.op("scalar", lambda e: e.activation(amax2[:, :], amax2[:, :], AF.Exp,
                                          scale=-1.0, bias=C.cv(LN127)),
         reads=[r(amax2), r(C.cvals)], writes=[r(amax2)], tag="q2e")

    P.fence()
    stW.close()
    stX.close()

    stW2 = contextlib.ExitStack()
    w2q = C.sb("w2q", [128, 32, D], BF16, stW2)
    stW2a = contextlib.ExitStack()
    w2S = [C.sb(f"w2S{i}", [128, 2, D], F32, stW2a) for i in range(2)]
    w2i = C.sb("w2i", [128, 2, D], I32, stW2a)
    w2_r = w2_d.rearrange("(a p) c -> p a c", p=128)
    for j in range(16):
        sl = j % 2
        C.dma("sync", w2S[sl][:, :, :], w2_r[:, 2 * j:2 * j + 2, :],
              [r(w2S[sl])], tag=f"w2p1_{j}")
        P.op("vector",
             lambda e, j=j, sl=sl: e.reduce_sum(
                 racc[:, j:j + 1], w2S[sl][:, :, :], AX.XY,
                 apply_absolute_value=True),
             reads=[r(w2S[sl])], writes=[r(racc, j)], tag="w2r")
    P.op("vector",
         lambda e: e.reduce_sum(rac1[:, :], racc[:, :], AX.X),
         reads=[r(racc, jj) for jj in range(16)], writes=[r(rac1)], tag="w2r2")
    P.op("scalar", lambda e: e.activation(racb[:, :], rac1[:, :], AF.Copy),
         reads=[r(rac1)], writes=[r(racb)], tag="w2cb")
    P.op("tensor",
         lambda e: e.matmul(pM[:, 0:1], C.ones1[:, :], racb[:, :],
                            start=True, stop=True),
         reads=[r(racb), r(C.ones1)], writes=[r(pM)], tag="w2sm")
    w_scale(msw2, sw2, "w2")
    for j in range(16):
        sl = j % 2
        C.dma("sync", w2S[sl][:, :, :], w2_r[:, 2 * j:2 * j + 2, :],
              [r(w2S[sl])], tag=f"w2p2_{j}")
        P.op("vector",
             lambda e, sl=sl: e.tensor_scalar(w2S[sl][:, :, :], w2S[sl][:, :, :],
                                              sw2[:, 0:1], None, ALU.mult),
             reads=[r(w2S[sl]), r(sw2)], writes=[r(w2S[sl])], tag="w2qm")
        P.op("scalar", lambda e, sl=sl: e.sign(w2i[:, :, :], w2S[sl][:, :, :]),
             reads=[r(w2S[sl])], writes=[r(w2i)], tag="w2sg")
        P.op("vector",
             lambda e, sl=sl: e.scalar_tensor_tensor(
                 w2S[sl][:, :, :], w2i[:, :, :], C.halfsh[:, 0:1],
                 w2S[sl][:, :, :], ALU.mult, ALU.add),
             reads=[r(w2i), r(C.halfsh), r(w2S[sl])], writes=[r(w2S[sl])],
             tag="w2sh")
        P.op("vector", lambda e, sl=sl: e.tensor_copy(w2i[:, :, :],
                                                      w2S[sl][:, :, :]),
             reads=[r(w2S[sl])], writes=[r(w2i)], tag="w2ri")
        P.op("vector",
             lambda e, j=j: e.tensor_scalar(w2q[:, 2 * j:2 * j + 2, :],
                                            w2i[:, :, :], 1.0, -1.0,
                                            ALU.min, ALU.max),
             reads=[r(w2i)], writes=[r(w2q, j // 2)], tag="w2qc")

    P.fence()
    stW2a.close()

    xq2 = C.sb("xq2", [128, 32, 512], BF16, stW2)
    h2r = [C.sb(f"h2r{i}", [128, 4, 512], F32, stW2) for i in range(2)]
    h2i = C.sb("h2i", [128, 4, 512], I32, stW2)
    oT2 = C.sb("f1oT", [128, 8, 512], BF16, stW2)
    for g in range(NG):
        c0 = g * 512
        gb = g % 2
        for m8 in range(8):
            sl = m8 % 2
            C.dma("sync", h2r[sl][:, :, :],
                  h2_r[:, 4 * m8:4 * m8 + 4, c0:c0 + 512], [r(h2r[sl])],
                  reads=[("dram:h2", 4 * m8 + i_) for i_ in range(4)],
                  tag=f"h2i{g}_{m8}")
            P.op("vector",
                 lambda e, sl=sl, c0=c0: e.tensor_tensor(
                     h2r[sl][:, :, :], h2r[sl][:, :, :],
                     sap(amax2, 0, 128, c0, [[0, 4], [1, 512]]), ALU.mult),
                 reads=[r(h2r[sl]), r(amax2)], writes=[r(h2r[sl])], tag="q2m")
            P.op("scalar", lambda e, sl=sl: e.sign(h2i[:, :, :], h2r[sl][:, :, :]),
                 reads=[r(h2r[sl])], writes=[r(h2i)], tag="q2sg")
            P.op("vector",
                 lambda e, sl=sl: e.scalar_tensor_tensor(
                     h2r[sl][:, :, :], h2i[:, :, :], C.halfsh[:, 0:1],
                     h2r[sl][:, :, :], ALU.mult, ALU.add),
                 reads=[r(h2i), r(C.halfsh), r(h2r[sl])], writes=[r(h2r[sl])],
                 tag="q2sh")
            P.op("vector", lambda e, sl=sl: e.tensor_copy(h2i[:, :, :],
                                                          h2r[sl][:, :, :]),
                 reads=[r(h2r[sl])], writes=[r(h2i)], tag="q2ri")
            P.op("vector",
                 lambda e, m8=m8: e.tensor_copy(xq2[:, 4 * m8:4 * m8 + 4, :],
                                                h2i[:, :, :]),
                 reads=[r(h2i)], writes=[r(xq2, m8)], tag="q2b")
        for j in range(8):
            for k in range(32):
                P.op("tensor",
                     lambda e, j=j, k=k: e.matmul(
                         pH[:, :], w2q[:, k, j * 128:(j + 1) * 128],
                         xq2[:, k, :], start=(k == 0), stop=(k == 31)),
                     reads=[r(w2q, k // 4), r(xq2, k // 4)], writes=[r(pH)],
                     tag="o2mm")
            P.op("vector",
                 lambda e, j=j, c0=c0: e.scalar_tensor_tensor(
                     oT2[:, j, :], pH[:, :], msw2[:, 0:1],
                     qsc2[:, c0:c0 + 512], ALU.mult, ALU.mult),
                 reads=[r(pH), r(msw2), r(qsc2, "inv")], writes=[r(oT2, j)],
                 tag="o2ev")
        emit_ln_stats(C, oT2, lambda pt: [r(oT2, pt)], pS2, C.ones_r, m2sb, r2sb,
                      tmpb, tag="f1ln2")
        C.dma("sync", xgf[gb][:, :, :], x_src_r[:, :, c0:c0 + 512],
              [r(xgf[gb])], reads=[("dram:" + src_nm, g)], tag=f"fxr2{g}")
        for pt in range(8):
            P.op("vector",
                 lambda e, pt=pt: e.tensor_sub(lntf[:, :], oT2[:, pt, :],
                                               m2sb[:, :]),
                 reads=[r(oT2, pt), r(m2sb)], writes=[r(lntf)], tag="fb1")
            P.op("vector",
                 lambda e, pt=pt: e.tensor_mul(lntf[:, :], lntf[:, :], r2sb[:, :]),
                 reads=[r(lntf), r(r2sb)], writes=[r(lntf)], tag="fb2")
            P.op("vector",
                 lambda e, pt=pt, gb=gb: e.tensor_add(
                     xgf[gb][:, pt, :], xgf[gb][:, pt, :], lntf[:, :]),
                 reads=[r(lntf), r(xgf[gb])], writes=[r(xgf[gb])], tag="fb3")
        C.dma("sync", x_dst_r[:, :, c0:c0 + 512], xgf[gb][:, :, :],
              [("dram:" + dst_nm, g)], reads=[r(xgf[gb])], tag=f"fxo2{g}")

    P.fence()
    stW2.close()
    st.close()


_CACHE = {}
PHASES = ("l0a", "l0f", "l1a", "l1f")


def _build(S, phases):
    C = Ctx(S)
    nc = C.nc
    n = len(phases)
    x_in = C.dram_in("x_in", [D, S])
    x_out = C.dram_out("x_out", [D, S])
    xres = C.dram_tmp("xres", [D, S]) if n > 1 else None
    onr_d = C.dram_in("ones_r", [128, 128], BF16)
    on1_d = C.dram_in("ones_1", [128, 128], BF16)
    cv_d = C.dram_in("consts", [128, len(CONSTS)])
    C.ones_r = C.sb("ones_r_g", [128, 128], BF16)
    C.ones1 = C.sb("ones_1_g", [128, 128], BF16)
    C.cvals = C.sb("cvals_g", [128, len(CONSTS)], F32)
    C.dma("sync", C.ones_r[:, :], onr_d, [r(C.ones_r)])
    C.dma("sync", C.ones1[:, :], on1_d, [r(C.ones1)])
    C.dma("sync", C.cvals[:, :], cv_d, [r(C.cvals)])
    C.halfsh = C.sb("halfsh_g", [128, 1], F32)
    _hstmp = C.sb("hstmp_g", [128, 1], I32)
    C.plan.op("vector",
              lambda e: e.tensor_copy(_hstmp[:, :], C.cvals[:, 6:7]),
              reads=[r(C.cvals)], writes=[r(_hstmp)], tag="shdet1")
    C.plan.op("vector",
              lambda e: e.tensor_scalar(C.halfsh[:, :], _hstmp[:, :],
                                        -0.5, 0.5, ALU.mult, ALU.add),
              reads=[r(_hstmp)], writes=[r(C.halfsh)], tag="shdet2")
    sems, dma_pools = {}, {}
    sem_stack = contextlib.ExitStack()
    KPOOL = {"sync": 12, "gpsimd": 12, "scalar": 8, "vector": 2, "tensor": 2}
    for e in ENGINES:
        sems[e] = sem_stack.enter_context(nc.semaphore("s_" + e))
        dma_pools[e] = [sem_stack.enter_context(nc.semaphore(f"d_{e}{i}"))
                        for i in range(KPOOL[e])]
    for i, ph in enumerate(phases):
        src = x_in if i == 0 else xres
        dst = x_out if i == n - 1 else xres
        src_nm = "x_in" if i == 0 else "xres"
        dst_nm = "x_out" if i == n - 1 else "xres"
        if ph == "l0a":
            emit_l0a(C, src, dst, src_nm, dst_nm)
        elif ph == "l0f":
            emit_l0f(C, src, dst, src_nm, dst_nm)
        elif ph == "l1a":
            emit_l1a(C, src, dst, src_nm, dst_nm)
        elif ph == "l1f":
            emit_l1f(C, src, dst, src_nm, dst_nm)
        else:
            raise ValueError(ph)
    with nc.Block() as block:
        C.plan.resolve_and_emit(block, sems, dma_pools)

        @block.sync
        def _(sy):
            for e in ENGINES:
                nd = sum(1 for o in C.plan.ops[e] if o.dma)
                K = len(dma_pools[e])
                for s in range(min(nd, K)):
                    uses = (nd - s + K - 1) // K
                    sy.wait_ge(dma_pools[e][s], 16 * uses)
    sem_stack.close()
    C.stack.close()
    return C


def _host_inputs(ins, S, phases):
    import ml_dtypes
    bf = ml_dtypes.bfloat16
    x = np.asarray(ins["L0_attn_x"])
    B = x.shape[0]
    common = {}
    common["ones_r"] = np.full((128, 128), 1.0 / 1024.0, dtype=bf)
    common["ones_1"] = np.ones((128, 128), dtype=bf)
    common["consts"] = np.ascontiguousarray(
        np.broadcast_to(np.asarray(CONSTS, np.float32), (128, len(CONSTS))))
    if "l0a" in phases:
        wq_f = np.asarray(ins["L0_attn_Wqkv"], dtype=np.float32)
        mm_ = np.arange(1024)
        qcols = (2 * (mm_ // 128) + (mm_ % 128) // 64) * 192 + (mm_ % 64)
        vcols = (mm_ // 64) * 192 + 128 + (mm_ % 64)
        cols = np.concatenate([qcols, qcols + 64, vcols])
        common["wqkv"] = np.ascontiguousarray(wq_f[:, cols])
        wo_f = np.asarray(ins["L0_attn_Wo"], dtype=np.float32)
        kk = np.arange(8)[:, None]
        pp = np.arange(128)[None, :]
        perm = (512 * (pp // 64) + 64 * kk + (pp % 64)).reshape(-1)
        common["wo"] = np.ascontiguousarray(wo_f[perm])
        bqkv = np.asarray(ins["L0_attn_bqkv"], dtype=np.float32)
        idx = np.zeros((128, 8), np.int64)
        for i in range(8):
            for p in range(128):
                idx[p, i] = (2 * i + p // 64) * 192 + (p % 64)
        common["bq_sb"] = np.ascontiguousarray(bqkv[idx])
        common["bk_sb"] = np.ascontiguousarray(bqkv[idx + 64])
        bv = bqkv.reshape(16, 192)[:, 128:]
        common["bvn_bc"] = np.ascontiguousarray(
            np.broadcast_to(bv.reshape(1, 16 * 64), (128, 16 * 64)).astype(np.float32))
        common["bo_sb"] = np.ascontiguousarray(
            np.asarray(ins["L0_attn_bo"], dtype=np.float32).reshape(8, 128).T)
        rr = np.arange(128)[:, None]
        cc = np.arange(128)[None, :]
        common["maskB"] = np.where(np.abs(rr - cc) <= 16, 0.0, NEG).astype(np.float32)
        a = np.arange(32)[:, None]
        c2 = np.arange(128)[None, :]
        common["maskA"] = np.where((c2 >= a - 48) & (c2 <= a - 16), 0.0,
                                   NEG).astype(np.float32)
        common["maskC"] = np.where(c2 >= a + 112, 0.0, NEG).astype(np.float32)
    if "l1a" in phases:
        wq_f = np.asarray(ins["L1_attn_Wq"], np.float32)
        common["wq_t"] = np.ascontiguousarray(wq_f.T)
        common["wk_t"] = np.ascontiguousarray(
            np.asarray(ins["L1_attn_Wk"], np.float32).T)
        common["wv_t"] = np.ascontiguousarray(
            np.asarray(ins["L1_attn_Wv"], np.float32).T)
        wf_f = np.asarray(ins["L1_attn_Wf"], np.float32).T
        kk = np.arange(8)[:, None]
        pp = np.arange(128)[None, :]
        perm = (512 * (pp // 64) + 64 * kk + (pp % 64)).reshape(-1)
        common["wf_t"] = np.ascontiguousarray(wf_f[perm])
        common["bf_sb"] = np.ascontiguousarray(
            np.asarray(ins["L1_attn_bf"], np.float32).reshape(8, 128).T)
        rr = np.arange(128)[:, None]
        cc = np.arange(128)[None, :]
        common["maskN"] = np.where(cc >= rr, 0.0, NEG).astype(np.float32)
        common["maskO"] = np.where(cc <= rr, 0.0, NEG).astype(np.float32)
    if "l1f" in phases:
        common["ident_f"] = np.eye(128, dtype=np.float32)
        s16 = np.zeros((16, 16 * 128), np.float32)
        for b_ in range(16):
            s16[b_, b_ * 128:(b_ + 1) * 128] = 1.0
        common["sel16f"] = s16
        common["w1_t"] = np.ascontiguousarray(
            np.asarray(ins["L1_ffn_w1"], np.float32).T)
        common["w2_t"] = np.ascontiguousarray(
            np.asarray(ins["L1_ffn_w2"], np.float32).T)
        cw_ = np.asarray(ins["L1_ffn_conv_w"], np.float32)[:, 0, :]
        cd = np.zeros((32, 3, 128, 128), np.float32)
        for m_ in range(32):
            for tp_ in range(3):
                np.fill_diagonal(cd[m_, tp_], cw_[m_ * 128:(m_ + 1) * 128, tp_])
        common["conv_diag"] = np.ascontiguousarray(cd.reshape(32 * 3 * 128, 128))
        common["convb_sb"] = np.ascontiguousarray(
            np.asarray(ins["L1_ffn_conv_b"], np.float32).reshape(32, 128).T)
    if "l0f" in phases:
        common["gate_wT"] = np.ascontiguousarray(
            np.asarray(ins["L0_ffn_gate_w"], np.float32).T)
        common["gbias_bc"] = np.ascontiguousarray(np.broadcast_to(
            np.asarray(ins["L0_ffn_gate_bias"], np.float32)[None, :], (128, 16)))
        es = np.zeros((16, 16 * 128), np.float32)
        for e_ in range(16):
            es[e_, e_ * 128:(e_ + 1) * 128] = 1.0
        common["esel"] = es.astype(bf)
        common["ident_bf"] = np.eye(128, dtype=np.float32).astype(bf)
        common["wsg"] = np.ascontiguousarray(
            np.asarray(ins["L0_ffn_Ws_g"], np.float32).T)
        common["wsu"] = np.ascontiguousarray(
            np.asarray(ins["L0_ffn_Ws_u"], np.float32).T)
        common["wsd"] = np.ascontiguousarray(
            np.asarray(ins["L0_ffn_Ws_d"], np.float32).T)
        common["wg_t"] = np.ascontiguousarray(
            np.asarray(ins["L0_ffn_Wg"], np.float32).transpose(0, 2, 1))
        common["wu_t"] = np.ascontiguousarray(
            np.asarray(ins["L0_ffn_Wu"], np.float32).transpose(0, 2, 1))
        common["wd_t"] = np.ascontiguousarray(
            np.asarray(ins["L0_ffn_Wd"], np.float32).transpose(0, 2, 1))
    in_maps = []
    for b in range(B):
        m = dict(common)
        m["x_in"] = np.ascontiguousarray(x[b].T.astype(np.float32))
        in_maps.append(m)
    return in_maps


def kernel(**ins):
    x = np.asarray(ins["L0_attn_x"])
    S = x.shape[1]
    key = (S, PHASES)
    if key not in _CACHE:
        _CACHE[key] = _build(S, PHASES)
    C = _CACHE[key]
    in_maps = _host_inputs(ins, S, PHASES)
    res = run_bass_kernel_spmd(C.nc, in_maps, core_ids=list(range(len(in_maps))))
    out = np.stack([res.results[b]["x_out"].T for b in range(len(in_maps))])
    return np.ascontiguousarray(out.astype(np.float32))


# version 14
# speedup vs baseline: 1.0955x; 1.0955x over previous
import contextlib

import numpy as np

import concourse.bass as bass
import concourse.mybir as mybir
from concourse.bass_utils import run_bass_kernel_spmd

F32 = mybir.dt.float32
BF16 = mybir.dt.bfloat16
I32 = mybir.dt.int32
AF = mybir.ActivationFunctionType
ALU = mybir.AluOpType
AX = mybir.AxisListType

NEG = -1.0e30
D = 1024

ENGINES = ("tensor", "vector", "scalar", "gpsimd", "sync")


class _Op:
    __slots__ = ("eng", "fn", "reads", "writes", "dma", "idx", "need_inc",
                 "deps", "dma_idx", "inc_no", "tag", "chain")

    def __init__(self, eng, fn, reads, writes, dma, idx, tag):
        self.eng, self.fn = eng, fn
        self.reads, self.writes = reads, writes
        self.dma, self.idx, self.tag = dma, idx, tag
        self.need_inc = False
        self.deps = set()
        self.dma_idx = None
        self.inc_no = None


class Plan:
    def __init__(self):
        self.ops = {e: [] for e in ENGINES}
        self.regions = {}
        self.fence_snaps = []
        self.buf_fence = {}
        self.cur_fence = -1

    def _reg(self, key):
        rg = self.regions.get(key)
        if rg is None:
            rg = [None, []]
            self.regions[key] = rg
        return rg

    def register_buf(self, name):
        self.buf_fence[name] = self.cur_fence

    def op(self, eng, fn, reads=(), writes=(), dma=False, tag="", chain=False):
        o = _Op(eng, fn, tuple(reads), tuple(writes), dma, len(self.ops[eng]), tag)
        o.chain = chain
        for rk in o.reads:
            rg = self._reg(rk)
            if rg[0] is not None:
                o.deps.add((rg[0].eng, rg[0].idx, rg[0].dma))
        for wk in o.writes:
            rg = self._reg(wk)
            if rg[0] is not None:
                o.deps.add((rg[0].eng, rg[0].idx, rg[0].dma))
            for rd in rg[1]:
                o.deps.add((rd.eng, rd.idx, rd.dma))
            f = self.buf_fence.get(wk[0], -1)
            if f >= 0:
                snap = self.fence_snaps[f]
                for e2 in ENGINES:
                    n = snap[e2]
                    if n > 0:
                        prev = self.ops[e2][n - 1]
                        o.deps.add((e2, n - 1, prev.dma))
                        last_dmas = []
                        for i2 in range(n - 1, -1, -1):
                            op2 = self.ops[e2][i2]
                            if op2.dma:
                                last_dmas.append(i2)
                                if len(last_dmas) >= 12:
                                    break
                        for i2 in last_dmas:
                            o.deps.add((e2, i2, True))
        for rk in o.reads:
            self._reg(rk)[1].append(o)
        for wk in o.writes:
            rg = self._reg(wk)
            rg[0] = o
            rg[1] = []
        o.deps = {d for d in o.deps if not (d[0] == eng and d[1] == o.idx)}
        self.ops[eng].append(o)
        return o

    def fence(self):
        self.fence_snaps.append({e: len(self.ops[e]) for e in ENGINES})
        self.cur_fence = len(self.fence_snaps) - 1

    def resolve_and_emit(self, block, sems, dma_pools):
        for e in ENGINES:
            for o in self.ops[e]:
                for (e2, i2, isdma) in o.deps:
                    if not isdma and e2 != e:
                        self.ops[e2][i2].need_inc = True
        for e in ENGINES:
            n = nd = 0
            K = len(dma_pools[e])
            for o in self.ops[e]:
                if o.dma:
                    o.dma_idx = nd
                    nd += 1
                elif o.need_inc:
                    n += 1
                    o.inc_no = n

        def dma_wait(eng_name, j):
            K = len(dma_pools[eng_name])
            return dma_pools[eng_name][j % K], 16 * (j // K + 1)

        def emit_engine(eng_name, eng):
            watermark = {}
            drained_upto = -1
            K = len(dma_pools[eng_name])
            for o in self.ops[eng_name]:
                waits = {}
                need_drain = False
                for (e2, i2, isdma) in sorted(o.deps):
                    if e2 == eng_name and not isdma:
                        if i2 > drained_upto and not (
                                eng_name == "tensor" and o.chain
                                and i2 == o.idx - 1):
                            need_drain = True
                        continue
                    if isdma:
                        j = self.ops[e2][i2].dma_idx
                        sem, val = dma_wait(e2, j)
                        k = (e2, "d", j % len(dma_pools[e2]))
                    else:
                        dep = self.ops[e2][i2]
                        sem, val = sems[e2], dep.inc_no
                        k = (e2, "c")
                    if watermark.get(k, -1) < val:
                        if k in waits:
                            waits[k] = (sem, max(val, waits[k][1]))
                        else:
                            waits[k] = (sem, val)
                if need_drain:
                    eng.drain()
                    drained_upto = o.idx - 1
                if o.dma and o.dma_idx >= K:
                    sem, val = dma_pools[eng_name][o.dma_idx % K],                         16 * (o.dma_idx // K)
                    k = (eng_name, "d", o.dma_idx % K)
                    if watermark.get(k, -1) < val:
                        waits[k] = (sem, max(val, waits.get(k, (sem, 0))[1]))
                for k, (sem, val) in waits.items():
                    eng.wait_ge(sem, val)
                    watermark[k] = val
                inst = o.fn(eng)
                if o.dma:
                    inst.then_inc(dma_pools[eng_name][o.dma_idx % K], 16)
                elif o.need_inc:
                    if eng_name == "tensor":
                        eng.drain().then_inc(sems[eng_name], 1)
                        drained_upto = o.idx
                    else:
                        inst.then_inc(sems[eng_name], 1)

        for name in ENGINES:
            getattr(block, name)(lambda eng, n=name: emit_engine(n, eng))


def r(buf, key=()):
    return (buf.name, key)


def sap(t, p0, pn, foff, dims):
    fs = 1
    for d in t.shape[1:]:
        fs *= d
    return bass.AP(t, p0 * fs + foff, [[fs, pn]] + [list(d) for d in dims])


CONSTS = [1e-5, 1e-20, 127.0, 0.5, 4.844187086458591, 2.384185791015625e-07, 0.7]


class Ctx:
    def __init__(self, S):
        self.S = S
        self.nc = bass.Bass("TRN2", target_bir_lowering=False)
        self.plan = Plan()
        self.stack = contextlib.ExitStack()
        self.input_names = []
        self.cvals = None
        self.ones_r = None
        self.ones1 = None
        self.halfsh = None

    def cv(self, value):
        j = CONSTS.index(value)
        return self.cvals[:, j:j + 1]

    def dram_in(self, name, shape, dtype=F32):
        t = self.nc.dram_tensor(name, list(shape), dtype, kind="ExternalInput")
        self.input_names.append(name)
        return t.ap()

    def dram_out(self, name, shape, dtype=F32):
        return self.nc.dram_tensor(name, list(shape), dtype, kind="ExternalOutput").ap()

    def dram_tmp(self, name, shape, dtype=F32):
        return self.nc.dram_tensor(name, list(shape), dtype, kind="Internal").ap()

    def sb(self, name, shape, dtype, stack=None):
        h = (stack or self.stack).enter_context(
            self.nc.sbuf_tensor(name, list(shape), dtype))
        self.plan.register_buf(name)
        return h

    def ps(self, name, shape, dtype=F32, stack=None):
        h = (stack or self.stack).enter_context(
            self.nc.psum_tensor(name, list(shape), dtype))
        self.plan.register_buf(name)
        return h

    def dma(self, eng, out_ap, in_ap, writes, reads=(), tag="dma"):
        return self.plan.op(eng, lambda e: e.dma_start(out=out_ap, in_=in_ap),
                            reads=reads, writes=writes, dma=True, tag=tag)


def emit_ln_stats(C, src, src_rd, pstat, ones_r, m_sb, r_sb, tmpb, tag, c0=0):
    P = C.plan
    for pt in range(8):
        P.op("scalar",
             lambda e, pt=pt: e.activation(tmpb[:, :], src[:, pt, c0:c0 + 512],
                                           AF.Copy),
             reads=src_rd(pt), writes=[r(tmpb)], tag=tag + "c")
        P.op("tensor",
             lambda e, pt=pt: e.matmul(pstat[:, :], ones_r[:, :], tmpb[:, :],
                                       start=(pt == 0), stop=(pt == 7)),
             reads=[r(tmpb), r(ones_r)], writes=[r(pstat)], tag=tag + "ms", chain=(pt != 0))
    P.op("scalar", lambda e: e.activation(m_sb[:, :], pstat[:, :], AF.Copy),
         reads=[r(pstat)], writes=[r(m_sb)], tag=tag + "mc")
    for pt in range(8):
        P.op("scalar",
             lambda e, pt=pt: e.activation(tmpb[:, :], src[:, pt, c0:c0 + 512],
                                           AF.Square),
             reads=src_rd(pt), writes=[r(tmpb)], tag=tag + "q")
        P.op("tensor",
             lambda e, pt=pt: e.matmul(pstat[:, :], ones_r[:, :], tmpb[:, :],
                                       start=(pt == 0), stop=(pt == 7)),
             reads=[r(tmpb), r(ones_r)], writes=[r(pstat)], tag=tag + "qs", chain=(pt != 0))
    P.op("vector", lambda e: e.tensor_mul(r_sb[:, :], m_sb[:, :], m_sb[:, :]),
         reads=[r(m_sb)], writes=[r(r_sb)], tag=tag + "mm")
    P.op("vector", lambda e: e.tensor_sub(r_sb[:, :], pstat[:, :], r_sb[:, :]),
         reads=[r(pstat), r(r_sb)], writes=[r(r_sb)], tag=tag + "var")
    P.op("scalar", lambda e: e.activation(r_sb[:, :], r_sb[:, :], AF.Ln,
                                          bias=C.cv(1e-5)),
         reads=[r(r_sb), r(C.cvals)], writes=[r(r_sb)], tag=tag + "ln")
    P.op("scalar", lambda e: e.activation(r_sb[:, :], r_sb[:, :], AF.Exp, scale=-0.5),
         reads=[r(r_sb)], writes=[r(r_sb)], tag=tag + "ex")


def emit_l0a(C, x_src_dram, x_dst_dram, src_nm, dst_nm):
    P, nc, S = C.plan, C.nc, C.S
    NT, NG = S // 128, S // 512
    st = contextlib.ExitStack()

    wqkv = C.dram_in("wqkv", [D, 3 * D])
    wo_d = C.dram_in("wo", [D, D])
    bq_d = C.dram_in("bq_sb", [128, 8])
    bk_d = C.dram_in("bk_sb", [128, 8])
    bo_d = C.dram_in("bo_sb", [128, 8])
    bvn_d = C.dram_in("bvn_bc", [128, 16 * 64])
    mB_d = C.dram_in("maskB", [128, 128])
    mA_d = C.dram_in("maskA", [32, 128])
    mC_d = C.dram_in("maskC", [32, 128])

    ones_r = C.ones_r
    ones1 = C.ones1
    mB = C.sb("mB_s", [128, 128], F32, st)
    mA = C.sb("mA_s", [128, 128], F32, st)
    mC = C.sb("mC_s", [128, 128], F32, st)
    bq = C.sb("bq_s", [128, 8], F32, st)
    bk = C.sb("bk_s", [128, 8], F32, st)
    bo = C.sb("bo_s", [128, 8], F32, st)
    bvn = C.sb("bvn_s", [128, 16 * 64], BF16, st)
    wqs = C.sb("wqs", [128, 8, 3 * D], BF16, st)
    wos = C.sb("wos", [128, 8, D], BF16, st)
    xg = [C.sb(f"xg{i}", [128, 8, 512], F32, st) for i in range(2)]
    ln1 = [C.sb(f"l0ln1_{i}", [128, 8, 512], BF16, st) for i in range(2)]
    lntmp = C.sb("l0lntmp", [128, 512], F32, st)
    qT = C.sb("qT", [128, 8, 8 * 128], BF16, st)
    kT = C.sb("kT", [128, 8, 8 * 128], BF16, st)
    vn = C.sb("vn", [128, 8, 16 * 64], BF16, st)
    vals = C.sb("vals", [128, 8, 2 * 512], BF16, st)
    exB = C.sb("exB", [128, 8 * 128], BF16, st)
    exAC = C.sb("exAC", [128, 8 * 128], BF16, st)
    sums = C.sb("sums", [128, 8 * 128], BF16, st)
    rinv = C.sb("rinv", [128, 8 * 128], BF16, st)
    m_sb = C.sb("l0m", [128, 512], F32, st)
    r_sb = C.sb("l0r", [128, 512], F32, st)
    m2_sb = C.sb("l0m2", [128, 512], F32, st)
    r2_sb = C.sb("l0r2", [128, 512], F32, st)
    tmpb = C.sb("l0tmpb", [128, 512], BF16, st)
    oT = C.sb("l0oT", [128, 8, 512], BF16, st)

    pSC = C.ps("pSC", [128, 1024], F32, st)
    pPV = C.ps("pPV", [128, 1024], F32, st)
    pBC = C.ps("pBC", [128, 1024], F32, st)
    pA1 = C.ps("pA1", [128, 512], F32, st)
    pA2 = C.ps("pA2", [128, 512], F32, st)

    C.dma("sync", mB[:, :], mB_d, [r(mB)])
    C.dma("sync", mA[96:128, :], mA_d, [r(mA)])
    C.dma("sync", mC[0:32, :], mC_d, [r(mC)])
    C.dma("sync", bq[:, :], bq_d, [r(bq)])
    C.dma("sync", bk[:, :], bk_d, [r(bk)])
    C.dma("sync", bo[:, :], bo_d, [r(bo)])
    C.dma("gpsimd", bvn[:, :], bvn_d, [r(bvn)])
    wq_r = wqkv.rearrange("(a p) c -> p a c", p=128)
    for h4 in range(4):
        C.dma("gpsimd", wqs[:, 2 * h4:2 * h4 + 2, :], wq_r[:, 2 * h4:2 * h4 + 2, :],
              [r(wqs, h4)], tag="wqkv")
    wo_r = wo_d.rearrange("(a p) c -> p a c", p=128)
    C.dma("gpsimd", wos[:, :, :], wo_r, [r(wos)], tag="wo")

    x_src_r = x_src_dram.rearrange("(a p) c -> p a c", p=128)
    x_dst_r = x_dst_dram.rearrange("(a p) c -> p a c", p=128)

    def ln1_group(g):
        gb = g % 2
        C.dma("sync", xg[gb][:, :, :], x_src_r[:, :, g * 512:(g + 1) * 512],
              [r(xg[gb])], reads=[("dram:" + src_nm, g)], tag=f"xg{g}")
        emit_ln_stats(C, xg[gb], lambda pt: [r(xg[gb])], pA2, ones_r, m_sb, r_sb,
                      tmpb, tag="ln1")
        for pt in range(8):
            P.op("vector",
                 lambda e, pt=pt, gb=gb: e.tensor_sub(
                     lntmp[:, :], xg[gb][:, pt, :], m_sb[:, :]),
                 reads=[r(xg[gb]), r(m_sb)], writes=[r(lntmp)], tag="l1s")
            P.op("vector",
                 lambda e, pt=pt, gb=gb: e.tensor_mul(
                     ln1[gb][:, pt, :], lntmp[:, :], r_sb[:, :]),
                 reads=[r(lntmp), r(r_sb)], writes=[r(ln1[gb])], tag="l1m")

    def qkv_group(g):
        gb = g % 2
        s0 = (4 * g) % 8
        for part, dst, bias, base in (("q", qT, bq, 0), ("k", kT, bk, D)):
            for i in range(8):
                for k in range(8):
                    P.op("tensor",
                         lambda e, i=i, k=k, gb=gb, base=base: e.matmul(
                             pA1[:, :], wqs[:, k, base + i * 128:base + (i + 1) * 128],
                             ln1[gb][:, k, :],
                             start=(k == 0), stop=(k == 7)),
                         reads=[r(wqs, k // 2), r(ln1[gb])], writes=[r(pA1)],
                         tag=f"{part}mm", chain=(k != 0))
                P.op("scalar",
                     lambda e, i=i, s0=s0, dst=dst, bias=bias: e.activation(
                         dst[:, i, s0 * 128:(s0 + 4) * 128], pA1[:, :], AF.Identity,
                         bias=bias[:, i:i + 1]),
                     reads=[r(pA1), r(bias)],
                     writes=[r(dst, (4 * g + w) % 8) for w in range(4)],
                     tag=f"{part}ev")
        for w in range(4):
            t = 4 * g + w
            sl = t % 8
            for ng in range(2):
                for k in range(8):
                    P.op("tensor",
                         lambda e, ng=ng, k=k, gb=gb, w=w: e.matmul(
                             pA2[:, :], ln1[gb][:, k, w * 128:(w + 1) * 128],
                             wqs[:, k, 2 * D + ng * 512:2 * D + (ng + 1) * 512],
                             start=(k == 0), stop=(k == 7)),
                         reads=[r(wqs, k // 2), r(ln1[gb])], writes=[r(pA2)],
                         tag="vmm", chain=(k != 0))
                P.op("vector",
                     lambda e, sl=sl, ng=ng: e.tensor_add(
                         vn[:, sl, ng * 512:(ng + 1) * 512], pA2[:, :],
                         bvn[:, ng * 512:(ng + 1) * 512]),
                     reads=[r(pA2), r(bvn)], writes=[r(vn, sl)], tag="vev")

    def attn_tile(t):
        sl = t % 8
        has_a, has_c = t > 0, t < NT - 1
        g, w = t // 4, t % 4
        gsl = g % 2
        for hg in range(2):
            hbase = hg * 8
            base = 0 if hg == 0 else 64
            for hl in range(8):
                h = hbase + hl
                po = (h % 2) * 64
                kap = sap(kT, po, 64, (h // 2) * 1024 + sl * 128, [[1, 128]])
                qap = sap(qT, po, 64, (h // 2) * 1024 + sl * 128, [[1, 128]])
                P.op("tensor",
                     lambda e, kap=kap, qap=qap, hl=hl, po=po: e.matmul(
                         pSC[:, hl * 128:(hl + 1) * 128], kap, qap,
                         start=True, stop=True),
                     reads=[r(kT, sl), r(qT, sl)], writes=[r(pSC)], tag="scB")
            P.op("vector",
                 lambda e: e.tensor_add(
                     pSC[:, :], pSC[:, :],
                     bass.AP(mB, 0, [[128, 128], [0, 8], [1, 128]])),
                 reads=[r(pSC), r(mB)], writes=[r(pSC)], tag="mskB")
            P.op("scalar", lambda e: e.activation(exB[:, :], pSC[:, :], AF.Exp,
                                                  scale=0.125),
                 reads=[r(pSC)], writes=[r(exB)], tag="expB")
            for hl in range(8):
                h = hbase + hl
                po = (h % 2) * 64
                if has_a:
                    kap = sap(kT, po, 64, (h // 2) * 1024 + ((t - 1) % 8) * 128 + 96,
                              [[1, 32]])
                    qap = sap(qT, po, 64, (h // 2) * 1024 + sl * 128, [[1, 128]])
                    P.op("tensor",
                         lambda e, kap=kap, qap=qap, hl=hl, po=po: e.matmul(
                             sap(pSC, 96, 32, hl * 128, [[1, 128]]), kap, qap,
                             start=True, stop=True, tile_position=(po, 96)),
                         reads=[r(kT, (t - 1) % 8), r(qT, sl)],
                         writes=[r(pSC)], tag="scA")
                if has_c:
                    kap = sap(kT, po, 64, (h // 2) * 1024 + ((t + 1) % 8) * 128,
                              [[1, 32]])
                    qap = sap(qT, po, 64, (h // 2) * 1024 + sl * 128, [[1, 128]])
                    P.op("tensor",
                         lambda e, kap=kap, qap=qap, hl=hl, po=po: e.matmul(
                             sap(pSC, 0, 32, hl * 128, [[1, 128]]), kap, qap,
                             start=True, stop=True, tile_position=(po, 0)),
                         reads=[r(kT, (t + 1) % 8), r(qT, sl)],
                         writes=[r(pSC)], tag="scC")
            if has_a:
                P.op("vector",
                     lambda e: e.tensor_add(
                         sap(pSC, 96, 32, 0, [[128, 8], [1, 128]]),
                         sap(pSC, 96, 32, 0, [[128, 8], [1, 128]]),
                         bass.AP(mA, 96 * 128, [[128, 32], [0, 8], [1, 128]])),
                     reads=[r(pSC), r(mA)], writes=[r(pSC)], tag="mskA")
                P.op("scalar",
                     lambda e: e.activation(exAC[96:128, :], pSC[96:128, :], AF.Exp,
                                            scale=0.125),
                     reads=[r(pSC)], writes=[r(exAC, "A")], tag="expA")
            if has_c:
                P.op("vector",
                     lambda e: e.tensor_add(
                         sap(pSC, 0, 32, 0, [[128, 8], [1, 128]]),
                         sap(pSC, 0, 32, 0, [[128, 8], [1, 128]]),
                         bass.AP(mC, 0, [[128, 32], [0, 8], [1, 128]])),
                     reads=[r(pSC), r(mC)], writes=[r(pSC)], tag="mskC")
                P.op("scalar",
                     lambda e: e.activation(exAC[0:32, :], pSC[0:32, :], AF.Exp,
                                            scale=0.125),
                     reads=[r(pSC)], writes=[r(exAC, "C")], tag="expC")
            chunks = [("B", exB, 0, 128, sl)]
            if has_a:
                chunks.append(("A", exAC, 96, 32, (t - 1) % 8))
            if has_c:
                chunks.append(("C", exAC, 0, 32, (t + 1) % 8))
            for half in range(2):
                for ci, (nm, exb, rb, rn, _ksl) in enumerate(chunks):
                    P.op("tensor",
                         lambda e, exb=exb, rb=rb, rn=rn, half=half,
                                ci=ci, last=(ci == len(chunks) - 1): e.matmul(
                             sap(pBC, 96, 1, half * 512, [[1, 512]]),
                             sap(ones1, rb, rn, 0, [[1, 1]]),
                             sap(exb, rb, rn, half * 512, [[1, 512]]),
                             start=(ci == 0), stop=last, tile_position=(rb, 96)),
                         reads=[r(exb, nm if exb is exAC else ()), r(ones1)],
                         writes=[r(pBC)], tag="sums", chain=(ci != 0))
            P.op("vector", lambda e: e.tensor_copy(sums[96:97, :], pBC[96:97, :]),
                 reads=[r(pBC)], writes=[r(sums)], tag="sumcp")
            for half in range(2):
                P.op("tensor",
                     lambda e, half=half, base=base: e.matmul(
                         sap(pBC, base, 64, half * 512, [[1, 512]]),
                         sap(ones1, 96, 1, 0, [[1, 64]]),
                         sap(sums, 96, 1, half * 512, [[1, 512]]),
                         start=True, stop=True, tile_position=(96, base)),
                     reads=[r(sums), r(ones1)], writes=[r(pBC)], tag="bcast")
            P.op("scalar",
                 lambda e, base=base: e.activation(
                     rinv[base:base + 64, :], pBC[base:base + 64, :], AF.Ln),
                 reads=[r(pBC)], writes=[r(rinv)], tag="rln")
            P.op("scalar",
                 lambda e, base=base: e.activation(
                     rinv[base:base + 64, :], rinv[base:base + 64, :], AF.Exp,
                     scale=-1.0),
                 reads=[r(rinv)], writes=[r(rinv)], tag="rex")
            for hl in range(8):
                h = hbase + hl
                vcol = h * 64
                for ci, (nm, exb, rb, rn, ksl) in enumerate(chunks):
                    P.op("tensor",
                         lambda e, exb=exb, rb=rb, rn=rn, ksl=ksl, vcol=vcol, hl=hl,
                                base=base, ci=ci,
                                last=(ci == len(chunks) - 1): e.matmul(
                             sap(pPV, base, 64, hl * 128, [[1, 128]]),
                             sap(vn, rb, rn, ksl * 1024 + vcol, [[1, 64]]),
                             sap(exb, rb, rn, hl * 128, [[1, 128]]),
                             start=(ci == 0), stop=last, tile_position=(rb, base)),
                         reads=[r(vn, ksl), r(exb, nm if exb is exAC else ())],
                         writes=[r(pPV)], tag="pv", chain=(ci != 0))
            vo = gsl * 512 + w * 128
            P.op("vector",
                 lambda e, base=base, vo=vo: e.tensor_mul(
                     sap(vals, base, 64, vo, [[1024, 8], [1, 128]]),
                     sap(pPV, base, 64, 0, [[128, 8], [1, 128]]),
                     sap(rinv, base, 64, 0, [[128, 8], [1, 128]])),
                 reads=[r(pPV), r(rinv)], writes=[r(vals, (gsl, w))],
                 tag="valev")

    def wo_group(g):
        gsl = g % 2
        for j in range(8):
            for k in range(8):
                P.op("tensor",
                     lambda e, j=j, k=k, gsl=gsl: e.matmul(
                         pA1[:, :], wos[:, k, j * 128:(j + 1) * 128],
                         vals[:, k, gsl * 512:(gsl + 1) * 512],
                         start=(k == 0), stop=(k == 7)),
                     reads=[r(wos)] + [r(vals, (gsl, w)) for w in range(4)],
                     writes=[r(pA1)], tag="womm", chain=(k != 0))
            P.op("scalar",
                 lambda e, j=j: e.activation(oT[:, j, :], pA1[:, :], AF.Identity,
                                             bias=bo[:, j:j + 1]),
                 reads=[r(pA1), r(bo)], writes=[r(oT, j)], tag="woev")

    def ln2_add_group(g):
        xbuf = xg[g % 2]
        emit_ln_stats(C, oT, lambda pt: [r(oT, pt)], pA2, ones_r, m2_sb, r2_sb,
                      tmpb, tag="ln2")
        for pt in range(8):
            P.op("vector",
                 lambda e, pt=pt: e.tensor_sub(lntmp[:, :], oT[:, pt, :], m2_sb[:, :]),
                 reads=[r(oT, pt), r(m2_sb)], writes=[r(lntmp)], tag="a1")
            P.op("vector",
                 lambda e, pt=pt: e.tensor_mul(lntmp[:, :], lntmp[:, :], r2_sb[:, :]),
                 reads=[r(lntmp), r(r2_sb)], writes=[r(lntmp)], tag="a2")
            P.op("vector",
                 lambda e, pt=pt, xbuf=xbuf: e.tensor_add(
                     xbuf[:, pt, :], xbuf[:, pt, :], lntmp[:, :]),
                 reads=[r(lntmp), r(xbuf)], writes=[r(xbuf)], tag="a3")
        C.dma("sync", x_dst_r[:, :, g * 512:(g + 1) * 512], xbuf[:, :, :],
              [("dram:" + dst_nm, g)], reads=[r(xbuf)], tag=f"xo{g}")

    def attn_batch(g):
        if g == 0:
            ts = list(range(0, min(3, NT)))
        else:
            ts = list(range(4 * g - 1, min(4 * g + 3, NT)))
        if g == NG - 1:
            ts += list(range(max(4 * g + 3, ts[-1] + 1 if ts else 0), NT))
        for t in ts:
            attn_tile(t)

    ln1_group(0)
    qkv_group(0)
    attn_batch(0)
    for g in range(1, NG):
        ln1_group(g)
        qkv_group(g)
        attn_batch(g)
        wo_group(g - 1)
        ln2_add_group(g - 1)
    wo_group(NG - 1)
    ln2_add_group(NG - 1)

    P.fence()
    st.close()


def emit_l0f(C, x_src_dram, x_dst_dram, src_nm, dst_nm):
    P, nc, S = C.plan, C.nc, C.S
    NT, NG = S // 128, S // 512
    st = contextlib.ExitStack()

    gw_d = C.dram_in("gate_wT", [D, 16])
    gb_d = C.dram_in("gbias_bc", [128, 16])
    esel_d = C.dram_in("esel", [16, 16 * 128], BF16)
    idb_d = C.dram_in("ident_bf", [128, 128], BF16)
    wsg_d = C.dram_in("wsg", [D, D])
    wsu_d = C.dram_in("wsu", [D, D])
    wsd_d = C.dram_in("wsd", [D, D])
    wg_d = C.dram_in("wg_t", [16, D, 512])
    wu_d = C.dram_in("wu_t", [16, D, 512])
    wd_d = C.dram_in("wd_t", [16, 512, D])

    ln1f = C.sb("ln1f", [128, 8, S], BF16, st)
    y_sb = C.sb("y_sb", [128, 8, S], F32, st)
    cwT = C.sb("cwT", [128, S], BF16, st)
    identb = C.sb("identb_s", [128, 128], BF16, st)
    gbias = C.sb("gbias_s", [128, 16], F32, st)
    gws = C.sb("gws", [128, 8, 16], BF16, st)
    xg2 = [C.sb(f"xg2_{i}", [128, 8, 512], F32, st) for i in range(2)]
    lnm = C.sb("f0m", [128, 512], F32, st)
    lnr = C.sb("f0r", [128, 512], F32, st)
    lnm2 = C.sb("f0m2", [128, 512], F32, st)
    lnr2 = C.sb("f0r2", [128, 512], F32, st)
    tmpb = C.sb("f0tmpb", [128, 512], BF16, st)
    lnt = C.sb("f0lnt", [128, 512], F32, st)

    C.dma("gpsimd", gws[:, :, :], gw_d.rearrange("(a p) c -> p a c", p=128),
          [r(gws)], tag="gw")
    C.dma("sync", gbias[:, :], gb_d, [r(gbias)])
    C.dma("sync", identb[:, :], idb_d, [r(identb)])

    x_src_r = x_src_dram.rearrange("(a p) c -> p a c", p=128)
    x_dst_r = x_dst_dram.rearrange("(a p) c -> p a c", p=128)

    stA = contextlib.ExitStack()
    wsg = C.sb("wsg_s", [128, 8, D], BF16, stA)
    wsu = C.sb("wsu_s", [128, 8, D], BF16, stA)
    wsd = C.sb("wsd_s", [128, 8, D], BF16, stA)
    sh_g = C.sb("sh_g", [128, 8, 512], BF16, stA)
    pF1 = C.ps("pF1", [128, 512], F32, stA)
    pF2 = C.ps("pF2", [128, 512], F32, stA)
    pF3 = C.ps("pF3", [128, 512], F32, stA)
    sco = C.sb("sco", [128, 16], F32, stA)
    sfc = C.sb("sfc", [128, 16], F32, stA)
    msk = C.sb("msk", [128, 16], F32, stA)
    sel = C.sb("sel", [128, 16], F32, stA)
    oneh = C.sb("oneh", [128, 16], F32, stA)
    emk = C.sb("emk", [128, 16], F32, stA)
    p8a = C.sb("p8a", [128, 8], F32, stA)
    p8b = C.sb("p8b", [128, 8], F32, stA)
    p4a = C.sb("p4a", [128, 4], F32, stA)
    p4b = C.sb("p4b", [128, 4], F32, stA)
    p4c = C.sb("p4c", [128, 4], F32, stA)
    p4d = C.sb("p4d", [128, 4], F32, stA)
    s1a = C.sb("s1a", [128, 1], F32, stA)
    s1b = C.sb("s1b", [128, 1], F32, stA)
    s1c = C.sb("s1c", [128, 1], F32, stA)
    cwb = C.sb("cwb", [128, 16], BF16, stA)

    C.dma("gpsimd", wsg[:, :, :], wsg_d.rearrange("(a p) c -> p a c", p=128),
          [r(wsg)], tag="wsg")
    C.dma("gpsimd", wsu[:, :, :], wsu_d.rearrange("(a p) c -> p a c", p=128),
          [r(wsu)], tag="wsu")
    C.dma("gpsimd", wsd[:, :, :], wsd_d.rearrange("(a p) c -> p a c", p=128),
          [r(wsd)], tag="wsd")

    def ln1_group(g):
        gb = g % 2
        C.dma("sync", xg2[gb][:, :, :], x_src_r[:, :, g * 512:(g + 1) * 512],
              [r(xg2[gb])], reads=[("dram:" + src_nm, g)], tag=f"fxg{g}")
        emit_ln_stats(C, xg2[gb], lambda pt: [r(xg2[gb])], pF1, C.ones_r, lnm, lnr,
                      tmpb, tag="fln1")
        for pt in range(8):
            P.op("vector",
                 lambda e, pt=pt, gb=gb: e.tensor_sub(
                     lnt[:, :], xg2[gb][:, pt, :], lnm[:, :]),
                 reads=[r(xg2[gb]), r(lnm)], writes=[r(lnt)], tag="f1s")
            P.op("vector",
                 lambda e, pt=pt, gb=gb, g=g: e.tensor_mul(
                     ln1f[:, pt, g * 512:(g + 1) * 512], lnt[:, :], lnr[:, :]),
                 reads=[r(lnt), r(lnr)], writes=[r(ln1f, (g, pt))], tag="f1m")

    def routing_tile(t):
        g = t // 4
        for k in range(8):
            P.op("tensor",
                 lambda e, k=k, t=t: e.matmul(
                     pF1[:, 0:16], ln1f[:, k, t * 128:(t + 1) * 128], gws[:, k, :],
                     start=(k == 0), stop=(k == 7)),
                 reads=[r(ln1f, (g, k)), r(gws)], writes=[r(pF1)], tag="gmm", chain=(k != 0))
        P.op("scalar", lambda e: e.activation(sco[:, :], pF1[:, 0:16], AF.Sigmoid),
             reads=[r(pF1)], writes=[r(sco)], tag="sig")
        P.op("vector", lambda e: e.tensor_add(sfc[:, :], sco[:, :], gbias[:, :]),
             reads=[r(sco), r(gbias)], writes=[r(sfc)], tag="sfc")
        ev = lambda b, o, n, s: sap(b, 0, 128, o, [[s, n]])
        P.op("vector", lambda e: e.tensor_max(p8a[:, :], ev(sfc, 0, 8, 2),
                                              ev(sfc, 1, 8, 2)),
             reads=[r(sfc)], writes=[r(p8a)], tag="hi")
        P.op("vector", lambda e: e.tensor_tensor(p8b[:, :], ev(sfc, 0, 8, 2),
                                                 ev(sfc, 1, 8, 2), ALU.min),
             reads=[r(sfc)], writes=[r(p8b)], tag="lo")
        P.op("vector", lambda e: e.tensor_max(p4a[:, :], ev(p8a, 0, 4, 2),
                                              ev(p8a, 1, 4, 2)),
             reads=[r(p8a)], writes=[r(p4a)], tag="t1")
        P.op("vector", lambda e: e.tensor_tensor(p4b[:, :], ev(p8a, 0, 4, 2),
                                                 ev(p8a, 1, 4, 2), ALU.min),
             reads=[r(p8a)], writes=[r(p4b)], tag="mn")
        P.op("vector", lambda e: e.tensor_max(p4c[:, :], ev(p8b, 0, 4, 2),
                                              ev(p8b, 1, 4, 2)),
             reads=[r(p8b)], writes=[r(p4c)], tag="ml")
        P.op("vector", lambda e: e.tensor_max(p4b[:, :], p4b[:, :], p4c[:, :]),
             reads=[r(p4b), r(p4c)], writes=[r(p4b)], tag="sec")
        P.op("vector", lambda e: e.tensor_add(p4a[:, :], p4a[:, :], p4b[:, :]),
             reads=[r(p4a), r(p4b)], writes=[r(p4a)], tag="g2")
        P.op("vector", lambda e: e.tensor_max(s1a[:, :], p4a[:, 0:1], p4a[:, 1:2]),
             reads=[r(p4a)], writes=[r(s1a)], tag="a1")
        P.op("vector", lambda e: e.tensor_max(s1b[:, :], p4a[:, 2:3], p4a[:, 3:4]),
             reads=[r(p4a)], writes=[r(s1b)], tag="b1")
        P.op("vector", lambda e: e.tensor_tensor(s1a[:, :], s1a[:, :], s1b[:, :],
                                                 ALU.min),
             reads=[r(s1a), r(s1b)], writes=[r(s1a)], tag="u")
        P.op("vector", lambda e: e.tensor_tensor(s1b[:, :], p4a[:, 0:1], p4a[:, 1:2],
                                                 ALU.min),
             reads=[r(p4a), r(s1b)], writes=[r(s1b)], tag="c1")
        P.op("vector", lambda e: e.tensor_tensor(s1c[:, :], p4a[:, 2:3], p4a[:, 3:4],
                                                 ALU.min),
             reads=[r(p4a)], writes=[r(s1c)], tag="d1")
        P.op("vector", lambda e: e.tensor_max(s1b[:, :], s1b[:, :], s1c[:, :]),
             reads=[r(s1b), r(s1c)], writes=[r(s1b)], tag="v")
        P.op("vector", lambda e: e.tensor_max(s1a[:, :], s1a[:, :], s1b[:, :]),
             reads=[r(s1a), r(s1b)], writes=[r(s1a)], tag="gthr")
        P.op("vector",
             lambda e: e.tensor_single_scalar(p4d[:, :], p4a[:, :], s1a[:, 0:1],
                                              ALU.is_ge),
             reads=[r(p4a), r(s1a)], writes=[r(p4d)], tag="gmask")
        P.op("vector",
             lambda e: e.tensor_copy(sap(emk, 0, 128, 0, [[4, 4], [1, 4]]),
                                     sap(p4d, 0, 128, 0, [[1, 4], [0, 4]])),
             reads=[r(p4d)], writes=[r(emk)], tag="emk")
        P.op("vector",
             lambda e: e.tensor_scalar(oneh[:, :], emk[:, :], 1e30, -1e30,
                                       ALU.mult, ALU.add),
             reads=[r(emk)], writes=[r(oneh)], tag="emsk")
        P.op("vector", lambda e: e.tensor_add(msk[:, :], sfc[:, :], oneh[:, :]),
             reads=[r(sfc), r(oneh)], writes=[r(msk)], tag="mskd")
        P.op("vector", lambda e: e.memset(sel[:, :], 0.0),
             writes=[r(sel)], tag="sel0")
        for it in range(4):
            P.op("vector", lambda e: e.reduce_max(s1b[:, :], msk[:, :], AX.X),
                 reads=[r(msk)], writes=[r(s1b)], tag="mx")
            P.op("vector",
                 lambda e: e.tensor_single_scalar(oneh[:, :], msk[:, :], s1b[:, 0:1],
                                                  ALU.is_ge),
                 reads=[r(msk), r(s1b)], writes=[r(oneh)], tag="oh")
            P.op("vector", lambda e: e.tensor_add(sel[:, :], sel[:, :], oneh[:, :]),
                 reads=[r(sel), r(oneh)], writes=[r(sel)], tag="seladd")
            if it < 3:
                P.op("vector",
                     lambda e: e.scalar_tensor_tensor(msk[:, :], oneh[:, :], -4e30,
                                                      msk[:, :], ALU.mult, ALU.add),
                     reads=[r(oneh), r(msk)], writes=[r(msk)], tag="knock")
        P.op("vector", lambda e: e.tensor_mul(oneh[:, :], sco[:, :], sel[:, :]),
             reads=[r(sco), r(sel)], writes=[r(oneh)], tag="cwr")
        P.op("vector", lambda e: e.reduce_sum(s1b[:, :], oneh[:, :], AX.X),
             reads=[r(oneh)], writes=[r(s1b)], tag="den")
        P.op("scalar", lambda e: e.activation(s1b[:, :], s1b[:, :], AF.Ln,
                                              bias=C.cv(1e-20)),
             reads=[r(s1b), r(C.cvals)], writes=[r(s1b)], tag="dln")
        P.op("scalar", lambda e: e.activation(s1b[:, :], s1b[:, :], AF.Exp,
                                              scale=-1.0),
             reads=[r(s1b)], writes=[r(s1b)], tag="dex")
        P.op("vector",
             lambda e: e.tensor_scalar(cwb[:, :], oneh[:, :], s1b[:, 0:1], 2.5,
                                       ALU.mult, ALU.mult),
             reads=[r(oneh), r(s1b)], writes=[r(cwb)], tag="cw")
        P.op("tensor",
             lambda e: e.matmul(pF1[0:16, 0:128], cwb[:, :], identb[:, :],
                                start=True, stop=True),
             reads=[r(cwb), r(identb)], writes=[r(pF1)], tag="cwt")
        P.op("vector",
             lambda e, t=t: e.tensor_copy(cwT[0:16, t * 128:(t + 1) * 128],
                                          pF1[0:16, 0:128]),
             reads=[r(pF1)], writes=[r(cwT, t)], tag="cwev")

    def shared_group(g):
        c0 = g * 512
        for j in range(8):
            for k in range(8):
                P.op("tensor",
                     lambda e, j=j, k=k, c0=c0: e.matmul(
                         pF2[:, :], wsg[:, k, j * 128:(j + 1) * 128],
                         ln1f[:, k, c0:c0 + 512], start=(k == 0), stop=(k == 7)),
                     reads=[r(wsg), r(ln1f, (g, k))], writes=[r(pF2)], tag="sgmm", chain=(k != 0))
            P.op("scalar", lambda e, j=j: e.activation(sh_g[:, j, :], pF2[:, :],
                                                       AF.Silu),
                 reads=[r(pF2)], writes=[r(sh_g, j)], tag="sgsilu")
        for j in range(8):
            for k in range(8):
                P.op("tensor",
                     lambda e, j=j, k=k, c0=c0: e.matmul(
                         pF2[:, :], wsu[:, k, j * 128:(j + 1) * 128],
                         ln1f[:, k, c0:c0 + 512], start=(k == 0), stop=(k == 7)),
                     reads=[r(wsu), r(ln1f, (g, k))], writes=[r(pF2)], tag="summ", chain=(k != 0))
            P.op("vector", lambda e, j=j: e.tensor_mul(sh_g[:, j, :], pF2[:, :],
                                                       sh_g[:, j, :]),
                 reads=[r(pF2), r(sh_g, j)], writes=[r(sh_g, j)], tag="suact")
        for j in range(8):
            for k in range(8):
                P.op("tensor",
                     lambda e, j=j, k=k: e.matmul(
                         pF3[:, :], wsd[:, k, j * 128:(j + 1) * 128],
                         sh_g[:, k, :], start=(k == 0), stop=(k == 7)),
                     reads=[r(wsd), r(sh_g, k)], writes=[r(pF3)], tag="sdmm", chain=(k != 0))
            P.op("scalar",
                 lambda e, j=j, c0=c0: e.activation(y_sb[:, j, c0:c0 + 512],
                                                    pF3[:, :], AF.Copy),
                 reads=[r(pF3)], writes=[r(y_sb, ("c", g))], tag="sdev")

    for g in range(NG):
        ln1_group(g)
        for w in range(4):
            routing_tile(4 * g + w)
        shared_group(g)

    P.fence()
    stA.close()

    stB = contextlib.ExitStack()
    esel = C.sb("esel_s", [128, 16 * 128], BF16, stB)
    C.dma("sync", esel[0:16, :], esel_d, [r(esel)])
    wgS = [C.sb(f"wgS{i}", [128, 8, 512], BF16, stB) for i in range(2)]
    wuS = [C.sb(f"wuS{i}", [128, 8, 512], BF16, stB) for i in range(2)]
    wdS = [C.sb(f"wdS{i}", [128, 4, D], BF16, stB) for i in range(2)]
    silu_e = C.sb("silu_e", [128, 2048], BF16, stB)
    act_e = silu_e
    cw_bc = C.sb("cw_bc", [128, S], BF16, stB)
    pGU = C.ps("pGU", [128, 2048], F32, stB)
    pY = C.ps("pY", [128, 2048], F32, stB)
    NC4 = S // 512

    def load_expert(e):
        sl = e % 2
        C.dma("gpsimd", wgS[sl][:, :, :],
              wg_d[e].rearrange("(a p) c -> p a c", p=128), [r(wgS[sl])],
              tag=f"wg{e}")
        C.dma("gpsimd", wuS[sl][:, :, :],
              wu_d[e].rearrange("(a p) c -> p a c", p=128), [r(wuS[sl])],
              tag=f"wu{e}")
        C.dma("gpsimd", wdS[sl][:, :, :],
              wd_d[e].rearrange("(a p) c -> p a c", p=128), [r(wdS[sl])],
              tag=f"wd{e}")

    load_expert(0)
    for e in range(16):
        sl = e % 2
        if e + 1 < 16:
            load_expert(e + 1)
        for q in range(NC4):
            P.op("tensor",
                 lambda e2, e=e, q=q: e2.matmul(
                     pGU[:, q * 512:(q + 1) * 512], esel[0:16, e * 128:(e + 1) * 128],
                     cwT[0:16, q * 512:(q + 1) * 512], start=True, stop=True),
                 reads=[r(esel)] + [r(cwT, 4 * q + w) for w in range(4)],
                 writes=[r(pGU)], tag="cwbc")
        P.op("scalar", lambda e2: e2.activation(cw_bc[:, 0:512 * NC4],
                                                pGU[:, 0:512 * NC4], AF.Copy),
             reads=[r(pGU)], writes=[r(cw_bc)], tag="cwbev")
        for c in range(NC4):
            c0 = c * 512
            for m in range(4):
                for k in range(8):
                    P.op("tensor",
                         lambda e2, m=m, k=k, c0=c0, sl=sl: e2.matmul(
                             pGU[:, m * 512:(m + 1) * 512],
                             wgS[sl][:, k, m * 128:(m + 1) * 128],
                             ln1f[:, k, c0:c0 + 512], start=(k == 0), stop=(k == 7)),
                         reads=[r(wgS[sl]), r(ln1f, (c, k))], writes=[r(pGU)],
                         tag="egmm", chain=(k != 0))
            P.op("scalar", lambda e2: e2.activation(silu_e[:, :], pGU[:, :], AF.Silu),
                 reads=[r(pGU)], writes=[r(silu_e)], tag="esilu")
            for m in range(4):
                for k in range(8):
                    P.op("tensor",
                         lambda e2, m=m, k=k, c0=c0, sl=sl: e2.matmul(
                             pGU[:, m * 512:(m + 1) * 512],
                             wuS[sl][:, k, m * 128:(m + 1) * 128],
                             ln1f[:, k, c0:c0 + 512], start=(k == 0), stop=(k == 7)),
                         reads=[r(wuS[sl]), r(ln1f, (c, k))], writes=[r(pGU)],
                         tag="eumm", chain=(k != 0))
            P.op("vector", lambda e2: e2.tensor_mul(act_e[:, :], pGU[:, :],
                                                    silu_e[:, :]),
                 reads=[r(pGU), r(silu_e)], writes=[r(silu_e)], tag="eact1")
            P.op("vector",
                 lambda e2, c0=c0: e2.tensor_mul(
                     act_e[:, :], act_e[:, :],
                     sap(cw_bc, 0, 128, c0, [[0, 4], [1, 512]])),
                 reads=[r(silu_e), r(cw_bc)], writes=[r(silu_e)], tag="eact2")
            for jh in range(2):
                for m2 in range(4):
                    j = jh * 4 + m2
                    for k in range(4):
                        P.op("tensor",
                             lambda e2, j=j, m2=m2, k=k, sl=sl: e2.matmul(
                                 pY[:, m2 * 512:(m2 + 1) * 512],
                                 wdS[sl][:, k, j * 128:(j + 1) * 128],
                                 sap(act_e, 0, 128, k * 512, [[1, 512]]),
                                 start=(k == 0), stop=(k == 3)),
                             reads=[r(wdS[sl]), r(silu_e)], writes=[r(pY)],
                             tag="edmm", chain=(k != 0))
                P.op("vector",
                     lambda e2, jh=jh, c0=c0: e2.tensor_add(
                         bass.AP(y_sb, (jh * 4) * S + c0, [[8 * S, 128], [S, 4], [1, 512]]),
                         bass.AP(y_sb, (jh * 4) * S + c0, [[8 * S, 128], [S, 4], [1, 512]]),
                         pGU[:, :] if False else bass.AP(pY, 0, [[2048, 128], [512, 4], [1, 512]])),
                     reads=[r(pY), r(y_sb, ("c", c))], writes=[r(y_sb, ("c", c))],
                     tag="eyadd")

    P.fence()
    stB.close()

    stC = contextlib.ExitStack()
    pT = C.ps("pT", [128, 512], F32, stC)
    for g in range(NG):
        c0 = g * 512
        P.op("vector",
             lambda e, c0=c0, g=g: e.tensor_add(
                 bass.AP(y_sb, c0, [[8 * S, 128], [S, 8], [1, 512]]),
                 bass.AP(y_sb, c0, [[8 * S, 128], [S, 8], [1, 512]]),
                 bass.AP(ln1f, c0, [[8 * S, 128], [S, 8], [1, 512]])),
             reads=[r(y_sb, ("c", g))] + [r(ln1f, (g, pt)) for pt in range(8)],
             writes=[r(y_sb, ("c", g))], tag="bout")
        emit_ln_stats(C, y_sb, lambda pt, g=g: [r(y_sb, ("c", g))], pT, C.ones_r,
                      lnm2, lnr2, tmpb, tag="fln2", c0=c0)
        gb = g % 2
        C.dma("sync", xg2[gb][:, :, :], x_src_r[:, :, c0:c0 + 512],
              [r(xg2[gb])], reads=[("dram:" + src_nm, g)], tag=f"fxr{g}")
        for pt in range(8):
            P.op("vector",
                 lambda e, pt=pt, c0=c0: e.tensor_sub(lnt[:, :],
                                                      y_sb[:, pt, c0:c0 + 512],
                                                      lnm2[:, :]),
                 reads=[r(y_sb, ("c", g)), r(lnm2)], writes=[r(lnt)], tag="fa1")
            P.op("vector",
                 lambda e, pt=pt: e.tensor_mul(lnt[:, :], lnt[:, :], lnr2[:, :]),
                 reads=[r(lnt), r(lnr2)], writes=[r(lnt)], tag="fa2")
            P.op("vector",
                 lambda e, pt=pt, gb=gb: e.tensor_add(
                     xg2[gb][:, pt, :], xg2[gb][:, pt, :], lnt[:, :]),
                 reads=[r(lnt), r(xg2[gb])], writes=[r(xg2[gb])], tag="fa3")
        C.dma("sync", x_dst_r[:, :, c0:c0 + 512], xg2[gb][:, :, :],
              [("dram:" + dst_nm, g)], reads=[r(xg2[gb])], tag=f"fxo{g}")

    P.fence()
    stC.close()
    st.close()


def emit_l1a(C, x_src_dram, x_dst_dram, src_nm, dst_nm):
    P, nc, S = C.plan, C.nc, C.S
    NT, NG = S // 128, S // 512
    st = contextlib.ExitStack()

    wq_d = C.dram_in("wq_t", [D, D])
    wk_d = C.dram_in("wk_t", [D, 64])
    wv_d = C.dram_in("wv_t", [D, 64])
    wf_d = C.dram_in("wf_t", [D, D])
    bf_d = C.dram_in("bf_sb", [128, 8])
    mo_d = C.dram_in("maskO", [128, 128])
    mn_d = C.dram_in("maskN", [128, 128])

    mO = C.sb("mO_s", [128, 128], F32, st)
    mN = C.sb("mN_s", [128, 128], F32, st)
    bf_s = C.sb("bf_s", [128, 8], F32, st)
    wqf = C.sb("wqf", [128, 8, D], BF16, st)
    wfs2 = C.sb("wfs2", [128, 8, D], BF16, st)
    wks = C.sb("wks", [128, 8, 64], BF16, st)
    wvs = C.sb("wvs", [128, 8, 64], BF16, st)
    qT1 = C.sb("qT1", [128, 8, S], BF16, st)
    kTd = C.sb("kTd", [128, S], BF16, st)
    v_n = C.sb("v_n", [128, NT * 64], BF16, st)
    ln1a = [C.sb(f"l1ln_{i}", [128, 8, 512], BF16, st) for i in range(2)]
    xga = [C.sb(f"xga{i}", [128, 8, 512], F32, st) for i in range(2)]
    vals1 = C.sb("vals1", [128, 8, 2 * 512], BF16, st)
    exL = C.sb("exL", [128, 3 * 1024], BF16, st)
    sums1 = C.sb("sums1", [128, 1024], BF16, st)
    rinv1 = C.sb("rinv1", [128, 1024], BF16, st)
    m_sb = C.sb("a1m", [128, 512], F32, st)
    r_sb = C.sb("a1r", [128, 512], F32, st)
    m2sb = C.sb("a1m2", [128, 512], F32, st)
    r2sb = C.sb("a1r2", [128, 512], F32, st)
    tmpb = C.sb("a1tmpb", [128, 512], BF16, st)
    lnt1 = C.sb("a1lnt", [128, 512], F32, st)
    oT1 = C.sb("a1oT", [128, 8, 512], BF16, st)

    pSC1 = C.ps("pSC1", [128, 1024], F32, st)
    pPV1 = C.ps("pPV1", [128, 1024], F32, st)
    pBC1 = C.ps("pBC1", [128, 1024], F32, st)
    pQ1 = C.ps("pQ1", [128, 512], F32, st)
    pS1 = C.ps("pS1", [128, 512], F32, st)

    C.dma("sync", mO[:, :], mo_d, [r(mO)])
    C.dma("sync", mN[:, :], mn_d, [r(mN)])
    C.dma("sync", bf_s[:, :], bf_d, [r(bf_s)])
    C.dma("gpsimd", wqf[:, :, :], wq_d.rearrange("(a p) c -> p a c", p=128),
          [r(wqf, "q")], tag="wq")
    C.dma("gpsimd", wfs2[:, :, :], wf_d.rearrange("(a p) c -> p a c", p=128),
          [r(wfs2)], tag="wf")
    C.dma("gpsimd", wks[:, :, :], wk_d.rearrange("(a p) c -> p a c", p=128),
          [r(wks)], tag="wk")
    C.dma("gpsimd", wvs[:, :, :], wv_d.rearrange("(a p) c -> p a c", p=128),
          [r(wvs)], tag="wv")

    x_src_r = x_src_dram.rearrange("(a p) c -> p a c", p=128)
    x_dst_r = x_dst_dram.rearrange("(a p) c -> p a c", p=128)

    def ln1_group(g):
        gb = g % 2
        C.dma("sync", xga[gb][:, :, :], x_src_r[:, :, g * 512:(g + 1) * 512],
              [r(xga[gb])], reads=[("dram:" + src_nm, g)], tag=f"axg{g}")
        emit_ln_stats(C, xga[gb], lambda pt: [r(xga[gb])], pS1, C.ones_r, m_sb,
                      r_sb, tmpb, tag="aln1")
        for pt in range(8):
            P.op("vector",
                 lambda e, pt=pt, gb=gb: e.tensor_sub(
                     lnt1[:, :], xga[gb][:, pt, :], m_sb[:, :]),
                 reads=[r(xga[gb]), r(m_sb)], writes=[r(lnt1)], tag="as")
            P.op("vector",
                 lambda e, pt=pt, gb=gb: e.tensor_mul(
                     ln1a[gb][:, pt, :], lnt1[:, :], r_sb[:, :]),
                 reads=[r(lnt1), r(r_sb)], writes=[r(ln1a[gb])], tag="am")

    def qkv_group(g):
        gb = g % 2
        c0 = g * 512
        for i in range(8):
            for k in range(8):
                P.op("tensor",
                     lambda e, i=i, k=k, gb=gb: e.matmul(
                         pQ1[:, :], wqf[:, k, i * 128:(i + 1) * 128],
                         ln1a[gb][:, k, :], start=(k == 0), stop=(k == 7)),
                     reads=[r(wqf, "q"), r(ln1a[gb])], writes=[r(pQ1)], tag="qmm", chain=(k != 0))
            P.op("scalar",
                 lambda e, i=i, c0=c0: e.activation(
                     qT1[:, i, c0:c0 + 512], pQ1[:, :], AF.Copy),
                 reads=[r(pQ1)], writes=[r(qT1, g)], tag="qev")
        for half in range(2):
            for k in range(8):
                P.op("tensor",
                     lambda e, half=half, k=k, gb=gb: e.matmul(
                         sap(pQ1, 64 * half, 64, 0, [[1, 512]]),
                         wks[:, k, :], ln1a[gb][:, k, :],
                         start=(k == 0), stop=(k == 7),
                         tile_position=(0, 64 * half)),
                     reads=[r(wks), r(ln1a[gb])], writes=[r(pQ1)], tag="kmm", chain=(k != 0))
        P.op("scalar",
             lambda e, c0=c0: e.activation(kTd[:, c0:c0 + 512], pQ1[:, :], AF.Copy),
             reads=[r(pQ1)], writes=[r(kTd, g)], tag="kev")
        for w in range(4):
            t = 4 * g + w
            for k in range(8):
                P.op("tensor",
                     lambda e, w=w, k=k, gb=gb: e.matmul(
                         sap(pQ1, 0, 128, 0, [[1, 64]]),
                         ln1a[gb][:, k, w * 128:(w + 1) * 128], wvs[:, k, :],
                         start=(k == 0), stop=(k == 7)),
                     reads=[r(wvs), r(ln1a[gb])], writes=[r(pQ1)], tag="vmm1", chain=(k != 0))
            P.op("scalar",
                 lambda e, t=t: e.activation(v_n[:, t * 64:(t + 1) * 64],
                                             sap(pQ1, 0, 128, 0, [[1, 64]]),
                                             AF.Copy),
                 reads=[r(pQ1)], writes=[r(v_n, t)], tag="vev1")

    def attn_tile(t):
        g, w = t // 4, t % 4
        gsl = g % 2
        kts = [kt for kt in (t - 2, t - 1, t) if kt >= 0]
        for hg in range(2):
            hbase = hg * 8
            base = 0 if hg == 0 else 64
            for ci, kt in enumerate(kts):
                for hl in range(8):
                    h = hbase + hl
                    po = (h % 2) * 64
                    P.op("tensor",
                         lambda e, hl=hl, h=h, po=po, kt=kt: e.matmul(
                             pSC1[:, hl * 128:(hl + 1) * 128],
                             sap(kTd, po, 64, kt * 128, [[1, 128]]),
                             sap(qT1, po, 64, (h // 2) * S + t * 128, [[1, 128]]),
                             start=True, stop=True),
                         reads=[r(kTd, kt // 4), r(qT1, g)], writes=[r(pSC1)],
                         tag="sc1")
                if kt == t:
                    P.op("vector",
                         lambda e: e.tensor_add(
                             pSC1[:, :], pSC1[:, :],
                             bass.AP(mN, 0, [[128, 128], [0, 8], [1, 128]])),
                         reads=[r(pSC1), r(mN)], writes=[r(pSC1)], tag="mn1")
                elif kt == t - 2:
                    P.op("vector",
                         lambda e: e.tensor_add(
                             pSC1[:, :], pSC1[:, :],
                             bass.AP(mO, 0, [[128, 128], [0, 8], [1, 128]])),
                         reads=[r(pSC1), r(mO)], writes=[r(pSC1)], tag="mo1")
                P.op("scalar",
                     lambda e, ci=ci: e.activation(
                         exL[:, ci * 1024:(ci + 1) * 1024], pSC1[:, :], AF.Exp,
                         scale=0.125),
                     reads=[r(pSC1)], writes=[r(exL, ci)], tag="ex1")
            for half in range(2):
                for ci in range(len(kts)):
                    P.op("tensor",
                         lambda e, ci=ci, half=half,
                                last=(ci == len(kts) - 1): e.matmul(
                             sap(pBC1, 96, 1, half * 512, [[1, 512]]),
                             sap(C.ones1, 0, 128, 0, [[1, 1]]),
                             sap(exL, 0, 128, ci * 1024 + half * 512, [[1, 512]]),
                             start=(ci == 0), stop=last, tile_position=(0, 96)),
                         reads=[r(exL, ci), r(C.ones1)], writes=[r(pBC1)],
                         tag="sums1", chain=(ci != 0))
            P.op("vector", lambda e: e.tensor_copy(sums1[96:97, :], pBC1[96:97, :]),
                 reads=[r(pBC1)], writes=[r(sums1)], tag="sumcp1")
            for half in range(2):
                P.op("tensor",
                     lambda e, half=half, base=base: e.matmul(
                         sap(pBC1, base, 64, half * 512, [[1, 512]]),
                         sap(C.ones1, 96, 1, 0, [[1, 64]]),
                         sap(sums1, 96, 1, half * 512, [[1, 512]]),
                         start=True, stop=True, tile_position=(96, base)),
                     reads=[r(sums1), r(C.ones1)], writes=[r(pBC1)], tag="bc1")
            P.op("scalar",
                 lambda e, base=base: e.activation(
                     rinv1[base:base + 64, :], pBC1[base:base + 64, :], AF.Ln),
                 reads=[r(pBC1)], writes=[r(rinv1)], tag="rln1")
            P.op("scalar",
                 lambda e, base=base: e.activation(
                     rinv1[base:base + 64, :], rinv1[base:base + 64, :], AF.Exp,
                     scale=-1.0),
                 reads=[r(rinv1)], writes=[r(rinv1)], tag="rex1")
            for hl in range(8):
                for ci, kt in enumerate(kts):
                    P.op("tensor",
                         lambda e, hl=hl, ci=ci, kt=kt, base=base,
                                last=(ci == len(kts) - 1): e.matmul(
                             sap(pPV1, base, 64, hl * 128, [[1, 128]]),
                             sap(v_n, 0, 128, kt * 64, [[1, 64]]),
                             sap(exL, 0, 128, ci * 1024 + hl * 128, [[1, 128]]),
                             start=(ci == 0), stop=last, tile_position=(0, base)),
                         reads=[r(v_n, kt), r(exL, ci)], writes=[r(pPV1)],
                         tag="pv1", chain=(ci != 0))
            vo = gsl * 512 + w * 128
            P.op("vector",
                 lambda e, base=base, vo=vo: e.tensor_mul(
                     sap(vals1, base, 64, vo, [[1024, 8], [1, 128]]),
                     sap(pPV1, base, 64, 0, [[128, 8], [1, 128]]),
                     sap(rinv1, base, 64, 0, [[128, 8], [1, 128]])),
                 reads=[r(pPV1), r(rinv1)], writes=[r(vals1, (gsl, w))],
                 tag="vev1b")

    def wf_group(g):
        gsl = g % 2

        for j in range(8):
            for k in range(8):
                P.op("tensor",
                     lambda e, j=j, k=k, gsl=gsl: e.matmul(
                         pQ1[:, :], wfs2[:, k, j * 128:(j + 1) * 128],
                         vals1[:, k, gsl * 512:(gsl + 1) * 512],
                         start=(k == 0), stop=(k == 7)),
                     reads=[r(wfs2)] + [r(vals1, (gsl, w)) for w in range(4)],
                     writes=[r(pQ1)], tag="wfmm", chain=(k != 0))
            P.op("scalar",
                 lambda e, j=j: e.activation(oT1[:, j, :], pQ1[:, :], AF.Identity,
                                             bias=bf_s[:, j:j + 1]),
                 reads=[r(pQ1), r(bf_s)], writes=[r(oT1, j)], tag="wfev")

    def ln2_add_group(g):
        xbuf = xga[g % 2]
        emit_ln_stats(C, oT1, lambda pt: [r(oT1, pt)], pS1, C.ones_r, m2sb, r2sb,
                      tmpb, tag="aln2")
        for pt in range(8):
            P.op("vector",
                 lambda e, pt=pt: e.tensor_sub(lnt1[:, :], oT1[:, pt, :], m2sb[:, :]),
                 reads=[r(oT1, pt), r(m2sb)], writes=[r(lnt1)], tag="aa1")
            P.op("vector",
                 lambda e, pt=pt: e.tensor_mul(lnt1[:, :], lnt1[:, :], r2sb[:, :]),
                 reads=[r(lnt1), r(r2sb)], writes=[r(lnt1)], tag="aa2")
            P.op("vector",
                 lambda e, pt=pt, xbuf=xbuf: e.tensor_add(
                     xbuf[:, pt, :], xbuf[:, pt, :], lnt1[:, :]),
                 reads=[r(lnt1), r(xbuf)], writes=[r(xbuf)], tag="aa3")
        C.dma("sync", x_dst_r[:, :, g * 512:(g + 1) * 512], xbuf[:, :, :],
              [("dram:" + dst_nm, g)], reads=[r(xbuf)], tag=f"axo{g}")

    for g in range(NG):
        ln1_group(g)
        qkv_group(g)
        for w in range(4):
            attn_tile(4 * g + w)
        wf_group(g)
        ln2_add_group(g)

    P.fence()
    st.close()


def emit_l1f(C, x_src_dram, x_dst_dram, src_nm, dst_nm):
    P, nc, S = C.plan, C.nc, C.S
    NT, NG = S // 128, S // 512
    LN127 = 4.844187086458591
    st = contextlib.ExitStack()

    w1_d = C.dram_in("w1_t", [D, 4096])
    idf_d = C.dram_in("ident_f", [128, 128])
    sel_d = C.dram_in("sel16f", [16, 16 * 128])
    w2_d = C.dram_in("w2_t", [4096, D])
    cvd_d = C.dram_in("conv_diag", [32 * 3 * 128, 128])
    cb_d = C.dram_in("convb_sb", [128, 32])
    h2_h = C.dram_tmp("h2_hbm", [32 * 128, S])

    cb_s = C.sb("cb_s", [128, 32], F32, st)
    idf = C.sb("idf", [128, 128], F32, st)
    sel16 = C.sb("sel16", [128, 16 * 128], F32, st)
    amcmp = C.sb("amcmp", [128, 16], F32, st)
    amrow = C.sb("amrow", [128, 128], F32, st)
    amax2 = C.sb("amax2", [128, S], F32, st)
    runm2 = C.sb("runm2", [128, S], F32, st)
    xgf = [C.sb(f"xgf{i}", [128, 8, 512], F32, st) for i in range(2)]
    m2sb = C.sb("f1m2", [128, 512], F32, st)
    r2sb = C.sb("f1r2", [128, 512], F32, st)
    tmpb = C.sb("f1tmpb", [128, 512], BF16, st)
    lntf = C.sb("f1lnt", [128, 512], F32, st)
    racc = C.sb("racc", [128, 16], F32, st)
    rac1 = C.sb("rac1", [128, 1], F32, st)
    racb = C.sb("racb", [128, 1], BF16, st)
    msw1 = C.sb("msw1", [128, 1], F32, st)
    sw1 = C.sb("sw1", [128, 1], F32, st)
    msw2 = C.sb("msw2", [128, 1], F32, st)
    sw2 = C.sb("sw2", [128, 1], F32, st)

    stX = contextlib.ExitStack()
    xq1 = C.sb("xq1", [128, 8, S], BF16, stX)
    inv1 = C.sb("inv1", [128, S], BF16, stX)
    m_sb = C.sb("f1m", [128, 512], F32, stX)
    r_sb = C.sb("f1r", [128, 512], F32, stX)
    runm1 = C.sb("runm1", [128, 512], F32, stX)
    qs1 = C.sb("qs1", [128, 512], F32, stX)
    xqi = C.sb("xqi", [128, 512], I32, stX)
    sgn1 = C.sb("sgn1", [128, 512], F32, stX)

    pH = C.ps("pH", [128, 512], F32, st)
    pC = C.ps("pC", [128, 512], F32, st)
    pM = C.ps("pM", [128, 512], F32, st)
    pS2 = C.ps("pS2", [128, 512], F32, st)

    C.dma("sync", cb_s[:, :], cb_d, [r(cb_s)])
    C.dma("sync", idf[:, :], idf_d, [r(idf)])
    C.dma("sync", sel16[0:16, :], sel_d, [r(sel16)])

    x_src_r = x_src_dram.rearrange("(a p) c -> p a c", p=128)
    x_dst_r = x_dst_dram.rearrange("(a p) c -> p a c", p=128)
    h2_r = h2_h.rearrange("(a p) c -> p a c", p=128)


    def q1_pmax(src_r, dst_w):
        for blk in range(4):
            P.op("tensor",
                 lambda e, blk=blk: e.matmul(
                     pS2[:, (blk % 4) * 128:(blk % 4) * 128 + 128],
                     runm1[:, blk * 128:(blk + 1) * 128], idf[:, :],
                     start=True, stop=True),
                 reads=src_r + [r(idf)], writes=[r(pS2)], tag="q1t")
            P.op("vector",
                 lambda e, blk=blk: e.reduce_max(
                     amcmp[:, blk:blk + 1],
                     pS2[:, (blk % 4) * 128:(blk % 4) * 128 + 128], AX.X),
                 reads=[r(pS2)], writes=[r(amcmp, blk)], tag="q1r")
        P.op("tensor",
             lambda e: e.matmul(pM[0:4, 0:128], amcmp[:, 0:4],
                                idf[:, :], start=True, stop=True),
             reads=[r(amcmp, bb) for bb in range(4)] + [r(idf)],
             writes=[r(pM)], tag="q1tb")
        P.op("vector", lambda e: e.tensor_copy(amrow[0:4, :],
                                               pM[0:4, 0:128]),
             reads=[r(pM)], writes=[r(amrow)], tag="q1ev")
        for blk in range(4):
            P.op("tensor",
                 lambda e, blk=blk: e.matmul(
                     pS2[:, (blk % 4) * 128:(blk % 4) * 128 + 128],
                     sel16[0:4, blk * 128:(blk + 1) * 128],
                     amrow[0:4, :], start=True, stop=True),
                 reads=[r(sel16), r(amrow)], writes=[r(pS2)], tag="q1b")
            P.op("scalar",
                 lambda e, blk=blk: e.activation(
                     runm1[:, blk * 128:(blk + 1) * 128],
                     pS2[:, (blk % 4) * 128:(blk % 4) * 128 + 128], AF.Copy),
                 reads=[r(pS2)], writes=dst_w, tag="q1be")

    def quant1_group(g):
        gb = g % 2
        c0 = g * 512
        C.dma("sync", xgf[gb][:, :, :], x_src_r[:, :, c0:c0 + 512],
              [r(xgf[gb])], reads=[("dram:" + src_nm, g)], tag=f"qxg{g}")
        emit_ln_stats(C, xgf[gb], lambda pt: [r(xgf[gb])], pS2, C.ones_r, m_sb,
                      r_sb, tmpb, tag="q1ln")
        P.op("vector", lambda e: e.memset(runm1[:, :], 0.0),
             writes=[r(runm1)], tag="q1a0")
        for pt in range(8):
            P.op("vector",
                 lambda e, pt=pt, gb=gb: e.tensor_sub(
                     lntf[:, :], xgf[gb][:, pt, :], m_sb[:, :]),
                 reads=[r(xgf[gb]), r(m_sb)], writes=[r(lntf)], tag="q1s")
            P.op("scalar",
                 lambda e: e.activation(sgn1[:, :], lntf[:, :], AF.Abs),
                 reads=[r(lntf)], writes=[r(sgn1)], tag="q1ab")
            P.op("vector",
                 lambda e: e.tensor_max(runm1[:, :], runm1[:, :], sgn1[:, :]),
                 reads=[r(runm1), r(sgn1)], writes=[r(runm1)], tag="q1a")
        q1_pmax([r(runm1)], [r(runm1)])
        P.op("vector", lambda e: e.tensor_mul(qs1[:, :], runm1[:, :], r_sb[:, :]),
             reads=[r(runm1), r(r_sb)], writes=[r(qs1)], tag="q1t")
        P.op("vector",
             lambda e: e.tensor_scalar(qs1[:, :], qs1[:, :], 1e-5, None, ALU.max),
             reads=[r(qs1)], writes=[r(qs1)], tag="q1c")
        P.op("vector",
             lambda e, c0=c0: e.tensor_scalar(inv1[:, c0:c0 + 512], qs1[:, :],
                                              1.0 / 127.0, None, ALU.mult),
             reads=[r(qs1)], writes=[r(inv1, g)], tag="q1i")
        P.op("scalar", lambda e: e.activation(qs1[:, :], qs1[:, :], AF.Ln),
             reads=[r(qs1)], writes=[r(qs1)], tag="q1l")
        P.op("scalar", lambda e: e.activation(qs1[:, :], qs1[:, :], AF.Exp,
                                              scale=-1.0, bias=C.cv(LN127)),
             reads=[r(qs1), r(C.cvals)], writes=[r(qs1)], tag="q1e")
        P.op("vector", lambda e: e.tensor_mul(qs1[:, :], qs1[:, :], r_sb[:, :]),
             reads=[r(qs1), r(r_sb)], writes=[r(qs1)], tag="q1rs")
        for pt in range(8):
            P.op("vector",
                 lambda e, pt=pt, gb=gb: e.tensor_sub(
                     lntf[:, :], xgf[gb][:, pt, :], m_sb[:, :]),
                 reads=[r(xgf[gb]), r(m_sb)], writes=[r(lntf)], tag="q1s2")
            P.op("vector",
                 lambda e: e.tensor_mul(lntf[:, :], lntf[:, :], qs1[:, :]),
                 reads=[r(lntf), r(qs1)], writes=[r(lntf)], tag="q1m")
            P.op("scalar",
                 lambda e: e.sign(sgn1[:, :], lntf[:, :]),
                 reads=[r(lntf)], writes=[r(sgn1)], tag="q1sg")
            P.op("vector",
                 lambda e: e.scalar_tensor_tensor(
                     lntf[:, :], sgn1[:, :], C.halfsh[:, 0:1], lntf[:, :],
                     ALU.mult, ALU.add),
                 reads=[r(sgn1), r(C.halfsh), r(lntf)], writes=[r(lntf)],
                 tag="q1sh")
            P.op("vector",
                 lambda e: e.tensor_copy(xqi[:, :], lntf[:, :]),
                 reads=[r(lntf)], writes=[r(xqi)], tag="q1r")
            P.op("vector",
                 lambda e, pt=pt, c0=c0: e.tensor_copy(xq1[:, pt, c0:c0 + 512],
                                                       xqi[:, :]),
                 reads=[r(xqi)], writes=[r(xq1, (g, pt))], tag="q1b")

    for g in range(NG):
        quant1_group(g)

    stW = contextlib.ExitStack()
    cds2 = [C.sb(f"cds{i}", [128, 12, 128], BF16, stW) for i in range(2)]
    cvd_r = cvd_d.rearrange("(a p) c -> p a c", p=128)
    w1S = [C.sb(f"w1S{i}", [128, 8, 512], F32, stW) for i in range(2)]
    w1q = [C.sb(f"w1q{i}", [128, 8, 512], BF16, stW) for i in range(2)]
    w1i = C.sb("w1i", [128, 8, 512], I32, stW)
    h_s = [C.sb(f"h_s{i}", [128, S + 2], BF16, stW) for i in range(2)]
    h2m = [C.sb(f"h2m{i}", [128, S], F32, stW) for i in range(2)]
    habs = lntf

    def w_mean_pass(w_r, wS, name):
        for j in range(8):
            sl = j % 2
            C.dma("sync", wS[sl][:, :, :], w_r[:, :, j * 512:(j + 1) * 512],
                  [r(wS[sl])], tag=f"{name}p1_{j}")
            P.op("vector",
                 lambda e, j=j, sl=sl: e.reduce_sum(
                     racc[:, j:j + 1], wS[sl][:, :, :], AX.XY,
                     apply_absolute_value=True),
                 reads=[r(wS[sl])], writes=[r(racc, j)], tag=f"{name}r")
        P.op("vector",
             lambda e: e.reduce_sum(rac1[:, :], racc[:, 0:8], AX.X),
             reads=[r(racc, jj) for jj in range(8)],
             writes=[r(rac1)], tag=f"{name}r2")
        P.op("scalar", lambda e: e.activation(racb[:, :], rac1[:, :], AF.Copy),
             reads=[r(rac1)], writes=[r(racb)], tag=f"{name}cb")
        P.op("tensor",
             lambda e: e.matmul(pM[:, 0:1], C.ones1[:, :], racb[:, :],
                                start=True, stop=True),
             reads=[r(racb), r(C.ones1)], writes=[r(pM)], tag=f"{name}sm")

    def w_scale(msw, sw, name):
        P.op("vector",
             lambda e: e.tensor_scalar(msw[:, :], pM[:, 0:1],
                                       2.384185791015625e-07, 1e-5,
                                       ALU.mult, ALU.max),
             reads=[r(pM)], writes=[r(msw)], tag=f"{name}ms")
        P.op("scalar", lambda e: e.activation(sw[:, :], msw[:, :], AF.Ln),
             reads=[r(msw)], writes=[r(sw)], tag=f"{name}l")
        P.op("scalar", lambda e: e.activation(sw[:, :], sw[:, :], AF.Exp,
                                              scale=-1.0),
             reads=[r(sw)], writes=[r(sw)], tag=f"{name}e")

    w1_r = w1_d.rearrange("(a p) c -> p a c", p=128)
    w_mean_pass(w1_r, w1S, "w1")
    w_scale(msw1, sw1, "w1")

    P.op("vector", lambda e: e.memset(runm2[:, :], 0.0), writes=[r(runm2)],
         tag="rm20")
    for j8 in range(8):
        sl = j8 % 2
        C.dma("sync", w1S[sl][:, :, :], w1_r[:, :, j8 * 512:(j8 + 1) * 512],
              [r(w1S[sl])], tag=f"w1p2_{j8}")
        C.dma("gpsimd", cds2[sl][:, :, :],
              cvd_r[:, j8 * 12:(j8 + 1) * 12, :], [r(cds2[sl])],
              tag=f"cds{j8}")
        P.op("vector",
             lambda e, sl=sl: e.tensor_scalar(w1S[sl][:, :, :], w1S[sl][:, :, :],
                                              sw1[:, 0:1], None, ALU.mult),
             reads=[r(w1S[sl]), r(sw1)], writes=[r(w1S[sl])], tag="w1qm")
        P.op("scalar", lambda e, sl=sl: e.sign(w1i[:, :, :], w1S[sl][:, :, :]),
             reads=[r(w1S[sl])], writes=[r(w1i)], tag="w1sg")
        P.op("vector",
             lambda e, sl=sl: e.scalar_tensor_tensor(
                 w1S[sl][:, :, :], w1i[:, :, :], C.halfsh[:, 0:1],
                 w1S[sl][:, :, :], ALU.mult, ALU.add),
             reads=[r(w1i), r(C.halfsh), r(w1S[sl])], writes=[r(w1S[sl])],
             tag="w1sh")
        P.op("vector", lambda e, sl=sl: e.tensor_copy(w1i[:, :, :],
                                                      w1S[sl][:, :, :]),
             reads=[r(w1S[sl])], writes=[r(w1i)], tag="w1ri")
        P.op("vector",
             lambda e, sl=sl: e.tensor_scalar(w1q[sl][:, :, :], w1i[:, :, :],
                                              1.0, -1.0, ALU.min, ALU.max),
             reads=[r(w1i)], writes=[r(w1q[sl])], tag="w1qc")
        for m4 in range(4):
            m = j8 * 4 + m4
            msl = m % 2
            for ng in range(NG):
                for k in range(8):
                    P.op("tensor",
                         lambda e, m4=m4, k=k, ng=ng, sl=sl: e.matmul(
                             pH[:, :], w1q[sl][:, k, m4 * 128:(m4 + 1) * 128],
                             xq1[:, k, ng * 512:(ng + 1) * 512],
                             start=(k == 0), stop=(k == 7)),
                         reads=[r(w1q[sl]), r(xq1, (ng, k))], writes=[r(pH)],
                         tag="h1mm", chain=(k != 0))
                P.op("vector",
                     lambda e, ng=ng, msl=msl: e.scalar_tensor_tensor(
                         h_s[msl][:, 1 + ng * 512:1 + (ng + 1) * 512], pH[:, :],
                         msw1[:, 0:1], inv1[:, ng * 512:(ng + 1) * 512],
                         ALU.mult, ALU.mult),
                     reads=[r(pH), r(msw1), r(inv1, ng)], writes=[r(h_s[msl])],
                     tag="h1ev")
            P.op("vector", lambda e, msl=msl: e.memset(h_s[msl][:, 0:1], 0.0),
                 writes=[r(h_s[msl])], tag="hz0")
            P.op("vector",
                 lambda e, msl=msl: e.memset(h_s[msl][:, S + 1:S + 2], 0.0),
                 writes=[r(h_s[msl])], tag="hz1")
            for ng in range(NG):
                for tap in range(3):
                    P.op("tensor",
                         lambda e, m4=m4, tap=tap, ng=ng, msl=msl, sl=sl: e.matmul(
                             pC[:, :],
                             cds2[sl][:, m4 * 3 + tap, :],
                             h_s[msl][:, ng * 512 + tap:ng * 512 + tap + 512],
                             start=(tap == 0), stop=(tap == 2)),
                         reads=[r(cds2[sl]), r(h_s[msl])], writes=[r(pC)],
                         tag="cvmm", chain=(tap != 0))
                P.op("scalar",
                     lambda e, m=m, ng=ng, msl=msl: e.activation(
                         h2m[msl][:, ng * 512:(ng + 1) * 512], pC[:, :], AF.Silu,
                         bias=cb_s[:, m:m + 1]),
                     reads=[r(pC), r(cb_s)], writes=[r(h2m[msl], ng)], tag="cvev")
            for ng in range(NG):
                P.op("scalar",
                     lambda e, msl=msl, ng=ng: e.activation(
                         habs[:, :], h2m[msl][:, ng * 512:(ng + 1) * 512], AF.Abs),
                     reads=[r(h2m[msl], ng)], writes=[r(habs)], tag="rm2a")
                P.op("vector",
                     lambda e, ng=ng: e.tensor_max(
                         runm2[:, ng * 512:(ng + 1) * 512],
                         runm2[:, ng * 512:(ng + 1) * 512], habs[:, :]),
                     reads=[r(runm2), r(habs)], writes=[r(runm2)], tag="rm2")
            C.dma("sync", h2_r[:, m, :], h2m[msl][:, :], [("dram:h2", m)],
                  reads=[r(h2m[msl], ng) for ng in range(NG)], tag=f"h2o{m}")


    def q2_pmax(src_r, dst_w):
        NBLK2 = S // 128
        for blk in range(NBLK2):
            P.op("tensor",
                 lambda e, blk=blk: e.matmul(
                     pH[:, (blk % 4) * 128:(blk % 4) * 128 + 128],
                     runm2[:, blk * 128:(blk + 1) * 128], idf[:, :],
                     start=True, stop=True),
                 reads=src_r + [r(idf)], writes=[r(pH)], tag="q2t")
            P.op("vector",
                 lambda e, blk=blk: e.reduce_max(
                     amcmp[:, blk:blk + 1],
                     pH[:, (blk % 4) * 128:(blk % 4) * 128 + 128], AX.X),
                 reads=[r(pH)], writes=[r(amcmp, blk)], tag="q2r")
        P.op("tensor",
             lambda e: e.matmul(pM[0:NBLK2, 0:128], amcmp[:, 0:NBLK2],
                                idf[:, :], start=True, stop=True),
             reads=[r(amcmp, bb) for bb in range(NBLK2)] + [r(idf)],
             writes=[r(pM)], tag="q2tb")
        P.op("vector", lambda e: e.tensor_copy(amrow[0:NBLK2, :],
                                               pM[0:NBLK2, 0:128]),
             reads=[r(pM)], writes=[r(amrow)], tag="q2ev")
        for blk in range(NBLK2):
            P.op("tensor",
                 lambda e, blk=blk: e.matmul(
                     pH[:, (blk % 4) * 128:(blk % 4) * 128 + 128],
                     sel16[0:NBLK2, blk * 128:(blk + 1) * 128],
                     amrow[0:NBLK2, :], start=True, stop=True),
                 reads=[r(sel16), r(amrow)], writes=[r(pH)], tag="q2b")
            P.op("scalar",
                 lambda e, blk=blk: e.activation(
                     amax2[:, blk * 128:(blk + 1) * 128],
                     pH[:, (blk % 4) * 128:(blk % 4) * 128 + 128], AF.Copy),
                 reads=[r(pH)], writes=dst_w, tag="q2be")

    q2_pmax([r(runm2)], [r(amax2)])
    qsc2 = runm2
    P.op("vector",
         lambda e: e.tensor_scalar(amax2[:, :], amax2[:, :], 1e-5, None, ALU.max),
         reads=[r(amax2)], writes=[r(amax2)], tag="q2c")
    P.op("vector",
         lambda e: e.tensor_scalar(qsc2[:, :], amax2[:, :], 1.0 / 127.0, None,
                                   ALU.mult),
         reads=[r(amax2)], writes=[r(qsc2, "inv"), r(runm2)], tag="q2i")
    P.op("scalar", lambda e: e.activation(amax2[:, :], amax2[:, :], AF.Ln),
         reads=[r(amax2)], writes=[r(amax2)], tag="q2l")
    P.op("scalar", lambda e: e.activation(amax2[:, :], amax2[:, :], AF.Exp,
                                          scale=-1.0, bias=C.cv(LN127)),
         reads=[r(amax2), r(C.cvals)], writes=[r(amax2)], tag="q2e")

    P.fence()
    stW.close()
    stX.close()

    stW2 = contextlib.ExitStack()
    w2q = C.sb("w2q", [128, 32, D], BF16, stW2)
    stW2a = contextlib.ExitStack()
    w2S = [C.sb(f"w2S{i}", [128, 2, D], F32, stW2a) for i in range(2)]
    w2i = C.sb("w2i", [128, 2, D], I32, stW2a)
    w2_r = w2_d.rearrange("(a p) c -> p a c", p=128)
    for j in range(16):
        sl = j % 2
        C.dma("sync", w2S[sl][:, :, :], w2_r[:, 2 * j:2 * j + 2, :],
              [r(w2S[sl])], tag=f"w2p1_{j}")
        P.op("vector",
             lambda e, j=j, sl=sl: e.reduce_sum(
                 racc[:, j:j + 1], w2S[sl][:, :, :], AX.XY,
                 apply_absolute_value=True),
             reads=[r(w2S[sl])], writes=[r(racc, j)], tag="w2r")
    P.op("vector",
         lambda e: e.reduce_sum(rac1[:, :], racc[:, :], AX.X),
         reads=[r(racc, jj) for jj in range(16)], writes=[r(rac1)], tag="w2r2")
    P.op("scalar", lambda e: e.activation(racb[:, :], rac1[:, :], AF.Copy),
         reads=[r(rac1)], writes=[r(racb)], tag="w2cb")
    P.op("tensor",
         lambda e: e.matmul(pM[:, 0:1], C.ones1[:, :], racb[:, :],
                            start=True, stop=True),
         reads=[r(racb), r(C.ones1)], writes=[r(pM)], tag="w2sm")
    w_scale(msw2, sw2, "w2")
    for j in range(16):
        sl = j % 2
        C.dma("sync", w2S[sl][:, :, :], w2_r[:, 2 * j:2 * j + 2, :],
              [r(w2S[sl])], tag=f"w2p2_{j}")
        P.op("vector",
             lambda e, sl=sl: e.tensor_scalar(w2S[sl][:, :, :], w2S[sl][:, :, :],
                                              sw2[:, 0:1], None, ALU.mult),
             reads=[r(w2S[sl]), r(sw2)], writes=[r(w2S[sl])], tag="w2qm")
        P.op("scalar", lambda e, sl=sl: e.sign(w2i[:, :, :], w2S[sl][:, :, :]),
             reads=[r(w2S[sl])], writes=[r(w2i)], tag="w2sg")
        P.op("vector",
             lambda e, sl=sl: e.scalar_tensor_tensor(
                 w2S[sl][:, :, :], w2i[:, :, :], C.halfsh[:, 0:1],
                 w2S[sl][:, :, :], ALU.mult, ALU.add),
             reads=[r(w2i), r(C.halfsh), r(w2S[sl])], writes=[r(w2S[sl])],
             tag="w2sh")
        P.op("vector", lambda e, sl=sl: e.tensor_copy(w2i[:, :, :],
                                                      w2S[sl][:, :, :]),
             reads=[r(w2S[sl])], writes=[r(w2i)], tag="w2ri")
        P.op("vector",
             lambda e, j=j: e.tensor_scalar(w2q[:, 2 * j:2 * j + 2, :],
                                            w2i[:, :, :], 1.0, -1.0,
                                            ALU.min, ALU.max),
             reads=[r(w2i)], writes=[r(w2q, j // 2)], tag="w2qc")

    P.fence()
    stW2a.close()

    xq2 = C.sb("xq2", [128, 32, 512], BF16, stW2)
    h2r = [C.sb(f"h2r{i}", [128, 4, 512], F32, stW2) for i in range(2)]
    h2i = C.sb("h2i", [128, 4, 512], I32, stW2)
    oT2 = C.sb("f1oT", [128, 8, 512], BF16, stW2)
    for g in range(NG):
        c0 = g * 512
        gb = g % 2
        for m8 in range(8):
            sl = m8 % 2
            C.dma("sync", h2r[sl][:, :, :],
                  h2_r[:, 4 * m8:4 * m8 + 4, c0:c0 + 512], [r(h2r[sl])],
                  reads=[("dram:h2", 4 * m8 + i_) for i_ in range(4)],
                  tag=f"h2i{g}_{m8}")
            P.op("vector",
                 lambda e, sl=sl, c0=c0: e.tensor_tensor(
                     h2r[sl][:, :, :], h2r[sl][:, :, :],
                     sap(amax2, 0, 128, c0, [[0, 4], [1, 512]]), ALU.mult),
                 reads=[r(h2r[sl]), r(amax2)], writes=[r(h2r[sl])], tag="q2m")
            P.op("scalar", lambda e, sl=sl: e.sign(h2i[:, :, :], h2r[sl][:, :, :]),
                 reads=[r(h2r[sl])], writes=[r(h2i)], tag="q2sg")
            P.op("vector",
                 lambda e, sl=sl: e.scalar_tensor_tensor(
                     h2r[sl][:, :, :], h2i[:, :, :], C.halfsh[:, 0:1],
                     h2r[sl][:, :, :], ALU.mult, ALU.add),
                 reads=[r(h2i), r(C.halfsh), r(h2r[sl])], writes=[r(h2r[sl])],
                 tag="q2sh")
            P.op("vector", lambda e, sl=sl: e.tensor_copy(h2i[:, :, :],
                                                          h2r[sl][:, :, :]),
                 reads=[r(h2r[sl])], writes=[r(h2i)], tag="q2ri")
            P.op("vector",
                 lambda e, m8=m8: e.tensor_copy(xq2[:, 4 * m8:4 * m8 + 4, :],
                                                h2i[:, :, :]),
                 reads=[r(h2i)], writes=[r(xq2, m8)], tag="q2b")
        for j in range(8):
            for k in range(32):
                P.op("tensor",
                     lambda e, j=j, k=k: e.matmul(
                         pH[:, :], w2q[:, k, j * 128:(j + 1) * 128],
                         xq2[:, k, :], start=(k == 0), stop=(k == 31)),
                     reads=[r(w2q, k // 4), r(xq2, k // 4)], writes=[r(pH)],
                     tag="o2mm", chain=(k != 0))
            P.op("vector",
                 lambda e, j=j, c0=c0: e.scalar_tensor_tensor(
                     oT2[:, j, :], pH[:, :], msw2[:, 0:1],
                     qsc2[:, c0:c0 + 512], ALU.mult, ALU.mult),
                 reads=[r(pH), r(msw2), r(qsc2, "inv")], writes=[r(oT2, j)],
                 tag="o2ev")
        emit_ln_stats(C, oT2, lambda pt: [r(oT2, pt)], pS2, C.ones_r, m2sb, r2sb,
                      tmpb, tag="f1ln2")
        C.dma("sync", xgf[gb][:, :, :], x_src_r[:, :, c0:c0 + 512],
              [r(xgf[gb])], reads=[("dram:" + src_nm, g)], tag=f"fxr2{g}")
        for pt in range(8):
            P.op("vector",
                 lambda e, pt=pt: e.tensor_sub(lntf[:, :], oT2[:, pt, :],
                                               m2sb[:, :]),
                 reads=[r(oT2, pt), r(m2sb)], writes=[r(lntf)], tag="fb1")
            P.op("vector",
                 lambda e, pt=pt: e.tensor_mul(lntf[:, :], lntf[:, :], r2sb[:, :]),
                 reads=[r(lntf), r(r2sb)], writes=[r(lntf)], tag="fb2")
            P.op("vector",
                 lambda e, pt=pt, gb=gb: e.tensor_add(
                     xgf[gb][:, pt, :], xgf[gb][:, pt, :], lntf[:, :]),
                 reads=[r(lntf), r(xgf[gb])], writes=[r(xgf[gb])], tag="fb3")
        C.dma("sync", x_dst_r[:, :, c0:c0 + 512], xgf[gb][:, :, :],
              [("dram:" + dst_nm, g)], reads=[r(xgf[gb])], tag=f"fxo2{g}")

    P.fence()
    stW2.close()
    st.close()


_CACHE = {}
PHASES = ("l0a", "l0f", "l1a", "l1f")


def _build(S, phases):
    C = Ctx(S)
    nc = C.nc
    n = len(phases)
    x_in = C.dram_in("x_in", [D, S])
    x_out = C.dram_out("x_out", [D, S])
    xres = C.dram_tmp("xres", [D, S]) if n > 1 else None
    onr_d = C.dram_in("ones_r", [128, 128], BF16)
    on1_d = C.dram_in("ones_1", [128, 128], BF16)
    cv_d = C.dram_in("consts", [128, len(CONSTS)])
    C.ones_r = C.sb("ones_r_g", [128, 128], BF16)
    C.ones1 = C.sb("ones_1_g", [128, 128], BF16)
    C.cvals = C.sb("cvals_g", [128, len(CONSTS)], F32)
    C.dma("sync", C.ones_r[:, :], onr_d, [r(C.ones_r)])
    C.dma("sync", C.ones1[:, :], on1_d, [r(C.ones1)])
    C.dma("sync", C.cvals[:, :], cv_d, [r(C.cvals)])
    C.halfsh = C.sb("halfsh_g", [128, 1], F32)
    _hstmp = C.sb("hstmp_g", [128, 1], I32)
    C.plan.op("vector",
              lambda e: e.tensor_copy(_hstmp[:, :], C.cvals[:, 6:7]),
              reads=[r(C.cvals)], writes=[r(_hstmp)], tag="shdet1")
    C.plan.op("vector",
              lambda e: e.tensor_scalar(C.halfsh[:, :], _hstmp[:, :],
                                        -0.5, 0.5, ALU.mult, ALU.add),
              reads=[r(_hstmp)], writes=[r(C.halfsh)], tag="shdet2")
    sems, dma_pools = {}, {}
    sem_stack = contextlib.ExitStack()
    KPOOL = {"sync": 12, "gpsimd": 12, "scalar": 8, "vector": 2, "tensor": 2}
    for e in ENGINES:
        sems[e] = sem_stack.enter_context(nc.semaphore("s_" + e))
        dma_pools[e] = [sem_stack.enter_context(nc.semaphore(f"d_{e}{i}"))
                        for i in range(KPOOL[e])]
    for i, ph in enumerate(phases):
        src = x_in if i == 0 else xres
        dst = x_out if i == n - 1 else xres
        src_nm = "x_in" if i == 0 else "xres"
        dst_nm = "x_out" if i == n - 1 else "xres"
        if ph == "l0a":
            emit_l0a(C, src, dst, src_nm, dst_nm)
        elif ph == "l0f":
            emit_l0f(C, src, dst, src_nm, dst_nm)
        elif ph == "l1a":
            emit_l1a(C, src, dst, src_nm, dst_nm)
        elif ph == "l1f":
            emit_l1f(C, src, dst, src_nm, dst_nm)
        else:
            raise ValueError(ph)
    with nc.Block() as block:
        C.plan.resolve_and_emit(block, sems, dma_pools)

        @block.sync
        def _(sy):
            for e in ENGINES:
                nd = sum(1 for o in C.plan.ops[e] if o.dma)
                K = len(dma_pools[e])
                for s in range(min(nd, K)):
                    uses = (nd - s + K - 1) // K
                    sy.wait_ge(dma_pools[e][s], 16 * uses)
    sem_stack.close()
    C.stack.close()
    return C


def _host_inputs(ins, S, phases):
    import ml_dtypes
    bf = ml_dtypes.bfloat16
    x = np.asarray(ins["L0_attn_x"])
    B = x.shape[0]
    common = {}
    common["ones_r"] = np.full((128, 128), 1.0 / 1024.0, dtype=bf)
    common["ones_1"] = np.ones((128, 128), dtype=bf)
    common["consts"] = np.ascontiguousarray(
        np.broadcast_to(np.asarray(CONSTS, np.float32), (128, len(CONSTS))))
    if "l0a" in phases:
        wq_f = np.asarray(ins["L0_attn_Wqkv"], dtype=np.float32)
        mm_ = np.arange(1024)
        qcols = (2 * (mm_ // 128) + (mm_ % 128) // 64) * 192 + (mm_ % 64)
        vcols = (mm_ // 64) * 192 + 128 + (mm_ % 64)
        cols = np.concatenate([qcols, qcols + 64, vcols])
        common["wqkv"] = np.ascontiguousarray(wq_f[:, cols])
        wo_f = np.asarray(ins["L0_attn_Wo"], dtype=np.float32)
        kk = np.arange(8)[:, None]
        pp = np.arange(128)[None, :]
        perm = (512 * (pp // 64) + 64 * kk + (pp % 64)).reshape(-1)
        common["wo"] = np.ascontiguousarray(wo_f[perm])
        bqkv = np.asarray(ins["L0_attn_bqkv"], dtype=np.float32)
        idx = np.zeros((128, 8), np.int64)
        for i in range(8):
            for p in range(128):
                idx[p, i] = (2 * i + p // 64) * 192 + (p % 64)
        common["bq_sb"] = np.ascontiguousarray(bqkv[idx])
        common["bk_sb"] = np.ascontiguousarray(bqkv[idx + 64])
        bv = bqkv.reshape(16, 192)[:, 128:]
        common["bvn_bc"] = np.ascontiguousarray(
            np.broadcast_to(bv.reshape(1, 16 * 64), (128, 16 * 64)).astype(np.float32))
        common["bo_sb"] = np.ascontiguousarray(
            np.asarray(ins["L0_attn_bo"], dtype=np.float32).reshape(8, 128).T)
        rr = np.arange(128)[:, None]
        cc = np.arange(128)[None, :]
        common["maskB"] = np.where(np.abs(rr - cc) <= 16, 0.0, NEG).astype(np.float32)
        a = np.arange(32)[:, None]
        c2 = np.arange(128)[None, :]
        common["maskA"] = np.where((c2 >= a - 48) & (c2 <= a - 16), 0.0,
                                   NEG).astype(np.float32)
        common["maskC"] = np.where(c2 >= a + 112, 0.0, NEG).astype(np.float32)
    if "l1a" in phases:
        wq_f = np.asarray(ins["L1_attn_Wq"], np.float32)
        common["wq_t"] = np.ascontiguousarray(wq_f.T)
        common["wk_t"] = np.ascontiguousarray(
            np.asarray(ins["L1_attn_Wk"], np.float32).T)
        common["wv_t"] = np.ascontiguousarray(
            np.asarray(ins["L1_attn_Wv"], np.float32).T)
        wf_f = np.asarray(ins["L1_attn_Wf"], np.float32).T
        kk = np.arange(8)[:, None]
        pp = np.arange(128)[None, :]
        perm = (512 * (pp // 64) + 64 * kk + (pp % 64)).reshape(-1)
        common["wf_t"] = np.ascontiguousarray(wf_f[perm])
        common["bf_sb"] = np.ascontiguousarray(
            np.asarray(ins["L1_attn_bf"], np.float32).reshape(8, 128).T)
        rr = np.arange(128)[:, None]
        cc = np.arange(128)[None, :]
        common["maskN"] = np.where(cc >= rr, 0.0, NEG).astype(np.float32)
        common["maskO"] = np.where(cc <= rr, 0.0, NEG).astype(np.float32)
    if "l1f" in phases:
        common["ident_f"] = np.eye(128, dtype=np.float32)
        s16 = np.zeros((16, 16 * 128), np.float32)
        for b_ in range(16):
            s16[b_, b_ * 128:(b_ + 1) * 128] = 1.0
        common["sel16f"] = s16
        common["w1_t"] = np.ascontiguousarray(
            np.asarray(ins["L1_ffn_w1"], np.float32).T)
        common["w2_t"] = np.ascontiguousarray(
            np.asarray(ins["L1_ffn_w2"], np.float32).T)
        cw_ = np.asarray(ins["L1_ffn_conv_w"], np.float32)[:, 0, :]
        cd = np.zeros((32, 3, 128, 128), np.float32)
        for m_ in range(32):
            for tp_ in range(3):
                np.fill_diagonal(cd[m_, tp_], cw_[m_ * 128:(m_ + 1) * 128, tp_])
        common["conv_diag"] = np.ascontiguousarray(cd.reshape(32 * 3 * 128, 128))
        common["convb_sb"] = np.ascontiguousarray(
            np.asarray(ins["L1_ffn_conv_b"], np.float32).reshape(32, 128).T)
    if "l0f" in phases:
        common["gate_wT"] = np.ascontiguousarray(
            np.asarray(ins["L0_ffn_gate_w"], np.float32).T)
        common["gbias_bc"] = np.ascontiguousarray(np.broadcast_to(
            np.asarray(ins["L0_ffn_gate_bias"], np.float32)[None, :], (128, 16)))
        es = np.zeros((16, 16 * 128), np.float32)
        for e_ in range(16):
            es[e_, e_ * 128:(e_ + 1) * 128] = 1.0
        common["esel"] = es.astype(bf)
        common["ident_bf"] = np.eye(128, dtype=np.float32).astype(bf)
        common["wsg"] = np.ascontiguousarray(
            np.asarray(ins["L0_ffn_Ws_g"], np.float32).T)
        common["wsu"] = np.ascontiguousarray(
            np.asarray(ins["L0_ffn_Ws_u"], np.float32).T)
        common["wsd"] = np.ascontiguousarray(
            np.asarray(ins["L0_ffn_Ws_d"], np.float32).T)
        common["wg_t"] = np.ascontiguousarray(
            np.asarray(ins["L0_ffn_Wg"], np.float32).transpose(0, 2, 1))
        common["wu_t"] = np.ascontiguousarray(
            np.asarray(ins["L0_ffn_Wu"], np.float32).transpose(0, 2, 1))
        common["wd_t"] = np.ascontiguousarray(
            np.asarray(ins["L0_ffn_Wd"], np.float32).transpose(0, 2, 1))
    in_maps = []
    for b in range(B):
        m = dict(common)
        m["x_in"] = np.ascontiguousarray(x[b].T.astype(np.float32))
        in_maps.append(m)
    return in_maps


def kernel(**ins):
    x = np.asarray(ins["L0_attn_x"])
    S = x.shape[1]
    key = (S, PHASES)
    if key not in _CACHE:
        _CACHE[key] = _build(S, PHASES)
    C = _CACHE[key]
    in_maps = _host_inputs(ins, S, PHASES)
    res = run_bass_kernel_spmd(C.nc, in_maps, core_ids=list(range(len(in_maps))))
    out = np.stack([res.results[b]["x_out"].T for b in range(len(in_maps))])
    return np.ascontiguousarray(out.astype(np.float32))


# version 15
# speedup vs baseline: 1.1200x; 1.0223x over previous
import contextlib

import numpy as np

import concourse.bass as bass
import concourse.mybir as mybir
from concourse.bass_utils import run_bass_kernel_spmd

F32 = mybir.dt.float32
BF16 = mybir.dt.bfloat16
I32 = mybir.dt.int32
AF = mybir.ActivationFunctionType
ALU = mybir.AluOpType
AX = mybir.AxisListType

NEG = -1.0e30
D = 1024

G_SC = True
G_AC = True
G_PV = True
G_SB = True


def colB(hl):
    return (hl % 2) * 512 + (hl // 2) * 128

ENGINES = ("tensor", "vector", "scalar", "gpsimd", "sync")


class _Op:
    __slots__ = ("eng", "fn", "reads", "writes", "dma", "idx", "need_inc",
                 "deps", "dma_idx", "inc_no", "tag", "chain")

    def __init__(self, eng, fn, reads, writes, dma, idx, tag):
        self.eng, self.fn = eng, fn
        self.reads, self.writes = reads, writes
        self.dma, self.idx, self.tag = dma, idx, tag
        self.need_inc = False
        self.deps = set()
        self.dma_idx = None
        self.inc_no = None


class Plan:
    def __init__(self):
        self.ops = {e: [] for e in ENGINES}
        self.regions = {}
        self.fence_snaps = []
        self.buf_fence = {}
        self.cur_fence = -1

    def _reg(self, key):
        rg = self.regions.get(key)
        if rg is None:
            rg = [None, []]
            self.regions[key] = rg
        return rg

    def register_buf(self, name):
        self.buf_fence[name] = self.cur_fence

    def op(self, eng, fn, reads=(), writes=(), dma=False, tag="", chain=False):
        o = _Op(eng, fn, tuple(reads), tuple(writes), dma, len(self.ops[eng]), tag)
        o.chain = chain
        for rk in o.reads:
            rg = self._reg(rk)
            if rg[0] is not None:
                o.deps.add((rg[0].eng, rg[0].idx, rg[0].dma))
        for wk in o.writes:
            rg = self._reg(wk)
            if rg[0] is not None:
                o.deps.add((rg[0].eng, rg[0].idx, rg[0].dma))
            for rd in rg[1]:
                o.deps.add((rd.eng, rd.idx, rd.dma))
            f = self.buf_fence.get(wk[0], -1)
            if f >= 0:
                snap = self.fence_snaps[f]
                for e2 in ENGINES:
                    n = snap[e2]
                    if n > 0:
                        prev = self.ops[e2][n - 1]
                        o.deps.add((e2, n - 1, prev.dma))
                        last_dmas = []
                        for i2 in range(n - 1, -1, -1):
                            op2 = self.ops[e2][i2]
                            if op2.dma:
                                last_dmas.append(i2)
                                if len(last_dmas) >= 12:
                                    break
                        for i2 in last_dmas:
                            o.deps.add((e2, i2, True))
        for rk in o.reads:
            self._reg(rk)[1].append(o)
        for wk in o.writes:
            rg = self._reg(wk)
            rg[0] = o
            rg[1] = []
        o.deps = {d for d in o.deps if not (d[0] == eng and d[1] == o.idx)}
        self.ops[eng].append(o)
        return o

    def fence(self):
        self.fence_snaps.append({e: len(self.ops[e]) for e in ENGINES})
        self.cur_fence = len(self.fence_snaps) - 1

    def resolve_and_emit(self, block, sems, dma_pools):
        for e in ENGINES:
            for o in self.ops[e]:
                for (e2, i2, isdma) in o.deps:
                    if not isdma and e2 != e:
                        self.ops[e2][i2].need_inc = True
        for e in ENGINES:
            n = nd = 0
            K = len(dma_pools[e])
            for o in self.ops[e]:
                if o.dma:
                    o.dma_idx = nd
                    nd += 1
                elif o.need_inc:
                    n += 1
                    o.inc_no = n

        def dma_wait(eng_name, j):
            K = len(dma_pools[eng_name])
            return dma_pools[eng_name][j % K], 16 * (j // K + 1)

        def emit_engine(eng_name, eng):
            watermark = {}
            drained_upto = -1
            K = len(dma_pools[eng_name])
            for o in self.ops[eng_name]:
                waits = {}
                need_drain = False
                for (e2, i2, isdma) in sorted(o.deps):
                    if e2 == eng_name and not isdma:
                        if i2 > drained_upto and not (
                                eng_name == "tensor" and o.chain
                                and i2 == o.idx - 1):
                            need_drain = True
                        continue
                    if isdma:
                        j = self.ops[e2][i2].dma_idx
                        sem, val = dma_wait(e2, j)
                        k = (e2, "d", j % len(dma_pools[e2]))
                    else:
                        dep = self.ops[e2][i2]
                        sem, val = sems[e2], dep.inc_no
                        k = (e2, "c")
                    if watermark.get(k, -1) < val:
                        if k in waits:
                            waits[k] = (sem, max(val, waits[k][1]))
                        else:
                            waits[k] = (sem, val)
                if need_drain:
                    eng.drain()
                    drained_upto = o.idx - 1
                if o.dma and o.dma_idx >= K:
                    sem, val = dma_pools[eng_name][o.dma_idx % K],                         16 * (o.dma_idx // K)
                    k = (eng_name, "d", o.dma_idx % K)
                    if watermark.get(k, -1) < val:
                        waits[k] = (sem, max(val, waits.get(k, (sem, 0))[1]))
                for k, (sem, val) in waits.items():
                    eng.wait_ge(sem, val)
                    watermark[k] = val
                inst = o.fn(eng)
                if o.dma:
                    inst.then_inc(dma_pools[eng_name][o.dma_idx % K], 16)
                elif o.need_inc:
                    if eng_name == "tensor":
                        eng.drain().then_inc(sems[eng_name], 1)
                        drained_upto = o.idx
                    else:
                        inst.then_inc(sems[eng_name], 1)

        for name in ENGINES:
            getattr(block, name)(lambda eng, n=name: emit_engine(n, eng))


def r(buf, key=()):
    return (buf.name, key)


def sap(t, p0, pn, foff, dims):
    fs = 1
    for d in t.shape[1:]:
        fs *= d
    return bass.AP(t, p0 * fs + foff, [[fs, pn]] + [list(d) for d in dims])


CONSTS = [1e-5, 1e-20, 127.0, 0.5, 4.844187086458591, 2.384185791015625e-07, 0.7]


class Ctx:
    def __init__(self, S):
        self.S = S
        self.nc = bass.Bass("TRN2", target_bir_lowering=False)
        self.plan = Plan()
        self.stack = contextlib.ExitStack()
        self.input_names = []
        self.cvals = None
        self.ones_r = None
        self.ones1 = None
        self.halfsh = None

    def cv(self, value):
        j = CONSTS.index(value)
        return self.cvals[:, j:j + 1]

    def dram_in(self, name, shape, dtype=F32):
        t = self.nc.dram_tensor(name, list(shape), dtype, kind="ExternalInput")
        self.input_names.append(name)
        return t.ap()

    def dram_out(self, name, shape, dtype=F32):
        return self.nc.dram_tensor(name, list(shape), dtype, kind="ExternalOutput").ap()

    def dram_tmp(self, name, shape, dtype=F32):
        return self.nc.dram_tensor(name, list(shape), dtype, kind="Internal").ap()

    def sb(self, name, shape, dtype, stack=None):
        h = (stack or self.stack).enter_context(
            self.nc.sbuf_tensor(name, list(shape), dtype))
        self.plan.register_buf(name)
        return h

    def ps(self, name, shape, dtype=F32, stack=None):
        h = (stack or self.stack).enter_context(
            self.nc.psum_tensor(name, list(shape), dtype))
        self.plan.register_buf(name)
        return h

    def dma(self, eng, out_ap, in_ap, writes, reads=(), tag="dma"):
        return self.plan.op(eng, lambda e: e.dma_start(out=out_ap, in_=in_ap),
                            reads=reads, writes=writes, dma=True, tag=tag)


def emit_ln_stats(C, src, src_rd, pstat, ones_r, m_sb, r_sb, tmpb, tag, c0=0):
    P = C.plan
    for pt in range(8):
        P.op("scalar",
             lambda e, pt=pt: e.activation(tmpb[:, :], src[:, pt, c0:c0 + 512],
                                           AF.Copy),
             reads=src_rd(pt), writes=[r(tmpb)], tag=tag + "c")
        P.op("tensor",
             lambda e, pt=pt: e.matmul(pstat[:, :], ones_r[:, :], tmpb[:, :],
                                       start=(pt == 0), stop=(pt == 7)),
             reads=[r(tmpb), r(ones_r)], writes=[r(pstat)], tag=tag + "ms", chain=(pt != 0))
    P.op("scalar", lambda e: e.activation(m_sb[:, :], pstat[:, :], AF.Copy),
         reads=[r(pstat)], writes=[r(m_sb)], tag=tag + "mc")
    for pt in range(8):
        P.op("scalar",
             lambda e, pt=pt: e.activation(tmpb[:, :], src[:, pt, c0:c0 + 512],
                                           AF.Square),
             reads=src_rd(pt), writes=[r(tmpb)], tag=tag + "q")
        P.op("tensor",
             lambda e, pt=pt: e.matmul(pstat[:, :], ones_r[:, :], tmpb[:, :],
                                       start=(pt == 0), stop=(pt == 7)),
             reads=[r(tmpb), r(ones_r)], writes=[r(pstat)], tag=tag + "qs", chain=(pt != 0))
    P.op("vector", lambda e: e.tensor_mul(r_sb[:, :], m_sb[:, :], m_sb[:, :]),
         reads=[r(m_sb)], writes=[r(r_sb)], tag=tag + "mm")
    P.op("vector", lambda e: e.tensor_sub(r_sb[:, :], pstat[:, :], r_sb[:, :]),
         reads=[r(pstat), r(r_sb)], writes=[r(r_sb)], tag=tag + "var")
    P.op("scalar", lambda e: e.activation(r_sb[:, :], r_sb[:, :], AF.Ln,
                                          bias=C.cv(1e-5)),
         reads=[r(r_sb), r(C.cvals)], writes=[r(r_sb)], tag=tag + "ln")
    P.op("scalar", lambda e: e.activation(r_sb[:, :], r_sb[:, :], AF.Exp, scale=-0.5),
         reads=[r(r_sb)], writes=[r(r_sb)], tag=tag + "ex")


def emit_l0a(C, x_src_dram, x_dst_dram, src_nm, dst_nm):
    P, nc, S = C.plan, C.nc, C.S
    NT, NG = S // 128, S // 512
    st = contextlib.ExitStack()

    wqkv = C.dram_in("wqkv", [D, 3 * D])
    wo_d = C.dram_in("wo", [D, D])
    bq_d = C.dram_in("bq_sb", [128, 8])
    bk_d = C.dram_in("bk_sb", [128, 8])
    bo_d = C.dram_in("bo_sb", [128, 8])
    bvn_d = C.dram_in("bvn_bc", [128, 16 * 64])
    mB_d = C.dram_in("maskB", [128, 128])
    mA_d = C.dram_in("maskA", [32, 128])
    mC_d = C.dram_in("maskC", [32, 128])

    ones_r = C.ones_r
    ones1 = C.ones1
    mB = C.sb("mB_s", [128, 128], F32, st)
    mA = C.sb("mA_s", [128, 128], F32, st)
    mC = C.sb("mC_s", [128, 128], F32, st)
    bq = C.sb("bq_s", [128, 8], F32, st)
    bk = C.sb("bk_s", [128, 8], F32, st)
    bo = C.sb("bo_s", [128, 8], F32, st)
    bvn = C.sb("bvn_s", [128, 16 * 64], BF16, st)
    wqs = C.sb("wqs", [128, 8, 3 * D], BF16, st)
    wos = C.sb("wos", [128, 8, D], BF16, st)
    xg = [C.sb(f"xg{i}", [128, 8, 512], F32, st) for i in range(2)]
    ln1 = [C.sb(f"l0ln1_{i}", [128, 8, 512], BF16, st) for i in range(2)]
    lntmp = C.sb("l0lntmp", [128, 512], F32, st)
    qT = C.sb("qT", [128, 8, 8 * 128], BF16, st)
    kT = C.sb("kT", [128, 8, 8 * 128], BF16, st)
    vn = C.sb("vn", [128, 8, 16 * 64], BF16, st)
    vals = C.sb("vals", [128, 8, 2 * 512], BF16, st)
    exB = C.sb("exB", [128, 8 * 128], BF16, st)
    exAC = C.sb("exAC", [128, 8 * 128], BF16, st)
    sums = C.sb("sums", [128, 8 * 128], BF16, st)
    rinv = C.sb("rinv", [128, 8 * 128], BF16, st)
    m_sb = C.sb("l0m", [128, 512], F32, st)
    r_sb = C.sb("l0r", [128, 512], F32, st)
    m2_sb = C.sb("l0m2", [128, 512], F32, st)
    r2_sb = C.sb("l0r2", [128, 512], F32, st)
    tmpb = C.sb("l0tmpb", [128, 512], BF16, st)
    oT = C.sb("l0oT", [128, 8, 512], BF16, st)

    pSC = C.ps("pSC", [128, 1024], F32, st)
    pPV = C.ps("pPV", [128, 1024], F32, st)
    pBC = C.ps("pBC", [128, 1024], F32, st)
    pA1 = C.ps("pA1", [128, 512], F32, st)
    pA2 = C.ps("pA2", [128, 512], F32, st)

    C.dma("sync", mB[:, :], mB_d, [r(mB)])
    C.dma("sync", mA[96:128, :], mA_d, [r(mA)])
    C.dma("sync", mC[0:32, :], mC_d, [r(mC)])
    C.dma("sync", bq[:, :], bq_d, [r(bq)])
    C.dma("sync", bk[:, :], bk_d, [r(bk)])
    C.dma("sync", bo[:, :], bo_d, [r(bo)])
    C.dma("gpsimd", bvn[:, :], bvn_d, [r(bvn)])
    wq_r = wqkv.rearrange("(a p) c -> p a c", p=128)
    for h4 in range(4):
        C.dma("gpsimd", wqs[:, 2 * h4:2 * h4 + 2, :], wq_r[:, 2 * h4:2 * h4 + 2, :],
              [r(wqs, h4)], tag="wqkv")
    wo_r = wo_d.rearrange("(a p) c -> p a c", p=128)
    C.dma("gpsimd", wos[:, :, :], wo_r, [r(wos)], tag="wo")

    x_src_r = x_src_dram.rearrange("(a p) c -> p a c", p=128)
    x_dst_r = x_dst_dram.rearrange("(a p) c -> p a c", p=128)

    def ln1_group(g):
        gb = g % 2
        C.dma("sync", xg[gb][:, :, :], x_src_r[:, :, g * 512:(g + 1) * 512],
              [r(xg[gb])], reads=[("dram:" + src_nm, g)], tag=f"xg{g}")
        emit_ln_stats(C, xg[gb], lambda pt: [r(xg[gb])], pA2, ones_r, m_sb, r_sb,
                      tmpb, tag="ln1")
        for pt in range(8):
            P.op("vector",
                 lambda e, pt=pt, gb=gb: e.tensor_sub(
                     lntmp[:, :], xg[gb][:, pt, :], m_sb[:, :]),
                 reads=[r(xg[gb]), r(m_sb)], writes=[r(lntmp)], tag="l1s")
            P.op("vector",
                 lambda e, pt=pt, gb=gb: e.tensor_mul(
                     ln1[gb][:, pt, :], lntmp[:, :], r_sb[:, :]),
                 reads=[r(lntmp), r(r_sb)], writes=[r(ln1[gb])], tag="l1m")

    def qkv_group(g):
        gb = g % 2
        s0 = (4 * g) % 8
        for part, dst, bias, base in (("q", qT, bq, 0), ("k", kT, bk, D)):
            for i in range(8):
                for k in range(8):
                    P.op("tensor",
                         lambda e, i=i, k=k, gb=gb, base=base: e.matmul(
                             pA1[:, :], wqs[:, k, base + i * 128:base + (i + 1) * 128],
                             ln1[gb][:, k, :],
                             start=(k == 0), stop=(k == 7)),
                         reads=[r(wqs, k // 2), r(ln1[gb])], writes=[r(pA1)],
                         tag=f"{part}mm", chain=(k != 0))
                P.op("scalar",
                     lambda e, i=i, s0=s0, dst=dst, bias=bias: e.activation(
                         dst[:, i, s0 * 128:(s0 + 4) * 128], pA1[:, :], AF.Identity,
                         bias=bias[:, i:i + 1]),
                     reads=[r(pA1), r(bias)],
                     writes=[r(dst, (4 * g + w) % 8) for w in range(4)],
                     tag=f"{part}ev")
        for w in range(4):
            t = 4 * g + w
            sl = t % 8
            for ng in range(2):
                for k in range(8):
                    P.op("tensor",
                         lambda e, ng=ng, k=k, gb=gb, w=w: e.matmul(
                             pA2[:, :], ln1[gb][:, k, w * 128:(w + 1) * 128],
                             wqs[:, k, 2 * D + ng * 512:2 * D + (ng + 1) * 512],
                             start=(k == 0), stop=(k == 7)),
                         reads=[r(wqs, k // 2), r(ln1[gb])], writes=[r(pA2)],
                         tag="vmm", chain=(k != 0))
                P.op("vector",
                     lambda e, sl=sl, ng=ng: e.tensor_add(
                         vn[:, sl, ng * 512:(ng + 1) * 512], pA2[:, :],
                         bvn[:, ng * 512:(ng + 1) * 512]),
                     reads=[r(pA2), r(bvn)], writes=[r(vn, sl)], tag="vev")

    def attn_tile(t):
        sl = t % 8
        has_a, has_c = t > 0, t < NT - 1
        g, w = t // 4, t % 4
        gsl = g % 2
        for hg in range(2):
            hbase = hg * 8
            base = 0 if hg == 0 else 64
            for hl in range(8):
                h = hbase + hl
                po = (h % 2) * 64
                kap = sap(kT, po, 64, (h // 2) * 1024 + sl * 128, [[1, 128]])
                qap = sap(qT, po, 64, (h // 2) * 1024 + sl * 128, [[1, 128]])
                P.op("tensor",
                     lambda e, kap=kap, qap=qap, hl=hl, po=po: e.matmul(
                         pSC[:, colB(hl):colB(hl) + 128], kap, qap,
                         start=(hl < 2) or not G_SC,
                         stop=(hl >= 6) or not G_SC,
                         tile_position=(po, 0),
                         skip_group_check=True),
                     reads=[r(kT, sl), r(qT, sl)], writes=[r(pSC)], tag="scB",
                     chain=G_SC and (hl != 0))
            P.op("vector",
                 lambda e: e.tensor_add(
                     pSC[:, :], pSC[:, :],
                     bass.AP(mB, 0, [[128, 128], [0, 8], [1, 128]])),
                 reads=[r(pSC), r(mB)], writes=[r(pSC)], tag="mskB")
            P.op("scalar", lambda e: e.activation(exB[:, :], pSC[:, :], AF.Exp,
                                                  scale=0.125),
                 reads=[r(pSC)], writes=[r(exB)], tag="expB")
            nsub = (1 if has_a else 0) + (1 if has_c else 0)
            si = [0]
            for hl in range(8):
                h = hbase + hl
                po = (h % 2) * 64
                first = (hl < 2) or not G_AC
                last = (hl >= 6) or not G_AC
                if has_a:
                    kap = sap(kT, po, 64, (h // 2) * 1024 + ((t - 1) % 8) * 128 + 96,
                              [[1, 32]])
                    qap = sap(qT, po, 64, (h // 2) * 1024 + sl * 128, [[1, 128]])
                    gfirst = si[0] == 0
                    si[0] += 1
                    P.op("tensor",
                         lambda e, kap=kap, qap=qap, hl=hl, po=po, first=first,
                                last=last: e.matmul(
                             sap(pSC, 96, 32, colB(hl), [[1, 128]]), kap, qap,
                             start=first, stop=last, tile_position=(po, 96),
                             skip_group_check=True),
                         reads=[r(kT, (t - 1) % 8), r(qT, sl)],
                         writes=[r(pSC)], tag="scA", chain=G_AC and not gfirst)
                if has_c:
                    kap = sap(kT, po, 64, (h // 2) * 1024 + ((t + 1) % 8) * 128,
                              [[1, 32]])
                    qap = sap(qT, po, 64, (h // 2) * 1024 + sl * 128, [[1, 128]])
                    gfirst = si[0] == 0
                    si[0] += 1
                    P.op("tensor",
                         lambda e, kap=kap, qap=qap, hl=hl, po=po, first=first,
                                last=last: e.matmul(
                             sap(pSC, 0, 32, colB(hl), [[1, 128]]), kap, qap,
                             start=first, stop=last, tile_position=(po, 0),
                             skip_group_check=True),
                         reads=[r(kT, (t + 1) % 8), r(qT, sl)],
                         writes=[r(pSC)], tag="scC", chain=G_AC and not gfirst)
            if has_a:
                P.op("vector",
                     lambda e: e.tensor_add(
                         sap(pSC, 96, 32, 0, [[128, 8], [1, 128]]),
                         sap(pSC, 96, 32, 0, [[128, 8], [1, 128]]),
                         bass.AP(mA, 96 * 128, [[128, 32], [0, 8], [1, 128]])),
                     reads=[r(pSC), r(mA)], writes=[r(pSC)], tag="mskA")
                P.op("scalar",
                     lambda e: e.activation(exAC[96:128, :], pSC[96:128, :], AF.Exp,
                                            scale=0.125),
                     reads=[r(pSC)], writes=[r(exAC, "A")], tag="expA")
            if has_c:
                P.op("vector",
                     lambda e: e.tensor_add(
                         sap(pSC, 0, 32, 0, [[128, 8], [1, 128]]),
                         sap(pSC, 0, 32, 0, [[128, 8], [1, 128]]),
                         bass.AP(mC, 0, [[128, 32], [0, 8], [1, 128]])),
                     reads=[r(pSC), r(mC)], writes=[r(pSC)], tag="mskC")
                P.op("scalar",
                     lambda e: e.activation(exAC[0:32, :], pSC[0:32, :], AF.Exp,
                                            scale=0.125),
                     reads=[r(pSC)], writes=[r(exAC, "C")], tag="expC")
            chunks = [("B", exB, 0, 128, sl)]
            if has_a:
                chunks.append(("A", exAC, 96, 32, (t - 1) % 8))
            if has_c:
                chunks.append(("C", exAC, 0, 32, (t + 1) % 8))
            for half in range(2):
                for ci, (nm, exb, rb, rn, _ksl) in enumerate(chunks):
                    P.op("tensor",
                         lambda e, exb=exb, rb=rb, rn=rn, half=half,
                                ci=ci, last=(ci == len(chunks) - 1): e.matmul(
                             sap(pBC, 96, 1, half * 512, [[1, 512]]),
                             sap(ones1, rb, rn, 0, [[1, 1]]),
                             sap(exb, rb, rn, half * 512, [[1, 512]]),
                             start=(ci == 0), stop=last, tile_position=(rb, 96)),
                         reads=[r(exb, nm if exb is exAC else ()), r(ones1)],
                         writes=[r(pBC)], tag="sums",
                         chain=(ci != 0) or (G_SB and half != 0))
            P.op("vector", lambda e: e.tensor_copy(sums[96:97, :], pBC[96:97, :]),
                 reads=[r(pBC)], writes=[r(sums)], tag="sumcp")
            for half in range(2):
                P.op("tensor",
                     lambda e, half=half, base=base: e.matmul(
                         sap(pBC, base, 64, half * 512, [[1, 512]]),
                         sap(ones1, 96, 1, 0, [[1, 64]]),
                         sap(sums, 96, 1, half * 512, [[1, 512]]),
                         start=True, stop=True, tile_position=(96, base)),
                     reads=[r(sums), r(ones1)], writes=[r(pBC)], tag="bcast",
                     chain=G_SB)
            P.op("scalar",
                 lambda e, base=base: e.activation(
                     rinv[base:base + 64, :], pBC[base:base + 64, :], AF.Ln),
                 reads=[r(pBC)], writes=[r(rinv)], tag="rln")
            P.op("scalar",
                 lambda e, base=base: e.activation(
                     rinv[base:base + 64, :], rinv[base:base + 64, :], AF.Exp,
                     scale=-1.0),
                 reads=[r(rinv)], writes=[r(rinv)], tag="rex")
            for hl in range(8):
                h = hbase + hl
                vcol = h * 64
                for ci, (nm, exb, rb, rn, ksl) in enumerate(chunks):
                    P.op("tensor",
                         lambda e, exb=exb, rb=rb, rn=rn, ksl=ksl, vcol=vcol, hl=hl,
                                base=base, ci=ci,
                                first=(ci == 0 and (hl < 2 or not G_PV)),
                                last=(ci == len(chunks) - 1
                                      and (hl >= 6 or not G_PV)): e.matmul(
                             sap(pPV, base, 64, colB(hl), [[1, 128]]),
                             sap(vn, rb, rn, ksl * 1024 + vcol, [[1, 64]]),
                             sap(exb, rb, rn, colB(hl), [[1, 128]]),
                             start=first, stop=last, tile_position=(rb, base),
                             skip_group_check=True),
                         reads=[r(vn, ksl), r(exb, nm if exb is exAC else ())],
                         writes=[r(pPV)], tag="pv",
                         chain=(ci != 0 or (G_PV and hl != 0)))
            vo = gsl * 512 + w * 128
            for par in range(2):
                P.op("vector",
                     lambda e, base=base, vo=vo, par=par: e.tensor_mul(
                         sap(vals, base, 64, par * 1024 + vo, [[2048, 4], [1, 128]]),
                         sap(pPV, base, 64, par * 512, [[128, 4], [1, 128]]),
                         sap(rinv, base, 64, par * 512, [[128, 4], [1, 128]])),
                     reads=[r(pPV), r(rinv)], writes=[r(vals, (gsl, w))],
                     tag="valev")

    def wo_group(g):
        gsl = g % 2
        for j in range(8):
            for k in range(8):
                P.op("tensor",
                     lambda e, j=j, k=k, gsl=gsl: e.matmul(
                         pA1[:, :], wos[:, k, j * 128:(j + 1) * 128],
                         vals[:, k, gsl * 512:(gsl + 1) * 512],
                         start=(k == 0), stop=(k == 7)),
                     reads=[r(wos)] + [r(vals, (gsl, w)) for w in range(4)],
                     writes=[r(pA1)], tag="womm", chain=(k != 0))
            P.op("scalar",
                 lambda e, j=j: e.activation(oT[:, j, :], pA1[:, :], AF.Identity,
                                             bias=bo[:, j:j + 1]),
                 reads=[r(pA1), r(bo)], writes=[r(oT, j)], tag="woev")

    def ln2_add_group(g):
        xbuf = xg[g % 2]
        emit_ln_stats(C, oT, lambda pt: [r(oT, pt)], pA2, ones_r, m2_sb, r2_sb,
                      tmpb, tag="ln2")
        for pt in range(8):
            P.op("vector",
                 lambda e, pt=pt: e.tensor_sub(lntmp[:, :], oT[:, pt, :], m2_sb[:, :]),
                 reads=[r(oT, pt), r(m2_sb)], writes=[r(lntmp)], tag="a1")
            P.op("vector",
                 lambda e, pt=pt: e.tensor_mul(lntmp[:, :], lntmp[:, :], r2_sb[:, :]),
                 reads=[r(lntmp), r(r2_sb)], writes=[r(lntmp)], tag="a2")
            P.op("vector",
                 lambda e, pt=pt, xbuf=xbuf: e.tensor_add(
                     xbuf[:, pt, :], xbuf[:, pt, :], lntmp[:, :]),
                 reads=[r(lntmp), r(xbuf)], writes=[r(xbuf)], tag="a3")
        C.dma("sync", x_dst_r[:, :, g * 512:(g + 1) * 512], xbuf[:, :, :],
              [("dram:" + dst_nm, g)], reads=[r(xbuf)], tag=f"xo{g}")

    def attn_batch(g):
        if g == 0:
            ts = list(range(0, min(3, NT)))
        else:
            ts = list(range(4 * g - 1, min(4 * g + 3, NT)))
        if g == NG - 1:
            ts += list(range(max(4 * g + 3, ts[-1] + 1 if ts else 0), NT))
        for t in ts:
            attn_tile(t)

    ln1_group(0)
    qkv_group(0)
    attn_batch(0)
    for g in range(1, NG):
        ln1_group(g)
        qkv_group(g)
        attn_batch(g)
        wo_group(g - 1)
        ln2_add_group(g - 1)
    wo_group(NG - 1)
    ln2_add_group(NG - 1)

    P.fence()
    st.close()


def emit_l0f(C, x_src_dram, x_dst_dram, src_nm, dst_nm):
    P, nc, S = C.plan, C.nc, C.S
    NT, NG = S // 128, S // 512
    st = contextlib.ExitStack()

    gw_d = C.dram_in("gate_wT", [D, 16])
    gb_d = C.dram_in("gbias_bc", [128, 16])
    esel_d = C.dram_in("esel", [16, 16 * 128], BF16)
    idb_d = C.dram_in("ident_bf", [128, 128], BF16)
    wsg_d = C.dram_in("wsg", [D, D])
    wsu_d = C.dram_in("wsu", [D, D])
    wsd_d = C.dram_in("wsd", [D, D])
    wg_d = C.dram_in("wg_t", [16, D, 512])
    wu_d = C.dram_in("wu_t", [16, D, 512])
    wd_d = C.dram_in("wd_t", [16, 512, D])

    ln1f = C.sb("ln1f", [128, 8, S], BF16, st)
    y_sb = C.sb("y_sb", [128, 8, S], F32, st)
    cwT = C.sb("cwT", [128, S], BF16, st)
    identb = C.sb("identb_s", [128, 128], BF16, st)
    gbias = C.sb("gbias_s", [128, 16], F32, st)
    gws = C.sb("gws", [128, 8, 16], BF16, st)
    xg2 = [C.sb(f"xg2_{i}", [128, 8, 512], F32, st) for i in range(2)]
    lnm = C.sb("f0m", [128, 512], F32, st)
    lnr = C.sb("f0r", [128, 512], F32, st)
    lnm2 = C.sb("f0m2", [128, 512], F32, st)
    lnr2 = C.sb("f0r2", [128, 512], F32, st)
    tmpb = C.sb("f0tmpb", [128, 512], BF16, st)
    lnt = C.sb("f0lnt", [128, 512], F32, st)

    C.dma("gpsimd", gws[:, :, :], gw_d.rearrange("(a p) c -> p a c", p=128),
          [r(gws)], tag="gw")
    C.dma("sync", gbias[:, :], gb_d, [r(gbias)])
    C.dma("sync", identb[:, :], idb_d, [r(identb)])

    x_src_r = x_src_dram.rearrange("(a p) c -> p a c", p=128)
    x_dst_r = x_dst_dram.rearrange("(a p) c -> p a c", p=128)

    stA = contextlib.ExitStack()
    wsg = C.sb("wsg_s", [128, 8, D], BF16, stA)
    wsu = C.sb("wsu_s", [128, 8, D], BF16, stA)
    wsd = C.sb("wsd_s", [128, 8, D], BF16, stA)
    sh_g = C.sb("sh_g", [128, 8, 512], BF16, stA)
    pF1 = C.ps("pF1", [128, 512], F32, stA)
    pF2 = C.ps("pF2", [128, 512], F32, stA)
    pF3 = C.ps("pF3", [128, 512], F32, stA)
    sco = C.sb("sco", [128, 16], F32, stA)
    sfc = C.sb("sfc", [128, 16], F32, stA)
    msk = C.sb("msk", [128, 16], F32, stA)
    sel = C.sb("sel", [128, 16], F32, stA)
    oneh = C.sb("oneh", [128, 16], F32, stA)
    emk = C.sb("emk", [128, 16], F32, stA)
    p8a = C.sb("p8a", [128, 8], F32, stA)
    p8b = C.sb("p8b", [128, 8], F32, stA)
    p4a = C.sb("p4a", [128, 4], F32, stA)
    p4b = C.sb("p4b", [128, 4], F32, stA)
    p4c = C.sb("p4c", [128, 4], F32, stA)
    p4d = C.sb("p4d", [128, 4], F32, stA)
    s1a = C.sb("s1a", [128, 1], F32, stA)
    s1b = C.sb("s1b", [128, 1], F32, stA)
    s1c = C.sb("s1c", [128, 1], F32, stA)
    cwb = C.sb("cwb", [128, 16], BF16, stA)

    C.dma("gpsimd", wsg[:, :, :], wsg_d.rearrange("(a p) c -> p a c", p=128),
          [r(wsg)], tag="wsg")
    C.dma("gpsimd", wsu[:, :, :], wsu_d.rearrange("(a p) c -> p a c", p=128),
          [r(wsu)], tag="wsu")
    C.dma("gpsimd", wsd[:, :, :], wsd_d.rearrange("(a p) c -> p a c", p=128),
          [r(wsd)], tag="wsd")

    def ln1_group(g):
        gb = g % 2
        C.dma("sync", xg2[gb][:, :, :], x_src_r[:, :, g * 512:(g + 1) * 512],
              [r(xg2[gb])], reads=[("dram:" + src_nm, g)], tag=f"fxg{g}")
        emit_ln_stats(C, xg2[gb], lambda pt: [r(xg2[gb])], pF1, C.ones_r, lnm, lnr,
                      tmpb, tag="fln1")
        for pt in range(8):
            P.op("vector",
                 lambda e, pt=pt, gb=gb: e.tensor_sub(
                     lnt[:, :], xg2[gb][:, pt, :], lnm[:, :]),
                 reads=[r(xg2[gb]), r(lnm)], writes=[r(lnt)], tag="f1s")
            P.op("vector",
                 lambda e, pt=pt, gb=gb, g=g: e.tensor_mul(
                     ln1f[:, pt, g * 512:(g + 1) * 512], lnt[:, :], lnr[:, :]),
                 reads=[r(lnt), r(lnr)], writes=[r(ln1f, (g, pt))], tag="f1m")

    def routing_tile(t):
        g = t // 4
        for k in range(8):
            P.op("tensor",
                 lambda e, k=k, t=t: e.matmul(
                     pF1[:, 0:16], ln1f[:, k, t * 128:(t + 1) * 128], gws[:, k, :],
                     start=(k == 0), stop=(k == 7)),
                 reads=[r(ln1f, (g, k)), r(gws)], writes=[r(pF1)], tag="gmm", chain=(k != 0))
        P.op("scalar", lambda e: e.activation(sco[:, :], pF1[:, 0:16], AF.Sigmoid),
             reads=[r(pF1)], writes=[r(sco)], tag="sig")
        P.op("vector", lambda e: e.tensor_add(sfc[:, :], sco[:, :], gbias[:, :]),
             reads=[r(sco), r(gbias)], writes=[r(sfc)], tag="sfc")
        ev = lambda b, o, n, s: sap(b, 0, 128, o, [[s, n]])
        P.op("vector", lambda e: e.tensor_max(p8a[:, :], ev(sfc, 0, 8, 2),
                                              ev(sfc, 1, 8, 2)),
             reads=[r(sfc)], writes=[r(p8a)], tag="hi")
        P.op("vector", lambda e: e.tensor_tensor(p8b[:, :], ev(sfc, 0, 8, 2),
                                                 ev(sfc, 1, 8, 2), ALU.min),
             reads=[r(sfc)], writes=[r(p8b)], tag="lo")
        P.op("vector", lambda e: e.tensor_max(p4a[:, :], ev(p8a, 0, 4, 2),
                                              ev(p8a, 1, 4, 2)),
             reads=[r(p8a)], writes=[r(p4a)], tag="t1")
        P.op("vector", lambda e: e.tensor_tensor(p4b[:, :], ev(p8a, 0, 4, 2),
                                                 ev(p8a, 1, 4, 2), ALU.min),
             reads=[r(p8a)], writes=[r(p4b)], tag="mn")
        P.op("vector", lambda e: e.tensor_max(p4c[:, :], ev(p8b, 0, 4, 2),
                                              ev(p8b, 1, 4, 2)),
             reads=[r(p8b)], writes=[r(p4c)], tag="ml")
        P.op("vector", lambda e: e.tensor_max(p4b[:, :], p4b[:, :], p4c[:, :]),
             reads=[r(p4b), r(p4c)], writes=[r(p4b)], tag="sec")
        P.op("vector", lambda e: e.tensor_add(p4a[:, :], p4a[:, :], p4b[:, :]),
             reads=[r(p4a), r(p4b)], writes=[r(p4a)], tag="g2")
        P.op("vector", lambda e: e.tensor_max(s1a[:, :], p4a[:, 0:1], p4a[:, 1:2]),
             reads=[r(p4a)], writes=[r(s1a)], tag="a1")
        P.op("vector", lambda e: e.tensor_max(s1b[:, :], p4a[:, 2:3], p4a[:, 3:4]),
             reads=[r(p4a)], writes=[r(s1b)], tag="b1")
        P.op("vector", lambda e: e.tensor_tensor(s1a[:, :], s1a[:, :], s1b[:, :],
                                                 ALU.min),
             reads=[r(s1a), r(s1b)], writes=[r(s1a)], tag="u")
        P.op("vector", lambda e: e.tensor_tensor(s1b[:, :], p4a[:, 0:1], p4a[:, 1:2],
                                                 ALU.min),
             reads=[r(p4a), r(s1b)], writes=[r(s1b)], tag="c1")
        P.op("vector", lambda e: e.tensor_tensor(s1c[:, :], p4a[:, 2:3], p4a[:, 3:4],
                                                 ALU.min),
             reads=[r(p4a)], writes=[r(s1c)], tag="d1")
        P.op("vector", lambda e: e.tensor_max(s1b[:, :], s1b[:, :], s1c[:, :]),
             reads=[r(s1b), r(s1c)], writes=[r(s1b)], tag="v")
        P.op("vector", lambda e: e.tensor_max(s1a[:, :], s1a[:, :], s1b[:, :]),
             reads=[r(s1a), r(s1b)], writes=[r(s1a)], tag="gthr")
        P.op("vector",
             lambda e: e.tensor_single_scalar(p4d[:, :], p4a[:, :], s1a[:, 0:1],
                                              ALU.is_ge),
             reads=[r(p4a), r(s1a)], writes=[r(p4d)], tag="gmask")
        P.op("vector",
             lambda e: e.tensor_copy(sap(emk, 0, 128, 0, [[4, 4], [1, 4]]),
                                     sap(p4d, 0, 128, 0, [[1, 4], [0, 4]])),
             reads=[r(p4d)], writes=[r(emk)], tag="emk")
        P.op("vector",
             lambda e: e.tensor_scalar(oneh[:, :], emk[:, :], 1e30, -1e30,
                                       ALU.mult, ALU.add),
             reads=[r(emk)], writes=[r(oneh)], tag="emsk")
        P.op("vector", lambda e: e.tensor_add(msk[:, :], sfc[:, :], oneh[:, :]),
             reads=[r(sfc), r(oneh)], writes=[r(msk)], tag="mskd")
        P.op("vector", lambda e: e.memset(sel[:, :], 0.0),
             writes=[r(sel)], tag="sel0")
        for it in range(4):
            P.op("vector", lambda e: e.reduce_max(s1b[:, :], msk[:, :], AX.X),
                 reads=[r(msk)], writes=[r(s1b)], tag="mx")
            P.op("vector",
                 lambda e: e.tensor_single_scalar(oneh[:, :], msk[:, :], s1b[:, 0:1],
                                                  ALU.is_ge),
                 reads=[r(msk), r(s1b)], writes=[r(oneh)], tag="oh")
            P.op("vector", lambda e: e.tensor_add(sel[:, :], sel[:, :], oneh[:, :]),
                 reads=[r(sel), r(oneh)], writes=[r(sel)], tag="seladd")
            if it < 3:
                P.op("vector",
                     lambda e: e.scalar_tensor_tensor(msk[:, :], oneh[:, :], -4e30,
                                                      msk[:, :], ALU.mult, ALU.add),
                     reads=[r(oneh), r(msk)], writes=[r(msk)], tag="knock")
        P.op("vector", lambda e: e.tensor_mul(oneh[:, :], sco[:, :], sel[:, :]),
             reads=[r(sco), r(sel)], writes=[r(oneh)], tag="cwr")
        P.op("vector", lambda e: e.reduce_sum(s1b[:, :], oneh[:, :], AX.X),
             reads=[r(oneh)], writes=[r(s1b)], tag="den")
        P.op("scalar", lambda e: e.activation(s1b[:, :], s1b[:, :], AF.Ln,
                                              bias=C.cv(1e-20)),
             reads=[r(s1b), r(C.cvals)], writes=[r(s1b)], tag="dln")
        P.op("scalar", lambda e: e.activation(s1b[:, :], s1b[:, :], AF.Exp,
                                              scale=-1.0),
             reads=[r(s1b)], writes=[r(s1b)], tag="dex")
        P.op("vector",
             lambda e: e.tensor_scalar(cwb[:, :], oneh[:, :], s1b[:, 0:1], 2.5,
                                       ALU.mult, ALU.mult),
             reads=[r(oneh), r(s1b)], writes=[r(cwb)], tag="cw")
        P.op("tensor",
             lambda e: e.matmul(pF1[0:16, 0:128], cwb[:, :], identb[:, :],
                                start=True, stop=True),
             reads=[r(cwb), r(identb)], writes=[r(pF1)], tag="cwt")
        P.op("vector",
             lambda e, t=t: e.tensor_copy(cwT[0:16, t * 128:(t + 1) * 128],
                                          pF1[0:16, 0:128]),
             reads=[r(pF1)], writes=[r(cwT, t)], tag="cwev")

    def shared_group(g):
        c0 = g * 512
        for j in range(8):
            for k in range(8):
                P.op("tensor",
                     lambda e, j=j, k=k, c0=c0: e.matmul(
                         pF2[:, :], wsg[:, k, j * 128:(j + 1) * 128],
                         ln1f[:, k, c0:c0 + 512], start=(k == 0), stop=(k == 7)),
                     reads=[r(wsg), r(ln1f, (g, k))], writes=[r(pF2)], tag="sgmm", chain=(k != 0))
            P.op("scalar", lambda e, j=j: e.activation(sh_g[:, j, :], pF2[:, :],
                                                       AF.Silu),
                 reads=[r(pF2)], writes=[r(sh_g, j)], tag="sgsilu")
        for j in range(8):
            for k in range(8):
                P.op("tensor",
                     lambda e, j=j, k=k, c0=c0: e.matmul(
                         pF2[:, :], wsu[:, k, j * 128:(j + 1) * 128],
                         ln1f[:, k, c0:c0 + 512], start=(k == 0), stop=(k == 7)),
                     reads=[r(wsu), r(ln1f, (g, k))], writes=[r(pF2)], tag="summ", chain=(k != 0))
            P.op("vector", lambda e, j=j: e.tensor_mul(sh_g[:, j, :], pF2[:, :],
                                                       sh_g[:, j, :]),
                 reads=[r(pF2), r(sh_g, j)], writes=[r(sh_g, j)], tag="suact")
        for j in range(8):
            for k in range(8):
                P.op("tensor",
                     lambda e, j=j, k=k: e.matmul(
                         pF3[:, :], wsd[:, k, j * 128:(j + 1) * 128],
                         sh_g[:, k, :], start=(k == 0), stop=(k == 7)),
                     reads=[r(wsd), r(sh_g, k)], writes=[r(pF3)], tag="sdmm", chain=(k != 0))
            P.op("scalar",
                 lambda e, j=j, c0=c0: e.activation(y_sb[:, j, c0:c0 + 512],
                                                    pF3[:, :], AF.Copy),
                 reads=[r(pF3)], writes=[r(y_sb, ("c", g))], tag="sdev")

    for g in range(NG):
        ln1_group(g)
        for w in range(4):
            routing_tile(4 * g + w)
        shared_group(g)

    P.fence()
    stA.close()

    stB = contextlib.ExitStack()
    esel = C.sb("esel_s", [128, 16 * 128], BF16, stB)
    C.dma("sync", esel[0:16, :], esel_d, [r(esel)])
    wgS = [C.sb(f"wgS{i}", [128, 8, 512], BF16, stB) for i in range(2)]
    wuS = [C.sb(f"wuS{i}", [128, 8, 512], BF16, stB) for i in range(2)]
    wdS = [C.sb(f"wdS{i}", [128, 4, D], BF16, stB) for i in range(2)]
    silu_e = C.sb("silu_e", [128, 2048], BF16, stB)
    act_e = silu_e
    cw_bc = C.sb("cw_bc", [128, S], BF16, stB)
    pGU = C.ps("pGU", [128, 2048], F32, stB)
    pY = C.ps("pY", [128, 2048], F32, stB)
    NC4 = S // 512

    def load_expert(e):
        sl = e % 2
        C.dma("gpsimd", wgS[sl][:, :, :],
              wg_d[e].rearrange("(a p) c -> p a c", p=128), [r(wgS[sl])],
              tag=f"wg{e}")
        C.dma("gpsimd", wuS[sl][:, :, :],
              wu_d[e].rearrange("(a p) c -> p a c", p=128), [r(wuS[sl])],
              tag=f"wu{e}")
        C.dma("gpsimd", wdS[sl][:, :, :],
              wd_d[e].rearrange("(a p) c -> p a c", p=128), [r(wdS[sl])],
              tag=f"wd{e}")

    load_expert(0)
    for e in range(16):
        sl = e % 2
        if e + 1 < 16:
            load_expert(e + 1)
        for q in range(NC4):
            P.op("tensor",
                 lambda e2, e=e, q=q: e2.matmul(
                     pGU[:, q * 512:(q + 1) * 512], esel[0:16, e * 128:(e + 1) * 128],
                     cwT[0:16, q * 512:(q + 1) * 512], start=True, stop=True),
                 reads=[r(esel)] + [r(cwT, 4 * q + w) for w in range(4)],
                 writes=[r(pGU)], tag="cwbc")
        P.op("scalar", lambda e2: e2.activation(cw_bc[:, 0:512 * NC4],
                                                pGU[:, 0:512 * NC4], AF.Copy),
             reads=[r(pGU)], writes=[r(cw_bc)], tag="cwbev")
        for c in range(NC4):
            c0 = c * 512
            for m in range(4):
                for k in range(8):
                    P.op("tensor",
                         lambda e2, m=m, k=k, c0=c0, sl=sl: e2.matmul(
                             pGU[:, m * 512:(m + 1) * 512],
                             wgS[sl][:, k, m * 128:(m + 1) * 128],
                             ln1f[:, k, c0:c0 + 512], start=(k == 0), stop=(k == 7)),
                         reads=[r(wgS[sl]), r(ln1f, (c, k))], writes=[r(pGU)],
                         tag="egmm", chain=(k != 0))
            P.op("scalar", lambda e2: e2.activation(silu_e[:, :], pGU[:, :], AF.Silu),
                 reads=[r(pGU)], writes=[r(silu_e)], tag="esilu")
            for m in range(4):
                for k in range(8):
                    P.op("tensor",
                         lambda e2, m=m, k=k, c0=c0, sl=sl: e2.matmul(
                             pGU[:, m * 512:(m + 1) * 512],
                             wuS[sl][:, k, m * 128:(m + 1) * 128],
                             ln1f[:, k, c0:c0 + 512], start=(k == 0), stop=(k == 7)),
                         reads=[r(wuS[sl]), r(ln1f, (c, k))], writes=[r(pGU)],
                         tag="eumm", chain=(k != 0))
            P.op("vector", lambda e2: e2.tensor_mul(act_e[:, :], pGU[:, :],
                                                    silu_e[:, :]),
                 reads=[r(pGU), r(silu_e)], writes=[r(silu_e)], tag="eact1")
            P.op("vector",
                 lambda e2, c0=c0: e2.tensor_mul(
                     act_e[:, :], act_e[:, :],
                     sap(cw_bc, 0, 128, c0, [[0, 4], [1, 512]])),
                 reads=[r(silu_e), r(cw_bc)], writes=[r(silu_e)], tag="eact2")
            for jh in range(2):
                for m2 in range(4):
                    j = jh * 4 + m2
                    for k in range(4):
                        P.op("tensor",
                             lambda e2, j=j, m2=m2, k=k, sl=sl: e2.matmul(
                                 pY[:, m2 * 512:(m2 + 1) * 512],
                                 wdS[sl][:, k, j * 128:(j + 1) * 128],
                                 sap(act_e, 0, 128, k * 512, [[1, 512]]),
                                 start=(k == 0), stop=(k == 3)),
                             reads=[r(wdS[sl]), r(silu_e)], writes=[r(pY)],
                             tag="edmm", chain=(k != 0))
                P.op("vector",
                     lambda e2, jh=jh, c0=c0: e2.tensor_add(
                         bass.AP(y_sb, (jh * 4) * S + c0, [[8 * S, 128], [S, 4], [1, 512]]),
                         bass.AP(y_sb, (jh * 4) * S + c0, [[8 * S, 128], [S, 4], [1, 512]]),
                         pGU[:, :] if False else bass.AP(pY, 0, [[2048, 128], [512, 4], [1, 512]])),
                     reads=[r(pY), r(y_sb, ("c", c))], writes=[r(y_sb, ("c", c))],
                     tag="eyadd")

    P.fence()
    stB.close()

    stC = contextlib.ExitStack()
    pT = C.ps("pT", [128, 512], F32, stC)
    for g in range(NG):
        c0 = g * 512
        P.op("vector",
             lambda e, c0=c0, g=g: e.tensor_add(
                 bass.AP(y_sb, c0, [[8 * S, 128], [S, 8], [1, 512]]),
                 bass.AP(y_sb, c0, [[8 * S, 128], [S, 8], [1, 512]]),
                 bass.AP(ln1f, c0, [[8 * S, 128], [S, 8], [1, 512]])),
             reads=[r(y_sb, ("c", g))] + [r(ln1f, (g, pt)) for pt in range(8)],
             writes=[r(y_sb, ("c", g))], tag="bout")
        emit_ln_stats(C, y_sb, lambda pt, g=g: [r(y_sb, ("c", g))], pT, C.ones_r,
                      lnm2, lnr2, tmpb, tag="fln2", c0=c0)
        gb = g % 2
        C.dma("sync", xg2[gb][:, :, :], x_src_r[:, :, c0:c0 + 512],
              [r(xg2[gb])], reads=[("dram:" + src_nm, g)], tag=f"fxr{g}")
        for pt in range(8):
            P.op("vector",
                 lambda e, pt=pt, c0=c0: e.tensor_sub(lnt[:, :],
                                                      y_sb[:, pt, c0:c0 + 512],
                                                      lnm2[:, :]),
                 reads=[r(y_sb, ("c", g)), r(lnm2)], writes=[r(lnt)], tag="fa1")
            P.op("vector",
                 lambda e, pt=pt: e.tensor_mul(lnt[:, :], lnt[:, :], lnr2[:, :]),
                 reads=[r(lnt), r(lnr2)], writes=[r(lnt)], tag="fa2")
            P.op("vector",
                 lambda e, pt=pt, gb=gb: e.tensor_add(
                     xg2[gb][:, pt, :], xg2[gb][:, pt, :], lnt[:, :]),
                 reads=[r(lnt), r(xg2[gb])], writes=[r(xg2[gb])], tag="fa3")
        C.dma("sync", x_dst_r[:, :, c0:c0 + 512], xg2[gb][:, :, :],
              [("dram:" + dst_nm, g)], reads=[r(xg2[gb])], tag=f"fxo{g}")

    P.fence()
    stC.close()
    st.close()


def emit_l1a(C, x_src_dram, x_dst_dram, src_nm, dst_nm):
    P, nc, S = C.plan, C.nc, C.S
    NT, NG = S // 128, S // 512
    st = contextlib.ExitStack()

    wq_d = C.dram_in("wq_t", [D, D])
    wk_d = C.dram_in("wk_t", [D, 64])
    wv_d = C.dram_in("wv_t", [D, 64])
    wf_d = C.dram_in("wf_t", [D, D])
    bf_d = C.dram_in("bf_sb", [128, 8])
    mo_d = C.dram_in("maskO", [128, 128])
    mn_d = C.dram_in("maskN", [128, 128])

    mO = C.sb("mO_s", [128, 128], F32, st)
    mN = C.sb("mN_s", [128, 128], F32, st)
    bf_s = C.sb("bf_s", [128, 8], F32, st)
    wqf = C.sb("wqf", [128, 8, D], BF16, st)
    wfs2 = C.sb("wfs2", [128, 8, D], BF16, st)
    wks = C.sb("wks", [128, 8, 64], BF16, st)
    wvs = C.sb("wvs", [128, 8, 64], BF16, st)
    qT1 = C.sb("qT1", [128, 8, S], BF16, st)
    kTd = C.sb("kTd", [128, S], BF16, st)
    v_n = C.sb("v_n", [128, NT * 64], BF16, st)
    ln1a = [C.sb(f"l1ln_{i}", [128, 8, 512], BF16, st) for i in range(2)]
    xga = [C.sb(f"xga{i}", [128, 8, 512], F32, st) for i in range(2)]
    vals1 = C.sb("vals1", [128, 8, 2 * 512], BF16, st)
    exL = C.sb("exL", [128, 3 * 1024], BF16, st)
    sums1 = C.sb("sums1", [128, 1024], BF16, st)
    rinv1 = C.sb("rinv1", [128, 1024], BF16, st)
    m_sb = C.sb("a1m", [128, 512], F32, st)
    r_sb = C.sb("a1r", [128, 512], F32, st)
    m2sb = C.sb("a1m2", [128, 512], F32, st)
    r2sb = C.sb("a1r2", [128, 512], F32, st)
    tmpb = C.sb("a1tmpb", [128, 512], BF16, st)
    lnt1 = C.sb("a1lnt", [128, 512], F32, st)
    oT1 = C.sb("a1oT", [128, 8, 512], BF16, st)

    pSC1 = C.ps("pSC1", [128, 1024], F32, st)
    pPV1 = C.ps("pPV1", [128, 1024], F32, st)
    pBC1 = C.ps("pBC1", [128, 1024], F32, st)
    pQ1 = C.ps("pQ1", [128, 512], F32, st)
    pS1 = C.ps("pS1", [128, 512], F32, st)

    C.dma("sync", mO[:, :], mo_d, [r(mO)])
    C.dma("sync", mN[:, :], mn_d, [r(mN)])
    C.dma("sync", bf_s[:, :], bf_d, [r(bf_s)])
    C.dma("gpsimd", wqf[:, :, :], wq_d.rearrange("(a p) c -> p a c", p=128),
          [r(wqf, "q")], tag="wq")
    C.dma("gpsimd", wfs2[:, :, :], wf_d.rearrange("(a p) c -> p a c", p=128),
          [r(wfs2)], tag="wf")
    C.dma("gpsimd", wks[:, :, :], wk_d.rearrange("(a p) c -> p a c", p=128),
          [r(wks)], tag="wk")
    C.dma("gpsimd", wvs[:, :, :], wv_d.rearrange("(a p) c -> p a c", p=128),
          [r(wvs)], tag="wv")

    x_src_r = x_src_dram.rearrange("(a p) c -> p a c", p=128)
    x_dst_r = x_dst_dram.rearrange("(a p) c -> p a c", p=128)

    def ln1_group(g):
        gb = g % 2
        C.dma("sync", xga[gb][:, :, :], x_src_r[:, :, g * 512:(g + 1) * 512],
              [r(xga[gb])], reads=[("dram:" + src_nm, g)], tag=f"axg{g}")
        emit_ln_stats(C, xga[gb], lambda pt: [r(xga[gb])], pS1, C.ones_r, m_sb,
                      r_sb, tmpb, tag="aln1")
        for pt in range(8):
            P.op("vector",
                 lambda e, pt=pt, gb=gb: e.tensor_sub(
                     lnt1[:, :], xga[gb][:, pt, :], m_sb[:, :]),
                 reads=[r(xga[gb]), r(m_sb)], writes=[r(lnt1)], tag="as")
            P.op("vector",
                 lambda e, pt=pt, gb=gb: e.tensor_mul(
                     ln1a[gb][:, pt, :], lnt1[:, :], r_sb[:, :]),
                 reads=[r(lnt1), r(r_sb)], writes=[r(ln1a[gb])], tag="am")

    def qkv_group(g):
        gb = g % 2
        c0 = g * 512
        for i in range(8):
            for k in range(8):
                P.op("tensor",
                     lambda e, i=i, k=k, gb=gb: e.matmul(
                         pQ1[:, :], wqf[:, k, i * 128:(i + 1) * 128],
                         ln1a[gb][:, k, :], start=(k == 0), stop=(k == 7)),
                     reads=[r(wqf, "q"), r(ln1a[gb])], writes=[r(pQ1)], tag="qmm", chain=(k != 0))
            P.op("scalar",
                 lambda e, i=i, c0=c0: e.activation(
                     qT1[:, i, c0:c0 + 512], pQ1[:, :], AF.Copy),
                 reads=[r(pQ1)], writes=[r(qT1, g)], tag="qev")
        for half in range(2):
            for k in range(8):
                P.op("tensor",
                     lambda e, half=half, k=k, gb=gb: e.matmul(
                         sap(pQ1, 64 * half, 64, 0, [[1, 512]]),
                         wks[:, k, :], ln1a[gb][:, k, :],
                         start=(k == 0), stop=(k == 7),
                         tile_position=(0, 64 * half)),
                     reads=[r(wks), r(ln1a[gb])], writes=[r(pQ1)], tag="kmm", chain=(k != 0))
        P.op("scalar",
             lambda e, c0=c0: e.activation(kTd[:, c0:c0 + 512], pQ1[:, :], AF.Copy),
             reads=[r(pQ1)], writes=[r(kTd, g)], tag="kev")
        for w in range(4):
            t = 4 * g + w
            for k in range(8):
                P.op("tensor",
                     lambda e, w=w, k=k, gb=gb: e.matmul(
                         sap(pQ1, 0, 128, 0, [[1, 64]]),
                         ln1a[gb][:, k, w * 128:(w + 1) * 128], wvs[:, k, :],
                         start=(k == 0), stop=(k == 7)),
                     reads=[r(wvs), r(ln1a[gb])], writes=[r(pQ1)], tag="vmm1", chain=(k != 0))
            P.op("scalar",
                 lambda e, t=t: e.activation(v_n[:, t * 64:(t + 1) * 64],
                                             sap(pQ1, 0, 128, 0, [[1, 64]]),
                                             AF.Copy),
                 reads=[r(pQ1)], writes=[r(v_n, t)], tag="vev1")

    def attn_tile(t):
        g, w = t // 4, t % 4
        gsl = g % 2
        kts = [kt for kt in (t - 2, t - 1, t) if kt >= 0]
        for hg in range(2):
            hbase = hg * 8
            base = 0 if hg == 0 else 64
            for ci, kt in enumerate(kts):
                for hl in range(8):
                    h = hbase + hl
                    po = (h % 2) * 64
                    P.op("tensor",
                         lambda e, hl=hl, h=h, po=po, kt=kt: e.matmul(
                             pSC1[:, colB(hl):colB(hl) + 128],
                             sap(kTd, po, 64, kt * 128, [[1, 128]]),
                             sap(qT1, po, 64, (h // 2) * S + t * 128, [[1, 128]]),
                             start=(hl < 2) or not G_SC,
                             stop=(hl >= 6) or not G_SC,
                             tile_position=(po, 0),
                             skip_group_check=True),
                         reads=[r(kTd, kt // 4), r(qT1, g)], writes=[r(pSC1)],
                         tag="sc1", chain=G_SC and (hl != 0))
                if kt == t:
                    P.op("vector",
                         lambda e: e.tensor_add(
                             pSC1[:, :], pSC1[:, :],
                             bass.AP(mN, 0, [[128, 128], [0, 8], [1, 128]])),
                         reads=[r(pSC1), r(mN)], writes=[r(pSC1)], tag="mn1")
                elif kt == t - 2:
                    P.op("vector",
                         lambda e: e.tensor_add(
                             pSC1[:, :], pSC1[:, :],
                             bass.AP(mO, 0, [[128, 128], [0, 8], [1, 128]])),
                         reads=[r(pSC1), r(mO)], writes=[r(pSC1)], tag="mo1")
                P.op("scalar",
                     lambda e, ci=ci: e.activation(
                         exL[:, ci * 1024:(ci + 1) * 1024], pSC1[:, :], AF.Exp,
                         scale=0.125),
                     reads=[r(pSC1)], writes=[r(exL, ci)], tag="ex1")
            for half in range(2):
                for ci in range(len(kts)):
                    P.op("tensor",
                         lambda e, ci=ci, half=half,
                                last=(ci == len(kts) - 1): e.matmul(
                             sap(pBC1, 96, 1, half * 512, [[1, 512]]),
                             sap(C.ones1, 0, 128, 0, [[1, 1]]),
                             sap(exL, 0, 128, ci * 1024 + half * 512, [[1, 512]]),
                             start=(ci == 0), stop=last, tile_position=(0, 96)),
                         reads=[r(exL, ci), r(C.ones1)], writes=[r(pBC1)],
                         tag="sums1", chain=(ci != 0) or (G_SB and half != 0))
            P.op("vector", lambda e: e.tensor_copy(sums1[96:97, :], pBC1[96:97, :]),
                 reads=[r(pBC1)], writes=[r(sums1)], tag="sumcp1")
            for half in range(2):
                P.op("tensor",
                     lambda e, half=half, base=base: e.matmul(
                         sap(pBC1, base, 64, half * 512, [[1, 512]]),
                         sap(C.ones1, 96, 1, 0, [[1, 64]]),
                         sap(sums1, 96, 1, half * 512, [[1, 512]]),
                         start=True, stop=True, tile_position=(96, base)),
                     reads=[r(sums1), r(C.ones1)], writes=[r(pBC1)], tag="bc1",
                     chain=G_SB)
            P.op("scalar",
                 lambda e, base=base: e.activation(
                     rinv1[base:base + 64, :], pBC1[base:base + 64, :], AF.Ln),
                 reads=[r(pBC1)], writes=[r(rinv1)], tag="rln1")
            P.op("scalar",
                 lambda e, base=base: e.activation(
                     rinv1[base:base + 64, :], rinv1[base:base + 64, :], AF.Exp,
                     scale=-1.0),
                 reads=[r(rinv1)], writes=[r(rinv1)], tag="rex1")
            for hl in range(8):
                for ci, kt in enumerate(kts):
                    P.op("tensor",
                         lambda e, hl=hl, ci=ci, kt=kt, base=base,
                                first=(ci == 0 and (hl < 2 or not G_PV)),
                                last=(ci == len(kts) - 1
                                      and (hl >= 6 or not G_PV)): e.matmul(
                             sap(pPV1, base, 64, colB(hl), [[1, 128]]),
                             sap(v_n, 0, 128, kt * 64, [[1, 64]]),
                             sap(exL, 0, 128, ci * 1024 + colB(hl), [[1, 128]]),
                             start=first, stop=last, tile_position=(0, base),
                             skip_group_check=True),
                         reads=[r(v_n, kt), r(exL, ci)], writes=[r(pPV1)],
                         tag="pv1", chain=(ci != 0 or (G_PV and hl != 0)))
            vo = gsl * 512 + w * 128
            for par in range(2):
                P.op("vector",
                     lambda e, base=base, vo=vo, par=par: e.tensor_mul(
                         sap(vals1, base, 64, par * 1024 + vo, [[2048, 4], [1, 128]]),
                         sap(pPV1, base, 64, par * 512, [[128, 4], [1, 128]]),
                         sap(rinv1, base, 64, par * 512, [[128, 4], [1, 128]])),
                     reads=[r(pPV1), r(rinv1)], writes=[r(vals1, (gsl, w))],
                     tag="vev1b")

    def wf_group(g):
        gsl = g % 2

        for j in range(8):
            for k in range(8):
                P.op("tensor",
                     lambda e, j=j, k=k, gsl=gsl: e.matmul(
                         pQ1[:, :], wfs2[:, k, j * 128:(j + 1) * 128],
                         vals1[:, k, gsl * 512:(gsl + 1) * 512],
                         start=(k == 0), stop=(k == 7)),
                     reads=[r(wfs2)] + [r(vals1, (gsl, w)) for w in range(4)],
                     writes=[r(pQ1)], tag="wfmm", chain=(k != 0))
            P.op("scalar",
                 lambda e, j=j: e.activation(oT1[:, j, :], pQ1[:, :], AF.Identity,
                                             bias=bf_s[:, j:j + 1]),
                 reads=[r(pQ1), r(bf_s)], writes=[r(oT1, j)], tag="wfev")

    def ln2_add_group(g):
        xbuf = xga[g % 2]
        emit_ln_stats(C, oT1, lambda pt: [r(oT1, pt)], pS1, C.ones_r, m2sb, r2sb,
                      tmpb, tag="aln2")
        for pt in range(8):
            P.op("vector",
                 lambda e, pt=pt: e.tensor_sub(lnt1[:, :], oT1[:, pt, :], m2sb[:, :]),
                 reads=[r(oT1, pt), r(m2sb)], writes=[r(lnt1)], tag="aa1")
            P.op("vector",
                 lambda e, pt=pt: e.tensor_mul(lnt1[:, :], lnt1[:, :], r2sb[:, :]),
                 reads=[r(lnt1), r(r2sb)], writes=[r(lnt1)], tag="aa2")
            P.op("vector",
                 lambda e, pt=pt, xbuf=xbuf: e.tensor_add(
                     xbuf[:, pt, :], xbuf[:, pt, :], lnt1[:, :]),
                 reads=[r(lnt1), r(xbuf)], writes=[r(xbuf)], tag="aa3")
        C.dma("sync", x_dst_r[:, :, g * 512:(g + 1) * 512], xbuf[:, :, :],
              [("dram:" + dst_nm, g)], reads=[r(xbuf)], tag=f"axo{g}")

    for g in range(NG):
        ln1_group(g)
        qkv_group(g)
        for w in range(4):
            attn_tile(4 * g + w)
        wf_group(g)
        ln2_add_group(g)

    P.fence()
    st.close()


def emit_l1f(C, x_src_dram, x_dst_dram, src_nm, dst_nm):
    P, nc, S = C.plan, C.nc, C.S
    NT, NG = S // 128, S // 512
    LN127 = 4.844187086458591
    st = contextlib.ExitStack()

    w1_d = C.dram_in("w1_t", [D, 4096])
    idf_d = C.dram_in("ident_f", [128, 128])
    sel_d = C.dram_in("sel16f", [16, 16 * 128])
    w2_d = C.dram_in("w2_t", [4096, D])
    cvd_d = C.dram_in("conv_diag", [32 * 3 * 128, 128])
    cb_d = C.dram_in("convb_sb", [128, 32])
    h2_h = C.dram_tmp("h2_hbm", [32 * 128, S])

    cb_s = C.sb("cb_s", [128, 32], F32, st)
    idf = C.sb("idf", [128, 128], F32, st)
    sel16 = C.sb("sel16", [128, 16 * 128], F32, st)
    amcmp = C.sb("amcmp", [128, 16], F32, st)
    amrow = C.sb("amrow", [128, 128], F32, st)
    amax2 = C.sb("amax2", [128, S], F32, st)
    runm2 = C.sb("runm2", [128, S], F32, st)
    xgf = [C.sb(f"xgf{i}", [128, 8, 512], F32, st) for i in range(2)]
    m2sb = C.sb("f1m2", [128, 512], F32, st)
    r2sb = C.sb("f1r2", [128, 512], F32, st)
    tmpb = C.sb("f1tmpb", [128, 512], BF16, st)
    lntf = C.sb("f1lnt", [128, 512], F32, st)
    racc = C.sb("racc", [128, 16], F32, st)
    rac1 = C.sb("rac1", [128, 1], F32, st)
    racb = C.sb("racb", [128, 1], BF16, st)
    msw1 = C.sb("msw1", [128, 1], F32, st)
    sw1 = C.sb("sw1", [128, 1], F32, st)
    msw2 = C.sb("msw2", [128, 1], F32, st)
    sw2 = C.sb("sw2", [128, 1], F32, st)

    stX = contextlib.ExitStack()
    xq1 = C.sb("xq1", [128, 8, S], BF16, stX)
    inv1 = C.sb("inv1", [128, S], BF16, stX)
    m_sb = C.sb("f1m", [128, 512], F32, stX)
    r_sb = C.sb("f1r", [128, 512], F32, stX)
    runm1 = C.sb("runm1", [128, 512], F32, stX)
    qs1 = C.sb("qs1", [128, 512], F32, stX)
    xqi = C.sb("xqi", [128, 512], I32, stX)
    sgn1 = C.sb("sgn1", [128, 512], F32, stX)

    pH = C.ps("pH", [128, 512], F32, st)
    pC = C.ps("pC", [128, 512], F32, st)
    pM = C.ps("pM", [128, 512], F32, st)
    pS2 = C.ps("pS2", [128, 512], F32, st)

    C.dma("sync", cb_s[:, :], cb_d, [r(cb_s)])
    C.dma("sync", idf[:, :], idf_d, [r(idf)])
    C.dma("sync", sel16[0:16, :], sel_d, [r(sel16)])

    x_src_r = x_src_dram.rearrange("(a p) c -> p a c", p=128)
    x_dst_r = x_dst_dram.rearrange("(a p) c -> p a c", p=128)
    h2_r = h2_h.rearrange("(a p) c -> p a c", p=128)


    def q1_pmax(src_r, dst_w):
        for blk in range(4):
            P.op("tensor",
                 lambda e, blk=blk: e.matmul(
                     pS2[:, (blk % 4) * 128:(blk % 4) * 128 + 128],
                     runm1[:, blk * 128:(blk + 1) * 128], idf[:, :],
                     start=True, stop=True),
                 reads=src_r + [r(idf)], writes=[r(pS2)], tag="q1t")
            P.op("vector",
                 lambda e, blk=blk: e.reduce_max(
                     amcmp[:, blk:blk + 1],
                     pS2[:, (blk % 4) * 128:(blk % 4) * 128 + 128], AX.X),
                 reads=[r(pS2)], writes=[r(amcmp, blk)], tag="q1r")
        P.op("tensor",
             lambda e: e.matmul(pM[0:4, 0:128], amcmp[:, 0:4],
                                idf[:, :], start=True, stop=True),
             reads=[r(amcmp, bb) for bb in range(4)] + [r(idf)],
             writes=[r(pM)], tag="q1tb")
        P.op("vector", lambda e: e.tensor_copy(amrow[0:4, :],
                                               pM[0:4, 0:128]),
             reads=[r(pM)], writes=[r(amrow)], tag="q1ev")
        for blk in range(4):
            P.op("tensor",
                 lambda e, blk=blk: e.matmul(
                     pS2[:, (blk % 4) * 128:(blk % 4) * 128 + 128],
                     sel16[0:4, blk * 128:(blk + 1) * 128],
                     amrow[0:4, :], start=True, stop=True),
                 reads=[r(sel16), r(amrow)], writes=[r(pS2)], tag="q1b")
            P.op("scalar",
                 lambda e, blk=blk: e.activation(
                     runm1[:, blk * 128:(blk + 1) * 128],
                     pS2[:, (blk % 4) * 128:(blk % 4) * 128 + 128], AF.Copy),
                 reads=[r(pS2)], writes=dst_w, tag="q1be")

    def quant1_group(g):
        gb = g % 2
        c0 = g * 512
        C.dma("sync", xgf[gb][:, :, :], x_src_r[:, :, c0:c0 + 512],
              [r(xgf[gb])], reads=[("dram:" + src_nm, g)], tag=f"qxg{g}")
        emit_ln_stats(C, xgf[gb], lambda pt: [r(xgf[gb])], pS2, C.ones_r, m_sb,
                      r_sb, tmpb, tag="q1ln")
        P.op("vector", lambda e: e.memset(runm1[:, :], 0.0),
             writes=[r(runm1)], tag="q1a0")
        for pt in range(8):
            P.op("vector",
                 lambda e, pt=pt, gb=gb: e.tensor_sub(
                     lntf[:, :], xgf[gb][:, pt, :], m_sb[:, :]),
                 reads=[r(xgf[gb]), r(m_sb)], writes=[r(lntf)], tag="q1s")
            P.op("scalar",
                 lambda e: e.activation(sgn1[:, :], lntf[:, :], AF.Abs),
                 reads=[r(lntf)], writes=[r(sgn1)], tag="q1ab")
            P.op("vector",
                 lambda e: e.tensor_max(runm1[:, :], runm1[:, :], sgn1[:, :]),
                 reads=[r(runm1), r(sgn1)], writes=[r(runm1)], tag="q1a")
        q1_pmax([r(runm1)], [r(runm1)])
        P.op("vector", lambda e: e.tensor_mul(qs1[:, :], runm1[:, :], r_sb[:, :]),
             reads=[r(runm1), r(r_sb)], writes=[r(qs1)], tag="q1t")
        P.op("vector",
             lambda e: e.tensor_scalar(qs1[:, :], qs1[:, :], 1e-5, None, ALU.max),
             reads=[r(qs1)], writes=[r(qs1)], tag="q1c")
        P.op("vector",
             lambda e, c0=c0: e.tensor_scalar(inv1[:, c0:c0 + 512], qs1[:, :],
                                              1.0 / 127.0, None, ALU.mult),
             reads=[r(qs1)], writes=[r(inv1, g)], tag="q1i")
        P.op("scalar", lambda e: e.activation(qs1[:, :], qs1[:, :], AF.Ln),
             reads=[r(qs1)], writes=[r(qs1)], tag="q1l")
        P.op("scalar", lambda e: e.activation(qs1[:, :], qs1[:, :], AF.Exp,
                                              scale=-1.0, bias=C.cv(LN127)),
             reads=[r(qs1), r(C.cvals)], writes=[r(qs1)], tag="q1e")
        P.op("vector", lambda e: e.tensor_mul(qs1[:, :], qs1[:, :], r_sb[:, :]),
             reads=[r(qs1), r(r_sb)], writes=[r(qs1)], tag="q1rs")
        for pt in range(8):
            P.op("vector",
                 lambda e, pt=pt, gb=gb: e.tensor_sub(
                     lntf[:, :], xgf[gb][:, pt, :], m_sb[:, :]),
                 reads=[r(xgf[gb]), r(m_sb)], writes=[r(lntf)], tag="q1s2")
            P.op("vector",
                 lambda e: e.tensor_mul(lntf[:, :], lntf[:, :], qs1[:, :]),
                 reads=[r(lntf), r(qs1)], writes=[r(lntf)], tag="q1m")
            P.op("scalar",
                 lambda e: e.sign(sgn1[:, :], lntf[:, :]),
                 reads=[r(lntf)], writes=[r(sgn1)], tag="q1sg")
            P.op("vector",
                 lambda e: e.scalar_tensor_tensor(
                     lntf[:, :], sgn1[:, :], C.halfsh[:, 0:1], lntf[:, :],
                     ALU.mult, ALU.add),
                 reads=[r(sgn1), r(C.halfsh), r(lntf)], writes=[r(lntf)],
                 tag="q1sh")
            P.op("vector",
                 lambda e: e.tensor_copy(xqi[:, :], lntf[:, :]),
                 reads=[r(lntf)], writes=[r(xqi)], tag="q1r")
            P.op("vector",
                 lambda e, pt=pt, c0=c0: e.tensor_copy(xq1[:, pt, c0:c0 + 512],
                                                       xqi[:, :]),
                 reads=[r(xqi)], writes=[r(xq1, (g, pt))], tag="q1b")

    for g in range(NG):
        quant1_group(g)

    stW = contextlib.ExitStack()
    cds2 = [C.sb(f"cds{i}", [128, 12, 128], BF16, stW) for i in range(2)]
    cvd_r = cvd_d.rearrange("(a p) c -> p a c", p=128)
    w1S = [C.sb(f"w1S{i}", [128, 8, 512], F32, stW) for i in range(2)]
    w1q = [C.sb(f"w1q{i}", [128, 8, 512], BF16, stW) for i in range(2)]
    w1i = C.sb("w1i", [128, 8, 512], I32, stW)
    h_s = [C.sb(f"h_s{i}", [128, S + 2], BF16, stW) for i in range(2)]
    h2m = [C.sb(f"h2m{i}", [128, S], F32, stW) for i in range(2)]
    habs = lntf

    def w_mean_pass(w_r, wS, name):
        for j in range(8):
            sl = j % 2
            C.dma("sync", wS[sl][:, :, :], w_r[:, :, j * 512:(j + 1) * 512],
                  [r(wS[sl])], tag=f"{name}p1_{j}")
            P.op("vector",
                 lambda e, j=j, sl=sl: e.reduce_sum(
                     racc[:, j:j + 1], wS[sl][:, :, :], AX.XY,
                     apply_absolute_value=True),
                 reads=[r(wS[sl])], writes=[r(racc, j)], tag=f"{name}r")
        P.op("vector",
             lambda e: e.reduce_sum(rac1[:, :], racc[:, 0:8], AX.X),
             reads=[r(racc, jj) for jj in range(8)],
             writes=[r(rac1)], tag=f"{name}r2")
        P.op("scalar", lambda e: e.activation(racb[:, :], rac1[:, :], AF.Copy),
             reads=[r(rac1)], writes=[r(racb)], tag=f"{name}cb")
        P.op("tensor",
             lambda e: e.matmul(pM[:, 0:1], C.ones1[:, :], racb[:, :],
                                start=True, stop=True),
             reads=[r(racb), r(C.ones1)], writes=[r(pM)], tag=f"{name}sm")

    def w_scale(msw, sw, name):
        P.op("vector",
             lambda e: e.tensor_scalar(msw[:, :], pM[:, 0:1],
                                       2.384185791015625e-07, 1e-5,
                                       ALU.mult, ALU.max),
             reads=[r(pM)], writes=[r(msw)], tag=f"{name}ms")
        P.op("scalar", lambda e: e.activation(sw[:, :], msw[:, :], AF.Ln),
             reads=[r(msw)], writes=[r(sw)], tag=f"{name}l")
        P.op("scalar", lambda e: e.activation(sw[:, :], sw[:, :], AF.Exp,
                                              scale=-1.0),
             reads=[r(sw)], writes=[r(sw)], tag=f"{name}e")

    w1_r = w1_d.rearrange("(a p) c -> p a c", p=128)
    w_mean_pass(w1_r, w1S, "w1")
    w_scale(msw1, sw1, "w1")

    P.op("vector", lambda e: e.memset(runm2[:, :], 0.0), writes=[r(runm2)],
         tag="rm20")
    for j8 in range(8):
        sl = j8 % 2
        C.dma("sync", w1S[sl][:, :, :], w1_r[:, :, j8 * 512:(j8 + 1) * 512],
              [r(w1S[sl])], tag=f"w1p2_{j8}")
        C.dma("gpsimd", cds2[sl][:, :, :],
              cvd_r[:, j8 * 12:(j8 + 1) * 12, :], [r(cds2[sl])],
              tag=f"cds{j8}")
        P.op("vector",
             lambda e, sl=sl: e.tensor_scalar(w1S[sl][:, :, :], w1S[sl][:, :, :],
                                              sw1[:, 0:1], None, ALU.mult),
             reads=[r(w1S[sl]), r(sw1)], writes=[r(w1S[sl])], tag="w1qm")
        P.op("scalar", lambda e, sl=sl: e.sign(w1i[:, :, :], w1S[sl][:, :, :]),
             reads=[r(w1S[sl])], writes=[r(w1i)], tag="w1sg")
        P.op("vector",
             lambda e, sl=sl: e.scalar_tensor_tensor(
                 w1S[sl][:, :, :], w1i[:, :, :], C.halfsh[:, 0:1],
                 w1S[sl][:, :, :], ALU.mult, ALU.add),
             reads=[r(w1i), r(C.halfsh), r(w1S[sl])], writes=[r(w1S[sl])],
             tag="w1sh")
        P.op("vector", lambda e, sl=sl: e.tensor_copy(w1i[:, :, :],
                                                      w1S[sl][:, :, :]),
             reads=[r(w1S[sl])], writes=[r(w1i)], tag="w1ri")
        P.op("vector",
             lambda e, sl=sl: e.tensor_scalar(w1q[sl][:, :, :], w1i[:, :, :],
                                              1.0, -1.0, ALU.min, ALU.max),
             reads=[r(w1i)], writes=[r(w1q[sl])], tag="w1qc")
        for m4 in range(4):
            m = j8 * 4 + m4
            msl = m % 2
            for ng in range(NG):
                for k in range(8):
                    P.op("tensor",
                         lambda e, m4=m4, k=k, ng=ng, sl=sl: e.matmul(
                             pH[:, :], w1q[sl][:, k, m4 * 128:(m4 + 1) * 128],
                             xq1[:, k, ng * 512:(ng + 1) * 512],
                             start=(k == 0), stop=(k == 7)),
                         reads=[r(w1q[sl]), r(xq1, (ng, k))], writes=[r(pH)],
                         tag="h1mm", chain=(k != 0))
                P.op("vector",
                     lambda e, ng=ng, msl=msl: e.scalar_tensor_tensor(
                         h_s[msl][:, 1 + ng * 512:1 + (ng + 1) * 512], pH[:, :],
                         msw1[:, 0:1], inv1[:, ng * 512:(ng + 1) * 512],
                         ALU.mult, ALU.mult),
                     reads=[r(pH), r(msw1), r(inv1, ng)], writes=[r(h_s[msl])],
                     tag="h1ev")
            P.op("vector", lambda e, msl=msl: e.memset(h_s[msl][:, 0:1], 0.0),
                 writes=[r(h_s[msl])], tag="hz0")
            P.op("vector",
                 lambda e, msl=msl: e.memset(h_s[msl][:, S + 1:S + 2], 0.0),
                 writes=[r(h_s[msl])], tag="hz1")
            for ng in range(NG):
                for tap in range(3):
                    P.op("tensor",
                         lambda e, m4=m4, tap=tap, ng=ng, msl=msl, sl=sl: e.matmul(
                             pC[:, :],
                             cds2[sl][:, m4 * 3 + tap, :],
                             h_s[msl][:, ng * 512 + tap:ng * 512 + tap + 512],
                             start=(tap == 0), stop=(tap == 2)),
                         reads=[r(cds2[sl]), r(h_s[msl])], writes=[r(pC)],
                         tag="cvmm", chain=(tap != 0))
                P.op("scalar",
                     lambda e, m=m, ng=ng, msl=msl: e.activation(
                         h2m[msl][:, ng * 512:(ng + 1) * 512], pC[:, :], AF.Silu,
                         bias=cb_s[:, m:m + 1]),
                     reads=[r(pC), r(cb_s)], writes=[r(h2m[msl], ng)], tag="cvev")
            for ng in range(NG):
                P.op("scalar",
                     lambda e, msl=msl, ng=ng: e.activation(
                         habs[:, :], h2m[msl][:, ng * 512:(ng + 1) * 512], AF.Abs),
                     reads=[r(h2m[msl], ng)], writes=[r(habs)], tag="rm2a")
                P.op("vector",
                     lambda e, ng=ng: e.tensor_max(
                         runm2[:, ng * 512:(ng + 1) * 512],
                         runm2[:, ng * 512:(ng + 1) * 512], habs[:, :]),
                     reads=[r(runm2), r(habs)], writes=[r(runm2)], tag="rm2")
            C.dma("sync", h2_r[:, m, :], h2m[msl][:, :], [("dram:h2", m)],
                  reads=[r(h2m[msl], ng) for ng in range(NG)], tag=f"h2o{m}")


    def q2_pmax(src_r, dst_w):
        NBLK2 = S // 128
        for blk in range(NBLK2):
            P.op("tensor",
                 lambda e, blk=blk: e.matmul(
                     pH[:, (blk % 4) * 128:(blk % 4) * 128 + 128],
                     runm2[:, blk * 128:(blk + 1) * 128], idf[:, :],
                     start=True, stop=True),
                 reads=src_r + [r(idf)], writes=[r(pH)], tag="q2t")
            P.op("vector",
                 lambda e, blk=blk: e.reduce_max(
                     amcmp[:, blk:blk + 1],
                     pH[:, (blk % 4) * 128:(blk % 4) * 128 + 128], AX.X),
                 reads=[r(pH)], writes=[r(amcmp, blk)], tag="q2r")
        P.op("tensor",
             lambda e: e.matmul(pM[0:NBLK2, 0:128], amcmp[:, 0:NBLK2],
                                idf[:, :], start=True, stop=True),
             reads=[r(amcmp, bb) for bb in range(NBLK2)] + [r(idf)],
             writes=[r(pM)], tag="q2tb")
        P.op("vector", lambda e: e.tensor_copy(amrow[0:NBLK2, :],
                                               pM[0:NBLK2, 0:128]),
             reads=[r(pM)], writes=[r(amrow)], tag="q2ev")
        for blk in range(NBLK2):
            P.op("tensor",
                 lambda e, blk=blk: e.matmul(
                     pH[:, (blk % 4) * 128:(blk % 4) * 128 + 128],
                     sel16[0:NBLK2, blk * 128:(blk + 1) * 128],
                     amrow[0:NBLK2, :], start=True, stop=True),
                 reads=[r(sel16), r(amrow)], writes=[r(pH)], tag="q2b")
            P.op("scalar",
                 lambda e, blk=blk: e.activation(
                     amax2[:, blk * 128:(blk + 1) * 128],
                     pH[:, (blk % 4) * 128:(blk % 4) * 128 + 128], AF.Copy),
                 reads=[r(pH)], writes=dst_w, tag="q2be")

    q2_pmax([r(runm2)], [r(amax2)])
    qsc2 = runm2
    P.op("vector",
         lambda e: e.tensor_scalar(amax2[:, :], amax2[:, :], 1e-5, None, ALU.max),
         reads=[r(amax2)], writes=[r(amax2)], tag="q2c")
    P.op("vector",
         lambda e: e.tensor_scalar(qsc2[:, :], amax2[:, :], 1.0 / 127.0, None,
                                   ALU.mult),
         reads=[r(amax2)], writes=[r(qsc2, "inv"), r(runm2)], tag="q2i")
    P.op("scalar", lambda e: e.activation(amax2[:, :], amax2[:, :], AF.Ln),
         reads=[r(amax2)], writes=[r(amax2)], tag="q2l")
    P.op("scalar", lambda e: e.activation(amax2[:, :], amax2[:, :], AF.Exp,
                                          scale=-1.0, bias=C.cv(LN127)),
         reads=[r(amax2), r(C.cvals)], writes=[r(amax2)], tag="q2e")

    P.fence()
    stW.close()
    stX.close()

    stW2 = contextlib.ExitStack()
    w2q = C.sb("w2q", [128, 32, D], BF16, stW2)
    stW2a = contextlib.ExitStack()
    w2S = [C.sb(f"w2S{i}", [128, 2, D], F32, stW2a) for i in range(2)]
    w2i = C.sb("w2i", [128, 2, D], I32, stW2a)
    w2_r = w2_d.rearrange("(a p) c -> p a c", p=128)
    for j in range(16):
        sl = j % 2
        C.dma("sync", w2S[sl][:, :, :], w2_r[:, 2 * j:2 * j + 2, :],
              [r(w2S[sl])], tag=f"w2p1_{j}")
        P.op("vector",
             lambda e, j=j, sl=sl: e.reduce_sum(
                 racc[:, j:j + 1], w2S[sl][:, :, :], AX.XY,
                 apply_absolute_value=True),
             reads=[r(w2S[sl])], writes=[r(racc, j)], tag="w2r")
    P.op("vector",
         lambda e: e.reduce_sum(rac1[:, :], racc[:, :], AX.X),
         reads=[r(racc, jj) for jj in range(16)], writes=[r(rac1)], tag="w2r2")
    P.op("scalar", lambda e: e.activation(racb[:, :], rac1[:, :], AF.Copy),
         reads=[r(rac1)], writes=[r(racb)], tag="w2cb")
    P.op("tensor",
         lambda e: e.matmul(pM[:, 0:1], C.ones1[:, :], racb[:, :],
                            start=True, stop=True),
         reads=[r(racb), r(C.ones1)], writes=[r(pM)], tag="w2sm")
    w_scale(msw2, sw2, "w2")
    for j in range(16):
        sl = j % 2
        C.dma("sync", w2S[sl][:, :, :], w2_r[:, 2 * j:2 * j + 2, :],
              [r(w2S[sl])], tag=f"w2p2_{j}")
        P.op("vector",
             lambda e, sl=sl: e.tensor_scalar(w2S[sl][:, :, :], w2S[sl][:, :, :],
                                              sw2[:, 0:1], None, ALU.mult),
             reads=[r(w2S[sl]), r(sw2)], writes=[r(w2S[sl])], tag="w2qm")
        P.op("scalar", lambda e, sl=sl: e.sign(w2i[:, :, :], w2S[sl][:, :, :]),
             reads=[r(w2S[sl])], writes=[r(w2i)], tag="w2sg")
        P.op("vector",
             lambda e, sl=sl: e.scalar_tensor_tensor(
                 w2S[sl][:, :, :], w2i[:, :, :], C.halfsh[:, 0:1],
                 w2S[sl][:, :, :], ALU.mult, ALU.add),
             reads=[r(w2i), r(C.halfsh), r(w2S[sl])], writes=[r(w2S[sl])],
             tag="w2sh")
        P.op("vector", lambda e, sl=sl: e.tensor_copy(w2i[:, :, :],
                                                      w2S[sl][:, :, :]),
             reads=[r(w2S[sl])], writes=[r(w2i)], tag="w2ri")
        P.op("vector",
             lambda e, j=j: e.tensor_scalar(w2q[:, 2 * j:2 * j + 2, :],
                                            w2i[:, :, :], 1.0, -1.0,
                                            ALU.min, ALU.max),
             reads=[r(w2i)], writes=[r(w2q, j // 2)], tag="w2qc")

    P.fence()
    stW2a.close()

    xq2 = C.sb("xq2", [128, 32, 512], BF16, stW2)
    h2r = [C.sb(f"h2r{i}", [128, 4, 512], F32, stW2) for i in range(2)]
    h2i = C.sb("h2i", [128, 4, 512], I32, stW2)
    oT2 = C.sb("f1oT", [128, 8, 512], BF16, stW2)
    for g in range(NG):
        c0 = g * 512
        gb = g % 2
        for m8 in range(8):
            sl = m8 % 2
            C.dma("sync", h2r[sl][:, :, :],
                  h2_r[:, 4 * m8:4 * m8 + 4, c0:c0 + 512], [r(h2r[sl])],
                  reads=[("dram:h2", 4 * m8 + i_) for i_ in range(4)],
                  tag=f"h2i{g}_{m8}")
            P.op("vector",
                 lambda e, sl=sl, c0=c0: e.tensor_tensor(
                     h2r[sl][:, :, :], h2r[sl][:, :, :],
                     sap(amax2, 0, 128, c0, [[0, 4], [1, 512]]), ALU.mult),
                 reads=[r(h2r[sl]), r(amax2)], writes=[r(h2r[sl])], tag="q2m")
            P.op("scalar", lambda e, sl=sl: e.sign(h2i[:, :, :], h2r[sl][:, :, :]),
                 reads=[r(h2r[sl])], writes=[r(h2i)], tag="q2sg")
            P.op("vector",
                 lambda e, sl=sl: e.scalar_tensor_tensor(
                     h2r[sl][:, :, :], h2i[:, :, :], C.halfsh[:, 0:1],
                     h2r[sl][:, :, :], ALU.mult, ALU.add),
                 reads=[r(h2i), r(C.halfsh), r(h2r[sl])], writes=[r(h2r[sl])],
                 tag="q2sh")
            P.op("vector", lambda e, sl=sl: e.tensor_copy(h2i[:, :, :],
                                                          h2r[sl][:, :, :]),
                 reads=[r(h2r[sl])], writes=[r(h2i)], tag="q2ri")
            P.op("vector",
                 lambda e, m8=m8: e.tensor_copy(xq2[:, 4 * m8:4 * m8 + 4, :],
                                                h2i[:, :, :]),
                 reads=[r(h2i)], writes=[r(xq2, m8)], tag="q2b")
        for j in range(8):
            for k in range(32):
                P.op("tensor",
                     lambda e, j=j, k=k: e.matmul(
                         pH[:, :], w2q[:, k, j * 128:(j + 1) * 128],
                         xq2[:, k, :], start=(k == 0), stop=(k == 31)),
                     reads=[r(w2q, k // 4), r(xq2, k // 4)], writes=[r(pH)],
                     tag="o2mm", chain=(k != 0))
            P.op("vector",
                 lambda e, j=j, c0=c0: e.scalar_tensor_tensor(
                     oT2[:, j, :], pH[:, :], msw2[:, 0:1],
                     qsc2[:, c0:c0 + 512], ALU.mult, ALU.mult),
                 reads=[r(pH), r(msw2), r(qsc2, "inv")], writes=[r(oT2, j)],
                 tag="o2ev")
        emit_ln_stats(C, oT2, lambda pt: [r(oT2, pt)], pS2, C.ones_r, m2sb, r2sb,
                      tmpb, tag="f1ln2")
        C.dma("sync", xgf[gb][:, :, :], x_src_r[:, :, c0:c0 + 512],
              [r(xgf[gb])], reads=[("dram:" + src_nm, g)], tag=f"fxr2{g}")
        for pt in range(8):
            P.op("vector",
                 lambda e, pt=pt: e.tensor_sub(lntf[:, :], oT2[:, pt, :],
                                               m2sb[:, :]),
                 reads=[r(oT2, pt), r(m2sb)], writes=[r(lntf)], tag="fb1")
            P.op("vector",
                 lambda e, pt=pt: e.tensor_mul(lntf[:, :], lntf[:, :], r2sb[:, :]),
                 reads=[r(lntf), r(r2sb)], writes=[r(lntf)], tag="fb2")
            P.op("vector",
                 lambda e, pt=pt, gb=gb: e.tensor_add(
                     xgf[gb][:, pt, :], xgf[gb][:, pt, :], lntf[:, :]),
                 reads=[r(lntf), r(xgf[gb])], writes=[r(xgf[gb])], tag="fb3")
        C.dma("sync", x_dst_r[:, :, c0:c0 + 512], xgf[gb][:, :, :],
              [("dram:" + dst_nm, g)], reads=[r(xgf[gb])], tag=f"fxo2{g}")

    P.fence()
    stW2.close()
    st.close()


_CACHE = {}
PHASES = ("l0a", "l0f", "l1a", "l1f")


def _build(S, phases):
    C = Ctx(S)
    nc = C.nc
    n = len(phases)
    x_in = C.dram_in("x_in", [D, S])
    x_out = C.dram_out("x_out", [D, S])
    xres = C.dram_tmp("xres", [D, S]) if n > 1 else None
    onr_d = C.dram_in("ones_r", [128, 128], BF16)
    on1_d = C.dram_in("ones_1", [128, 128], BF16)
    cv_d = C.dram_in("consts", [128, len(CONSTS)])
    C.ones_r = C.sb("ones_r_g", [128, 128], BF16)
    C.ones1 = C.sb("ones_1_g", [128, 128], BF16)
    C.cvals = C.sb("cvals_g", [128, len(CONSTS)], F32)
    C.dma("sync", C.ones_r[:, :], onr_d, [r(C.ones_r)])
    C.dma("sync", C.ones1[:, :], on1_d, [r(C.ones1)])
    C.dma("sync", C.cvals[:, :], cv_d, [r(C.cvals)])
    C.halfsh = C.sb("halfsh_g", [128, 1], F32)
    _hstmp = C.sb("hstmp_g", [128, 1], I32)
    C.plan.op("vector",
              lambda e: e.tensor_copy(_hstmp[:, :], C.cvals[:, 6:7]),
              reads=[r(C.cvals)], writes=[r(_hstmp)], tag="shdet1")
    C.plan.op("vector",
              lambda e: e.tensor_scalar(C.halfsh[:, :], _hstmp[:, :],
                                        -0.5, 0.5, ALU.mult, ALU.add),
              reads=[r(_hstmp)], writes=[r(C.halfsh)], tag="shdet2")
    sems, dma_pools = {}, {}
    sem_stack = contextlib.ExitStack()
    KPOOL = {"sync": 12, "gpsimd": 12, "scalar": 8, "vector": 2, "tensor": 2}
    for e in ENGINES:
        sems[e] = sem_stack.enter_context(nc.semaphore("s_" + e))
        dma_pools[e] = [sem_stack.enter_context(nc.semaphore(f"d_{e}{i}"))
                        for i in range(KPOOL[e])]
    for i, ph in enumerate(phases):
        src = x_in if i == 0 else xres
        dst = x_out if i == n - 1 else xres
        src_nm = "x_in" if i == 0 else "xres"
        dst_nm = "x_out" if i == n - 1 else "xres"
        if ph == "l0a":
            emit_l0a(C, src, dst, src_nm, dst_nm)
        elif ph == "l0f":
            emit_l0f(C, src, dst, src_nm, dst_nm)
        elif ph == "l1a":
            emit_l1a(C, src, dst, src_nm, dst_nm)
        elif ph == "l1f":
            emit_l1f(C, src, dst, src_nm, dst_nm)
        else:
            raise ValueError(ph)
    with nc.Block() as block:
        C.plan.resolve_and_emit(block, sems, dma_pools)

        @block.sync
        def _(sy):
            for e in ENGINES:
                nd = sum(1 for o in C.plan.ops[e] if o.dma)
                K = len(dma_pools[e])
                for s in range(min(nd, K)):
                    uses = (nd - s + K - 1) // K
                    sy.wait_ge(dma_pools[e][s], 16 * uses)
    sem_stack.close()
    C.stack.close()
    return C


def _host_inputs(ins, S, phases):
    import ml_dtypes
    bf = ml_dtypes.bfloat16
    x = np.asarray(ins["L0_attn_x"])
    B = x.shape[0]
    common = {}
    common["ones_r"] = np.full((128, 128), 1.0 / 1024.0, dtype=bf)
    common["ones_1"] = np.ones((128, 128), dtype=bf)
    common["consts"] = np.ascontiguousarray(
        np.broadcast_to(np.asarray(CONSTS, np.float32), (128, len(CONSTS))))
    if "l0a" in phases:
        wq_f = np.asarray(ins["L0_attn_Wqkv"], dtype=np.float32)
        mm_ = np.arange(1024)
        qcols = (2 * (mm_ // 128) + (mm_ % 128) // 64) * 192 + (mm_ % 64)
        vcols = (mm_ // 64) * 192 + 128 + (mm_ % 64)
        cols = np.concatenate([qcols, qcols + 64, vcols])
        common["wqkv"] = np.ascontiguousarray(wq_f[:, cols])
        wo_f = np.asarray(ins["L0_attn_Wo"], dtype=np.float32)
        kk = np.arange(8)[:, None]
        pp = np.arange(128)[None, :]
        perm = (512 * (pp // 64) + 64 * kk + (pp % 64)).reshape(-1)
        common["wo"] = np.ascontiguousarray(wo_f[perm])
        bqkv = np.asarray(ins["L0_attn_bqkv"], dtype=np.float32)
        idx = np.zeros((128, 8), np.int64)
        for i in range(8):
            for p in range(128):
                idx[p, i] = (2 * i + p // 64) * 192 + (p % 64)
        common["bq_sb"] = np.ascontiguousarray(bqkv[idx])
        common["bk_sb"] = np.ascontiguousarray(bqkv[idx + 64])
        bv = bqkv.reshape(16, 192)[:, 128:]
        common["bvn_bc"] = np.ascontiguousarray(
            np.broadcast_to(bv.reshape(1, 16 * 64), (128, 16 * 64)).astype(np.float32))
        common["bo_sb"] = np.ascontiguousarray(
            np.asarray(ins["L0_attn_bo"], dtype=np.float32).reshape(8, 128).T)
        rr = np.arange(128)[:, None]
        cc = np.arange(128)[None, :]
        common["maskB"] = np.where(np.abs(rr - cc) <= 16, 0.0, NEG).astype(np.float32)
        a = np.arange(32)[:, None]
        c2 = np.arange(128)[None, :]
        common["maskA"] = np.where((c2 >= a - 48) & (c2 <= a - 16), 0.0,
                                   NEG).astype(np.float32)
        common["maskC"] = np.where(c2 >= a + 112, 0.0, NEG).astype(np.float32)
    if "l1a" in phases:
        wq_f = np.asarray(ins["L1_attn_Wq"], np.float32)
        common["wq_t"] = np.ascontiguousarray(wq_f.T)
        common["wk_t"] = np.ascontiguousarray(
            np.asarray(ins["L1_attn_Wk"], np.float32).T)
        common["wv_t"] = np.ascontiguousarray(
            np.asarray(ins["L1_attn_Wv"], np.float32).T)
        wf_f = np.asarray(ins["L1_attn_Wf"], np.float32).T
        kk = np.arange(8)[:, None]
        pp = np.arange(128)[None, :]
        perm = (512 * (pp // 64) + 64 * kk + (pp % 64)).reshape(-1)
        common["wf_t"] = np.ascontiguousarray(wf_f[perm])
        common["bf_sb"] = np.ascontiguousarray(
            np.asarray(ins["L1_attn_bf"], np.float32).reshape(8, 128).T)
        rr = np.arange(128)[:, None]
        cc = np.arange(128)[None, :]
        common["maskN"] = np.where(cc >= rr, 0.0, NEG).astype(np.float32)
        common["maskO"] = np.where(cc <= rr, 0.0, NEG).astype(np.float32)
    if "l1f" in phases:
        common["ident_f"] = np.eye(128, dtype=np.float32)
        s16 = np.zeros((16, 16 * 128), np.float32)
        for b_ in range(16):
            s16[b_, b_ * 128:(b_ + 1) * 128] = 1.0
        common["sel16f"] = s16
        common["w1_t"] = np.ascontiguousarray(
            np.asarray(ins["L1_ffn_w1"], np.float32).T)
        common["w2_t"] = np.ascontiguousarray(
            np.asarray(ins["L1_ffn_w2"], np.float32).T)
        cw_ = np.asarray(ins["L1_ffn_conv_w"], np.float32)[:, 0, :]
        cd = np.zeros((32, 3, 128, 128), np.float32)
        for m_ in range(32):
            for tp_ in range(3):
                np.fill_diagonal(cd[m_, tp_], cw_[m_ * 128:(m_ + 1) * 128, tp_])
        common["conv_diag"] = np.ascontiguousarray(cd.reshape(32 * 3 * 128, 128))
        common["convb_sb"] = np.ascontiguousarray(
            np.asarray(ins["L1_ffn_conv_b"], np.float32).reshape(32, 128).T)
    if "l0f" in phases:
        common["gate_wT"] = np.ascontiguousarray(
            np.asarray(ins["L0_ffn_gate_w"], np.float32).T)
        common["gbias_bc"] = np.ascontiguousarray(np.broadcast_to(
            np.asarray(ins["L0_ffn_gate_bias"], np.float32)[None, :], (128, 16)))
        es = np.zeros((16, 16 * 128), np.float32)
        for e_ in range(16):
            es[e_, e_ * 128:(e_ + 1) * 128] = 1.0
        common["esel"] = es.astype(bf)
        common["ident_bf"] = np.eye(128, dtype=np.float32).astype(bf)
        common["wsg"] = np.ascontiguousarray(
            np.asarray(ins["L0_ffn_Ws_g"], np.float32).T)
        common["wsu"] = np.ascontiguousarray(
            np.asarray(ins["L0_ffn_Ws_u"], np.float32).T)
        common["wsd"] = np.ascontiguousarray(
            np.asarray(ins["L0_ffn_Ws_d"], np.float32).T)
        common["wg_t"] = np.ascontiguousarray(
            np.asarray(ins["L0_ffn_Wg"], np.float32).transpose(0, 2, 1))
        common["wu_t"] = np.ascontiguousarray(
            np.asarray(ins["L0_ffn_Wu"], np.float32).transpose(0, 2, 1))
        common["wd_t"] = np.ascontiguousarray(
            np.asarray(ins["L0_ffn_Wd"], np.float32).transpose(0, 2, 1))
    in_maps = []
    for b in range(B):
        m = dict(common)
        m["x_in"] = np.ascontiguousarray(x[b].T.astype(np.float32))
        in_maps.append(m)
    return in_maps


def kernel(**ins):
    x = np.asarray(ins["L0_attn_x"])
    S = x.shape[1]
    key = (S, PHASES)
    if key not in _CACHE:
        _CACHE[key] = _build(S, PHASES)
    C = _CACHE[key]
    in_maps = _host_inputs(ins, S, PHASES)
    res = run_bass_kernel_spmd(C.nc, in_maps, core_ids=list(range(len(in_maps))))
    out = np.stack([res.results[b]["x_out"].T for b in range(len(in_maps))])
    return np.ascontiguousarray(out.astype(np.float32))


# version 16
# speedup vs baseline: 1.5887x; 1.4184x over previous
import contextlib

import numpy as np

import concourse.bass as bass
import concourse.mybir as mybir
from concourse.bass_utils import run_bass_kernel_spmd

F32 = mybir.dt.float32
BF16 = mybir.dt.bfloat16
I32 = mybir.dt.int32
AF = mybir.ActivationFunctionType
ALU = mybir.AluOpType
AX = mybir.AxisListType

NEG = -1.0e30
D = 1024

G_SC = True
G_AC = True
G_PV = True
G_SB = True


def colB(hl):
    return (hl % 2) * 512 + (hl // 2) * 128

ENGINES = ("tensor", "vector", "scalar", "gpsimd", "sync")


class _Op:
    __slots__ = ("eng", "fn", "reads", "writes", "dma", "idx", "need_inc",
                 "deps", "dma_idx", "inc_no", "tag", "chain")

    def __init__(self, eng, fn, reads, writes, dma, idx, tag):
        self.eng, self.fn = eng, fn
        self.reads, self.writes = reads, writes
        self.dma, self.idx, self.tag = dma, idx, tag
        self.need_inc = False
        self.deps = set()
        self.dma_idx = None
        self.inc_no = None


class Plan:
    def __init__(self):
        self.ops = {e: [] for e in ENGINES}
        self.regions = {}
        self.fence_snaps = []
        self.buf_fence = {}
        self.cur_fence = -1

    def _reg(self, key):
        rg = self.regions.get(key)
        if rg is None:
            rg = [None, []]
            self.regions[key] = rg
        return rg

    def register_buf(self, name):
        self.buf_fence[name] = self.cur_fence

    def op(self, eng, fn, reads=(), writes=(), dma=False, tag="", chain=False):
        o = _Op(eng, fn, tuple(reads), tuple(writes), dma, len(self.ops[eng]), tag)
        o.chain = chain
        for rk in o.reads:
            rg = self._reg(rk)
            if rg[0] is not None:
                o.deps.add((rg[0].eng, rg[0].idx, rg[0].dma))
        for wk in o.writes:
            rg = self._reg(wk)
            if rg[0] is not None:
                o.deps.add((rg[0].eng, rg[0].idx, rg[0].dma))
            for rd in rg[1]:
                o.deps.add((rd.eng, rd.idx, rd.dma))
            f = self.buf_fence.get(wk[0], -1)
            if f >= 0:
                snap = self.fence_snaps[f]
                for e2 in ENGINES:
                    n = snap[e2]
                    if n > 0:
                        prev = self.ops[e2][n - 1]
                        o.deps.add((e2, n - 1, prev.dma))
                        last_dmas = []
                        for i2 in range(n - 1, -1, -1):
                            op2 = self.ops[e2][i2]
                            if op2.dma:
                                last_dmas.append(i2)
                                if len(last_dmas) >= 12:
                                    break
                        for i2 in last_dmas:
                            o.deps.add((e2, i2, True))
        for rk in o.reads:
            self._reg(rk)[1].append(o)
        for wk in o.writes:
            rg = self._reg(wk)
            rg[0] = o
            rg[1] = []
        o.deps = {d for d in o.deps if not (d[0] == eng and d[1] == o.idx)}
        self.ops[eng].append(o)
        return o

    def fence(self):
        self.fence_snaps.append({e: len(self.ops[e]) for e in ENGINES})
        self.cur_fence = len(self.fence_snaps) - 1

    def resolve_and_emit(self, block, sems, dma_pools):
        for e in ENGINES:
            for o in self.ops[e]:
                keep, xmax = set(), {}
                for d in o.deps:
                    (e2, i2, isdma) = d
                    if isdma or e2 == e:
                        keep.add(d)
                    elif i2 > xmax.get(e2, -1):
                        xmax[e2] = i2
                for e2, i2 in xmax.items():
                    keep.add((e2, i2, False))
                o.deps = keep
        for e in ENGINES:
            for o in self.ops[e]:
                for (e2, i2, isdma) in o.deps:
                    if not isdma and e2 != e:
                        self.ops[e2][i2].need_inc = True
        for e in ENGINES:
            n = nd = 0
            K = len(dma_pools[e])
            for o in self.ops[e]:
                if o.dma:
                    o.dma_idx = nd
                    nd += 1
                elif o.need_inc:
                    n += 1
                    o.inc_no = n

        def dma_wait(eng_name, j):
            K = len(dma_pools[eng_name])
            return dma_pools[eng_name][j % K], 16 * (j // K + 1)

        def emit_engine(eng_name, eng):
            watermark = {}
            drained_upto = -1
            K = len(dma_pools[eng_name])
            for o in self.ops[eng_name]:
                waits = {}
                need_drain = False
                for (e2, i2, isdma) in sorted(o.deps):
                    if e2 == eng_name and not isdma:
                        if i2 > drained_upto and not (
                                eng_name == "tensor" and o.chain
                                and i2 == o.idx - 1):
                            need_drain = True
                        continue
                    if isdma:
                        j = self.ops[e2][i2].dma_idx
                        sem, val = dma_wait(e2, j)
                        k = (e2, "d", j % len(dma_pools[e2]))
                    else:
                        dep = self.ops[e2][i2]
                        sem, val = sems[e2], dep.inc_no
                        k = (e2, "c")
                    if watermark.get(k, -1) < val:
                        if k in waits:
                            waits[k] = (sem, max(val, waits[k][1]))
                        else:
                            waits[k] = (sem, val)
                if need_drain:
                    eng.drain()
                    drained_upto = o.idx - 1
                if o.dma and o.dma_idx >= K:
                    sem, val = dma_pools[eng_name][o.dma_idx % K],                         16 * (o.dma_idx // K)
                    k = (eng_name, "d", o.dma_idx % K)
                    if watermark.get(k, -1) < val:
                        waits[k] = (sem, max(val, waits.get(k, (sem, 0))[1]))
                for k, (sem, val) in waits.items():
                    eng.wait_ge(sem, val)
                    watermark[k] = val
                inst = o.fn(eng)
                if o.dma:
                    inst.then_inc(dma_pools[eng_name][o.dma_idx % K], 16)
                elif o.need_inc:
                    if eng_name == "tensor":
                        eng.drain().then_inc(sems[eng_name], 1)
                        drained_upto = o.idx
                    else:
                        inst.then_inc(sems[eng_name], 1)

        for name in ENGINES:
            getattr(block, name)(lambda eng, n=name: emit_engine(n, eng))


def r(buf, key=()):
    return (buf.name, key)


def sap(t, p0, pn, foff, dims):
    fs = 1
    for d in t.shape[1:]:
        fs *= d
    return bass.AP(t, p0 * fs + foff, [[fs, pn]] + [list(d) for d in dims])


CONSTS = [1e-5, 1e-20, 127.0, 0.5, 4.844187086458591, 2.384185791015625e-07, 0.7]


class Ctx:
    def __init__(self, S):
        self.S = S
        self.nc = bass.Bass("TRN2", target_bir_lowering=False)
        self.plan = Plan()
        self.stack = contextlib.ExitStack()
        self.input_names = []
        self.cvals = None
        self.ones_r = None
        self.ones1 = None
        self.halfsh = None

    def cv(self, value):
        j = CONSTS.index(value)
        return self.cvals[:, j:j + 1]

    def dram_in(self, name, shape, dtype=F32):
        t = self.nc.dram_tensor(name, list(shape), dtype, kind="ExternalInput")
        self.input_names.append(name)
        return t.ap()

    def dram_out(self, name, shape, dtype=F32):
        return self.nc.dram_tensor(name, list(shape), dtype, kind="ExternalOutput").ap()

    def dram_tmp(self, name, shape, dtype=F32):
        return self.nc.dram_tensor(name, list(shape), dtype, kind="Internal").ap()

    def sb(self, name, shape, dtype, stack=None):
        h = (stack or self.stack).enter_context(
            self.nc.sbuf_tensor(name, list(shape), dtype))
        self.plan.register_buf(name)
        return h

    def ps(self, name, shape, dtype=F32, stack=None):
        h = (stack or self.stack).enter_context(
            self.nc.psum_tensor(name, list(shape), dtype))
        self.plan.register_buf(name)
        return h

    def dma(self, eng, out_ap, in_ap, writes, reads=(), tag="dma"):
        return self.plan.op(eng, lambda e: e.dma_start(out=out_ap, in_=in_ap),
                            reads=reads, writes=writes, dma=True, tag=tag)


def emit_ln_stats(C, src, src_rd, pstat, ones_r, m_sb, r_sb, tmpb, tag, c0=0):
    P = C.plan
    for pt in range(8):
        P.op("scalar",
             lambda e, pt=pt: e.activation(tmpb[:, :], src[:, pt, c0:c0 + 512],
                                           AF.Copy),
             reads=src_rd(pt), writes=[r(tmpb)], tag=tag + "c")
        P.op("tensor",
             lambda e, pt=pt: e.matmul(pstat[:, :], ones_r[:, :], tmpb[:, :],
                                       start=(pt == 0), stop=(pt == 7)),
             reads=[r(tmpb), r(ones_r)], writes=[r(pstat)], tag=tag + "ms", chain=(pt != 0))
    P.op("scalar", lambda e: e.activation(m_sb[:, :], pstat[:, :], AF.Copy),
         reads=[r(pstat)], writes=[r(m_sb)], tag=tag + "mc")
    for pt in range(8):
        P.op("scalar",
             lambda e, pt=pt: e.activation(tmpb[:, :], src[:, pt, c0:c0 + 512],
                                           AF.Square),
             reads=src_rd(pt), writes=[r(tmpb)], tag=tag + "q")
        P.op("tensor",
             lambda e, pt=pt: e.matmul(pstat[:, :], ones_r[:, :], tmpb[:, :],
                                       start=(pt == 0), stop=(pt == 7)),
             reads=[r(tmpb), r(ones_r)], writes=[r(pstat)], tag=tag + "qs", chain=(pt != 0))
    P.op("vector", lambda e: e.tensor_mul(r_sb[:, :], m_sb[:, :], m_sb[:, :]),
         reads=[r(m_sb)], writes=[r(r_sb)], tag=tag + "mm")
    P.op("vector", lambda e: e.tensor_sub(r_sb[:, :], pstat[:, :], r_sb[:, :]),
         reads=[r(pstat), r(r_sb)], writes=[r(r_sb)], tag=tag + "var")
    P.op("scalar", lambda e: e.activation(r_sb[:, :], r_sb[:, :], AF.Ln,
                                          bias=C.cv(1e-5)),
         reads=[r(r_sb), r(C.cvals)], writes=[r(r_sb)], tag=tag + "ln")
    P.op("scalar", lambda e: e.activation(r_sb[:, :], r_sb[:, :], AF.Exp, scale=-0.5),
         reads=[r(r_sb)], writes=[r(r_sb)], tag=tag + "ex")


def emit_l0a(C, x_src_dram, x_dst_dram, src_nm, dst_nm):
    P, nc, S = C.plan, C.nc, C.S
    NT, NG = S // 128, S // 512
    st = contextlib.ExitStack()

    wqkv = C.dram_in("wqkv", [D, 3 * D])
    wo_d = C.dram_in("wo", [D, D])
    bq_d = C.dram_in("bq_sb", [128, 8])
    bk_d = C.dram_in("bk_sb", [128, 8])
    bo_d = C.dram_in("bo_sb", [128, 8])
    bvn_d = C.dram_in("bvn_bc", [128, 16 * 64])
    mB_d = C.dram_in("maskB", [128, 128])
    mA_d = C.dram_in("maskA", [32, 128])
    mC_d = C.dram_in("maskC", [32, 128])

    ones_r = C.ones_r
    ones1 = C.ones1
    mB = C.sb("mB_s", [128, 128], F32, st)
    mA = C.sb("mA_s", [128, 128], F32, st)
    mC = C.sb("mC_s", [128, 128], F32, st)
    bq = C.sb("bq_s", [128, 8], F32, st)
    bk = C.sb("bk_s", [128, 8], F32, st)
    bo = C.sb("bo_s", [128, 8], F32, st)
    bvn = C.sb("bvn_s", [128, 16 * 64], BF16, st)
    wqs = C.sb("wqs", [128, 8, 3 * D], BF16, st)
    wos = C.sb("wos", [128, 8, D], BF16, st)
    xg = [C.sb(f"xg{i}", [128, 8, 512], F32, st) for i in range(2)]
    ln1 = [C.sb(f"l0ln1_{i}", [128, 8, 512], BF16, st) for i in range(2)]
    lntmp = C.sb("l0lntmp", [128, 512], F32, st)
    qT = C.sb("qT", [128, 8, 8 * 128], BF16, st)
    kT = C.sb("kT", [128, 8, 8 * 128], BF16, st)
    vn = C.sb("vn", [128, 8, 16 * 64], BF16, st)
    vals = C.sb("vals", [128, 8, 2 * 512], BF16, st)
    exB = C.sb("exB", [128, 8 * 128], BF16, st)
    exAC = C.sb("exAC", [128, 8 * 128], BF16, st)
    sums = C.sb("sums", [128, 8 * 128], BF16, st)
    rinv = C.sb("rinv", [128, 8 * 128], BF16, st)
    m_sb = C.sb("l0m", [128, 512], F32, st)
    r_sb = C.sb("l0r", [128, 512], F32, st)
    m2_sb = C.sb("l0m2", [128, 512], F32, st)
    r2_sb = C.sb("l0r2", [128, 512], F32, st)
    tmpb = C.sb("l0tmpb", [128, 512], BF16, st)
    oT = C.sb("l0oT", [128, 8, 512], BF16, st)

    pSC = C.ps("pSC", [128, 1024], F32, st)
    pPV = C.ps("pPV", [128, 1024], F32, st)
    pBC = C.ps("pBC", [128, 1024], F32, st)
    pA1 = C.ps("pA1", [128, 512], F32, st)
    pA2 = C.ps("pA2", [128, 512], F32, st)

    C.dma("sync", mB[:, :], mB_d, [r(mB)])
    C.dma("sync", mA[96:128, :], mA_d, [r(mA)])
    C.dma("sync", mC[0:32, :], mC_d, [r(mC)])
    C.dma("sync", bq[:, :], bq_d, [r(bq)])
    C.dma("sync", bk[:, :], bk_d, [r(bk)])
    C.dma("sync", bo[:, :], bo_d, [r(bo)])
    C.dma("gpsimd", bvn[:, :], bvn_d, [r(bvn)])
    wq_r = wqkv.rearrange("(a p) c -> p a c", p=128)
    for h4 in range(4):
        C.dma("gpsimd", wqs[:, 2 * h4:2 * h4 + 2, :], wq_r[:, 2 * h4:2 * h4 + 2, :],
              [r(wqs, h4)], tag="wqkv")
    wo_r = wo_d.rearrange("(a p) c -> p a c", p=128)
    C.dma("gpsimd", wos[:, :, :], wo_r, [r(wos)], tag="wo")

    x_src_r = x_src_dram.rearrange("(a p) c -> p a c", p=128)
    x_dst_r = x_dst_dram.rearrange("(a p) c -> p a c", p=128)

    def ln1_group(g):
        gb = g % 2
        C.dma("sync", xg[gb][:, :, :], x_src_r[:, :, g * 512:(g + 1) * 512],
              [r(xg[gb])], reads=[("dram:" + src_nm, g)], tag=f"xg{g}")
        emit_ln_stats(C, xg[gb], lambda pt: [r(xg[gb])], pA2, ones_r, m_sb, r_sb,
                      tmpb, tag="ln1")
        for pt in range(8):
            P.op("vector",
                 lambda e, pt=pt, gb=gb: e.tensor_sub(
                     lntmp[:, :], xg[gb][:, pt, :], m_sb[:, :]),
                 reads=[r(xg[gb]), r(m_sb)], writes=[r(lntmp)], tag="l1s")
            P.op("vector",
                 lambda e, pt=pt, gb=gb: e.tensor_mul(
                     ln1[gb][:, pt, :], lntmp[:, :], r_sb[:, :]),
                 reads=[r(lntmp), r(r_sb)], writes=[r(ln1[gb])], tag="l1m")

    def qkv_group(g):
        gb = g % 2
        s0 = (4 * g) % 8
        for part, dst, bias, base in (("q", qT, bq, 0), ("k", kT, bk, D)):
            for i in range(8):
                for k in range(8):
                    P.op("tensor",
                         lambda e, i=i, k=k, gb=gb, base=base: e.matmul(
                             pA1[:, :], wqs[:, k, base + i * 128:base + (i + 1) * 128],
                             ln1[gb][:, k, :],
                             start=(k == 0), stop=(k == 7)),
                         reads=[r(wqs, k // 2), r(ln1[gb])], writes=[r(pA1)],
                         tag=f"{part}mm", chain=(k != 0))
                P.op("scalar",
                     lambda e, i=i, s0=s0, dst=dst, bias=bias: e.activation(
                         dst[:, i, s0 * 128:(s0 + 4) * 128], pA1[:, :], AF.Identity,
                         bias=bias[:, i:i + 1]),
                     reads=[r(pA1), r(bias)],
                     writes=[r(dst, (4 * g + w) % 8) for w in range(4)],
                     tag=f"{part}ev")
        for w in range(4):
            t = 4 * g + w
            sl = t % 8
            for ng in range(2):
                for k in range(8):
                    P.op("tensor",
                         lambda e, ng=ng, k=k, gb=gb, w=w: e.matmul(
                             pA2[:, :], ln1[gb][:, k, w * 128:(w + 1) * 128],
                             wqs[:, k, 2 * D + ng * 512:2 * D + (ng + 1) * 512],
                             start=(k == 0), stop=(k == 7)),
                         reads=[r(wqs, k // 2), r(ln1[gb])], writes=[r(pA2)],
                         tag="vmm", chain=(k != 0))
                P.op("vector",
                     lambda e, sl=sl, ng=ng: e.tensor_add(
                         vn[:, sl, ng * 512:(ng + 1) * 512], pA2[:, :],
                         bvn[:, ng * 512:(ng + 1) * 512]),
                     reads=[r(pA2), r(bvn)], writes=[r(vn, sl)], tag="vev")

    def attn_tile(t):
        sl = t % 8
        has_a, has_c = t > 0, t < NT - 1
        g, w = t // 4, t % 4
        gsl = g % 2
        for hg in range(2):
            hbase = hg * 8
            base = 0 if hg == 0 else 64
            for hl in range(8):
                h = hbase + hl
                po = (h % 2) * 64
                kap = sap(kT, po, 64, (h // 2) * 1024 + sl * 128, [[1, 128]])
                qap = sap(qT, po, 64, (h // 2) * 1024 + sl * 128, [[1, 128]])
                P.op("tensor",
                     lambda e, kap=kap, qap=qap, hl=hl, po=po: e.matmul(
                         pSC[:, colB(hl):colB(hl) + 128], kap, qap,
                         start=(hl < 2) or not G_SC,
                         stop=(hl >= 6) or not G_SC,
                         tile_position=(po, 0),
                         skip_group_check=True),
                     reads=[r(kT, sl), r(qT, sl)], writes=[r(pSC)], tag="scB",
                     chain=G_SC and (hl != 0))
            P.op("vector",
                 lambda e: e.tensor_add(
                     pSC[:, :], pSC[:, :],
                     bass.AP(mB, 0, [[128, 128], [0, 8], [1, 128]])),
                 reads=[r(pSC), r(mB)], writes=[r(pSC)], tag="mskB")
            P.op("scalar", lambda e: e.activation(exB[:, :], pSC[:, :], AF.Exp,
                                                  scale=0.125),
                 reads=[r(pSC)], writes=[r(exB)], tag="expB")
            nsub = (1 if has_a else 0) + (1 if has_c else 0)
            si = [0]
            for hl in range(8):
                h = hbase + hl
                po = (h % 2) * 64
                first = (hl < 2) or not G_AC
                last = (hl >= 6) or not G_AC
                if has_a:
                    kap = sap(kT, po, 64, (h // 2) * 1024 + ((t - 1) % 8) * 128 + 96,
                              [[1, 32]])
                    qap = sap(qT, po, 64, (h // 2) * 1024 + sl * 128, [[1, 128]])
                    gfirst = si[0] == 0
                    si[0] += 1
                    P.op("tensor",
                         lambda e, kap=kap, qap=qap, hl=hl, po=po, first=first,
                                last=last: e.matmul(
                             sap(pSC, 96, 32, colB(hl), [[1, 128]]), kap, qap,
                             start=first, stop=last, tile_position=(po, 96),
                             skip_group_check=True),
                         reads=[r(kT, (t - 1) % 8), r(qT, sl)],
                         writes=[r(pSC)], tag="scA", chain=G_AC and not gfirst)
                if has_c:
                    kap = sap(kT, po, 64, (h // 2) * 1024 + ((t + 1) % 8) * 128,
                              [[1, 32]])
                    qap = sap(qT, po, 64, (h // 2) * 1024 + sl * 128, [[1, 128]])
                    gfirst = si[0] == 0
                    si[0] += 1
                    P.op("tensor",
                         lambda e, kap=kap, qap=qap, hl=hl, po=po, first=first,
                                last=last: e.matmul(
                             sap(pSC, 0, 32, colB(hl), [[1, 128]]), kap, qap,
                             start=first, stop=last, tile_position=(po, 0),
                             skip_group_check=True),
                         reads=[r(kT, (t + 1) % 8), r(qT, sl)],
                         writes=[r(pSC)], tag="scC", chain=G_AC and not gfirst)
            if has_a:
                P.op("vector",
                     lambda e: e.tensor_add(
                         sap(pSC, 96, 32, 0, [[128, 8], [1, 128]]),
                         sap(pSC, 96, 32, 0, [[128, 8], [1, 128]]),
                         bass.AP(mA, 96 * 128, [[128, 32], [0, 8], [1, 128]])),
                     reads=[r(pSC), r(mA)], writes=[r(pSC)], tag="mskA")
                P.op("scalar",
                     lambda e: e.activation(exAC[96:128, :], pSC[96:128, :], AF.Exp,
                                            scale=0.125),
                     reads=[r(pSC)], writes=[r(exAC, "A")], tag="expA")
            if has_c:
                P.op("vector",
                     lambda e: e.tensor_add(
                         sap(pSC, 0, 32, 0, [[128, 8], [1, 128]]),
                         sap(pSC, 0, 32, 0, [[128, 8], [1, 128]]),
                         bass.AP(mC, 0, [[128, 32], [0, 8], [1, 128]])),
                     reads=[r(pSC), r(mC)], writes=[r(pSC)], tag="mskC")
                P.op("scalar",
                     lambda e: e.activation(exAC[0:32, :], pSC[0:32, :], AF.Exp,
                                            scale=0.125),
                     reads=[r(pSC)], writes=[r(exAC, "C")], tag="expC")
            chunks = []
            if has_a:
                chunks.append(("A", exAC, 96, 32, (t - 1) % 8))
            chunks.append(("B", exB, 0, 128, sl))
            if has_c:
                chunks.append(("C", exAC, 0, 32, (t + 1) % 8))
            for half in range(2):
                for ci, (nm, exb, rb, rn, _ksl) in enumerate(chunks):
                    P.op("tensor",
                         lambda e, exb=exb, rb=rb, rn=rn, half=half,
                                ci=ci, last=(ci == len(chunks) - 1): e.matmul(
                             sap(pBC, 96, 1, half * 512, [[1, 512]]),
                             sap(ones1, rb, rn, 0, [[1, 1]]),
                             sap(exb, rb, rn, half * 512, [[1, 512]]),
                             start=(ci == 0), stop=last, tile_position=(rb, 96)),
                         reads=[r(exb, nm if exb is exAC else ()), r(ones1)],
                         writes=[r(pBC)], tag="sums",
                         chain=(ci != 0) or (G_SB and half != 0))
            P.op("vector", lambda e: e.tensor_copy(sums[96:97, :], pBC[96:97, :]),
                 reads=[r(pBC)], writes=[r(sums)], tag="sumcp")
            for half in range(2):
                P.op("tensor",
                     lambda e, half=half, base=base: e.matmul(
                         sap(pBC, base, 64, half * 512, [[1, 512]]),
                         sap(ones1, 96, 1, 0, [[1, 64]]),
                         sap(sums, 96, 1, half * 512, [[1, 512]]),
                         start=True, stop=True, tile_position=(96, base)),
                     reads=[r(sums), r(ones1)], writes=[r(pBC)], tag="bcast",
                     chain=G_SB)
            P.op("scalar",
                 lambda e, base=base: e.activation(
                     rinv[base:base + 64, :], pBC[base:base + 64, :], AF.Ln),
                 reads=[r(pBC)], writes=[r(rinv)], tag="rln")
            P.op("scalar",
                 lambda e, base=base: e.activation(
                     rinv[base:base + 64, :], rinv[base:base + 64, :], AF.Exp,
                     scale=-1.0),
                 reads=[r(rinv)], writes=[r(rinv)], tag="rex")
            for hl in range(8):
                h = hbase + hl
                vcol = h * 64
                for ci, (nm, exb, rb, rn, ksl) in enumerate(chunks):
                    P.op("tensor",
                         lambda e, exb=exb, rb=rb, rn=rn, ksl=ksl, vcol=vcol, hl=hl,
                                base=base, ci=ci,
                                first=(ci == 0 and (hl < 2 or not G_PV)),
                                last=(ci == len(chunks) - 1
                                      and (hl >= 6 or not G_PV)): e.matmul(
                             sap(pPV, base, 64, colB(hl), [[1, 128]]),
                             sap(vn, rb, rn, ksl * 1024 + vcol, [[1, 64]]),
                             sap(exb, rb, rn, colB(hl), [[1, 128]]),
                             start=first, stop=last, tile_position=(rb, base),
                             skip_group_check=True),
                         reads=[r(vn, ksl), r(exb, nm if exb is exAC else ())],
                         writes=[r(pPV)], tag="pv",
                         chain=(ci != 0 or (G_PV and hl != 0)))
            vo = gsl * 512 + w * 128
            for par in range(2):
                P.op("vector",
                     lambda e, base=base, vo=vo, par=par: e.tensor_mul(
                         sap(vals, base, 64, par * 1024 + vo, [[2048, 4], [1, 128]]),
                         sap(pPV, base, 64, par * 512, [[128, 4], [1, 128]]),
                         sap(rinv, base, 64, par * 512, [[128, 4], [1, 128]])),
                     reads=[r(pPV), r(rinv)], writes=[r(vals, (gsl, w))],
                     tag="valev")

    def wo_group(g):
        gsl = g % 2
        for j in range(8):
            for k in range(8):
                P.op("tensor",
                     lambda e, j=j, k=k, gsl=gsl: e.matmul(
                         pA1[:, :], wos[:, k, j * 128:(j + 1) * 128],
                         vals[:, k, gsl * 512:(gsl + 1) * 512],
                         start=(k == 0), stop=(k == 7)),
                     reads=[r(wos)] + [r(vals, (gsl, w)) for w in range(4)],
                     writes=[r(pA1)], tag="womm", chain=(k != 0))
            P.op("scalar",
                 lambda e, j=j: e.activation(oT[:, j, :], pA1[:, :], AF.Identity,
                                             bias=bo[:, j:j + 1]),
                 reads=[r(pA1), r(bo)], writes=[r(oT, j)], tag="woev")

    def ln2_add_group(g):
        xbuf = xg[g % 2]
        emit_ln_stats(C, oT, lambda pt: [r(oT, pt)], pA2, ones_r, m2_sb, r2_sb,
                      tmpb, tag="ln2")
        for pt in range(8):
            P.op("vector",
                 lambda e, pt=pt: e.tensor_sub(lntmp[:, :], oT[:, pt, :], m2_sb[:, :]),
                 reads=[r(oT, pt), r(m2_sb)], writes=[r(lntmp)], tag="a1")
            P.op("vector",
                 lambda e, pt=pt: e.tensor_mul(lntmp[:, :], lntmp[:, :], r2_sb[:, :]),
                 reads=[r(lntmp), r(r2_sb)], writes=[r(lntmp)], tag="a2")
            P.op("vector",
                 lambda e, pt=pt, xbuf=xbuf: e.tensor_add(
                     xbuf[:, pt, :], xbuf[:, pt, :], lntmp[:, :]),
                 reads=[r(lntmp), r(xbuf)], writes=[r(xbuf)], tag="a3")
        C.dma("sync", x_dst_r[:, :, g * 512:(g + 1) * 512], xbuf[:, :, :],
              [("dram:" + dst_nm, g)], reads=[r(xbuf)], tag=f"xo{g}")

    def attn_batch(g):
        if g == 0:
            ts = list(range(0, min(3, NT)))
        else:
            ts = list(range(4 * g - 1, min(4 * g + 3, NT)))
        if g == NG - 1:
            ts += list(range(max(4 * g + 3, ts[-1] + 1 if ts else 0), NT))
        for t in ts:
            attn_tile(t)

    ln1_group(0)
    qkv_group(0)
    attn_batch(0)
    for g in range(1, NG):
        ln1_group(g)
        qkv_group(g)
        attn_batch(g)
        wo_group(g - 1)
        ln2_add_group(g - 1)
    wo_group(NG - 1)
    ln2_add_group(NG - 1)

    P.fence()
    st.close()


def emit_l0f(C, x_src_dram, x_dst_dram, src_nm, dst_nm):
    P, nc, S = C.plan, C.nc, C.S
    NT, NG = S // 128, S // 512
    st = contextlib.ExitStack()

    gw_d = C.dram_in("gate_wT", [D, 16])
    gb_d = C.dram_in("gbias_bc", [128, 16])
    esel_d = C.dram_in("esel", [16, 16 * 128], BF16)
    idb_d = C.dram_in("ident_bf", [128, 128], BF16)
    wsg_d = C.dram_in("wsg", [D, D])
    wsu_d = C.dram_in("wsu", [D, D])
    wsd_d = C.dram_in("wsd", [D, D])
    wg_d = C.dram_in("wg_t", [16, D, 512])
    wu_d = C.dram_in("wu_t", [16, D, 512])
    wd_d = C.dram_in("wd_t", [16, 512, D])

    ln1f = C.sb("ln1f", [128, 8, S], BF16, st)
    y_sb = C.sb("y_sb", [128, 8, S], F32, st)
    cwT = C.sb("cwT", [128, S], BF16, st)
    identb = C.sb("identb_s", [128, 128], BF16, st)
    gbias = C.sb("gbias_s", [128, 16], F32, st)
    gws = C.sb("gws", [128, 8, 16], BF16, st)
    xg2 = [C.sb(f"xg2_{i}", [128, 8, 512], F32, st) for i in range(2)]
    lnm = C.sb("f0m", [128, 512], F32, st)
    lnr = C.sb("f0r", [128, 512], F32, st)
    lnm2 = C.sb("f0m2", [128, 512], F32, st)
    lnr2 = C.sb("f0r2", [128, 512], F32, st)
    tmpb = C.sb("f0tmpb", [128, 512], BF16, st)
    lnt = C.sb("f0lnt", [128, 512], F32, st)

    C.dma("gpsimd", gws[:, :, :], gw_d.rearrange("(a p) c -> p a c", p=128),
          [r(gws)], tag="gw")
    C.dma("sync", gbias[:, :], gb_d, [r(gbias)])
    C.dma("sync", identb[:, :], idb_d, [r(identb)])

    x_src_r = x_src_dram.rearrange("(a p) c -> p a c", p=128)
    x_dst_r = x_dst_dram.rearrange("(a p) c -> p a c", p=128)

    stA = contextlib.ExitStack()
    wsg = C.sb("wsg_s", [128, 8, D], BF16, stA)
    wsu = C.sb("wsu_s", [128, 8, D], BF16, stA)
    wsd = C.sb("wsd_s", [128, 8, D], BF16, stA)
    sh_g = C.sb("sh_g", [128, 8, 512], BF16, stA)
    pF1 = C.ps("pF1", [128, 512], F32, stA)
    pF2 = C.ps("pF2", [128, 512], F32, stA)
    pF3 = C.ps("pF3", [128, 512], F32, stA)
    sco = C.sb("sco", [128, 16], F32, stA)
    sfc = C.sb("sfc", [128, 16], F32, stA)
    msk = C.sb("msk", [128, 16], F32, stA)
    sel = C.sb("sel", [128, 16], F32, stA)
    oneh = C.sb("oneh", [128, 16], F32, stA)
    emk = C.sb("emk", [128, 16], F32, stA)
    p8a = C.sb("p8a", [128, 8], F32, stA)
    p8b = C.sb("p8b", [128, 8], F32, stA)
    p4a = C.sb("p4a", [128, 4], F32, stA)
    p4b = C.sb("p4b", [128, 4], F32, stA)
    p4c = C.sb("p4c", [128, 4], F32, stA)
    p4d = C.sb("p4d", [128, 4], F32, stA)
    s1a = C.sb("s1a", [128, 1], F32, stA)
    s1b = C.sb("s1b", [128, 1], F32, stA)
    s1c = C.sb("s1c", [128, 1], F32, stA)
    cwb = C.sb("cwb", [128, 16], BF16, stA)

    C.dma("gpsimd", wsg[:, :, :], wsg_d.rearrange("(a p) c -> p a c", p=128),
          [r(wsg)], tag="wsg")
    C.dma("gpsimd", wsu[:, :, :], wsu_d.rearrange("(a p) c -> p a c", p=128),
          [r(wsu)], tag="wsu")
    C.dma("gpsimd", wsd[:, :, :], wsd_d.rearrange("(a p) c -> p a c", p=128),
          [r(wsd)], tag="wsd")

    def ln1_group(g):
        gb = g % 2
        C.dma("sync", xg2[gb][:, :, :], x_src_r[:, :, g * 512:(g + 1) * 512],
              [r(xg2[gb])], reads=[("dram:" + src_nm, g)], tag=f"fxg{g}")
        emit_ln_stats(C, xg2[gb], lambda pt: [r(xg2[gb])], pF1, C.ones_r, lnm, lnr,
                      tmpb, tag="fln1")
        for pt in range(8):
            P.op("vector",
                 lambda e, pt=pt, gb=gb: e.tensor_sub(
                     lnt[:, :], xg2[gb][:, pt, :], lnm[:, :]),
                 reads=[r(xg2[gb]), r(lnm)], writes=[r(lnt)], tag="f1s")
            P.op("vector",
                 lambda e, pt=pt, gb=gb, g=g: e.tensor_mul(
                     ln1f[:, pt, g * 512:(g + 1) * 512], lnt[:, :], lnr[:, :]),
                 reads=[r(lnt), r(lnr)], writes=[r(ln1f, (g, pt))], tag="f1m")

    def routing_tile(t):
        g = t // 4
        for k in range(8):
            P.op("tensor",
                 lambda e, k=k, t=t: e.matmul(
                     pF1[:, 0:16], ln1f[:, k, t * 128:(t + 1) * 128], gws[:, k, :],
                     start=(k == 0), stop=(k == 7)),
                 reads=[r(ln1f, (g, k)), r(gws)], writes=[r(pF1)], tag="gmm", chain=(k != 0))
        P.op("scalar", lambda e: e.activation(sco[:, :], pF1[:, 0:16], AF.Sigmoid),
             reads=[r(pF1)], writes=[r(sco)], tag="sig")
        P.op("vector", lambda e: e.tensor_add(sfc[:, :], sco[:, :], gbias[:, :]),
             reads=[r(sco), r(gbias)], writes=[r(sfc)], tag="sfc")
        ev = lambda b, o, n, s: sap(b, 0, 128, o, [[s, n]])
        P.op("vector", lambda e: e.tensor_max(p8a[:, :], ev(sfc, 0, 8, 2),
                                              ev(sfc, 1, 8, 2)),
             reads=[r(sfc)], writes=[r(p8a)], tag="hi")
        P.op("vector", lambda e: e.tensor_tensor(p8b[:, :], ev(sfc, 0, 8, 2),
                                                 ev(sfc, 1, 8, 2), ALU.min),
             reads=[r(sfc)], writes=[r(p8b)], tag="lo")
        P.op("vector", lambda e: e.tensor_max(p4a[:, :], ev(p8a, 0, 4, 2),
                                              ev(p8a, 1, 4, 2)),
             reads=[r(p8a)], writes=[r(p4a)], tag="t1")
        P.op("vector", lambda e: e.tensor_tensor(p4b[:, :], ev(p8a, 0, 4, 2),
                                                 ev(p8a, 1, 4, 2), ALU.min),
             reads=[r(p8a)], writes=[r(p4b)], tag="mn")
        P.op("vector", lambda e: e.tensor_max(p4c[:, :], ev(p8b, 0, 4, 2),
                                              ev(p8b, 1, 4, 2)),
             reads=[r(p8b)], writes=[r(p4c)], tag="ml")
        P.op("vector", lambda e: e.tensor_max(p4b[:, :], p4b[:, :], p4c[:, :]),
             reads=[r(p4b), r(p4c)], writes=[r(p4b)], tag="sec")
        P.op("vector", lambda e: e.tensor_add(p4a[:, :], p4a[:, :], p4b[:, :]),
             reads=[r(p4a), r(p4b)], writes=[r(p4a)], tag="g2")
        P.op("vector", lambda e: e.tensor_max(s1a[:, :], p4a[:, 0:1], p4a[:, 1:2]),
             reads=[r(p4a)], writes=[r(s1a)], tag="a1")
        P.op("vector", lambda e: e.tensor_max(s1b[:, :], p4a[:, 2:3], p4a[:, 3:4]),
             reads=[r(p4a)], writes=[r(s1b)], tag="b1")
        P.op("vector", lambda e: e.tensor_tensor(s1a[:, :], s1a[:, :], s1b[:, :],
                                                 ALU.min),
             reads=[r(s1a), r(s1b)], writes=[r(s1a)], tag="u")
        P.op("vector", lambda e: e.tensor_tensor(s1b[:, :], p4a[:, 0:1], p4a[:, 1:2],
                                                 ALU.min),
             reads=[r(p4a), r(s1b)], writes=[r(s1b)], tag="c1")
        P.op("vector", lambda e: e.tensor_tensor(s1c[:, :], p4a[:, 2:3], p4a[:, 3:4],
                                                 ALU.min),
             reads=[r(p4a)], writes=[r(s1c)], tag="d1")
        P.op("vector", lambda e: e.tensor_max(s1b[:, :], s1b[:, :], s1c[:, :]),
             reads=[r(s1b), r(s1c)], writes=[r(s1b)], tag="v")
        P.op("vector", lambda e: e.tensor_max(s1a[:, :], s1a[:, :], s1b[:, :]),
             reads=[r(s1a), r(s1b)], writes=[r(s1a)], tag="gthr")
        P.op("vector",
             lambda e: e.tensor_single_scalar(p4d[:, :], p4a[:, :], s1a[:, 0:1],
                                              ALU.is_ge),
             reads=[r(p4a), r(s1a)], writes=[r(p4d)], tag="gmask")
        P.op("vector",
             lambda e: e.tensor_copy(sap(emk, 0, 128, 0, [[4, 4], [1, 4]]),
                                     sap(p4d, 0, 128, 0, [[1, 4], [0, 4]])),
             reads=[r(p4d)], writes=[r(emk)], tag="emk")
        P.op("vector",
             lambda e: e.tensor_scalar(oneh[:, :], emk[:, :], 1e30, -1e30,
                                       ALU.mult, ALU.add),
             reads=[r(emk)], writes=[r(oneh)], tag="emsk")
        P.op("vector", lambda e: e.tensor_add(msk[:, :], sfc[:, :], oneh[:, :]),
             reads=[r(sfc), r(oneh)], writes=[r(msk)], tag="mskd")
        P.op("vector", lambda e: e.memset(sel[:, :], 0.0),
             writes=[r(sel)], tag="sel0")
        for it in range(4):
            P.op("vector", lambda e: e.reduce_max(s1b[:, :], msk[:, :], AX.X),
                 reads=[r(msk)], writes=[r(s1b)], tag="mx")
            P.op("vector",
                 lambda e: e.tensor_single_scalar(oneh[:, :], msk[:, :], s1b[:, 0:1],
                                                  ALU.is_ge),
                 reads=[r(msk), r(s1b)], writes=[r(oneh)], tag="oh")
            P.op("vector", lambda e: e.tensor_add(sel[:, :], sel[:, :], oneh[:, :]),
                 reads=[r(sel), r(oneh)], writes=[r(sel)], tag="seladd")
            if it < 3:
                P.op("vector",
                     lambda e: e.scalar_tensor_tensor(msk[:, :], oneh[:, :], -4e30,
                                                      msk[:, :], ALU.mult, ALU.add),
                     reads=[r(oneh), r(msk)], writes=[r(msk)], tag="knock")
        P.op("vector", lambda e: e.tensor_mul(oneh[:, :], sco[:, :], sel[:, :]),
             reads=[r(sco), r(sel)], writes=[r(oneh)], tag="cwr")
        P.op("vector", lambda e: e.reduce_sum(s1b[:, :], oneh[:, :], AX.X),
             reads=[r(oneh)], writes=[r(s1b)], tag="den")
        P.op("scalar", lambda e: e.activation(s1b[:, :], s1b[:, :], AF.Ln,
                                              bias=C.cv(1e-20)),
             reads=[r(s1b), r(C.cvals)], writes=[r(s1b)], tag="dln")
        P.op("scalar", lambda e: e.activation(s1b[:, :], s1b[:, :], AF.Exp,
                                              scale=-1.0),
             reads=[r(s1b)], writes=[r(s1b)], tag="dex")
        P.op("vector",
             lambda e: e.tensor_scalar(cwb[:, :], oneh[:, :], s1b[:, 0:1], 2.5,
                                       ALU.mult, ALU.mult),
             reads=[r(oneh), r(s1b)], writes=[r(cwb)], tag="cw")
        P.op("tensor",
             lambda e: e.matmul(pF1[0:16, 0:128], cwb[:, :], identb[:, :],
                                start=True, stop=True),
             reads=[r(cwb), r(identb)], writes=[r(pF1)], tag="cwt")
        P.op("vector",
             lambda e, t=t: e.tensor_copy(cwT[0:16, t * 128:(t + 1) * 128],
                                          pF1[0:16, 0:128]),
             reads=[r(pF1)], writes=[r(cwT, t)], tag="cwev")

    def shared_group(g):
        c0 = g * 512
        for j in range(8):
            for k in range(8):
                P.op("tensor",
                     lambda e, j=j, k=k, c0=c0: e.matmul(
                         pF2[:, :], wsg[:, k, j * 128:(j + 1) * 128],
                         ln1f[:, k, c0:c0 + 512], start=(k == 0), stop=(k == 7)),
                     reads=[r(wsg), r(ln1f, (g, k))], writes=[r(pF2)], tag="sgmm", chain=(k != 0))
            P.op("scalar", lambda e, j=j: e.activation(sh_g[:, j, :], pF2[:, :],
                                                       AF.Silu),
                 reads=[r(pF2)], writes=[r(sh_g, j)], tag="sgsilu")
        for j in range(8):
            for k in range(8):
                P.op("tensor",
                     lambda e, j=j, k=k, c0=c0: e.matmul(
                         pF2[:, :], wsu[:, k, j * 128:(j + 1) * 128],
                         ln1f[:, k, c0:c0 + 512], start=(k == 0), stop=(k == 7)),
                     reads=[r(wsu), r(ln1f, (g, k))], writes=[r(pF2)], tag="summ", chain=(k != 0))
            P.op("vector", lambda e, j=j: e.tensor_mul(sh_g[:, j, :], pF2[:, :],
                                                       sh_g[:, j, :]),
                 reads=[r(pF2), r(sh_g, j)], writes=[r(sh_g, j)], tag="suact")
        for j in range(8):
            for k in range(8):
                P.op("tensor",
                     lambda e, j=j, k=k: e.matmul(
                         pF3[:, :], wsd[:, k, j * 128:(j + 1) * 128],
                         sh_g[:, k, :], start=(k == 0), stop=(k == 7)),
                     reads=[r(wsd), r(sh_g, k)], writes=[r(pF3)], tag="sdmm", chain=(k != 0))
            P.op("scalar",
                 lambda e, j=j, c0=c0: e.activation(y_sb[:, j, c0:c0 + 512],
                                                    pF3[:, :], AF.Copy),
                 reads=[r(pF3)], writes=[r(y_sb, ("c", g))], tag="sdev")

    for g in range(NG):
        ln1_group(g)
        for w in range(4):
            routing_tile(4 * g + w)
        shared_group(g)

    P.fence()
    stA.close()

    stB = contextlib.ExitStack()
    esel = C.sb("esel_s", [128, 16 * 128], BF16, stB)
    C.dma("sync", esel[0:16, :], esel_d, [r(esel)])
    wgS = [C.sb(f"wgS{i}", [128, 8, 512], BF16, stB) for i in range(2)]
    wuS = [C.sb(f"wuS{i}", [128, 8, 512], BF16, stB) for i in range(2)]
    wdS = [C.sb(f"wdS{i}", [128, 4, D], BF16, stB) for i in range(2)]
    silu_e = C.sb("silu_e", [128, 2048], BF16, stB)
    act_e = silu_e
    cw_bc = C.sb("cw_bc", [128, S], BF16, stB)
    pGU = C.ps("pGU", [128, 2048], F32, stB)
    pY = C.ps("pY", [128, 2048], F32, stB)
    NC4 = S // 512

    def load_expert(e):
        sl = e % 2
        C.dma("gpsimd", wgS[sl][:, :, :],
              wg_d[e].rearrange("(a p) c -> p a c", p=128), [r(wgS[sl])],
              tag=f"wg{e}")
        C.dma("gpsimd", wuS[sl][:, :, :],
              wu_d[e].rearrange("(a p) c -> p a c", p=128), [r(wuS[sl])],
              tag=f"wu{e}")
        C.dma("gpsimd", wdS[sl][:, :, :],
              wd_d[e].rearrange("(a p) c -> p a c", p=128), [r(wdS[sl])],
              tag=f"wd{e}")

    load_expert(0)
    for e in range(16):
        sl = e % 2
        if e + 1 < 16:
            load_expert(e + 1)
        for q in range(NC4):
            P.op("tensor",
                 lambda e2, e=e, q=q: e2.matmul(
                     pGU[:, q * 512:(q + 1) * 512], esel[0:16, e * 128:(e + 1) * 128],
                     cwT[0:16, q * 512:(q + 1) * 512], start=True, stop=True),
                 reads=[r(esel)] + [r(cwT, 4 * q + w) for w in range(4)],
                 writes=[r(pGU)], tag="cwbc")
        P.op("scalar", lambda e2: e2.activation(cw_bc[:, 0:512 * NC4],
                                                pGU[:, 0:512 * NC4], AF.Copy),
             reads=[r(pGU)], writes=[r(cw_bc)], tag="cwbev")
        for c in range(NC4):
            c0 = c * 512
            for m in range(4):
                for k in range(8):
                    P.op("tensor",
                         lambda e2, m=m, k=k, c0=c0, sl=sl: e2.matmul(
                             pGU[:, m * 512:(m + 1) * 512],
                             wgS[sl][:, k, m * 128:(m + 1) * 128],
                             ln1f[:, k, c0:c0 + 512], start=(k == 0), stop=(k == 7)),
                         reads=[r(wgS[sl]), r(ln1f, (c, k))], writes=[r(pGU)],
                         tag="egmm", chain=(k != 0))
            P.op("scalar", lambda e2: e2.activation(silu_e[:, :], pGU[:, :], AF.Silu),
                 reads=[r(pGU)], writes=[r(silu_e)], tag="esilu")
            for m in range(4):
                for k in range(8):
                    P.op("tensor",
                         lambda e2, m=m, k=k, c0=c0, sl=sl: e2.matmul(
                             pGU[:, m * 512:(m + 1) * 512],
                             wuS[sl][:, k, m * 128:(m + 1) * 128],
                             ln1f[:, k, c0:c0 + 512], start=(k == 0), stop=(k == 7)),
                         reads=[r(wuS[sl]), r(ln1f, (c, k))], writes=[r(pGU)],
                         tag="eumm", chain=(k != 0))
            P.op("vector", lambda e2: e2.tensor_mul(act_e[:, :], pGU[:, :],
                                                    silu_e[:, :]),
                 reads=[r(pGU), r(silu_e)], writes=[r(silu_e)], tag="eact1")
            P.op("vector",
                 lambda e2, c0=c0: e2.tensor_mul(
                     act_e[:, :], act_e[:, :],
                     sap(cw_bc, 0, 128, c0, [[0, 4], [1, 512]])),
                 reads=[r(silu_e), r(cw_bc)], writes=[r(silu_e)], tag="eact2")
            for jh in range(2):
                for m2 in range(4):
                    j = jh * 4 + m2
                    for k in range(4):
                        P.op("tensor",
                             lambda e2, j=j, m2=m2, k=k, sl=sl: e2.matmul(
                                 pY[:, m2 * 512:(m2 + 1) * 512],
                                 wdS[sl][:, k, j * 128:(j + 1) * 128],
                                 sap(act_e, 0, 128, k * 512, [[1, 512]]),
                                 start=(k == 0), stop=(k == 3)),
                             reads=[r(wdS[sl]), r(silu_e)], writes=[r(pY)],
                             tag="edmm", chain=(k != 0))
                P.op("vector",
                     lambda e2, jh=jh, c0=c0: e2.tensor_add(
                         bass.AP(y_sb, (jh * 4) * S + c0, [[8 * S, 128], [S, 4], [1, 512]]),
                         bass.AP(y_sb, (jh * 4) * S + c0, [[8 * S, 128], [S, 4], [1, 512]]),
                         pGU[:, :] if False else bass.AP(pY, 0, [[2048, 128], [512, 4], [1, 512]])),
                     reads=[r(pY), r(y_sb, ("c", c))], writes=[r(y_sb, ("c", c))],
                     tag="eyadd")

    P.fence()
    stB.close()

    stC = contextlib.ExitStack()
    pT = C.ps("pT", [128, 512], F32, stC)
    for g in range(NG):
        c0 = g * 512
        P.op("vector",
             lambda e, c0=c0, g=g: e.tensor_add(
                 bass.AP(y_sb, c0, [[8 * S, 128], [S, 8], [1, 512]]),
                 bass.AP(y_sb, c0, [[8 * S, 128], [S, 8], [1, 512]]),
                 bass.AP(ln1f, c0, [[8 * S, 128], [S, 8], [1, 512]])),
             reads=[r(y_sb, ("c", g))] + [r(ln1f, (g, pt)) for pt in range(8)],
             writes=[r(y_sb, ("c", g))], tag="bout")
        emit_ln_stats(C, y_sb, lambda pt, g=g: [r(y_sb, ("c", g))], pT, C.ones_r,
                      lnm2, lnr2, tmpb, tag="fln2", c0=c0)
        gb = g % 2
        C.dma("sync", xg2[gb][:, :, :], x_src_r[:, :, c0:c0 + 512],
              [r(xg2[gb])], reads=[("dram:" + src_nm, g)], tag=f"fxr{g}")
        for pt in range(8):
            P.op("vector",
                 lambda e, pt=pt, c0=c0: e.tensor_sub(lnt[:, :],
                                                      y_sb[:, pt, c0:c0 + 512],
                                                      lnm2[:, :]),
                 reads=[r(y_sb, ("c", g)), r(lnm2)], writes=[r(lnt)], tag="fa1")
            P.op("vector",
                 lambda e, pt=pt: e.tensor_mul(lnt[:, :], lnt[:, :], lnr2[:, :]),
                 reads=[r(lnt), r(lnr2)], writes=[r(lnt)], tag="fa2")
            P.op("vector",
                 lambda e, pt=pt, gb=gb: e.tensor_add(
                     xg2[gb][:, pt, :], xg2[gb][:, pt, :], lnt[:, :]),
                 reads=[r(lnt), r(xg2[gb])], writes=[r(xg2[gb])], tag="fa3")
        C.dma("sync", x_dst_r[:, :, c0:c0 + 512], xg2[gb][:, :, :],
              [("dram:" + dst_nm, g)], reads=[r(xg2[gb])], tag=f"fxo{g}")

    P.fence()
    stC.close()
    st.close()


def emit_l1a(C, x_src_dram, x_dst_dram, src_nm, dst_nm):
    P, nc, S = C.plan, C.nc, C.S
    NT, NG = S // 128, S // 512
    st = contextlib.ExitStack()

    wq_d = C.dram_in("wq_t", [D, D])
    wk_d = C.dram_in("wk_t", [D, 64])
    wv_d = C.dram_in("wv_t", [D, 64])
    wf_d = C.dram_in("wf_t", [D, D])
    bf_d = C.dram_in("bf_sb", [128, 8])
    mo_d = C.dram_in("maskO", [128, 128])
    mn_d = C.dram_in("maskN", [128, 128])

    mO = C.sb("mO_s", [128, 128], F32, st)
    mN = C.sb("mN_s", [128, 128], F32, st)
    bf_s = C.sb("bf_s", [128, 8], F32, st)
    wqf = C.sb("wqf", [128, 8, D], BF16, st)
    wfs2 = C.sb("wfs2", [128, 8, D], BF16, st)
    wks = C.sb("wks", [128, 8, 64], BF16, st)
    wvs = C.sb("wvs", [128, 8, 64], BF16, st)
    qT1 = C.sb("qT1", [128, 8, S], BF16, st)
    kTd = C.sb("kTd", [128, S], BF16, st)
    v_n = C.sb("v_n", [128, NT * 64], BF16, st)
    ln1a = [C.sb(f"l1ln_{i}", [128, 8, 512], BF16, st) for i in range(2)]
    xga = [C.sb(f"xga{i}", [128, 8, 512], F32, st) for i in range(2)]
    vals1 = C.sb("vals1", [128, 8, 2 * 512], BF16, st)
    exL = C.sb("exL", [128, 3 * 1024], BF16, st)
    sums1 = C.sb("sums1", [128, 1024], BF16, st)
    rinv1 = C.sb("rinv1", [128, 1024], BF16, st)
    m_sb = C.sb("a1m", [128, 512], F32, st)
    r_sb = C.sb("a1r", [128, 512], F32, st)
    m2sb = C.sb("a1m2", [128, 512], F32, st)
    r2sb = C.sb("a1r2", [128, 512], F32, st)
    tmpb = C.sb("a1tmpb", [128, 512], BF16, st)
    lnt1 = C.sb("a1lnt", [128, 512], F32, st)
    oT1 = C.sb("a1oT", [128, 8, 512], BF16, st)

    pSC1 = C.ps("pSC1", [128, 1024], F32, st)
    pPV1 = C.ps("pPV1", [128, 1024], F32, st)
    pBC1 = C.ps("pBC1", [128, 1024], F32, st)
    pQ1 = C.ps("pQ1", [128, 512], F32, st)
    pS1 = C.ps("pS1", [128, 512], F32, st)

    C.dma("sync", mO[:, :], mo_d, [r(mO)])
    C.dma("sync", mN[:, :], mn_d, [r(mN)])
    C.dma("sync", bf_s[:, :], bf_d, [r(bf_s)])
    C.dma("gpsimd", wqf[:, :, :], wq_d.rearrange("(a p) c -> p a c", p=128),
          [r(wqf, "q")], tag="wq")
    C.dma("gpsimd", wfs2[:, :, :], wf_d.rearrange("(a p) c -> p a c", p=128),
          [r(wfs2)], tag="wf")
    C.dma("gpsimd", wks[:, :, :], wk_d.rearrange("(a p) c -> p a c", p=128),
          [r(wks)], tag="wk")
    C.dma("gpsimd", wvs[:, :, :], wv_d.rearrange("(a p) c -> p a c", p=128),
          [r(wvs)], tag="wv")

    x_src_r = x_src_dram.rearrange("(a p) c -> p a c", p=128)
    x_dst_r = x_dst_dram.rearrange("(a p) c -> p a c", p=128)

    def ln1_group(g):
        gb = g % 2
        C.dma("sync", xga[gb][:, :, :], x_src_r[:, :, g * 512:(g + 1) * 512],
              [r(xga[gb])], reads=[("dram:" + src_nm, g)], tag=f"axg{g}")
        emit_ln_stats(C, xga[gb], lambda pt: [r(xga[gb])], pS1, C.ones_r, m_sb,
                      r_sb, tmpb, tag="aln1")
        for pt in range(8):
            P.op("vector",
                 lambda e, pt=pt, gb=gb: e.tensor_sub(
                     lnt1[:, :], xga[gb][:, pt, :], m_sb[:, :]),
                 reads=[r(xga[gb]), r(m_sb)], writes=[r(lnt1)], tag="as")
            P.op("vector",
                 lambda e, pt=pt, gb=gb: e.tensor_mul(
                     ln1a[gb][:, pt, :], lnt1[:, :], r_sb[:, :]),
                 reads=[r(lnt1), r(r_sb)], writes=[r(ln1a[gb])], tag="am")

    def qkv_group(g):
        gb = g % 2
        c0 = g * 512
        for i in range(8):
            for k in range(8):
                P.op("tensor",
                     lambda e, i=i, k=k, gb=gb: e.matmul(
                         pQ1[:, :], wqf[:, k, i * 128:(i + 1) * 128],
                         ln1a[gb][:, k, :], start=(k == 0), stop=(k == 7)),
                     reads=[r(wqf, "q"), r(ln1a[gb])], writes=[r(pQ1)], tag="qmm", chain=(k != 0))
            P.op("scalar",
                 lambda e, i=i, c0=c0: e.activation(
                     qT1[:, i, c0:c0 + 512], pQ1[:, :], AF.Copy),
                 reads=[r(pQ1)], writes=[r(qT1, g)], tag="qev")
        for half in range(2):
            for k in range(8):
                P.op("tensor",
                     lambda e, half=half, k=k, gb=gb: e.matmul(
                         sap(pQ1, 64 * half, 64, 0, [[1, 512]]),
                         wks[:, k, :], ln1a[gb][:, k, :],
                         start=(k == 0), stop=(k == 7),
                         tile_position=(0, 64 * half)),
                     reads=[r(wks), r(ln1a[gb])], writes=[r(pQ1)], tag="kmm", chain=(k != 0))
        P.op("scalar",
             lambda e, c0=c0: e.activation(kTd[:, c0:c0 + 512], pQ1[:, :], AF.Copy),
             reads=[r(pQ1)], writes=[r(kTd, g)], tag="kev")
        for w in range(4):
            t = 4 * g + w
            for k in range(8):
                P.op("tensor",
                     lambda e, w=w, k=k, gb=gb: e.matmul(
                         sap(pQ1, 0, 128, 0, [[1, 64]]),
                         ln1a[gb][:, k, w * 128:(w + 1) * 128], wvs[:, k, :],
                         start=(k == 0), stop=(k == 7)),
                     reads=[r(wvs), r(ln1a[gb])], writes=[r(pQ1)], tag="vmm1", chain=(k != 0))
            P.op("scalar",
                 lambda e, t=t: e.activation(v_n[:, t * 64:(t + 1) * 64],
                                             sap(pQ1, 0, 128, 0, [[1, 64]]),
                                             AF.Copy),
                 reads=[r(pQ1)], writes=[r(v_n, t)], tag="vev1")

    def attn_tile(t):
        g, w = t // 4, t % 4
        gsl = g % 2
        kts = [kt for kt in (t - 2, t - 1, t) if kt >= 0]
        for hg in range(2):
            hbase = hg * 8
            base = 0 if hg == 0 else 64
            for ci, kt in enumerate(kts):
                for hl in range(8):
                    h = hbase + hl
                    po = (h % 2) * 64
                    P.op("tensor",
                         lambda e, hl=hl, h=h, po=po, kt=kt: e.matmul(
                             pSC1[:, colB(hl):colB(hl) + 128],
                             sap(kTd, po, 64, kt * 128, [[1, 128]]),
                             sap(qT1, po, 64, (h // 2) * S + t * 128, [[1, 128]]),
                             start=(hl < 2) or not G_SC,
                             stop=(hl >= 6) or not G_SC,
                             tile_position=(po, 0),
                             skip_group_check=True),
                         reads=[r(kTd, kt // 4), r(qT1, g)], writes=[r(pSC1)],
                         tag="sc1", chain=G_SC and (hl != 0))
                if kt == t:
                    P.op("vector",
                         lambda e: e.tensor_add(
                             pSC1[:, :], pSC1[:, :],
                             bass.AP(mN, 0, [[128, 128], [0, 8], [1, 128]])),
                         reads=[r(pSC1), r(mN)], writes=[r(pSC1)], tag="mn1")
                elif kt == t - 2:
                    P.op("vector",
                         lambda e: e.tensor_add(
                             pSC1[:, :], pSC1[:, :],
                             bass.AP(mO, 0, [[128, 128], [0, 8], [1, 128]])),
                         reads=[r(pSC1), r(mO)], writes=[r(pSC1)], tag="mo1")
                P.op("scalar",
                     lambda e, ci=ci: e.activation(
                         exL[:, ci * 1024:(ci + 1) * 1024], pSC1[:, :], AF.Exp,
                         scale=0.125),
                     reads=[r(pSC1)], writes=[r(exL, ci)], tag="ex1")
            for half in range(2):
                for ci in range(len(kts)):
                    P.op("tensor",
                         lambda e, ci=ci, half=half,
                                last=(ci == len(kts) - 1): e.matmul(
                             sap(pBC1, 96, 1, half * 512, [[1, 512]]),
                             sap(C.ones1, 0, 128, 0, [[1, 1]]),
                             sap(exL, 0, 128, ci * 1024 + half * 512, [[1, 512]]),
                             start=(ci == 0), stop=last, tile_position=(0, 96)),
                         reads=[r(exL, ci), r(C.ones1)], writes=[r(pBC1)],
                         tag="sums1", chain=(ci != 0) or (G_SB and half != 0))
            P.op("vector", lambda e: e.tensor_copy(sums1[96:97, :], pBC1[96:97, :]),
                 reads=[r(pBC1)], writes=[r(sums1)], tag="sumcp1")
            for half in range(2):
                P.op("tensor",
                     lambda e, half=half, base=base: e.matmul(
                         sap(pBC1, base, 64, half * 512, [[1, 512]]),
                         sap(C.ones1, 96, 1, 0, [[1, 64]]),
                         sap(sums1, 96, 1, half * 512, [[1, 512]]),
                         start=True, stop=True, tile_position=(96, base)),
                     reads=[r(sums1), r(C.ones1)], writes=[r(pBC1)], tag="bc1",
                     chain=G_SB)
            P.op("scalar",
                 lambda e, base=base: e.activation(
                     rinv1[base:base + 64, :], pBC1[base:base + 64, :], AF.Ln),
                 reads=[r(pBC1)], writes=[r(rinv1)], tag="rln1")
            P.op("scalar",
                 lambda e, base=base: e.activation(
                     rinv1[base:base + 64, :], rinv1[base:base + 64, :], AF.Exp,
                     scale=-1.0),
                 reads=[r(rinv1)], writes=[r(rinv1)], tag="rex1")
            for hl in range(8):
                for ci, kt in enumerate(kts):
                    P.op("tensor",
                         lambda e, hl=hl, ci=ci, kt=kt, base=base,
                                first=(ci == 0 and (hl < 2 or not G_PV)),
                                last=(ci == len(kts) - 1
                                      and (hl >= 6 or not G_PV)): e.matmul(
                             sap(pPV1, base, 64, colB(hl), [[1, 128]]),
                             sap(v_n, 0, 128, kt * 64, [[1, 64]]),
                             sap(exL, 0, 128, ci * 1024 + colB(hl), [[1, 128]]),
                             start=first, stop=last, tile_position=(0, base),
                             skip_group_check=True),
                         reads=[r(v_n, kt), r(exL, ci)], writes=[r(pPV1)],
                         tag="pv1", chain=(ci != 0 or (G_PV and hl != 0)))
            vo = gsl * 512 + w * 128
            for par in range(2):
                P.op("vector",
                     lambda e, base=base, vo=vo, par=par: e.tensor_mul(
                         sap(vals1, base, 64, par * 1024 + vo, [[2048, 4], [1, 128]]),
                         sap(pPV1, base, 64, par * 512, [[128, 4], [1, 128]]),
                         sap(rinv1, base, 64, par * 512, [[128, 4], [1, 128]])),
                     reads=[r(pPV1), r(rinv1)], writes=[r(vals1, (gsl, w))],
                     tag="vev1b")

    def wf_group(g):
        gsl = g % 2

        for j in range(8):
            for k in range(8):
                P.op("tensor",
                     lambda e, j=j, k=k, gsl=gsl: e.matmul(
                         pQ1[:, :], wfs2[:, k, j * 128:(j + 1) * 128],
                         vals1[:, k, gsl * 512:(gsl + 1) * 512],
                         start=(k == 0), stop=(k == 7)),
                     reads=[r(wfs2)] + [r(vals1, (gsl, w)) for w in range(4)],
                     writes=[r(pQ1)], tag="wfmm", chain=(k != 0))
            P.op("scalar",
                 lambda e, j=j: e.activation(oT1[:, j, :], pQ1[:, :], AF.Identity,
                                             bias=bf_s[:, j:j + 1]),
                 reads=[r(pQ1), r(bf_s)], writes=[r(oT1, j)], tag="wfev")

    def ln2_add_group(g):
        xbuf = xga[g % 2]
        emit_ln_stats(C, oT1, lambda pt: [r(oT1, pt)], pS1, C.ones_r, m2sb, r2sb,
                      tmpb, tag="aln2")
        for pt in range(8):
            P.op("vector",
                 lambda e, pt=pt: e.tensor_sub(lnt1[:, :], oT1[:, pt, :], m2sb[:, :]),
                 reads=[r(oT1, pt), r(m2sb)], writes=[r(lnt1)], tag="aa1")
            P.op("vector",
                 lambda e, pt=pt: e.tensor_mul(lnt1[:, :], lnt1[:, :], r2sb[:, :]),
                 reads=[r(lnt1), r(r2sb)], writes=[r(lnt1)], tag="aa2")
            P.op("vector",
                 lambda e, pt=pt, xbuf=xbuf: e.tensor_add(
                     xbuf[:, pt, :], xbuf[:, pt, :], lnt1[:, :]),
                 reads=[r(lnt1), r(xbuf)], writes=[r(xbuf)], tag="aa3")
        C.dma("sync", x_dst_r[:, :, g * 512:(g + 1) * 512], xbuf[:, :, :],
              [("dram:" + dst_nm, g)], reads=[r(xbuf)], tag=f"axo{g}")

    for g in range(NG):
        ln1_group(g)
        qkv_group(g)
        for w in range(4):
            attn_tile(4 * g + w)
        wf_group(g)
        ln2_add_group(g)

    P.fence()
    st.close()


def emit_l1f(C, x_src_dram, x_dst_dram, src_nm, dst_nm):
    P, nc, S = C.plan, C.nc, C.S
    NT, NG = S // 128, S // 512
    LN127 = 4.844187086458591
    st = contextlib.ExitStack()

    w1_d = C.dram_in("w1_t", [D, 4096])
    idf_d = C.dram_in("ident_f", [128, 128])
    sel_d = C.dram_in("sel16f", [16, 16 * 128])
    w2_d = C.dram_in("w2_t", [4096, D])
    cvd_d = C.dram_in("conv_diag", [32 * 3 * 128, 128])
    cb_d = C.dram_in("convb_sb", [128, 32])
    h2_h = C.dram_tmp("h2_hbm", [32 * 128, S])

    cb_s = C.sb("cb_s", [128, 32], F32, st)
    idf = C.sb("idf", [128, 128], F32, st)
    sel16 = C.sb("sel16", [128, 16 * 128], F32, st)
    amcmp = C.sb("amcmp", [128, 16], F32, st)
    amrow = C.sb("amrow", [128, 128], F32, st)
    amax2 = C.sb("amax2", [128, S], F32, st)
    runm2 = C.sb("runm2", [128, S], F32, st)
    xgf = [C.sb(f"xgf{i}", [128, 8, 512], F32, st) for i in range(2)]
    m2sb = C.sb("f1m2", [128, 512], F32, st)
    r2sb = C.sb("f1r2", [128, 512], F32, st)
    tmpb = C.sb("f1tmpb", [128, 512], BF16, st)
    lntf = C.sb("f1lnt", [128, 512], F32, st)
    racc = C.sb("racc", [128, 16], F32, st)
    rac1 = C.sb("rac1", [128, 1], F32, st)
    racb = C.sb("racb", [128, 1], BF16, st)
    msw1 = C.sb("msw1", [128, 1], F32, st)
    sw1 = C.sb("sw1", [128, 1], F32, st)
    msw2 = C.sb("msw2", [128, 1], F32, st)
    sw2 = C.sb("sw2", [128, 1], F32, st)

    stX = contextlib.ExitStack()
    xq1 = C.sb("xq1", [128, 8, S], BF16, stX)
    inv1 = C.sb("inv1", [128, S], BF16, stX)
    m_sb = C.sb("f1m", [128, 512], F32, stX)
    r_sb = C.sb("f1r", [128, 512], F32, stX)
    runm1 = C.sb("runm1", [128, 512], F32, stX)
    qs1 = C.sb("qs1", [128, 512], F32, stX)
    xqi = C.sb("xqi", [128, 512], I32, stX)
    sgn1 = C.sb("sgn1", [128, 512], F32, stX)

    pH = C.ps("pH", [128, 512], F32, st)
    pC = C.ps("pC", [128, 512], F32, st)
    pM = C.ps("pM", [128, 512], F32, st)
    pS2 = C.ps("pS2", [128, 512], F32, st)

    C.dma("sync", cb_s[:, :], cb_d, [r(cb_s)])
    C.dma("sync", idf[:, :], idf_d, [r(idf)])
    C.dma("sync", sel16[0:16, :], sel_d, [r(sel16)])

    x_src_r = x_src_dram.rearrange("(a p) c -> p a c", p=128)
    x_dst_r = x_dst_dram.rearrange("(a p) c -> p a c", p=128)
    h2_r = h2_h.rearrange("(a p) c -> p a c", p=128)


    def q1_pmax(src_r, dst_w):
        for blk in range(4):
            P.op("tensor",
                 lambda e, blk=blk: e.matmul(
                     pS2[:, (blk % 4) * 128:(blk % 4) * 128 + 128],
                     runm1[:, blk * 128:(blk + 1) * 128], idf[:, :],
                     start=True, stop=True),
                 reads=src_r + [r(idf)], writes=[r(pS2)], tag="q1t")
            P.op("vector",
                 lambda e, blk=blk: e.reduce_max(
                     amcmp[:, blk:blk + 1],
                     pS2[:, (blk % 4) * 128:(blk % 4) * 128 + 128], AX.X),
                 reads=[r(pS2)], writes=[r(amcmp, blk)], tag="q1r")
        P.op("tensor",
             lambda e: e.matmul(pM[0:4, 0:128], amcmp[:, 0:4],
                                idf[:, :], start=True, stop=True),
             reads=[r(amcmp, bb) for bb in range(4)] + [r(idf)],
             writes=[r(pM)], tag="q1tb")
        P.op("vector", lambda e: e.tensor_copy(amrow[0:4, :],
                                               pM[0:4, 0:128]),
             reads=[r(pM)], writes=[r(amrow)], tag="q1ev")
        for blk in range(4):
            P.op("tensor",
                 lambda e, blk=blk: e.matmul(
                     pS2[:, (blk % 4) * 128:(blk % 4) * 128 + 128],
                     sel16[0:4, blk * 128:(blk + 1) * 128],
                     amrow[0:4, :], start=True, stop=True),
                 reads=[r(sel16), r(amrow)], writes=[r(pS2)], tag="q1b")
            P.op("scalar",
                 lambda e, blk=blk: e.activation(
                     runm1[:, blk * 128:(blk + 1) * 128],
                     pS2[:, (blk % 4) * 128:(blk % 4) * 128 + 128], AF.Copy),
                 reads=[r(pS2)], writes=dst_w, tag="q1be")

    def quant1_group(g):
        gb = g % 2
        c0 = g * 512
        C.dma("sync", xgf[gb][:, :, :], x_src_r[:, :, c0:c0 + 512],
              [r(xgf[gb])], reads=[("dram:" + src_nm, g)], tag=f"qxg{g}")
        emit_ln_stats(C, xgf[gb], lambda pt: [r(xgf[gb])], pS2, C.ones_r, m_sb,
                      r_sb, tmpb, tag="q1ln")
        P.op("vector", lambda e: e.memset(runm1[:, :], 0.0),
             writes=[r(runm1)], tag="q1a0")
        for pt in range(8):
            P.op("vector",
                 lambda e, pt=pt, gb=gb: e.tensor_sub(
                     lntf[:, :], xgf[gb][:, pt, :], m_sb[:, :]),
                 reads=[r(xgf[gb]), r(m_sb)], writes=[r(lntf)], tag="q1s")
            P.op("scalar",
                 lambda e: e.activation(sgn1[:, :], lntf[:, :], AF.Abs),
                 reads=[r(lntf)], writes=[r(sgn1)], tag="q1ab")
            P.op("vector",
                 lambda e: e.tensor_max(runm1[:, :], runm1[:, :], sgn1[:, :]),
                 reads=[r(runm1), r(sgn1)], writes=[r(runm1)], tag="q1a")
        q1_pmax([r(runm1)], [r(runm1)])
        P.op("vector", lambda e: e.tensor_mul(qs1[:, :], runm1[:, :], r_sb[:, :]),
             reads=[r(runm1), r(r_sb)], writes=[r(qs1)], tag="q1t")
        P.op("vector",
             lambda e: e.tensor_scalar(qs1[:, :], qs1[:, :], 1e-5, None, ALU.max),
             reads=[r(qs1)], writes=[r(qs1)], tag="q1c")
        P.op("vector",
             lambda e, c0=c0: e.tensor_scalar(inv1[:, c0:c0 + 512], qs1[:, :],
                                              1.0 / 127.0, None, ALU.mult),
             reads=[r(qs1)], writes=[r(inv1, g)], tag="q1i")
        P.op("scalar", lambda e: e.activation(qs1[:, :], qs1[:, :], AF.Ln),
             reads=[r(qs1)], writes=[r(qs1)], tag="q1l")
        P.op("scalar", lambda e: e.activation(qs1[:, :], qs1[:, :], AF.Exp,
                                              scale=-1.0, bias=C.cv(LN127)),
             reads=[r(qs1), r(C.cvals)], writes=[r(qs1)], tag="q1e")
        P.op("vector", lambda e: e.tensor_mul(qs1[:, :], qs1[:, :], r_sb[:, :]),
             reads=[r(qs1), r(r_sb)], writes=[r(qs1)], tag="q1rs")
        for pt in range(8):
            P.op("vector",
                 lambda e, pt=pt, gb=gb: e.tensor_sub(
                     lntf[:, :], xgf[gb][:, pt, :], m_sb[:, :]),
                 reads=[r(xgf[gb]), r(m_sb)], writes=[r(lntf)], tag="q1s2")
            P.op("vector",
                 lambda e: e.tensor_mul(lntf[:, :], lntf[:, :], qs1[:, :]),
                 reads=[r(lntf), r(qs1)], writes=[r(lntf)], tag="q1m")
            P.op("scalar",
                 lambda e: e.sign(sgn1[:, :], lntf[:, :]),
                 reads=[r(lntf)], writes=[r(sgn1)], tag="q1sg")
            P.op("vector",
                 lambda e: e.scalar_tensor_tensor(
                     lntf[:, :], sgn1[:, :], C.halfsh[:, 0:1], lntf[:, :],
                     ALU.mult, ALU.add),
                 reads=[r(sgn1), r(C.halfsh), r(lntf)], writes=[r(lntf)],
                 tag="q1sh")
            P.op("vector",
                 lambda e: e.tensor_copy(xqi[:, :], lntf[:, :]),
                 reads=[r(lntf)], writes=[r(xqi)], tag="q1r")
            P.op("vector",
                 lambda e, pt=pt, c0=c0: e.tensor_copy(xq1[:, pt, c0:c0 + 512],
                                                       xqi[:, :]),
                 reads=[r(xqi)], writes=[r(xq1, (g, pt))], tag="q1b")

    for g in range(NG):
        quant1_group(g)

    stW = contextlib.ExitStack()
    cds2 = [C.sb(f"cds{i}", [128, 12, 128], BF16, stW) for i in range(2)]
    cvd_r = cvd_d.rearrange("(a p) c -> p a c", p=128)
    w1S = [C.sb(f"w1S{i}", [128, 8, 512], F32, stW) for i in range(2)]
    w1q = [C.sb(f"w1q{i}", [128, 8, 512], BF16, stW) for i in range(2)]
    w1i = C.sb("w1i", [128, 8, 512], I32, stW)
    h_s = [C.sb(f"h_s{i}", [128, S + 2], BF16, stW) for i in range(2)]
    h2m = [C.sb(f"h2m{i}", [128, S], F32, stW) for i in range(2)]
    habs = lntf

    def w_mean_pass(w_r, wS, name):
        for j in range(8):
            sl = j % 2
            C.dma("sync", wS[sl][:, :, :], w_r[:, :, j * 512:(j + 1) * 512],
                  [r(wS[sl])], tag=f"{name}p1_{j}")
            P.op("vector",
                 lambda e, j=j, sl=sl: e.reduce_sum(
                     racc[:, j:j + 1], wS[sl][:, :, :], AX.XY,
                     apply_absolute_value=True),
                 reads=[r(wS[sl])], writes=[r(racc, j)], tag=f"{name}r")
        P.op("vector",
             lambda e: e.reduce_sum(rac1[:, :], racc[:, 0:8], AX.X),
             reads=[r(racc, jj) for jj in range(8)],
             writes=[r(rac1)], tag=f"{name}r2")
        P.op("scalar", lambda e: e.activation(racb[:, :], rac1[:, :], AF.Copy),
             reads=[r(rac1)], writes=[r(racb)], tag=f"{name}cb")
        P.op("tensor",
             lambda e: e.matmul(pM[:, 0:1], C.ones1[:, :], racb[:, :],
                                start=True, stop=True),
             reads=[r(racb), r(C.ones1)], writes=[r(pM)], tag=f"{name}sm")

    def w_scale(msw, sw, name):
        P.op("vector",
             lambda e: e.tensor_scalar(msw[:, :], pM[:, 0:1],
                                       2.384185791015625e-07, 1e-5,
                                       ALU.mult, ALU.max),
             reads=[r(pM)], writes=[r(msw)], tag=f"{name}ms")
        P.op("scalar", lambda e: e.activation(sw[:, :], msw[:, :], AF.Ln),
             reads=[r(msw)], writes=[r(sw)], tag=f"{name}l")
        P.op("scalar", lambda e: e.activation(sw[:, :], sw[:, :], AF.Exp,
                                              scale=-1.0),
             reads=[r(sw)], writes=[r(sw)], tag=f"{name}e")

    w1_r = w1_d.rearrange("(a p) c -> p a c", p=128)
    w_mean_pass(w1_r, w1S, "w1")
    w_scale(msw1, sw1, "w1")

    P.op("vector", lambda e: e.memset(runm2[:, :], 0.0), writes=[r(runm2)],
         tag="rm20")
    for j8 in range(8):
        sl = j8 % 2
        C.dma("sync", w1S[sl][:, :, :], w1_r[:, :, j8 * 512:(j8 + 1) * 512],
              [r(w1S[sl])], tag=f"w1p2_{j8}")
        C.dma("gpsimd", cds2[sl][:, :, :],
              cvd_r[:, j8 * 12:(j8 + 1) * 12, :], [r(cds2[sl])],
              tag=f"cds{j8}")
        P.op("vector",
             lambda e, sl=sl: e.tensor_scalar(w1S[sl][:, :, :], w1S[sl][:, :, :],
                                              sw1[:, 0:1], None, ALU.mult),
             reads=[r(w1S[sl]), r(sw1)], writes=[r(w1S[sl])], tag="w1qm")
        P.op("scalar", lambda e, sl=sl: e.sign(w1i[:, :, :], w1S[sl][:, :, :]),
             reads=[r(w1S[sl])], writes=[r(w1i)], tag="w1sg")
        P.op("vector",
             lambda e, sl=sl: e.scalar_tensor_tensor(
                 w1S[sl][:, :, :], w1i[:, :, :], C.halfsh[:, 0:1],
                 w1S[sl][:, :, :], ALU.mult, ALU.add),
             reads=[r(w1i), r(C.halfsh), r(w1S[sl])], writes=[r(w1S[sl])],
             tag="w1sh")
        P.op("vector", lambda e, sl=sl: e.tensor_copy(w1i[:, :, :],
                                                      w1S[sl][:, :, :]),
             reads=[r(w1S[sl])], writes=[r(w1i)], tag="w1ri")
        P.op("vector",
             lambda e, sl=sl: e.tensor_scalar(w1q[sl][:, :, :], w1i[:, :, :],
                                              1.0, -1.0, ALU.min, ALU.max),
             reads=[r(w1i)], writes=[r(w1q[sl])], tag="w1qc")
        for m4 in range(4):
            m = j8 * 4 + m4
            msl = m % 2
            for ng in range(NG):
                for k in range(8):
                    P.op("tensor",
                         lambda e, m4=m4, k=k, ng=ng, sl=sl: e.matmul(
                             pH[:, :], w1q[sl][:, k, m4 * 128:(m4 + 1) * 128],
                             xq1[:, k, ng * 512:(ng + 1) * 512],
                             start=(k == 0), stop=(k == 7)),
                         reads=[r(w1q[sl]), r(xq1, (ng, k))], writes=[r(pH)],
                         tag="h1mm", chain=(k != 0))
                P.op("vector",
                     lambda e, ng=ng, msl=msl: e.scalar_tensor_tensor(
                         h_s[msl][:, 1 + ng * 512:1 + (ng + 1) * 512], pH[:, :],
                         msw1[:, 0:1], inv1[:, ng * 512:(ng + 1) * 512],
                         ALU.mult, ALU.mult),
                     reads=[r(pH), r(msw1), r(inv1, ng)], writes=[r(h_s[msl])],
                     tag="h1ev")
            P.op("vector", lambda e, msl=msl: e.memset(h_s[msl][:, 0:1], 0.0),
                 writes=[r(h_s[msl])], tag="hz0")
            P.op("vector",
                 lambda e, msl=msl: e.memset(h_s[msl][:, S + 1:S + 2], 0.0),
                 writes=[r(h_s[msl])], tag="hz1")
            for ng in range(NG):
                for tap in range(3):
                    P.op("tensor",
                         lambda e, m4=m4, tap=tap, ng=ng, msl=msl, sl=sl: e.matmul(
                             pC[:, :],
                             cds2[sl][:, m4 * 3 + tap, :],
                             h_s[msl][:, ng * 512 + tap:ng * 512 + tap + 512],
                             start=(tap == 0), stop=(tap == 2)),
                         reads=[r(cds2[sl]), r(h_s[msl])], writes=[r(pC)],
                         tag="cvmm", chain=(tap != 0))
                P.op("scalar",
                     lambda e, m=m, ng=ng, msl=msl: e.activation(
                         h2m[msl][:, ng * 512:(ng + 1) * 512], pC[:, :], AF.Silu,
                         bias=cb_s[:, m:m + 1]),
                     reads=[r(pC), r(cb_s)], writes=[r(h2m[msl], ng)], tag="cvev")
            for ng in range(NG):
                P.op("scalar",
                     lambda e, msl=msl, ng=ng: e.activation(
                         habs[:, :], h2m[msl][:, ng * 512:(ng + 1) * 512], AF.Abs),
                     reads=[r(h2m[msl], ng)], writes=[r(habs)], tag="rm2a")
                P.op("vector",
                     lambda e, ng=ng: e.tensor_max(
                         runm2[:, ng * 512:(ng + 1) * 512],
                         runm2[:, ng * 512:(ng + 1) * 512], habs[:, :]),
                     reads=[r(runm2), r(habs)], writes=[r(runm2)], tag="rm2")
            C.dma("sync", h2_r[:, m, :], h2m[msl][:, :], [("dram:h2", m)],
                  reads=[r(h2m[msl], ng) for ng in range(NG)], tag=f"h2o{m}")


    def q2_pmax(src_r, dst_w):
        NBLK2 = S // 128
        for blk in range(NBLK2):
            P.op("tensor",
                 lambda e, blk=blk: e.matmul(
                     pH[:, (blk % 4) * 128:(blk % 4) * 128 + 128],
                     runm2[:, blk * 128:(blk + 1) * 128], idf[:, :],
                     start=True, stop=True),
                 reads=src_r + [r(idf)], writes=[r(pH)], tag="q2t")
            P.op("vector",
                 lambda e, blk=blk: e.reduce_max(
                     amcmp[:, blk:blk + 1],
                     pH[:, (blk % 4) * 128:(blk % 4) * 128 + 128], AX.X),
                 reads=[r(pH)], writes=[r(amcmp, blk)], tag="q2r")
        P.op("tensor",
             lambda e: e.matmul(pM[0:NBLK2, 0:128], amcmp[:, 0:NBLK2],
                                idf[:, :], start=True, stop=True),
             reads=[r(amcmp, bb) for bb in range(NBLK2)] + [r(idf)],
             writes=[r(pM)], tag="q2tb")
        P.op("vector", lambda e: e.tensor_copy(amrow[0:NBLK2, :],
                                               pM[0:NBLK2, 0:128]),
             reads=[r(pM)], writes=[r(amrow)], tag="q2ev")
        for blk in range(NBLK2):
            P.op("tensor",
                 lambda e, blk=blk: e.matmul(
                     pH[:, (blk % 4) * 128:(blk % 4) * 128 + 128],
                     sel16[0:NBLK2, blk * 128:(blk + 1) * 128],
                     amrow[0:NBLK2, :], start=True, stop=True),
                 reads=[r(sel16), r(amrow)], writes=[r(pH)], tag="q2b")
            P.op("scalar",
                 lambda e, blk=blk: e.activation(
                     amax2[:, blk * 128:(blk + 1) * 128],
                     pH[:, (blk % 4) * 128:(blk % 4) * 128 + 128], AF.Copy),
                 reads=[r(pH)], writes=dst_w, tag="q2be")

    q2_pmax([r(runm2)], [r(amax2)])
    qsc2 = runm2
    P.op("vector",
         lambda e: e.tensor_scalar(amax2[:, :], amax2[:, :], 1e-5, None, ALU.max),
         reads=[r(amax2)], writes=[r(amax2)], tag="q2c")
    P.op("vector",
         lambda e: e.tensor_scalar(qsc2[:, :], amax2[:, :], 1.0 / 127.0, None,
                                   ALU.mult),
         reads=[r(amax2)], writes=[r(qsc2, "inv"), r(runm2)], tag="q2i")
    P.op("scalar", lambda e: e.activation(amax2[:, :], amax2[:, :], AF.Ln),
         reads=[r(amax2)], writes=[r(amax2)], tag="q2l")
    P.op("scalar", lambda e: e.activation(amax2[:, :], amax2[:, :], AF.Exp,
                                          scale=-1.0, bias=C.cv(LN127)),
         reads=[r(amax2), r(C.cvals)], writes=[r(amax2)], tag="q2e")

    P.fence()
    stW.close()
    stX.close()

    stW2 = contextlib.ExitStack()
    w2q = C.sb("w2q", [128, 32, D], BF16, stW2)
    stW2a = contextlib.ExitStack()
    w2S = [C.sb(f"w2S{i}", [128, 2, D], F32, stW2a) for i in range(2)]
    w2i = C.sb("w2i", [128, 2, D], I32, stW2a)
    w2_r = w2_d.rearrange("(a p) c -> p a c", p=128)
    for j in range(16):
        sl = j % 2
        C.dma("sync", w2S[sl][:, :, :], w2_r[:, 2 * j:2 * j + 2, :],
              [r(w2S[sl])], tag=f"w2p1_{j}")
        P.op("vector",
             lambda e, j=j, sl=sl: e.reduce_sum(
                 racc[:, j:j + 1], w2S[sl][:, :, :], AX.XY,
                 apply_absolute_value=True),
             reads=[r(w2S[sl])], writes=[r(racc, j)], tag="w2r")
    P.op("vector",
         lambda e: e.reduce_sum(rac1[:, :], racc[:, :], AX.X),
         reads=[r(racc, jj) for jj in range(16)], writes=[r(rac1)], tag="w2r2")
    P.op("scalar", lambda e: e.activation(racb[:, :], rac1[:, :], AF.Copy),
         reads=[r(rac1)], writes=[r(racb)], tag="w2cb")
    P.op("tensor",
         lambda e: e.matmul(pM[:, 0:1], C.ones1[:, :], racb[:, :],
                            start=True, stop=True),
         reads=[r(racb), r(C.ones1)], writes=[r(pM)], tag="w2sm")
    w_scale(msw2, sw2, "w2")
    for j in range(16):
        sl = j % 2
        C.dma("sync", w2S[sl][:, :, :], w2_r[:, 2 * j:2 * j + 2, :],
              [r(w2S[sl])], tag=f"w2p2_{j}")
        P.op("vector",
             lambda e, sl=sl: e.tensor_scalar(w2S[sl][:, :, :], w2S[sl][:, :, :],
                                              sw2[:, 0:1], None, ALU.mult),
             reads=[r(w2S[sl]), r(sw2)], writes=[r(w2S[sl])], tag="w2qm")
        P.op("scalar", lambda e, sl=sl: e.sign(w2i[:, :, :], w2S[sl][:, :, :]),
             reads=[r(w2S[sl])], writes=[r(w2i)], tag="w2sg")
        P.op("vector",
             lambda e, sl=sl: e.scalar_tensor_tensor(
                 w2S[sl][:, :, :], w2i[:, :, :], C.halfsh[:, 0:1],
                 w2S[sl][:, :, :], ALU.mult, ALU.add),
             reads=[r(w2i), r(C.halfsh), r(w2S[sl])], writes=[r(w2S[sl])],
             tag="w2sh")
        P.op("vector", lambda e, sl=sl: e.tensor_copy(w2i[:, :, :],
                                                      w2S[sl][:, :, :]),
             reads=[r(w2S[sl])], writes=[r(w2i)], tag="w2ri")
        P.op("vector",
             lambda e, j=j: e.tensor_scalar(w2q[:, 2 * j:2 * j + 2, :],
                                            w2i[:, :, :], 1.0, -1.0,
                                            ALU.min, ALU.max),
             reads=[r(w2i)], writes=[r(w2q, j // 2)], tag="w2qc")

    P.fence()
    stW2a.close()

    xq2 = C.sb("xq2", [128, 32, 512], BF16, stW2)
    h2r = [C.sb(f"h2r{i}", [128, 4, 512], F32, stW2) for i in range(2)]
    h2i = C.sb("h2i", [128, 4, 512], I32, stW2)
    oT2 = C.sb("f1oT", [128, 8, 512], BF16, stW2)
    for g in range(NG):
        c0 = g * 512
        gb = g % 2
        for m8 in range(8):
            sl = m8 % 2
            C.dma("sync", h2r[sl][:, :, :],
                  h2_r[:, 4 * m8:4 * m8 + 4, c0:c0 + 512], [r(h2r[sl])],
                  reads=[("dram:h2", 4 * m8 + i_) for i_ in range(4)],
                  tag=f"h2i{g}_{m8}")
            P.op("vector",
                 lambda e, sl=sl, c0=c0: e.tensor_tensor(
                     h2r[sl][:, :, :], h2r[sl][:, :, :],
                     sap(amax2, 0, 128, c0, [[0, 4], [1, 512]]), ALU.mult),
                 reads=[r(h2r[sl]), r(amax2)], writes=[r(h2r[sl])], tag="q2m")
            P.op("scalar", lambda e, sl=sl: e.sign(h2i[:, :, :], h2r[sl][:, :, :]),
                 reads=[r(h2r[sl])], writes=[r(h2i)], tag="q2sg")
            P.op("vector",
                 lambda e, sl=sl: e.scalar_tensor_tensor(
                     h2r[sl][:, :, :], h2i[:, :, :], C.halfsh[:, 0:1],
                     h2r[sl][:, :, :], ALU.mult, ALU.add),
                 reads=[r(h2i), r(C.halfsh), r(h2r[sl])], writes=[r(h2r[sl])],
                 tag="q2sh")
            P.op("vector", lambda e, sl=sl: e.tensor_copy(h2i[:, :, :],
                                                          h2r[sl][:, :, :]),
                 reads=[r(h2r[sl])], writes=[r(h2i)], tag="q2ri")
            P.op("vector",
                 lambda e, m8=m8: e.tensor_copy(xq2[:, 4 * m8:4 * m8 + 4, :],
                                                h2i[:, :, :]),
                 reads=[r(h2i)], writes=[r(xq2, m8)], tag="q2b")
        for j in range(8):
            for k in range(32):
                P.op("tensor",
                     lambda e, j=j, k=k: e.matmul(
                         pH[:, :], w2q[:, k, j * 128:(j + 1) * 128],
                         xq2[:, k, :], start=(k == 0), stop=(k == 31)),
                     reads=[r(w2q, k // 4), r(xq2, k // 4)], writes=[r(pH)],
                     tag="o2mm", chain=(k != 0))
            P.op("vector",
                 lambda e, j=j, c0=c0: e.scalar_tensor_tensor(
                     oT2[:, j, :], pH[:, :], msw2[:, 0:1],
                     qsc2[:, c0:c0 + 512], ALU.mult, ALU.mult),
                 reads=[r(pH), r(msw2), r(qsc2, "inv")], writes=[r(oT2, j)],
                 tag="o2ev")
        emit_ln_stats(C, oT2, lambda pt: [r(oT2, pt)], pS2, C.ones_r, m2sb, r2sb,
                      tmpb, tag="f1ln2")
        C.dma("sync", xgf[gb][:, :, :], x_src_r[:, :, c0:c0 + 512],
              [r(xgf[gb])], reads=[("dram:" + src_nm, g)], tag=f"fxr2{g}")
        for pt in range(8):
            P.op("vector",
                 lambda e, pt=pt: e.tensor_sub(lntf[:, :], oT2[:, pt, :],
                                               m2sb[:, :]),
                 reads=[r(oT2, pt), r(m2sb)], writes=[r(lntf)], tag="fb1")
            P.op("vector",
                 lambda e, pt=pt: e.tensor_mul(lntf[:, :], lntf[:, :], r2sb[:, :]),
                 reads=[r(lntf), r(r2sb)], writes=[r(lntf)], tag="fb2")
            P.op("vector",
                 lambda e, pt=pt, gb=gb: e.tensor_add(
                     xgf[gb][:, pt, :], xgf[gb][:, pt, :], lntf[:, :]),
                 reads=[r(lntf), r(xgf[gb])], writes=[r(xgf[gb])], tag="fb3")
        C.dma("sync", x_dst_r[:, :, c0:c0 + 512], xgf[gb][:, :, :],
              [("dram:" + dst_nm, g)], reads=[r(xgf[gb])], tag=f"fxo2{g}")

    P.fence()
    stW2.close()
    st.close()


_CACHE = {}
PHASES = ("l0a", "l0f", "l1a", "l1f")


def _build(S, phases):
    C = Ctx(S)
    nc = C.nc
    n = len(phases)
    x_in = C.dram_in("x_in", [D, S])
    x_out = C.dram_out("x_out", [D, S])
    xres = C.dram_tmp("xres", [D, S]) if n > 1 else None
    onr_d = C.dram_in("ones_r", [128, 128], BF16)
    on1_d = C.dram_in("ones_1", [128, 128], BF16)
    cv_d = C.dram_in("consts", [128, len(CONSTS)])
    C.ones_r = C.sb("ones_r_g", [128, 128], BF16)
    C.ones1 = C.sb("ones_1_g", [128, 128], BF16)
    C.cvals = C.sb("cvals_g", [128, len(CONSTS)], F32)
    C.dma("sync", C.ones_r[:, :], onr_d, [r(C.ones_r)])
    C.dma("sync", C.ones1[:, :], on1_d, [r(C.ones1)])
    C.dma("sync", C.cvals[:, :], cv_d, [r(C.cvals)])
    C.halfsh = C.sb("halfsh_g", [128, 1], F32)
    _hstmp = C.sb("hstmp_g", [128, 1], I32)
    C.plan.op("vector",
              lambda e: e.tensor_copy(_hstmp[:, :], C.cvals[:, 6:7]),
              reads=[r(C.cvals)], writes=[r(_hstmp)], tag="shdet1")
    C.plan.op("vector",
              lambda e: e.tensor_scalar(C.halfsh[:, :], _hstmp[:, :],
                                        -0.5, 0.5, ALU.mult, ALU.add),
              reads=[r(_hstmp)], writes=[r(C.halfsh)], tag="shdet2")
    sems, dma_pools = {}, {}
    sem_stack = contextlib.ExitStack()
    KPOOL = {"sync": 12, "gpsimd": 12, "scalar": 8, "vector": 2, "tensor": 2}
    for e in ENGINES:
        sems[e] = sem_stack.enter_context(nc.semaphore("s_" + e))
        dma_pools[e] = [sem_stack.enter_context(nc.semaphore(f"d_{e}{i}"))
                        for i in range(KPOOL[e])]
    for i, ph in enumerate(phases):
        src = x_in if i == 0 else xres
        dst = x_out if i == n - 1 else xres
        src_nm = "x_in" if i == 0 else "xres"
        dst_nm = "x_out" if i == n - 1 else "xres"
        if ph == "l0a":
            emit_l0a(C, src, dst, src_nm, dst_nm)
        elif ph == "l0f":
            emit_l0f(C, src, dst, src_nm, dst_nm)
        elif ph == "l1a":
            emit_l1a(C, src, dst, src_nm, dst_nm)
        elif ph == "l1f":
            emit_l1f(C, src, dst, src_nm, dst_nm)
        else:
            raise ValueError(ph)
    with nc.Block() as block:
        C.plan.resolve_and_emit(block, sems, dma_pools)

        @block.sync
        def _(sy):
            for e in ENGINES:
                nd = sum(1 for o in C.plan.ops[e] if o.dma)
                K = len(dma_pools[e])
                for s in range(min(nd, K)):
                    uses = (nd - s + K - 1) // K
                    sy.wait_ge(dma_pools[e][s], 16 * uses)
    sem_stack.close()
    C.stack.close()
    return C


def _host_inputs(ins, S, phases):
    import ml_dtypes
    bf = ml_dtypes.bfloat16
    x = np.asarray(ins["L0_attn_x"])
    B = x.shape[0]
    common = {}
    common["ones_r"] = np.full((128, 128), 1.0 / 1024.0, dtype=bf)
    common["ones_1"] = np.ones((128, 128), dtype=bf)
    common["consts"] = np.ascontiguousarray(
        np.broadcast_to(np.asarray(CONSTS, np.float32), (128, len(CONSTS))))
    if "l0a" in phases:
        wq_f = np.asarray(ins["L0_attn_Wqkv"], dtype=np.float32)
        mm_ = np.arange(1024)
        qcols = (2 * (mm_ // 128) + (mm_ % 128) // 64) * 192 + (mm_ % 64)
        vcols = (mm_ // 64) * 192 + 128 + (mm_ % 64)
        cols = np.concatenate([qcols, qcols + 64, vcols])
        common["wqkv"] = np.ascontiguousarray(wq_f[:, cols])
        wo_f = np.asarray(ins["L0_attn_Wo"], dtype=np.float32)
        kk = np.arange(8)[:, None]
        pp = np.arange(128)[None, :]
        perm = (512 * (pp // 64) + 64 * kk + (pp % 64)).reshape(-1)
        common["wo"] = np.ascontiguousarray(wo_f[perm])
        bqkv = np.asarray(ins["L0_attn_bqkv"], dtype=np.float32)
        idx = np.zeros((128, 8), np.int64)
        for i in range(8):
            for p in range(128):
                idx[p, i] = (2 * i + p // 64) * 192 + (p % 64)
        common["bq_sb"] = np.ascontiguousarray(bqkv[idx])
        common["bk_sb"] = np.ascontiguousarray(bqkv[idx + 64])
        bv = bqkv.reshape(16, 192)[:, 128:]
        common["bvn_bc"] = np.ascontiguousarray(
            np.broadcast_to(bv.reshape(1, 16 * 64), (128, 16 * 64)).astype(np.float32))
        common["bo_sb"] = np.ascontiguousarray(
            np.asarray(ins["L0_attn_bo"], dtype=np.float32).reshape(8, 128).T)
        rr = np.arange(128)[:, None]
        cc = np.arange(128)[None, :]
        common["maskB"] = np.where(np.abs(rr - cc) <= 16, 0.0, NEG).astype(np.float32)
        a = np.arange(32)[:, None]
        c2 = np.arange(128)[None, :]
        common["maskA"] = np.where((c2 >= a - 48) & (c2 <= a - 16), 0.0,
                                   NEG).astype(np.float32)
        common["maskC"] = np.where(c2 >= a + 112, 0.0, NEG).astype(np.float32)
    if "l1a" in phases:
        wq_f = np.asarray(ins["L1_attn_Wq"], np.float32)
        common["wq_t"] = np.ascontiguousarray(wq_f.T)
        common["wk_t"] = np.ascontiguousarray(
            np.asarray(ins["L1_attn_Wk"], np.float32).T)
        common["wv_t"] = np.ascontiguousarray(
            np.asarray(ins["L1_attn_Wv"], np.float32).T)
        wf_f = np.asarray(ins["L1_attn_Wf"], np.float32).T
        kk = np.arange(8)[:, None]
        pp = np.arange(128)[None, :]
        perm = (512 * (pp // 64) + 64 * kk + (pp % 64)).reshape(-1)
        common["wf_t"] = np.ascontiguousarray(wf_f[perm])
        common["bf_sb"] = np.ascontiguousarray(
            np.asarray(ins["L1_attn_bf"], np.float32).reshape(8, 128).T)
        rr = np.arange(128)[:, None]
        cc = np.arange(128)[None, :]
        common["maskN"] = np.where(cc >= rr, 0.0, NEG).astype(np.float32)
        common["maskO"] = np.where(cc <= rr, 0.0, NEG).astype(np.float32)
    if "l1f" in phases:
        common["ident_f"] = np.eye(128, dtype=np.float32)
        s16 = np.zeros((16, 16 * 128), np.float32)
        for b_ in range(16):
            s16[b_, b_ * 128:(b_ + 1) * 128] = 1.0
        common["sel16f"] = s16
        common["w1_t"] = np.ascontiguousarray(
            np.asarray(ins["L1_ffn_w1"], np.float32).T)
        common["w2_t"] = np.ascontiguousarray(
            np.asarray(ins["L1_ffn_w2"], np.float32).T)
        cw_ = np.asarray(ins["L1_ffn_conv_w"], np.float32)[:, 0, :]
        cd = np.zeros((32, 3, 128, 128), np.float32)
        for m_ in range(32):
            for tp_ in range(3):
                np.fill_diagonal(cd[m_, tp_], cw_[m_ * 128:(m_ + 1) * 128, tp_])
        common["conv_diag"] = np.ascontiguousarray(cd.reshape(32 * 3 * 128, 128))
        common["convb_sb"] = np.ascontiguousarray(
            np.asarray(ins["L1_ffn_conv_b"], np.float32).reshape(32, 128).T)
    if "l0f" in phases:
        common["gate_wT"] = np.ascontiguousarray(
            np.asarray(ins["L0_ffn_gate_w"], np.float32).T)
        common["gbias_bc"] = np.ascontiguousarray(np.broadcast_to(
            np.asarray(ins["L0_ffn_gate_bias"], np.float32)[None, :], (128, 16)))
        es = np.zeros((16, 16 * 128), np.float32)
        for e_ in range(16):
            es[e_, e_ * 128:(e_ + 1) * 128] = 1.0
        common["esel"] = es.astype(bf)
        common["ident_bf"] = np.eye(128, dtype=np.float32).astype(bf)
        common["wsg"] = np.ascontiguousarray(
            np.asarray(ins["L0_ffn_Ws_g"], np.float32).T)
        common["wsu"] = np.ascontiguousarray(
            np.asarray(ins["L0_ffn_Ws_u"], np.float32).T)
        common["wsd"] = np.ascontiguousarray(
            np.asarray(ins["L0_ffn_Ws_d"], np.float32).T)
        common["wg_t"] = np.ascontiguousarray(
            np.asarray(ins["L0_ffn_Wg"], np.float32).transpose(0, 2, 1))
        common["wu_t"] = np.ascontiguousarray(
            np.asarray(ins["L0_ffn_Wu"], np.float32).transpose(0, 2, 1))
        common["wd_t"] = np.ascontiguousarray(
            np.asarray(ins["L0_ffn_Wd"], np.float32).transpose(0, 2, 1))
    in_maps = []
    for b in range(B):
        m = dict(common)
        m["x_in"] = np.ascontiguousarray(x[b].T.astype(np.float32))
        in_maps.append(m)
    return in_maps


def kernel(**ins):
    x = np.asarray(ins["L0_attn_x"])
    S = x.shape[1]
    key = (S, PHASES)
    if key not in _CACHE:
        _CACHE[key] = _build(S, PHASES)
    C = _CACHE[key]
    in_maps = _host_inputs(ins, S, PHASES)
    res = run_bass_kernel_spmd(C.nc, in_maps, core_ids=list(range(len(in_maps))))
    out = np.stack([res.results[b]["x_out"].T for b in range(len(in_maps))])
    return np.ascontiguousarray(out.astype(np.float32))
